# version 01
# baseline (speedup 1.0000x reference)
from contextlib import ExitStack

import numpy as np

import concourse.bass as bass
import concourse.mybir as mybir
from concourse.bass_utils import run_bass_kernel_spmd

F32 = mybir.dt.float32
BF16 = mybir.dt.bfloat16
AF = mybir.ActivationFunctionType
ALU = mybir.AluOpType
AX = mybir.AxisListType.X

NCORES = 8
T = 4096
D = 1024
NCH = D // 128


class Sem:
    def __init__(self, h):
        self.h = h
        self.n = 0

    def inc(self, k=1):
        self.n += k
        return (self.h, k)

    def now(self):
        return (self.h, self.n)


class Rec:
    ENGS = ("tensor", "vector", "scalar", "gpsimd", "sync")

    def __init__(self):
        self.ops = {e: [] for e in self.ENGS}

    def op(self, eng, fn, wait=(), inc=None):
        self.ops[eng].append((fn, tuple(wait), inc))

    def t(self, fn, wait=(), inc=None):
        self.op("tensor", fn, wait, inc)

    def v(self, fn, wait=(), inc=None):
        self.op("vector", fn, wait, inc)

    def s(self, fn, wait=(), inc=None):
        self.op("scalar", fn, wait, inc)

    def g(self, fn, wait=(), inc=None):
        self.op("gpsimd", fn, wait, inc)

    def y(self, fn, wait=(), inc=None):
        self.op("sync", fn, wait, inc)

    def emit(self, block):
        for ename in self.ENGS:
            ops = self.ops[ename]
            if not ops:
                continue

            def make(ops):
                def _(eng):
                    cur = {}
                    for fn, waits, inc in ops:
                        for (sem, v) in waits:
                            if v > 0 and cur.get(sem, -1) < v:
                                eng.wait_ge(sem, v)
                                cur[sem] = v
                        ins = fn(eng)
                        if inc is not None:
                            sem, k = inc
                            ins.then_inc(sem, k)
                return _

            getattr(block, ename)(make(ops))


class Bld:

    def __init__(self, name):
        self.name = name
        self.nc = bass.Bass("TRN2", target_bir_lowering=False)
        self.stack = ExitStack()
        self.r = Rec()
        self.feeds = {}
        self.outs = []
        self._semid = 0

    def sem(self):
        self._semid += 1
        h = self.stack.enter_context(self.nc.semaphore("sm%d" % self._semid))
        return Sem(h)

    def sbuf(self, name, shape, dt):
        return self.stack.enter_context(self.nc.sbuf_tensor(name, shape, dt))

    def psum(self, name, shape, dt=F32):
        return self.stack.enter_context(self.nc.psum_tensor(name, shape, dt))

    def din(self, name, arr, dtype=F32):
        a0 = arr[0] if isinstance(arr, list) else arr
        self.feeds[name] = arr
        return self.nc.dram_tensor(name, list(a0.shape), dtype, kind="ExternalInput").ap()

    def dout(self, name, shape, dtype=F32):
        self.outs.append(name)
        return self.nc.dram_tensor(name, shape, dtype, kind="ExternalOutput").ap()

    def finish(self):
        with self.nc.Block() as blk:
            self.r.emit(blk)
        self.stack.close()
        return self

    def in_maps(self, n=NCORES):
        return [{k: (v[c] if isinstance(v, list) else v) for k, v in self.feeds.items()}
                for c in range(n)]

    def run(self, cores=None, **kw):
        cores = cores if cores is not None else list(range(NCORES))
        maps = self.in_maps(len(cores))
        return run_bass_kernel_spmd(self.nc, maps, core_ids=cores, **kw)


class Pool:

    def __init__(self, tiles):
        self.tiles = tiles
        self.war = [None] * len(tiles)
        self.i = 0

    def take(self):
        i = self.i
        self.i = (self.i + 1) % len(self.tiles)
        w = self.war[i]
        return i, self.tiles[i], ([w] if w is not None else [])

    def done(self, i, dep):
        self.war[i] = dep


def MM(t, out, lhsT, rhs, start, stop):
    return t.matmul(out, lhsT, rhs, start=start, stop=stop, skip_group_check=True)


def common_consts():
    c = {}
    c["ones_row"] = np.ones((1, 128), np.float32)
    c["ones_row_b"] = np.ones((1, 128), np.float32)
    c["ones_col"] = np.ones((128, 1), np.float32)
    c["ident"] = np.eye(128, dtype=np.float32)
    c["triu"] = np.triu(np.ones((128, 128), np.float32))
    p = np.arange(128, dtype=np.float32)[:, None] + 128.0 * np.arange(32, dtype=np.float32)[None, :]
    c["pos_tm"] = p + 1.0
    c["cvals"] = np.array([[1.0, 1e-5]], np.float32)
    return c


def reg_consts(b, names, ld):
    cc = common_consts()
    out = {}
    for nm, dt in names:
        arr = cc[nm]
        ap = b.din("c_" + nm, np.ascontiguousarray(arr, np.float32))
        tile = b.sbuf("sb_" + nm, list(arr.shape), dt)
        b.r.g(lambda g, t=tile, a=ap: g.dma_start(out=t[:, :], in_=a[:, :]),
              inc=ld.inc(16))
        out[nm] = tile
    return out


def layernorm_rows(b, rows4, psS, psQ, dep_mm, ev, es, eps_ap=None):
    r = b.r
    mu = rows4[0:1, 0:512]
    var = rows4[0:1, 512:1024]
    rstd = rows4[0:1, 1024:1536]
    brow = rows4[0:1, 1536:2048]
    r.v(lambda v, mu=mu, psS=psS: v.tensor_scalar(mu, psS[0:1, 0:512], 1.0 / D, None, ALU.mult),
        wait=[dep_mm])
    r.v(lambda v, var=var, psQ=psQ: v.tensor_scalar(var, psQ[0:1, 0:512], 1.0 / D, None, ALU.mult),
        inc=ev.inc())
    r.s(lambda s, brow=brow, mu=mu: s.activation(brow, mu, AF.Square), wait=[ev.now()], inc=es.inc())
    r.v(lambda v, var=var, brow=brow: v.tensor_tensor(var, var, brow, ALU.subtract),
        wait=[es.now()], inc=ev.inc())
    r.s(lambda s, var=var, e=eps_ap: s.activation(var, var, AF.Ln, bias=e), wait=[ev.now()])
    r.s(lambda s, rstd=rstd, var=var: s.activation(rstd, var, AF.Exp, scale=-0.5), inc=es.inc())
    r.v(lambda v, brow=brow, mu=mu, rstd=rstd: v.tensor_tensor(brow, mu, rstd, ALU.mult),
        wait=[es.now()])
    r.v(lambda v, brow=brow: v.tensor_scalar(brow, brow, -1.0, None, ALU.mult), inc=ev.inc())
    return rstd, brow


def build_blockA(x_percore, qkv_w, proj_w, alpha_log, beta_log, dbg=False):
    b = Bld("blockA")
    r = b.r

    x_in = b.din("x", x_percore)
    wq_in = b.din("wqkv", np.ascontiguousarray(qkv_w.T))
    wp_in = b.din("wproj", np.ascontiguousarray(proj_w.T))
    al_in = b.din("alpha", np.ascontiguousarray(alpha_log.reshape(1, 16)))
    be_in = b.din("beta", np.ascontiguousarray(beta_log.reshape(1, 16)))
    x1_out = b.dout("x1", [D, T], F32)

    ld = b.sem()
    C = reg_consts(b, [("ones_row", F32), ("ones_col", BF16), ("ident", BF16),
                       ("triu", BF16), ("pos_tm", F32), ("cvals", F32)], ld)

    wq = [b.sbuf("wq%d" % i, [128, 3072], BF16) for i in range(NCH)]
    wp = [b.sbuf("wp%d" % i, [128, 1024], BF16) for i in range(NCH)]
    for i in range(NCH):
        r.g(lambda g, t=wq[i], i=i: g.dma_start(out=t[:, :], in_=wq_in[i * 128:(i + 1) * 128, :]),
            inc=ld.inc(16))
        r.g(lambda g, t=wp[i], i=i: g.dma_start(out=t[:, :], in_=wp_in[i * 128:(i + 1) * 128, :]),
            inc=ld.inc(16))
    alr = b.sbuf("alr", [1, 16], F32)
    ber = b.sbuf("ber", [1, 16], F32)
    r.g(lambda g: g.dma_start(out=alr[:, :], in_=al_in[:, :]), inc=ld.inc(16))
    r.g(lambda g: g.dma_start(out=ber[:, :], in_=be_in[:, :]), inc=ld.inc(16))

    ldx = b.sem()
    mm = b.sem()
    ev = b.sem()
    es = b.sem()
    st = b.sem()

    xs = b.sbuf("xs", [128, 8 * 1024], F32)
    lnx = b.sbuf("lnx", [128, 8 * 1024], BF16)
    xbs = b.sbuf("xbs", [128, 512], BF16)
    sqs = b.sbuf("sqs", [128, 512], BF16)
    tmp32 = b.sbuf("tmp32", [128, 512], F32)
    qphi = b.sbuf("qphi", [128, 8 * 1024], BF16)
    kphi = b.sbuf("kphi", [128, 8 * 1024], BF16)
    vtm = b.sbuf("vtm", [128, 8 * 1024], BF16)
    ofm = b.sbuf("ofm", [128, 8 * 1024], BF16)
    S32 = b.sbuf("S32", [64, 1024], F32)
    S_sb = b.sbuf("S_sb", [64, 1024], BF16)
    stage_q = b.sbuf("stage_q", [64, 128], BF16)
    stage_k = b.sbuf("stage_k", [64, 128], BF16)
    fkT = [b.sbuf("fkT%d" % i, [128, 64], BF16) for i in range(2)]
    At_sb = [b.sbuf("At%d" % i, [128, 128], BF16) for i in range(2)]
    otm = b.sbuf("otm", [128, 64], BF16)
    rows4 = b.sbuf("rows4", [1, 4 * 512], F32)
    sc16 = b.sbuf("sc16", [1, 16], F32)
    sp16 = b.sbuf("sp16", [1, 16], F32)
    scm16 = b.sbuf("scm16", [1, 16], F32)
    scf16 = b.sbuf("scf16", [1, 16], F32)
    rdn_t1 = b.sbuf("rdn_t1", [128, 32], F32)
    rdn_t2 = b.sbuf("rdn_t2", [128, 32], F32)
    scb = b.sbuf("scb", [128, 16], F32)
    rdn = b.sbuf("rdn", [128, 16 * 32], F32)

    pA = b.psum("pA", [128, 6 * 512], F32)
    pT = b.psum("pT", [128, 2048], BF16)
    pool = Pool([pA[:, i * 512:(i + 1) * 512] for i in range(4)])
    poolD = Pool([pA[:, i * 512:(i + 1) * 512] for i in range(4, 6)])
    poolT = Pool([pT[:, 0:128], pT[:, 1024:1152]])

    wld = ld.now()
    r.s(lambda s: s.activation(sc16[:, :], alr[:, :], AF.Exp), wait=[wld])
    r.s(lambda s: s.activation(sc16[:, :], sc16[:, :], AF.Ln, bias=C["cvals"][0:1, 0:1]))
    r.s(lambda s: s.activation(sp16[:, :], ber[:, :], AF.Exp))
    r.s(lambda s: s.activation(sp16[:, :], sp16[:, :], AF.Ln, bias=C["cvals"][0:1, 0:1]),
        inc=es.inc())
    r.v(lambda v: v.tensor_scalar(sc16[:, :], sc16[:, :], 1e-6, None, ALU.add), wait=[es.now()],
        inc=ev.inc())
    r.v(lambda v: v.tensor_scalar(sp16[:, :], sp16[:, :], 1e-6, None, ALU.add),
        wait=[ev.now()], inc=ev.inc())
    r.v(lambda v: v.tensor_tensor(scm16[:, :], sc16[:, :], sp16[:, :], ALU.mult),
        wait=[ev.now()], inc=ev.inc())
    r.v(lambda v: v.tensor_scalar(scf16[:, :], scm16[:, :], 8.0, None, ALU.mult),
        wait=[ev.now()], inc=ev.inc())
    i0, ps0, w0 = pool.take()
    r.t(lambda t, p=ps0: MM(t, p[:, 0:16], C["ones_row"][:, :], scf16[:, :], True, True),
        wait=[ev.now()] + w0, inc=mm.inc())
    r.v(lambda v, p=ps0: v.tensor_copy(scb[:, :], p[:, 0:16]), wait=[mm.now()], inc=ev.inc())
    pool.done(i0, ev.now())
    for h in range(16):
        sl = rdn[:, h * 32:(h + 1) * 32]
        r.v(lambda v, h=h: v.tensor_scalar(rdn_t1[:, :], C["pos_tm"][:, :], scb[:, h:h + 1],
                                           None, ALU.mult), wait=[ev.now()], inc=ev.inc())
        r.v(lambda v: v.tensor_scalar(rdn_t2[:, :], rdn_t1[:, :], 1e-6, None, ALU.max),
            wait=[ev.now()], inc=ev.inc())
        r.v(lambda v, sl=sl: v.reciprocal(sl, rdn_t2[:, :]), wait=[ev.now()], inc=ev.inc())
    r.v(lambda v: v.memset(S32[:, :], 0.0))
    r.v(lambda v: v.memset(S_sb[:, :], 0.0), inc=ev.inc())

    if dbg:
        d_scf = b.dout("d_scf", [1, 16], F32)
        d_sc16 = b.dout("d_sc16", [1, 16], F32)
        d_sp16 = b.dout("d_sp16", [1, 16], F32)
        r.y(lambda y: y.dma_start(out=d_scf[:, :], in_=scf16[:, :]), wait=[ev.now()],
            inc=st.inc(16))
        r.y(lambda y: y.dma_start(out=d_sc16[:, :], in_=sc16[:, :]), inc=st.inc(16))
        r.y(lambda y: y.dma_start(out=d_sp16[:, :], in_=sp16[:, :]), inc=st.inc(16))
        d_scb = b.dout("d_scb", [128, 16], F32)
        d_rdn = b.dout("d_rdn", [128, 512], F32)
        d_lnx = b.dout("d_lnx", [128, 8192], F32)
        d_qphi = b.dout("d_qphi", [128, 8192], F32)
        d_kphi = b.dout("d_kphi", [128, 8192], F32)
        d_vtm = b.dout("d_vtm", [128, 8192], F32)
        d_ofm = b.dout("d_ofm", [128, 8192], F32)
        d_S = b.dout("d_S", [64, 1024], F32)
        r.y(lambda y: y.dma_start(out=d_scb[:, :], in_=scb[:, :]), wait=[ev.now()],
            inc=st.inc(16))
        r.y(lambda y: y.dma_start(out=d_rdn[:, :], in_=rdn[:, :]), inc=st.inc(16))

    xview = x_in.rearrange("(c p) t -> p c t", p=128)
    oview = x1_out.rearrange("(c p) t -> p c t", p=128)

    for sti in range(4):
        t0 = sti * 1024
        barrier = [mm.now(), ev.now(), es.now(), st.now(), ld.now()]
        r.y(lambda y, t0=t0: y.dma_start(out=xs[:, :].rearrange("p (c t) -> p c t", t=1024),
                                         in_=xview[:, :, t0:t0 + 1024]),
            wait=barrier, inc=ldx.inc(16))
        xw = ldx.now()

        for tau in range(2):
            def xsl(c, tau=tau):
                return xs[:, c * 1024 + tau * 512: c * 1024 + tau * 512 + 512]

            def lsl(c, tau=tau):
                return lnx[:, c * 1024 + tau * 512: c * 1024 + tau * 512 + 512]

            iS, psS_, wS = pool.take()
            iQ, psQ_, wQ = pool.take()
            for c in range(NCH):
                r.v(lambda v, x=xsl(c): v.tensor_copy(xbs[:, :], x),
                    wait=[xw, mm.now(), es.now()] + (wS + wQ if c == 0 else []), inc=ev.inc())
                r.s(lambda s: s.activation(sqs[:, :], xbs[:, :], AF.Square),
                    wait=[ev.now(), mm.now()], inc=es.inc())
                r.t(lambda t, c=c, p=psS_: MM(t, p[0:1, 0:512], C["ones_col"][:, :], xbs[:, :],
                                              c == 0, c == NCH - 1),
                    wait=[ev.now()])
                r.t(lambda t, c=c, p=psQ_: MM(t, p[0:1, 0:512], C["ones_col"][:, :], sqs[:, :],
                                              c == 0, c == NCH - 1),
                    wait=[es.now()], inc=mm.inc())
            rstd, brow = layernorm_rows(b, rows4, psS_, psQ_, mm.now(), ev, es,
                                        eps_ap=C["cvals"][0:1, 1:2])
            pool.done(iS, ev.now())
            pool.done(iQ, ev.now())
            iA, psA_, wA = pool.take()
            iB, psB_, wB = pool.take()
            r.t(lambda t, p=psA_, q=rstd: MM(t, p[:, 0:512], C["ones_row"][:, :], q, True, True),
                wait=[ev.now()] + wA)
            r.t(lambda t, p=psB_, q=brow: MM(t, p[:, 0:512], C["ones_row"][:, :], q, True, True),
                wait=wB, inc=mm.inc())
            mw2 = mm.now()
            for c in range(NCH):
                r.v(lambda v, x=xsl(c), p=psA_: v.tensor_tensor(tmp32[:, :], x, p[:, 0:512], ALU.mult),
                    wait=[mw2])
                r.v(lambda v, o=lsl(c), p=psB_: v.tensor_tensor(o, tmp32[:, :], p[:, 0:512], ALU.add),
                    inc=(ev.inc() if c == NCH - 1 else None))
            pool.done(iA, ev.now())
            pool.done(iB, ev.now())
            lnw = ev.now()

            for qk in range(2):
                dst = qphi if qk == 0 else kphi
                for oc in range(8):
                    iP, psP_, wP = pool.take()
                    col = qk * 1024 + oc * 128
                    for dc in range(NCH):
                        r.t(lambda t, dc=dc, col=col, p=psP_, q=lsl(dc):
                            MM(t, p[:, 0:512], wq[dc][:, col:col + 128], q,
                               dc == 0, dc == NCH - 1),
                            wait=([lnw] + wP) if dc == 0 else [],
                            inc=(mm.inc() if dc == NCH - 1 else None))
                    r.v(lambda v, d=dst[:, oc * 1024 + tau * 512: oc * 1024 + tau * 512 + 512],
                        p=psP_:
                        v.tensor_scalar(d, p[:, 0:512], 0.0, 1e-6, ALU.max, ALU.add),
                        wait=[mm.now()], inc=ev.inc())
                    pool.done(iP, ev.now())
            for sub in range(4):
                cg = tau * 4 + sub
                for half in range(2):
                    iP, psP_, wP = pool.take()
                    for dc in range(NCH):
                        r.t(lambda t, dc=dc, cg=cg, half=half, p=psP_:
                            MM(t, p[:, 0:512],
                               lnx[:, dc * 1024 + cg * 128: dc * 1024 + cg * 128 + 128],
                               wq[dc][:, 2048 + half * 512: 2048 + half * 512 + 512],
                               dc == 0, dc == NCH - 1),
                            wait=([lnw] + wP) if dc == 0 else [],
                            inc=(mm.inc() if dc == NCH - 1 else None))
                    r.v(lambda v, d=vtm[:, cg * 1024 + half * 512: cg * 1024 + half * 512 + 512],
                        p=psP_: v.tensor_copy(d, p[:, 0:512]),
                        wait=[mm.now()], inc=ev.inc())
                    pool.done(iP, ev.now())

        for c in range(8):
            Cg = sti * 8 + c
            qkv_dep = ev.now()
            iD0, psD0, wD0 = poolD.take()
            iD1, psD1, wD1 = poolD.take()
            for h in range(16):
                hp, half = h // 2, h % 2
                r0 = half * 64
                qsl = qphi[r0:r0 + 64, hp * 1024 + c * 128: hp * 1024 + c * 128 + 128]
                ksl = kphi[r0:r0 + 64, hp * 1024 + c * 128: hp * 1024 + c * 128 + 128]
                vsl = vtm[:, c * 1024 + h * 64: c * 1024 + h * 64 + 64]
                if half == 1:
                    r.v(lambda v, q=qsl: v.tensor_copy(stage_q[:, :], q),
                        wait=[qkv_dep, mm.now()])
                    r.v(lambda v, k=ksl: v.tensor_copy(stage_k[:, :], k), inc=ev.inc())
                    qop, kop = stage_q[:, :], stage_k[:, :]
                    stg_dep = [ev.now()]
                else:
                    qop, kop = qsl, ksl
                    stg_dep = [qkv_dep]
                iS, psS_, wS = pool.take()
                r.t(lambda t, k=kop, q=qop, p=psS_: MM(t, p[:, 0:128], k, q, True, True),
                    wait=stg_dep + wS, inc=mm.inc())
                At = At_sb[h % 2][:, :]
                r.v(lambda v, p=psS_, At=At: v.tensor_tensor(At, p[:, 0:128], C["triu"][:, :],
                                                             ALU.mult),
                    wait=[mm.now()], inc=ev.inc())
                pool.done(iS, ev.now())
                at_dep = ev.now()
                iT, ptr_, wT = poolT.take()
                r.t(lambda t, k=kop, p=ptr_: t.transpose(p[:, 0:64], k, C["ident"][0:64, 0:64]),
                    wait=stg_dep + wT, inc=mm.inc())
                fk = fkT[h % 2][:, :]
                r.v(lambda v, p=ptr_, fk=fk: v.tensor_copy(fk, p[:, 0:64]),
                    wait=[mm.now()], inc=ev.inc())
                poolT.done(iT, ev.now())
                fk_dep = ev.now()
                iO, psO_, wO = pool.take()
                r.t(lambda t, At=At, vv=vsl, p=psO_: MM(t, p[:, 0:64], At, vv, True, False),
                    wait=[at_dep] + wO)
                ss = (h % 2) * 512 + (h // 2) * 64
                r.t(lambda t, q=qop, ss=ss, p=psO_:
                    MM(t, p[:, 0:64], q, S_sb[:, ss:ss + 64], False, True),
                    inc=mm.inc())
                psD = psD0 if half == 0 else psD1
                r.t(lambda t, fk=fk, vv=vsl, hp=hp, p=psD:
                    MM(t, p[0:64, hp * 64:hp * 64 + 64], fk, vv, True, True),
                    wait=[fk_dep] + (wD0 + wD1 if h < 2 else []), inc=mm.inc())
                mm_o = mm.now()
                r.v(lambda v, p=psO_, h=h, Cg=Cg:
                    v.tensor_scalar(otm[:, :], p[:, 0:64], rdn[:, h * 32 + Cg:h * 32 + Cg + 1],
                                    None, ALU.mult),
                    wait=[mm_o], inc=ev.inc())
                pool.done(iO, ev.now())
                iT2, ptr2_, wT2 = poolT.take()
                r.t(lambda t, p=ptr2_: t.transpose(p[0:64, 0:128], otm[:, :], C["ident"][:, :]),
                    wait=[ev.now()] + wT2, inc=mm.inc())
                r.v(lambda v, p=ptr2_,
                    d=ofm[r0:r0 + 64, hp * 1024 + c * 128: hp * 1024 + c * 128 + 128]:
                    v.tensor_copy(d, p[0:64, 0:128]),
                    wait=[mm.now()], inc=ev.inc())
                poolT.done(iT2, ev.now())
            dmm = mm.now()
            r.v(lambda v, p=psD0: v.tensor_tensor(S32[:, 0:512], S32[:, 0:512], p[0:64, 0:512],
                                                  ALU.add), wait=[dmm], inc=ev.inc())
            r.v(lambda v, p=psD1: v.tensor_tensor(S32[:, 512:1024], S32[:, 512:1024], p[0:64, 0:512],
                                                  ALU.add), wait=[ev.now()], inc=ev.inc())
            r.v(lambda v: v.tensor_copy(S_sb[:, :], S32[:, :]), wait=[ev.now()], inc=ev.inc())
            poolD.done(iD0, ev.now())
            poolD.done(iD1, ev.now())

        if dbg and sti == 0:
            r.g(lambda g: g.dma_start(out=d_lnx[:, :], in_=lnx[:, :]), wait=[ev.now()],
                inc=st.inc(16))
            r.g(lambda g: g.dma_start(out=d_qphi[:, :], in_=qphi[:, :]), inc=st.inc(16))
            r.g(lambda g: g.dma_start(out=d_kphi[:, :], in_=kphi[:, :]), inc=st.inc(16))
            r.g(lambda g: g.dma_start(out=d_vtm[:, :], in_=vtm[:, :]), inc=st.inc(16))
            r.g(lambda g: g.dma_start(out=d_ofm[:, :], in_=ofm[:, :]), inc=st.inc(16))
            r.g(lambda g: g.dma_start(out=d_S[:, :], in_=S32[:, :]), inc=st.inc(16))

        o_dep = ev.now()
        for ic in range(NCH):
            for tau2 in range(2):
                iP, psP_, wP = pool.take()
                for jc in range(NCH):
                    r.t(lambda t, jc=jc, ic=ic, tau2=tau2, p=psP_:
                        MM(t, p[:, 0:512], wp[jc][:, ic * 128:ic * 128 + 128],
                           ofm[:, jc * 1024 + tau2 * 512: jc * 1024 + tau2 * 512 + 512],
                           jc == 0, jc == NCH - 1),
                        wait=([o_dep] + wP) if jc == 0 else [],
                        inc=(mm.inc() if jc == NCH - 1 else None))
                r.v(lambda v, x=xs[:, ic * 1024 + tau2 * 512: ic * 1024 + tau2 * 512 + 512],
                    p=psP_: v.tensor_tensor(x, x, p[:, 0:512], ALU.add),
                    wait=[mm.now()], inc=ev.inc())
                pool.done(iP, ev.now())
        r.y(lambda y, t0=t0: y.dma_start(out=oview[:, :, t0:t0 + 1024],
                                         in_=xs[:, :].rearrange("p (c t) -> p c t", t=1024)),
            wait=[ev.now()], inc=st.inc(16))

    r.y(lambda y: y.nop(), wait=[st.now()])
    return b.finish()





def build_ln_pass(x_percore, pad=0, name="lnpass"):
    b = Bld(name)
    r = b.r
    x_in = b.din("x", x_percore)
    y_out = b.dout("y", [D, T + pad], F32)

    ld = b.sem()
    C = reg_consts(b, [("ones_row", F32), ("ones_col", BF16), ("cvals", F32)], ld)
    ldx = b.sem()
    mm = b.sem()
    ev = b.sem()
    es = b.sem()
    st = b.sem()

    xs = b.sbuf("xs", [128, 8 * 512], F32)
    ys = b.sbuf("ys", [128, 8 * 512], F32)
    xbs = b.sbuf("xbs", [128, 512], BF16)
    sqs = b.sbuf("sqs", [128, 512], BF16)
    tmp32 = b.sbuf("tmp32", [128, 512], F32)
    rows4 = b.sbuf("rows4", [1, 4 * 512], F32)
    zrow = b.sbuf("zrow", [128, 8], F32)

    pA = b.psum("pA", [128, 4 * 512], F32)
    pool = Pool([pA[:, i * 512:(i + 1) * 512] for i in range(4)])

    xview = x_in.rearrange("(c p) t -> p c t", p=128)
    yview = y_out.rearrange("(c p) t -> p c t", p=128)

    if pad:
        r.v(lambda v: v.memset(zrow[:, :], 0.0), inc=ev.inc())
        r.y(lambda y: y.dma_start(out=yview[:, :, 0:1],
                                  in_=zrow[:, 0:8].rearrange("p (c o) -> p c o", o=1),
                                  allow_slow_non_contiguous=True),
            wait=[ev.now()], inc=st.inc(16))

    for ti in range(8):
        t0 = ti * 512
        barrier = [mm.now(), ev.now(), es.now(), ld.now()]
        r.y(lambda y, t0=t0: y.dma_start(out=xs[:, :].rearrange("p (c t) -> p c t", t=512),
                                         in_=xview[:, :, t0:t0 + 512]),
            wait=barrier, inc=ldx.inc(16))
        xw = ldx.now()
        iS, psS_, wS = pool.take()
        iQ, psQ_, wQ = pool.take()
        for c in range(NCH):
            r.v(lambda v, c=c: v.tensor_copy(xbs[:, :], xs[:, c * 512:(c + 1) * 512]),
                wait=[xw, mm.now(), es.now()] + (wS + wQ if c == 0 else []), inc=ev.inc())
            r.s(lambda s: s.activation(sqs[:, :], xbs[:, :], AF.Square),
                wait=[ev.now(), mm.now()], inc=es.inc())
            r.t(lambda t, c=c, p=psS_: MM(t, p[0:1, 0:512], C["ones_col"][:, :], xbs[:, :],
                                          c == 0, c == NCH - 1), wait=[ev.now()])
            r.t(lambda t, c=c, p=psQ_: MM(t, p[0:1, 0:512], C["ones_col"][:, :], sqs[:, :],
                                          c == 0, c == NCH - 1), wait=[es.now()], inc=mm.inc())
        rstd, brow = layernorm_rows(b, rows4, psS_, psQ_, mm.now(), ev, es,
                                    eps_ap=C["cvals"][0:1, 1:2])
        pool.done(iS, ev.now())
        pool.done(iQ, ev.now())
        iA, psA_, wA = pool.take()
        iB, psB_, wB = pool.take()
        r.t(lambda t, p=psA_, q=rstd: MM(t, p[:, 0:512], C["ones_row"][:, :], q, True, True),
            wait=[ev.now()] + wA)
        r.t(lambda t, p=psB_, q=brow: MM(t, p[:, 0:512], C["ones_row"][:, :], q, True, True),
            wait=wB, inc=mm.inc())
        mw2 = mm.now()
        for c in range(NCH):
            r.v(lambda v, c=c, p=psA_: v.tensor_tensor(tmp32[:, :], xs[:, c * 512:(c + 1) * 512],
                                                       p[:, 0:512], ALU.mult), wait=[mw2])
            r.v(lambda v, c=c, p=psB_: v.tensor_tensor(ys[:, c * 512:(c + 1) * 512], tmp32[:, :],
                                                       p[:, 0:512], ALU.add),
                inc=(ev.inc() if c == NCH - 1 else None))
        pool.done(iA, ev.now())
        pool.done(iB, ev.now())
        r.y(lambda y, t0=t0: y.dma_start(out=yview[:, :, pad + t0: pad + t0 + 512],
                                         in_=ys[:, :].rearrange("p (c t) -> p c t", t=512)),
            wait=[ev.now()], inc=st.inc(16))
    r.y(lambda y: y.nop(), wait=[st.now()])
    return b.finish()


def rope_tables():
    DH, WINDOW = 64, 512
    half = np.arange(0, DH, 2, dtype=np.float64)
    inv_freq = 1.0 / 10000.0 ** (half / DH)
    t = np.arange(T, dtype=np.float64)
    freqs = t[:, None] * inv_freq[None, :]
    freqs = np.concatenate([freqs, freqs], axis=-1)
    base = (half + 0.4 * DH) / (1.4 * DH)
    power = (t - T // 2) / float(WINDOW // 2)
    scale = base[None, :] ** power[:, None]
    scale = np.concatenate([scale, scale], axis=-1)
    cos, sin = np.cos(freqs), np.sin(freqs)
    return (np.ascontiguousarray((cos * scale).T, np.float32),
            np.ascontiguousarray((sin * scale).T, np.float32),
            np.ascontiguousarray((cos / scale).T, np.float32),
            np.ascontiguousarray((sin / scale).T, np.float32))


def bB_masks():
    m = np.zeros((8, 128, 512), np.float32)
    tt = np.arange(128)[:, None]
    ss = np.arange(512)[None, :]
    for j in range(8):
        rel = (512 - 128 * j) + ss - tt
        m[j] = np.where((rel >= 0) & (rel <= 512), 0.0, -1e30)
    return m


def build_B_qkv(xb_percore, wqkv_l, q_scale_l, k_scale_l, name="b_qkv"):
    b = Bld(name)
    r = b.r
    xb_in = b.din("xb", xb_percore)
    wq_in = b.din("wqkv", np.ascontiguousarray(wqkv_l))
    qs_in = b.din("qs", np.ascontiguousarray(q_scale_l.reshape(1, 64)))
    ks_in = b.din("ks", np.ascontiguousarray(k_scale_l.reshape(1, 64)))
    qr_out = b.dout("qr", [128, 4 * T], BF16)
    kr_out = b.dout("kr", [128, 4 * T], BF16)
    vp_out = b.dout("vp", [128, 32 * 520], BF16)

    ld = b.sem()
    C = reg_consts(b, [("ones_row", F32), ("ones_col", BF16), ("cvals", F32)], ld)
    tabs = {}
    cms_a, sms_a, cds_a, sds_a = rope_tables()
    for nm, arr in (("cms", cms_a), ("sms", sms_a), ("cds", cds_a), ("sds", sds_a)):
        ap = b.din("t_" + nm, arr)
        tile = b.sbuf("sb_" + nm, [64, T], F32)
        r.g(lambda g, t=tile, a=ap: g.dma_start(out=t[:, :], in_=a[:, :]), inc=ld.inc(16))
        tabs[nm] = tile
    wq = [b.sbuf("wq%d" % i, [128, 1536], BF16) for i in range(NCH)]
    for i in range(NCH):
        r.g(lambda g, t=wq[i], i=i: g.dma_start(out=t[:, :], in_=wq_in[i * 128:(i + 1) * 128, :]),
            inc=ld.inc(16))
    qsr = b.sbuf("qsr", [1, 64], F32)
    ksr = b.sbuf("ksr", [1, 64], F32)
    r.g(lambda g: g.dma_start(out=qsr[:, :], in_=qs_in[:, :]), inc=ld.inc(16))
    r.g(lambda g: g.dma_start(out=ksr[:, :], in_=ks_in[:, :]), inc=ld.inc(16))

    ldx = b.sem()
    mm = b.sem()
    ev = b.sem()
    es = b.sem()
    st = b.sem()

    xs = b.sbuf("xs", [128, 8 * 513], F32)
    xsb = b.sbuf("xsb", [128, 8 * 512], BF16)
    sq = b.sbuf("sq", [64, 512], BF16)
    rows2 = b.sbuf("rows2", [1, 3 * 512], F32)
    qn = b.sbuf("qn", [64, 512], F32)
    qh = b.sbuf("qh", [64, 512], F32)
    qh2 = b.sbuf("qh2", [64, 512], F32)
    swp = b.sbuf("swp", [64, 512], F32)
    rt1 = b.sbuf("rt1", [64, 512], F32)
    rt2 = b.sbuf("rt2", [64, 512], F32)
    qrt = b.sbuf("qrt", [128, 4 * 512], BF16)
    krt = b.sbuf("krt", [128, 4 * 512], BF16)
    vpt = b.sbuf("vpt", [128, 4 * 520], BF16)
    scl = b.sbuf("scl", [64, 2], F32)

    pA = b.psum("pA", [128, 6 * 512], F32)
    pool = Pool([pA[:, i * 512:(i + 1) * 512] for i in range(6)])

    r.g(lambda g: g.dma_start(out=scl[:, 0:1], in_=qs_in.rearrange("o e -> e o")),
        inc=ld.inc(16))
    r.g(lambda g: g.dma_start(out=scl[:, 1:2], in_=ks_in.rearrange("o e -> e o")),
        inc=ld.inc(16))

    xview = xb_in.rearrange("(c p) t -> p c t", p=128)

    for ti in range(8):
        t0 = ti * 512
        barrier = [mm.now(), ev.now(), es.now(), st.now(), ld.now()]
        r.y(lambda y, t0=t0: y.dma_start(out=xs[:, :].rearrange("p (c t) -> p c t", t=513),
                                         in_=xview[:, :, t0:t0 + 513]),
            wait=barrier, inc=ldx.inc(16))
        xw = ldx.now()
        for c in range(NCH):
            off = 1 if c < 4 else 0
            r.v(lambda v, c=c, off=off:
                v.tensor_copy(xsb[:, c * 512:(c + 1) * 512],
                              xs[:, c * 513 + off: c * 513 + off + 512]),
                wait=[xw] + ([mm.now()] if ti > 0 else []),
                inc=(ev.inc() if c == NCH - 1 else None))
        xbw = ev.now()

        for qk in range(2):
            dst = qrt if qk == 0 else krt
            tabc = tabs["cms"] if qk == 0 else tabs["cds"]
            tabs_ = tabs["sms"] if qk == 0 else tabs["sds"]
            for h in range(8):
                iP, psP_, wP = pool.take()
                col = qk * 512 + h * 64
                for dc in range(NCH):
                    r.t(lambda t, dc=dc, col=col, p=psP_:
                        MM(t, p[0:64, 0:512], wq[dc][:, col:col + 64],
                           xsb[:, dc * 512:(dc + 1) * 512], dc == 0, dc == NCH - 1),
                        wait=([xbw] + wP) if dc == 0 else [],
                        inc=(mm.inc() if dc == NCH - 1 else None))
                mq = mm.now()
                r.s(lambda s, p=psP_: s.activation(sq[:, :], p[0:64, 0:512], AF.Square),
                    wait=[mq, ev.now()], inc=es.inc())
                iN, psN_, wN = pool.take()
                r.t(lambda t, p=psN_: MM(t, p[0:1, 0:512], C["ones_col"][0:64, :], sq[:, :],
                                         True, True), wait=[es.now()] + wN, inc=mm.inc())
                nr = rows2[0:1, 0:512]
                ns = rows2[0:1, 512:1024]
                r.v(lambda v, nr=nr, p=psN_: v.tensor_scalar(nr, p[0:1, 0:512], 1e-24, None,
                                                             ALU.max),
                    wait=[mm.now()], inc=ev.inc())
                pool.done(iN, ev.now())
                r.s(lambda s, nr=nr: s.activation(nr, nr, AF.Ln), wait=[ev.now()])
                r.s(lambda s, ns=ns, nr=nr: s.activation(ns, nr, AF.Exp, scale=-0.5),
                    inc=es.inc())
                iR, psR_, wR = pool.take()
                r.t(lambda t, ns=ns, p=psR_: MM(t, p[0:64, 0:512], C["ones_row"][0:1, 0:64],
                                                ns, True, True),
                    wait=[es.now()] + wR, inc=mm.inc())
                r.v(lambda v, p=psR_: v.tensor_copy(qn[:, :], p[0:64, 0:512]),
                    wait=[mm.now()], inc=ev.inc())
                pool.done(iR, ev.now())
                r.v(lambda v, p=psP_: v.tensor_tensor(qh[:, :], p[0:64, 0:512], qn[:, :],
                                                      ALU.mult),
                    wait=[ev.now()], inc=ev.inc())
                pool.done(iP, ev.now())
                if qk == 0:
                    r.v(lambda v: v.tensor_scalar(qh2[:, :], qh[:, :], scl[:, 0:1], 8.0,
                                                  ALU.mult, ALU.mult),
                        wait=[ev.now()], inc=ev.inc())
                else:
                    r.v(lambda v: v.tensor_scalar(qh2[:, :], qh[:, :], scl[:, 1:2], None,
                                                  ALU.mult),
                        wait=[ev.now()], inc=ev.inc())
                r.v(lambda v: v.tensor_scalar(swp[0:32, :], qh2[32:64, :], -1.0, None,
                                              ALU.mult), wait=[ev.now()], inc=ev.inc())
                r.v(lambda v: v.tensor_copy(swp[32:64, :], qh2[0:32, :]),
                    wait=[ev.now()], inc=ev.inc())
                r.v(lambda v, tc=tabc, t0=t0: v.tensor_tensor(rt1[:, :], qh2[:, :],
                                                              tc[:, t0:t0 + 512], ALU.mult),
                    wait=[ev.now()], inc=ev.inc())
                r.v(lambda v, ts_=tabs_, t0=t0: v.tensor_tensor(rt2[:, :], swp[:, :],
                                                                ts_[:, t0:t0 + 512], ALU.mult),
                    wait=[ev.now()], inc=ev.inc())
                r0 = (h % 2) * 64
                r.v(lambda v, d=dst[r0:r0 + 64, (h // 2) * 512:(h // 2) * 512 + 512]:
                    v.tensor_tensor(d, rt1[:, :], rt2[:, :], ALU.add),
                    wait=[ev.now()], inc=ev.inc())

        for sub in range(4):
            iP, psP_, wP = pool.take()
            for dc in range(NCH):
                r.t(lambda t, dc=dc, sub=sub, p=psP_:
                    MM(t, p[:, 0:512],
                       xsb[:, dc * 512 + sub * 128: dc * 512 + sub * 128 + 128],
                       wq[dc][:, 1024:1536], dc == 0, dc == NCH - 1),
                    wait=([xbw] + wP) if dc == 0 else [],
                    inc=(mm.inc() if dc == NCH - 1 else None))
            r.v(lambda v, sub=sub, p=psP_:
                v.tensor_copy(vpt[:, sub * 520:sub * 520 + 520]
                              .rearrange("p (h e) -> p h e", e=65)[:, :, 0:64],
                              p[:, 0:512].rearrange("p (h e) -> p h e", e=64)),
                wait=[mm.now()], inc=ev.inc())
            r.v(lambda v, sub=sub: v.memset(vpt[:, sub * 520:sub * 520 + 520]
                                            .rearrange("p (h e) -> p h e", e=65)[:, :, 64:65],
                                            1.0), inc=ev.inc())
            pool.done(iP, ev.now())

        r.y(lambda y, t0=t0: y.dma_start(
            out=qr_out.rearrange("p (g t) -> p g t", t=T)[:, :, t0:t0 + 512],
            in_=qrt[:, :].rearrange("p (g t) -> p g t", t=512)),
            wait=[ev.now()], inc=st.inc(16))
        r.y(lambda y, t0=t0: y.dma_start(
            out=kr_out.rearrange("p (g t) -> p g t", t=T)[:, :, t0:t0 + 512],
            in_=krt[:, :].rearrange("p (g t) -> p g t", t=512)),
            inc=st.inc(16))
        r.y(lambda y, ti=ti: y.dma_start(
            out=vp_out[:, ti * 2080: ti * 2080 + 2080], in_=vpt[:, :]),
            inc=st.inc(16))
    r.y(lambda y: y.nop(), wait=[st.now()])
    return b.finish()


def build_B_attn(xb_percore, qr_percore, kr_percore, vp_percore, wo_l,
                 q_scale_l, k_scale_l, name="b_attn"):
    b = Bld(name)
    r = b.r
    xb_in = b.din("xb", xb_percore)
    qr_in = b.din("qr", qr_percore, BF16)
    kr_in = b.din("kr", kr_percore, BF16)
    vp_in = b.din("vp", vp_percore, BF16)
    wo_in = b.din("wo", np.ascontiguousarray(wo_l))
    qs_in = b.din("qs", np.ascontiguousarray(q_scale_l.reshape(1, 64)))
    ks_in = b.din("ks", np.ascontiguousarray(k_scale_l.reshape(1, 64)))
    mk_in = b.din("masks", np.ascontiguousarray(
        bB_masks().transpose(1, 0, 2).reshape(128, 8 * 512)))
    xb_out = b.dout("xb2", [D, T + 1], F32)

    ld = b.sem()
    C = reg_consts(b, [("ones_row", F32), ("cvals", F32)], ld)
    qr = b.sbuf("qr_s", [128, 4 * T], BF16)
    kr = b.sbuf("kr_s", [128, 4 * T], BF16)
    vp = b.sbuf("vp_s", [128, 32 * 520], BF16)
    wo = [b.sbuf("wo%d" % i, [128, 1024], BF16) for i in range(4)]
    msk = b.sbuf("msk", [128, 8 * 512], BF16)
    qsr = b.sbuf("qsr", [1, 64], F32)
    ksr = b.sbuf("ksr", [1, 64], F32)
    r.g(lambda g: g.dma_start(out=qr[:, :], in_=qr_in[:, :]), inc=ld.inc(16))
    r.g(lambda g: g.dma_start(out=kr[:, :], in_=kr_in[:, :]), inc=ld.inc(16))
    r.g(lambda g: g.dma_start(out=vp[:, :], in_=vp_in[:, :]), inc=ld.inc(16))
    for i in range(4):
        r.g(lambda g, i=i: g.dma_start(out=wo[i][:, :], in_=wo_in[i * 128:(i + 1) * 128, :]),
            inc=ld.inc(16))
    r.g(lambda g: g.dma_start(out=msk[:, :], in_=mk_in[:, :]), inc=ld.inc(16))
    r.g(lambda g: g.dma_start(out=qsr[:, :], in_=qs_in[:, :]), inc=ld.inc(16))
    r.g(lambda g: g.dma_start(out=ksr[:, :], in_=ks_in[:, :]), inc=ld.inc(16))

    ldx = b.sem()
    mm = b.sem()
    ev = b.sem()
    es = b.sem()
    st = b.sem()

    rows = b.sbuf("rows", [1, 1024], F32)
    negM = b.sbuf("negM", [128, 1], F32)
    expT = b.sbuf("expT", [128, 512], BF16)
    rden = b.sbuf("rden", [1, 512], F32)
    rb = b.sbuf("rb", [64, 512], F32)
    ofm = b.sbuf("ofm", [128, 4 * 512], BF16)
    xt = b.sbuf("xt", [128, 8 * 512], F32)

    pA = b.psum("pA", [128, 6 * 512], F32)
    pool = Pool([pA[:, i * 512:(i + 1) * 512] for i in range(4)])
    poolO = Pool([pA[:, i * 512:(i + 1) * 512] for i in range(4, 6)])

    wld = ld.now()
    r.v(lambda v: v.reduce_max(rows[0:1, 0:1], qsr[:, :], axis=AX, apply_absolute_value=True),
        wait=[wld], inc=ev.inc())
    r.v(lambda v: v.reduce_max(rows[0:1, 1:2], ksr[:, :], axis=AX, apply_absolute_value=True),
        wait=[ev.now()], inc=ev.inc())
    r.v(lambda v: v.tensor_tensor(rows[0:1, 2:3], rows[0:1, 0:1], rows[0:1, 1:2], ALU.mult),
        wait=[ev.now()], inc=ev.inc())
    r.v(lambda v: v.tensor_scalar(rows[0:1, 3:4], rows[0:1, 2:3], -8.0, None, ALU.mult),
        wait=[ev.now()], inc=ev.inc())
    i0, ps0, w0 = pool.take()
    r.t(lambda t, p=ps0: MM(t, p[:, 0:1], C["ones_row"][:, :], rows[0:1, 3:4], True, True),
        wait=[ev.now()] + w0, inc=mm.inc())
    r.v(lambda v, p=ps0: v.tensor_copy(negM[:, :], p[:, 0:1]), wait=[mm.now()], inc=ev.inc())
    pool.done(i0, ev.now())

    xview = xb_in.rearrange("(c p) t -> p c t", p=128)
    oview = xb_out.rearrange("(c p) t -> p c t", p=128)
    zcol = b.sbuf("zcol", [128, 8], F32)
    r.v(lambda v: v.memset(zcol[:, :], 0.0), inc=ev.inc())

    for ti in range(8):
        t0 = ti * 512
        barrier = [mm.now(), ev.now(), es.now(), st.now()]
        r.y(lambda y, t0=t0: y.dma_start(out=xt[:, :].rearrange("p (c t) -> p c t", t=512),
                                         in_=xview[:, :, t0 + 1:t0 + 513]),
            wait=barrier, inc=ldx.inc(16))
        xw = ldx.now()
        for h in range(8):
            r0 = (h % 2) * 64
            qsl = qr[r0:r0 + 64, (h // 2) * T + t0: (h // 2) * T + t0 + 512]
            iO, psO_, wO = poolO.take()
            nkc = 0
            for j in range(8):
                kbase = t0 - 512 + 128 * j
                if kbase < 0:
                    continue
                kc = kbase // 128
                iS, psS_, wS = pool.take()
                r.t(lambda t, kc=kc, r0=r0, h=h, qsl=qsl, p=psS_:
                    MM(t, p[:, 0:512],
                       kr[r0:r0 + 64, (h // 2) * T + kc * 128: (h // 2) * T + kc * 128 + 128],
                       qsl, True, True),
                    wait=([wld] + wS), inc=mm.inc())
                r.v(lambda v, j=j, p=psS_:
                    v.tensor_tensor(p[:, 0:512], p[:, 0:512], msk[:, j * 512:(j + 1) * 512],
                                    ALU.add),
                    wait=[mm.now()], inc=ev.inc())
                r.s(lambda s, p=psS_: s.activation(expT[:, :], p[:, 0:512], AF.Exp,
                                                   bias=negM[:, 0:1]),
                    wait=[ev.now(), mm.now()], inc=es.inc())
                r.t(lambda t, kc=kc, h=h, p=psO_, first=(nkc == 0):
                    MM(t, p[0:65, 0:512], vp[:, kc * 520 + h * 65: kc * 520 + h * 65 + 65],
                       expT[:, :], first, False),
                    wait=[es.now()] + (wO if nkc == 0 else []), inc=mm.inc())
                pool.done(iS, es.now())
                nkc += 1
            r.v(lambda v, p=psO_: v.reciprocal(rden[:, :], p[64:65, 0:512]),
                wait=[mm.now()], inc=ev.inc())
            iR, psR_, wR = pool.take()
            r.t(lambda t, p=psR_: MM(t, p[0:64, 0:512], C["ones_row"][0:1, 0:64], rden[:, :],
                                     True, True),
                wait=[ev.now()] + wR, inc=mm.inc())
            r.v(lambda v, p=psR_: v.tensor_copy(rb[:, :], p[0:64, 0:512]),
                wait=[mm.now()], inc=ev.inc())
            pool.done(iR, ev.now())
            r.v(lambda v, p=psO_, d=ofm[r0:r0 + 64, (h // 2) * 512:(h // 2) * 512 + 512]:
                v.tensor_tensor(d, p[0:64, 0:512], rb[:, :], ALU.mult),
                wait=[ev.now(), mm.now()], inc=ev.inc())
            poolO.done(iO, ev.now())
        odep = ev.now()
        for dc in range(NCH):
            iP, psP_, wP = pool.take()
            for jc in range(4):
                r.t(lambda t, jc=jc, dc=dc, p=psP_:
                    MM(t, p[:, 0:512], wo[jc][:, dc * 128:dc * 128 + 128],
                       ofm[:, jc * 512:(jc + 1) * 512], jc == 0, jc == 3),
                    wait=([odep, xw] + wP) if jc == 0 else [],
                    inc=(mm.inc() if jc == 3 else None))
            r.v(lambda v, dc=dc, p=psP_:
                v.tensor_tensor(xt[:, dc * 512:(dc + 1) * 512], xt[:, dc * 512:(dc + 1) * 512],
                                p[:, 0:512], ALU.add),
                wait=[mm.now()], inc=ev.inc())
            pool.done(iP, ev.now())
        r.y(lambda y, t0=t0: y.dma_start(out=oview[:, :, t0 + 1:t0 + 513],
                                         in_=xt[:, :].rearrange("p (c t) -> p c t", t=512)),
            wait=[ev.now()], inc=st.inc(16))
    r.y(lambda y: y.dma_start(out=oview[:, :, 0:1],
                              in_=zcol[:, :].rearrange("p (c z) -> p c z", z=1),
                              allow_slow_non_contiguous=True),
        wait=[ev.now()], inc=st.inc(16))
    r.y(lambda y: y.nop(), wait=[st.now()])
    return b.finish()


def build_B_ffn(xb_percore, w1p_l, b1p_col, w2p_l, b2_col, lg_col, lb_col,
                name="b_ffn"):
    NF1 = 44
    NF2 = 22
    b = Bld(name)
    r = b.r
    xb_in = b.din("xb", xb_percore)
    w1_in = b.din("w1p", np.ascontiguousarray(w1p_l))
    w2_in = b.din("w2p", np.ascontiguousarray(w2p_l))
    b1_in = b.din("b1c", np.ascontiguousarray(b1p_col))
    b2_in = b.din("b2c", np.ascontiguousarray(b2_col))
    lg_in = b.din("lgc", np.ascontiguousarray(lg_col))
    lb_in = b.din("lbc", np.ascontiguousarray(lb_col))
    xb_out = b.dout("xb2", [D, T + 1], F32)

    ld = b.sem()
    C = reg_consts(b, [("ones_row", F32), ("ones_col", BF16), ("cvals", F32)], ld)
    w1 = [b.sbuf("w1_%d" % i, [128, 5632], BF16) for i in range(NCH)]
    for i in range(NCH):
        r.g(lambda g, t=w1[i], i=i: g.dma_start(out=t[:, :], in_=w1_in[i * 128:(i + 1) * 128, :]),
            inc=ld.inc(16))
    w2 = [b.sbuf("w2_%d" % i, [128, 1024], BF16) for i in range(NF2)]
    for i in range(NF2):
        r.g(lambda g, t=w2[i], i=i: g.dma_start(out=t[:, :], in_=w2_in[i * 128:(i + 1) * 128, :]),
            inc=ld.inc(16))
    cols = b.sbuf("cols", [128, NF1 + 3 * 8], F32)
    b1c = cols[:, 0:NF1]
    b2c = cols[:, NF1:NF1 + 8]
    lgc = cols[:, NF1 + 8:NF1 + 16]
    lbc = cols[:, NF1 + 16:NF1 + 24]
    r.g(lambda g: g.dma_start(out=b1c, in_=b1_in[:, :]), inc=ld.inc(16))
    r.g(lambda g: g.dma_start(out=b2c, in_=b2_in[:, :]), inc=ld.inc(16))
    r.g(lambda g: g.dma_start(out=lgc, in_=lg_in[:, :]), inc=ld.inc(16))
    r.g(lambda g: g.dma_start(out=lbc, in_=lb_in[:, :]), inc=ld.inc(16))

    ldx = b.sem()
    mm = b.sem()
    ev = b.sem()
    es = b.sem()
    st = b.sem()

    xs = b.sbuf("xs", [128, 8 * 513], F32)
    xsb = b.sbuf("xsb", [128, 8 * 512], BF16)
    sqs = b.sbuf("sqs", [128, 512], BF16)
    xbs2 = b.sbuf("xbs2", [128, 512], BF16)
    tmp32 = b.sbuf("tmp32", [128, 512], F32)
    lnh = b.sbuf("lnh", [128, 8 * 512], BF16)
    aslab = b.sbuf("aslab", [128, NF2 * 512], BF16)
    gtmp = b.sbuf("gtmp", [128, 512], BF16)
    rows4 = b.sbuf("rows4", [1, 4 * 512], F32)
    zcol2 = b.sbuf("zcol2", [128, 8], F32)

    pA = b.psum("pA", [128, 6 * 512], F32)
    pool = Pool([pA[:, i * 512:(i + 1) * 512] for i in range(6)])

    xview = xb_in.rearrange("(c p) t -> p c t", p=128)
    oview = xb_out.rearrange("(c p) t -> p c t", p=128)

    for ti in range(8):
        t0 = ti * 512
        barrier = [mm.now(), ev.now(), es.now(), st.now(), ld.now()]
        r.y(lambda y, t0=t0: y.dma_start(out=xs[:, :].rearrange("p (c t) -> p c t", t=513),
                                         in_=xview[:, :, t0:t0 + 513]),
            wait=barrier, inc=ldx.inc(16))
        xw = ldx.now()
        for c in range(NCH):
            off = 1 if c < 4 else 0
            r.v(lambda v, c=c, off=off:
                v.tensor_copy(xsb[:, c * 512:(c + 1) * 512],
                              xs[:, c * 513 + off: c * 513 + off + 512]),
                wait=[xw, mm.now(), es.now()],
                inc=(ev.inc() if c == NCH - 1 else None))
        xbw = ev.now()
        iS, psS_, wS = pool.take()
        iQ, psQ_, wQ = pool.take()
        for c in range(NCH):
            r.v(lambda v, c=c: v.tensor_copy(xbs2[:, :], xsb[:, c * 512:(c + 1) * 512]),
                wait=[xbw, mm.now(), es.now()] + (wS + wQ if c == 0 else []), inc=ev.inc())
            r.s(lambda s: s.activation(sqs[:, :], xbs2[:, :], AF.Square),
                wait=[ev.now(), mm.now()], inc=es.inc())
            r.t(lambda t, c=c, p=psS_: MM(t, p[0:1, 0:512], C["ones_col"][:, :], xbs2[:, :],
                                          c == 0, c == NCH - 1), wait=[ev.now()])
            r.t(lambda t, c=c, p=psQ_: MM(t, p[0:1, 0:512], C["ones_col"][:, :], sqs[:, :],
                                          c == 0, c == NCH - 1), wait=[es.now()], inc=mm.inc())
        rstd, brow = layernorm_rows(b, rows4, psS_, psQ_, mm.now(), ev, es,
                                    eps_ap=C["cvals"][0:1, 1:2])
        pool.done(iS, ev.now())
        pool.done(iQ, ev.now())
        iA, psA_, wA = pool.take()
        iB, psB_, wB = pool.take()
        r.t(lambda t, p=psA_, q=rstd: MM(t, p[:, 0:512], C["ones_row"][:, :], q, True, True),
            wait=[ev.now()] + wA)
        r.t(lambda t, p=psB_, q=brow: MM(t, p[:, 0:512], C["ones_row"][:, :], q, True, True),
            wait=wB, inc=mm.inc())
        mw2 = mm.now()
        for c in range(NCH):
            off = 1 if c < 4 else 0
            hsl = xs[:, c * 513 + off: c * 513 + off + 512]
            r.v(lambda v, hsl=hsl, p=psA_: v.tensor_tensor(tmp32[:, :], hsl, p[:, 0:512],
                                                           ALU.mult), wait=[mw2])
            r.v(lambda v, p=psB_: v.tensor_tensor(tmp32[:, :], tmp32[:, :], p[:, 0:512],
                                                  ALU.add))
            r.v(lambda v, c=c: v.tensor_scalar(lnh[:, c * 512:(c + 1) * 512], tmp32[:, :],
                                               lgc[:, c:c + 1], lbc[:, c:c + 1],
                                               ALU.mult, ALU.add),
                inc=(ev.inc() if c == NCH - 1 else None))
        pool.done(iA, ev.now())
        pool.done(iB, ev.now())
        lnw = ev.now()
        for fc in range(NF1):
            iP, psP_, wP = pool.take()
            for dc in range(NCH):
                r.t(lambda t, dc=dc, fc=fc, p=psP_:
                    MM(t, p[:, 0:512], w1[dc][:, fc * 128:fc * 128 + 128],
                       lnh[:, dc * 512:(dc + 1) * 512], dc == 0, dc == NCH - 1),
                    wait=([lnw] + wP) if dc == 0 else [],
                    inc=(mm.inc() if dc == NCH - 1 else None))
            if fc < NF2:
                r.v(lambda v, fc=fc, p=psP_:
                    v.tensor_scalar(aslab[:, fc * 512:(fc + 1) * 512], p[:, 0:512],
                                    b1c[:, fc:fc + 1], None, ALU.add),
                    wait=[mm.now()], inc=ev.inc())
                pool.done(iP, ev.now())
            else:
                r.s(lambda s, fc=fc, p=psP_:
                    s.activation(gtmp[:, :], p[:, 0:512], AF.Gelu, bias=b1c[:, fc:fc + 1]),
                    wait=[mm.now(), ev.now()], inc=es.inc())
                pool.done(iP, es.now())
                g2 = fc - NF2
                r.v(lambda v, g2=g2:
                    v.tensor_tensor(aslab[:, g2 * 512:(g2 + 1) * 512],
                                    aslab[:, g2 * 512:(g2 + 1) * 512], gtmp[:, :], ALU.mult),
                    wait=[es.now()], inc=ev.inc())
        gw = ev.now()
        for dc in range(NCH):
            iP, psP_, wP = pool.take()
            for fc in range(NF2):
                r.t(lambda t, fc=fc, dc=dc, p=psP_:
                    MM(t, p[:, 0:512], w2[fc][:, dc * 128:dc * 128 + 128],
                       aslab[:, fc * 512:(fc + 1) * 512], fc == 0, fc == NF2 - 1),
                    wait=([gw] + wP) if fc == 0 else [],
                    inc=(mm.inc() if fc == NF2 - 1 else None))
            xsl = xs[:, dc * 513 + 1: dc * 513 + 1 + 512]
            r.v(lambda v, xsl=xsl, p=psP_: v.tensor_tensor(tmp32[:, :], xsl, p[:, 0:512],
                                                           ALU.add),
                wait=[mm.now()])
            r.v(lambda v, xsl=xsl, dc=dc: v.tensor_scalar(xsl, tmp32[:, :],
                                                          b2c[:, dc:dc + 1], None, ALU.add),
                inc=ev.inc())
            pool.done(iP, ev.now())
        r.y(lambda y, t0=t0: y.dma_start(
            out=oview[:, :, t0 + 1:t0 + 513],
            in_=xs[:, :].rearrange("p (c t) -> p c t", t=513)[:, :, 1:513]),
            wait=[ev.now()], inc=st.inc(16))
    r.v(lambda v: v.memset(zcol2[:, :], 0.0), inc=ev.inc())
    r.y(lambda y: y.dma_start(out=oview[:, :, 0:1],
                              in_=zcol2[:, :].rearrange("p (c z) -> p c z", z=1),
                              allow_slow_non_contiguous=True),
        wait=[ev.now()], inc=st.inc(16))
    r.y(lambda y: y.nop(), wait=[st.now()])
    return b.finish()


def build_add_pass(a_percore, b_percore, b_pad=0, name="addpass"):
    b_ = Bld(name)
    r = b_.r
    a_in = b_.din("a", a_percore)
    bb_in = b_.din("b", b_percore)
    y_out = b_.dout("y", [D, T], F32)
    ldx = b_.sem()
    ev = b_.sem()
    st = b_.sem()
    at = b_.sbuf("at", [128, 8 * 512], F32)
    bt = b_.sbuf("bt", [128, 8 * 512], F32)
    aview = a_in.rearrange("(c p) t -> p c t", p=128)
    bview = bb_in.rearrange("(c p) t -> p c t", p=128)
    yview = y_out.rearrange("(c p) t -> p c t", p=128)
    for ti in range(8):
        t0 = ti * 512
        r.y(lambda y, t0=t0: y.dma_start(out=at[:, :].rearrange("p (c t) -> p c t", t=512),
                                         in_=aview[:, :, t0:t0 + 512]),
            wait=[ev.now(), st.now()], inc=ldx.inc(16))
        r.y(lambda y, t0=t0: y.dma_start(out=bt[:, :].rearrange("p (c t) -> p c t", t=512),
                                         in_=bview[:, :, b_pad + t0:b_pad + t0 + 512]),
            inc=ldx.inc(16))
        r.v(lambda v: v.tensor_tensor(at[:, :], at[:, :], bt[:, :], ALU.add),
            wait=[ldx.now()], inc=ev.inc())
        r.y(lambda y, t0=t0: y.dma_start(out=yview[:, :, t0:t0 + 512],
                                         in_=at[:, :].rearrange("p (c t) -> p c t", t=512)),
            wait=[ev.now()], inc=st.inc(16))
    r.y(lambda y: y.nop(), wait=[st.now()])
    return b_.finish()


def _hsel_const():
    a = np.zeros((64, 16 * 16), np.float32)
    for h in range(16):
        a[:, h * 16 + h] = 1.0
    return a


def _hsel_const():
    a = np.zeros((64, 16 * 16), np.float32)
    for h in range(16):
        a[:, h * 16 + h] = 1.0
    return a


def build_C_attn(h_percore, ln1w_col, ln1b_col, qwT_sb, qb_col, kwT_sb, kb_col,
                 vwT_sb, vb_flat, name="c_attn"):
    b = Bld(name)
    r = b.r
    h_in = b.din("h", h_percore)
    qw_in = b.din("qw", np.ascontiguousarray(qwT_sb))
    kw_in = b.din("kw", np.ascontiguousarray(kwT_sb))
    vw_in = b.din("vw", np.ascontiguousarray(vwT_sb))
    l1w_in = b.din("l1w", np.ascontiguousarray(ln1w_col))
    l1b_in = b.din("l1b", np.ascontiguousarray(ln1b_col))
    qb_in = b.din("qbc", np.ascontiguousarray(qb_col))
    kb_in = b.din("kbc", np.ascontiguousarray(kb_col))
    vb_in = b.din("vbf", np.ascontiguousarray(vb_flat.reshape(1, 1040)))
    hs_in = b.din("hsel", _hsel_const())
    h_out = b.dout("h2", [D, T], F32)

    ld = b.sem()
    C = reg_consts(b, [("ones_row", F32), ("ones_row_b", BF16), ("ones_col", BF16),
                       ("ident", BF16), ("cvals", F32)], ld)
    qw = b.sbuf("qw_s", [64, 1024], BF16)
    kw = b.sbuf("kw_s", [64, 1024], BF16)
    vw = b.sbuf("vw_s", [64, 1024], BF16)
    hsel = b.sbuf("hsel_s", [64, 256], BF16)
    cols = b.sbuf("cols_s", [128, 4 * 8], F32)
    l1wc, l1bc, qbc, kbc = (cols[:, 0:8], cols[:, 8:16], cols[:, 16:24], cols[:, 24:32])
    vbb = b.sbuf("vbb", [128, 1040], F32)
    r.g(lambda g: g.dma_start(out=qw[:, :], in_=qw_in[:, :]), inc=ld.inc(16))
    r.g(lambda g: g.dma_start(out=kw[:, :], in_=kw_in[:, :]), inc=ld.inc(16))
    r.g(lambda g: g.dma_start(out=vw[:, :], in_=vw_in[:, :]), inc=ld.inc(16))
    r.g(lambda g: g.dma_start(out=hsel[:, :], in_=hs_in[:, :]), inc=ld.inc(16))
    r.g(lambda g: g.dma_start(out=l1wc, in_=l1w_in[:, :]), inc=ld.inc(16))
    r.g(lambda g: g.dma_start(out=l1bc, in_=l1b_in[:, :]), inc=ld.inc(16))
    r.g(lambda g: g.dma_start(out=qbc, in_=qb_in[:, :]), inc=ld.inc(16))
    r.g(lambda g: g.dma_start(out=kbc, in_=kb_in[:, :]), inc=ld.inc(16))
    r.g(lambda g: g.dma_start(out=vbb[:, :], in_=vb_in.broadcast_to((128, 1040))),
        inc=ld.inc(16))

    ldx = b.sem()
    mm = b.sem()
    ev = b.sem()
    es = b.sem()
    st = b.sem()

    Ks = b.sbuf("Ks", [128, 8 * T], BF16)
    Vs = b.sbuf("Vs", [128, 32 * 1040], BF16)
    xs = b.sbuf("xs", [128, 8 * 512], F32)
    lnx = b.sbuf("lnx", [128, 8 * 512], BF16)
    lstg = b.sbuf("lstg", [64, 512], BF16)
    xbs = b.sbuf("xbs", [128, 512], BF16)
    sqs = b.sbuf("sqs", [128, 512], BF16)
    tmp32 = b.sbuf("tmp32", [128, 512], F32)
    tmp32c = b.sbuf("tmp32c", [128, 512], F32)
    rows4 = b.sbuf("rows4", [1, 4 * 512], F32)
    arowst = b.sbuf("arowst", [1, 1024], F32)
    arows_d = b.nc.dram_tensor("arows_d", [8192], F32, kind="Internal").ap()
    sqh = b.sbuf("sqh", [64, 512], BF16)
    qsb = b.sbuf("qsb", [128, 8 * 512], BF16)
    mk2 = b.sbuf("mk2", [16, 1], F32)
    mcol = b.sbuf("mcol", [16, 4], F32)
    mbf = b.sbuf("mbf", [16, 1], BF16)
    mrow = b.sbuf("mrow", [1, 16], BF16)
    negM = b.sbuf("negM", [128, 16], F32)
    expT = b.sbuf("expT", [128, 512], BF16)
    rden = b.sbuf("rden", [1, 512], F32)
    rb = b.sbuf("rb", [64, 512], F32)

    pA = b.psum("pA", [128, 6 * 512], F32)
    pT = b.psum("pT", [128, 1024], BF16)
    pool = Pool([pA[:, i * 512:(i + 1) * 512] for i in range(4)])
    poolO = Pool([pA[:, i * 512:(i + 1) * 512] for i in range(4, 6)])

    hview = h_in.rearrange("(c p) t -> p c t", p=128)
    oview = h_out.rearrange("(c p) t -> p c t", p=128)

    r.v(lambda v: v.memset(mk2[:, :], 0.0), inc=ev.inc())

    def stream_ln(ti, save_rows):
        t0 = ti * 512
        barrier = [mm.now(), ev.now(), es.now(), ld.now(), st.now()]
        r.y(lambda y, t0=t0: y.dma_start(out=xs[:, :].rearrange("p (c t) -> p c t", t=512),
                                         in_=hview[:, :, t0:t0 + 512]),
            wait=barrier, inc=ldx.inc(16))
        xw = ldx.now()
        if save_rows:
            iS, psS_, wS = pool.take()
            iQ, psQ_, wQ = pool.take()
            for c in range(NCH):
                r.v(lambda v, c=c: v.tensor_copy(xbs[:, :], xs[:, c * 512:(c + 1) * 512]),
                    wait=[xw, mm.now(), es.now()] + (wS + wQ if c == 0 else []), inc=ev.inc())
                r.s(lambda s: s.activation(sqs[:, :], xbs[:, :], AF.Square),
                    wait=[ev.now(), mm.now()], inc=es.inc())
                r.t(lambda t, c=c, p=psS_: MM(t, p[0:1, 0:512], C["ones_col"][:, :], xbs[:, :],
                                              c == 0, c == NCH - 1), wait=[ev.now()])
                r.t(lambda t, c=c, p=psQ_: MM(t, p[0:1, 0:512], C["ones_col"][:, :], sqs[:, :],
                                              c == 0, c == NCH - 1), wait=[es.now()],
                    inc=mm.inc())
            rstd, brow = layernorm_rows(b, rows4, psS_, psQ_, mm.now(), ev, es,
                                        eps_ap=C["cvals"][0:1, 1:2])
            r.y(lambda y, ti=ti, q=rstd: y.dma_start(out=arows_d[ti * 1024:ti * 1024 + 512][None, :],
                                                     in_=q),
                wait=[ev.now()], inc=st.inc(16))
            r.y(lambda y, ti=ti, q=brow: y.dma_start(
                out=arows_d[ti * 1024 + 512:ti * 1024 + 1024][None, :], in_=q),
                inc=st.inc(16))
            pool.done(iS, ev.now())
            pool.done(iQ, ev.now())
        else:
            r.y(lambda y, ti=ti: y.dma_start(out=arowst[:, :],
                                             in_=arows_d[ti * 1024:(ti + 1) * 1024][None, :]),
                wait=[ev.now(), mm.now()], inc=ldx.inc(16))
            rstd = arowst[0:1, 0:512]
            brow = arowst[0:1, 512:1024]
        iA, psA_, wA = pool.take()
        iB, psB_, wB = pool.take()
        r.t(lambda t, p=psA_, q=rstd: MM(t, p[:, 0:512], C["ones_row"][:, :], q, True, True),
            wait=[ev.now(), ldx.now()] + wA)
        r.t(lambda t, p=psB_, q=brow: MM(t, p[:, 0:512], C["ones_row"][:, :], q, True, True),
            wait=wB, inc=mm.inc())
        mw2 = mm.now()
        for c in range(NCH):
            r.v(lambda v, c=c, p=psA_: v.tensor_tensor(tmp32[:, :], xs[:, c * 512:(c + 1) * 512],
                                                       p[:, 0:512], ALU.mult), wait=[mw2])
            r.v(lambda v, c=c, p=psB_: v.tensor_tensor(tmp32[:, :], tmp32[:, :], p[:, 0:512],
                                                       ALU.add))
            r.v(lambda v, c=c: v.tensor_scalar(lnx[:, c * 512:(c + 1) * 512], tmp32[:, :],
                                               l1wc[:, c:c + 1], l1bc[:, c:c + 1],
                                               ALU.mult, ALU.add),
                inc=(ev.inc() if c == NCH - 1 else None))
        pool.done(iA, ev.now())
        pool.done(iB, ev.now())
        return ev.now()

    for ti in range(8):
        t0 = ti * 512
        lnw = stream_ln(ti, save_rows=True)
        iMK, psMK, wMK = poolO.take()
        for h in range(16):
            r0, hp = (h % 2) * 64, h // 2
            lsl = lnx[r0:r0 + 64, hp * 512: hp * 512 + 512]
            if h % 2 == 1:
                r.v(lambda v, lsl=lsl: v.tensor_copy(lstg[:, :], lsl),
                    wait=[lnw, mm.now()], inc=ev.inc())
                lop = lstg[:, :]
                ldep = [ev.now()]
            else:
                lop = lsl
                ldep = [lnw]
            iP, psP_, wP = pool.take()
            r.t(lambda t, h=h, q=lop, p=psP_: MM(t, p[0:64, 0:512], kw[:, h * 64:h * 64 + 64],
                                                 q, True, True),
                wait=ldep + wP, inc=mm.inc())
            ksl = Ks[r0:r0 + 64, hp * T + t0: hp * T + t0 + 512]
            r.v(lambda v, h=h, r0=r0, ksl=ksl, p=psP_:
                v.tensor_scalar(ksl, p[0:64, 0:512], kbc[r0:r0 + 64, hp_col(h)], None, ALU.add),
                wait=[mm.now()], inc=ev.inc())
            pool.done(iP, ev.now())
            r.v(lambda v, ksl=ksl: v.tensor_tensor(sqh[:, :], ksl, ksl, ALU.mult),
                wait=[ev.now()], inc=ev.inc())
            r.t(lambda t, h=h, p=psMK: MM(t, p[0:16, 0:512], hsel[:, h * 16:h * 16 + 16],
                                          sqh[:, :], h == 0, h == 15),
                wait=[ev.now()] + (wMK if h == 0 else []), inc=mm.inc())
            for sub in range(4):
                iV, psV_, wV = pool.take()
                r.t(lambda t, h=h, sub=sub, q=lop, p=psV_:
                    MM(t, p[:, 0:64], q[:, sub * 128:sub * 128 + 128],
                       vw[:, h * 64:h * 64 + 64], True, True),
                    wait=ldep + wV, inc=mm.inc())
                voff = (ti * 4 + sub) * 1040 + h * 65
                r.v(lambda v, voff=voff, h=h, p=psV_:
                    v.tensor_tensor(Vs[:, voff:voff + 64], p[:, 0:64],
                                    vbb[:, h * 65:h * 65 + 64], ALU.add),
                    wait=[mm.now()], inc=ev.inc())
                pool.done(iV, ev.now())
        r.v(lambda v, p=psMK: v.reduce_max(mcol[:, 0:1], p[0:16, 0:512], axis=AX),
            wait=[mm.now()], inc=ev.inc())
        r.v(lambda v: v.tensor_tensor(mk2[:, :], mk2[:, :], mcol[:, 0:1], ALU.max),
            wait=[ev.now()], inc=ev.inc())
        poolO.done(iMK, ev.now())
        for sub in range(4):
            voff = (ti * 4 + sub) * 1040
            r.v(lambda v, voff=voff:
                v.memset(Vs[:, voff:voff + 1040].rearrange("p (h e) -> p h e", e=65)[:, :, 64:65],
                         1.0), inc=ev.inc())

    for ti in range(8):
        t0 = ti * 512
        lnw = stream_ln(ti, save_rows=False)
        iNQ, psNQ, wNQ = poolO.take()
        for h in range(16):
            r0, hp = (h % 2) * 64, h // 2
            lsl = lnx[r0:r0 + 64, hp * 512: hp * 512 + 512]
            if h % 2 == 1:
                r.v(lambda v, lsl=lsl: v.tensor_copy(lstg[:, :], lsl),
                    wait=[lnw, mm.now()], inc=ev.inc())
                lop = lstg[:, :]
                ldep = [ev.now()]
            else:
                lop = lsl
                ldep = [lnw]
            iP, psP_, wP = pool.take()
            r.t(lambda t, h=h, q=lop, p=psP_: MM(t, p[0:64, 0:512], qw[:, h * 64:h * 64 + 64],
                                                 q, True, True),
                wait=ldep + wP, inc=mm.inc())
            qdst = qsb[r0:r0 + 64, hp * 512: hp * 512 + 512]
            r.v(lambda v, h=h, r0=r0, qdst=qdst, p=psP_:
                v.tensor_scalar(qdst, p[0:64, 0:512], qbc[r0:r0 + 64, hp_col(h)], None, ALU.add),
                wait=[mm.now()], inc=ev.inc())
            pool.done(iP, ev.now())
            r.v(lambda v, qdst=qdst: v.tensor_tensor(sqh[:, :], qdst, qdst, ALU.mult),
                wait=[ev.now()], inc=ev.inc())
            r.t(lambda t, h=h, p=psNQ: MM(t, p[0:16, 0:512], hsel[:, h * 16:h * 16 + 16],
                                          sqh[:, :], h == 0, h == 15),
                wait=[ev.now()] + (wNQ if h == 0 else []), inc=mm.inc())
        r.v(lambda v, p=psNQ: v.reduce_max(mcol[:, 1:2], p[0:16, 0:512], axis=AX),
            wait=[mm.now()], inc=ev.inc())
        poolO.done(iNQ, ev.now())
        r.v(lambda v: v.tensor_tensor(mcol[:, 2:3], mcol[:, 1:2], mk2[:, 0:1], ALU.mult),
            wait=[ev.now()], inc=ev.inc())
        r.v(lambda v: v.tensor_scalar(mcol[:, 3:4], mcol[:, 2:3], 1e-30, None, ALU.max),
            wait=[ev.now()], inc=ev.inc())
        r.s(lambda s: s.activation(mcol[:, 2:3], mcol[:, 3:4], AF.Ln), wait=[ev.now()],
            inc=es.inc())
        r.s(lambda s: s.activation(mcol[:, 3:4], mcol[:, 2:3], AF.Exp, scale=0.5),
            inc=es.inc())
        r.v(lambda v: v.tensor_scalar(mbf[:, :], mcol[:, 3:4], 0.133, None, ALU.mult),
            wait=[es.now()], inc=ev.inc())
        iT, ptr_, wT = Pool([pT[:, 0:16]]).take()
        r.t(lambda t: t.transpose(pT[0:1, 0:16], mbf[:, :], C["ident"][0:16, 0:16]),
            wait=[ev.now()], inc=mm.inc())
        r.v(lambda v: v.tensor_copy(mrow[:, :], pT[0:1, 0:16]), wait=[mm.now()], inc=ev.inc())
        iM, psM_, wM = pool.take()
        r.t(lambda t, p=psM_: MM(t, p[:, 0:16], C["ones_row_b"][:, :], mrow[:, :], True, True),
            wait=[ev.now()] + wM, inc=mm.inc())
        r.v(lambda v, p=psM_: v.tensor_scalar(negM[:, :], p[:, 0:16], -1.0, None, ALU.mult),
            wait=[mm.now()], inc=ev.inc())
        pool.done(iM, ev.now())
        ndep = ev.now()
        for h in range(16):
            r0, hp = (h % 2) * 64, h // 2
            qop = qsb[r0:r0 + 64, hp * 512: hp * 512 + 512]
            iO, psO_, wO = poolO.take()
            for tc in range(32):
                iS2, psS2, wS2 = pool.take()
                r.t(lambda t, h=h, r0=r0, hp=hp, tc=tc, qop=qop, p=psS2:
                    MM(t, p[:, 0:512], Ks[r0:r0 + 64, hp * T + tc * 128: hp * T + tc * 128 + 128],
                       qop, True, True),
                    wait=([ndep] + wS2), inc=mm.inc())
                r.s(lambda s, h=h, p=psS2: s.activation(expT[:, :], p[:, 0:512], AF.Exp,
                                                        scale=0.125, bias=negM[:, h:h + 1]),
                    wait=[mm.now()], inc=es.inc())
                pool.done(iS2, es.now())
                r.t(lambda t, tc=tc, h=h, p=psO_, first=(tc == 0):
                    MM(t, p[0:65, 0:512], Vs[:, tc * 1040 + h * 65: tc * 1040 + h * 65 + 65],
                       expT[:, :], first, tc == 31),
                    wait=[es.now()] + (wO if tc == 0 else []), inc=mm.inc())
            r.v(lambda v, p=psO_: v.reciprocal(rden[:, :], p[64:65, 0:512]),
                wait=[mm.now()], inc=ev.inc())
            iR, psR_, wR = pool.take()
            r.t(lambda t, p=psR_: MM(t, p[0:64, 0:512], C["ones_row"][0:1, 0:64], rden[:, :],
                                     True, True),
                wait=[ev.now()] + wR, inc=mm.inc())
            r.v(lambda v, p=psR_: v.tensor_copy(rb[:, :], p[0:64, 0:512]),
                wait=[mm.now()], inc=ev.inc())
            pool.done(iR, ev.now())
            r.v(lambda v, r0=r0, p=psO_: v.tensor_tensor(tmp32c[r0:r0 + 64, :], p[0:64, 0:512],
                                                         rb[:, :], ALU.mult),
                wait=[ev.now()], inc=ev.inc())
            poolO.done(iO, ev.now())
            xsl = xs[r0:r0 + 64, hp * 512: hp * 512 + 512]
            r.v(lambda v, xsl=xsl, r0=r0: v.tensor_tensor(xsl, xsl, tmp32c[r0:r0 + 64, :],
                                                          ALU.add),
                wait=[ev.now()], inc=ev.inc())
        r.y(lambda y, t0=t0: y.dma_start(out=oview[:, :, t0:t0 + 512],
                                         in_=xs[:, :].rearrange("p (c t) -> p c t", t=512)),
            wait=[ev.now()], inc=st.inc(16))
    r.y(lambda y: y.nop(), wait=[st.now()])
    return b.finish()


def hp_col(h):
    return slice(h // 2, h // 2 + 1)


def build_C_ffn(h_percore, ln2w_col, ln2b_col, w1_l, b1_col, w2_l, b2_col,
                name="c_ffn"):
    NF = 32
    b = Bld(name)
    r = b.r
    h_in = b.din("h", h_percore)
    w1_in = b.din("w1", np.ascontiguousarray(w1_l))
    w2_in = b.din("w2", np.ascontiguousarray(w2_l))
    l2w_in = b.din("l2w", np.ascontiguousarray(ln2w_col))
    l2b_in = b.din("l2b", np.ascontiguousarray(ln2b_col))
    b1_in = b.din("b1c", np.ascontiguousarray(b1_col))
    b2_in = b.din("b2c", np.ascontiguousarray(b2_col))
    h_out = b.dout("h2", [D, T], F32)

    ld = b.sem()
    C = reg_consts(b, [("ones_row", F32), ("ones_col", BF16), ("cvals", F32)], ld)
    w1 = [b.sbuf("w1_%d" % i, [128, 4096], BF16) for i in range(NCH)]
    for i in range(NCH):
        r.g(lambda g, t=w1[i], i=i: g.dma_start(out=t[:, :], in_=w1_in[i * 128:(i + 1) * 128, :]),
            inc=ld.inc(16))
    w2 = [b.sbuf("w2_%d" % i, [128, 1024], BF16) for i in range(NF)]
    for i in range(NF):
        r.g(lambda g, t=w2[i], i=i: g.dma_start(out=t[:, :], in_=w2_in[i * 128:(i + 1) * 128, :]),
            inc=ld.inc(16))
    cols = b.sbuf("cols", [128, 32 + 3 * 8], F32)
    b1c, b2c, l2wc, l2bc = (cols[:, 0:32], cols[:, 32:40], cols[:, 40:48], cols[:, 48:56])
    r.g(lambda g: g.dma_start(out=b1c, in_=b1_in[:, :]), inc=ld.inc(16))
    r.g(lambda g: g.dma_start(out=b2c, in_=b2_in[:, :]), inc=ld.inc(16))
    r.g(lambda g: g.dma_start(out=l2wc, in_=l2w_in[:, :]), inc=ld.inc(16))
    r.g(lambda g: g.dma_start(out=l2bc, in_=l2b_in[:, :]), inc=ld.inc(16))

    ldx = b.sem()
    mm = b.sem()
    ev = b.sem()
    es = b.sem()
    st = b.sem()

    xs = b.sbuf("xs", [128, 8 * 512], F32)
    lnx = b.sbuf("lnx", [128, 8 * 512], BF16)
    h1 = b.sbuf("h1", [128, NF * 512], BF16)
    xbs = b.sbuf("xbs", [128, 512], BF16)
    sqs = b.sbuf("sqs", [128, 512], BF16)
    tmp32 = b.sbuf("tmp32", [128, 512], F32)
    rows4 = b.sbuf("rows4", [1, 4 * 512], F32)

    pA = b.psum("pA", [128, 6 * 512], F32)
    pool = Pool([pA[:, i * 512:(i + 1) * 512] for i in range(6)])

    hview = h_in.rearrange("(c p) t -> p c t", p=128)
    oview = h_out.rearrange("(c p) t -> p c t", p=128)

    for ti in range(8):
        t0 = ti * 512
        barrier = [mm.now(), ev.now(), es.now(), ld.now(), st.now()]
        r.y(lambda y, t0=t0: y.dma_start(out=xs[:, :].rearrange("p (c t) -> p c t", t=512),
                                         in_=hview[:, :, t0:t0 + 512]),
            wait=barrier, inc=ldx.inc(16))
        xw = ldx.now()
        iS, psS_, wS = pool.take()
        iQ, psQ_, wQ = pool.take()
        for c in range(NCH):
            r.v(lambda v, c=c: v.tensor_copy(xbs[:, :], xs[:, c * 512:(c + 1) * 512]),
                wait=[xw, mm.now(), es.now()] + (wS + wQ if c == 0 else []), inc=ev.inc())
            r.s(lambda s: s.activation(sqs[:, :], xbs[:, :], AF.Square),
                wait=[ev.now(), mm.now()], inc=es.inc())
            r.t(lambda t, c=c, p=psS_: MM(t, p[0:1, 0:512], C["ones_col"][:, :], xbs[:, :],
                                          c == 0, c == NCH - 1), wait=[ev.now()])
            r.t(lambda t, c=c, p=psQ_: MM(t, p[0:1, 0:512], C["ones_col"][:, :], sqs[:, :],
                                          c == 0, c == NCH - 1), wait=[es.now()], inc=mm.inc())
        rstd, brow = layernorm_rows(b, rows4, psS_, psQ_, mm.now(), ev, es,
                                    eps_ap=C["cvals"][0:1, 1:2])
        pool.done(iS, ev.now())
        pool.done(iQ, ev.now())
        iA, psA_, wA = pool.take()
        iB, psB_, wB = pool.take()
        r.t(lambda t, p=psA_, q=rstd: MM(t, p[:, 0:512], C["ones_row"][:, :], q, True, True),
            wait=[ev.now()] + wA)
        r.t(lambda t, p=psB_, q=brow: MM(t, p[:, 0:512], C["ones_row"][:, :], q, True, True),
            wait=wB, inc=mm.inc())
        mw2 = mm.now()
        for c in range(NCH):
            r.v(lambda v, c=c, p=psA_: v.tensor_tensor(tmp32[:, :], xs[:, c * 512:(c + 1) * 512],
                                                       p[:, 0:512], ALU.mult), wait=[mw2])
            r.v(lambda v, c=c, p=psB_: v.tensor_tensor(tmp32[:, :], tmp32[:, :], p[:, 0:512],
                                                       ALU.add))
            r.v(lambda v, c=c: v.tensor_scalar(lnx[:, c * 512:(c + 1) * 512], tmp32[:, :],
                                               l2wc[:, c:c + 1], l2bc[:, c:c + 1],
                                               ALU.mult, ALU.add),
                inc=(ev.inc() if c == NCH - 1 else None))
        pool.done(iA, ev.now())
        pool.done(iB, ev.now())
        lnw = ev.now()
        for fc in range(NF):
            iP, psP_, wP = pool.take()
            for dc in range(NCH):
                r.t(lambda t, dc=dc, fc=fc, p=psP_:
                    MM(t, p[:, 0:512], w1[dc][:, fc * 128:fc * 128 + 128],
                       lnx[:, dc * 512:(dc + 1) * 512], dc == 0, dc == NCH - 1),
                    wait=([lnw] + wP) if dc == 0 else [],
                    inc=(mm.inc() if dc == NCH - 1 else None))
            r.s(lambda s, fc=fc, p=psP_:
                s.activation(h1[:, fc * 512:(fc + 1) * 512], p[:, 0:512], AF.Gelu,
                             bias=b1c[:, fc:fc + 1]),
                wait=[mm.now()], inc=es.inc())
            pool.done(iP, es.now())
        gw = es.now()
        for dc in range(NCH):
            iP, psP_, wP = pool.take()
            for fc in range(NF):
                r.t(lambda t, fc=fc, dc=dc, p=psP_:
                    MM(t, p[:, 0:512], w2[fc][:, dc * 128:dc * 128 + 128],
                       h1[:, fc * 512:(fc + 1) * 512], fc == 0, fc == NF - 1),
                    wait=([gw] + wP) if fc == 0 else [],
                    inc=(mm.inc() if fc == NF - 1 else None))
            xsl = xs[:, dc * 512:(dc + 1) * 512]
            r.v(lambda v, xsl=xsl, p=psP_: v.tensor_tensor(xsl, xsl, p[:, 0:512], ALU.add),
                wait=[mm.now()])
            r.v(lambda v, xsl=xsl, dc=dc: v.tensor_scalar(xsl, xsl, b2c[:, dc:dc + 1],
                                                          None, ALU.add),
                inc=ev.inc())
            pool.done(iP, ev.now())
        r.y(lambda y, t0=t0: y.dma_start(out=oview[:, :, t0:t0 + 512],
                                         in_=xs[:, :].rearrange("p (c t) -> p c t", t=512)),
            wait=[ev.now()], inc=st.inc(16))
    r.y(lambda y: y.nop(), wait=[st.now()])
    return b.finish()


def bD_masks01():
    m = np.zeros((4, 128, 512), np.float32)
    tt = np.arange(128)[:, None]
    ss = np.arange(512)[None, :]
    for j in range(4):
        m[j] = (tt <= ss - 128 * j).astype(np.float32)
    return np.ascontiguousarray(m.transpose(1, 0, 2).reshape(128, 4 * 512))


def _ln_gain_stream(b, C, r, sems, bufs, hview, ti, wb_cols, save, arows_d):
    ld, ldx, mm, ev, es, st = sems
    xs, lnx, xbs, sqs, tmp32, rows4, arowst, pool = bufs
    lwc, lbc = wb_cols
    t0 = ti * 512
    barrier = [mm.now(), ev.now(), es.now(), ld.now(), st.now()]
    r.y(lambda y, t0=t0: y.dma_start(out=xs[:, :].rearrange("p (c t) -> p c t", t=512),
                                     in_=hview[:, :, t0:t0 + 512]),
        wait=barrier, inc=ldx.inc(16))
    xw = ldx.now()
    if save:
        iS, psS_, wS = pool.take()
        iQ, psQ_, wQ = pool.take()
        for c in range(NCH):
            r.v(lambda v, c=c: v.tensor_copy(xbs[:, :], xs[:, c * 512:(c + 1) * 512]),
                wait=[xw, mm.now(), es.now()] + (wS + wQ if c == 0 else []), inc=ev.inc())
            r.s(lambda s: s.activation(sqs[:, :], xbs[:, :], AF.Square),
                wait=[ev.now(), mm.now()], inc=es.inc())
            r.t(lambda t, c=c, p=psS_: MM(t, p[0:1, 0:512], C["ones_col"][:, :], xbs[:, :],
                                          c == 0, c == NCH - 1), wait=[ev.now()])
            r.t(lambda t, c=c, p=psQ_: MM(t, p[0:1, 0:512], C["ones_col"][:, :], sqs[:, :],
                                          c == 0, c == NCH - 1), wait=[es.now()], inc=mm.inc())
        rstd, brow = layernorm_rows(b, rows4, psS_, psQ_, mm.now(), ev, es,
                                    eps_ap=C["cvals"][0:1, 1:2])
        if arows_d is not None:
            r.y(lambda y, ti=ti, q=rstd: y.dma_start(
                out=arows_d[ti * 1024:ti * 1024 + 512][None, :], in_=q),
                wait=[ev.now()], inc=st.inc(16))
            r.y(lambda y, ti=ti, q=brow: y.dma_start(
                out=arows_d[ti * 1024 + 512:ti * 1024 + 1024][None, :], in_=q),
                inc=st.inc(16))
        pool.done(iS, ev.now())
        pool.done(iQ, ev.now())
    else:
        r.y(lambda y, ti=ti: y.dma_start(out=arowst[:, :],
                                         in_=arows_d[ti * 1024:(ti + 1) * 1024][None, :]),
            wait=[ev.now(), mm.now()], inc=ldx.inc(16))
        rstd = arowst[0:1, 0:512]
        brow = arowst[0:1, 512:1024]
    iA, psA_, wA = pool.take()
    iB, psB_, wB = pool.take()
    r.t(lambda t, p=psA_, q=rstd: MM(t, p[:, 0:512], C["ones_row"][:, :], q, True, True),
        wait=[ev.now(), ldx.now()] + wA)
    r.t(lambda t, p=psB_, q=brow: MM(t, p[:, 0:512], C["ones_row"][:, :], q, True, True),
        wait=wB, inc=mm.inc())
    mw2 = mm.now()
    for c in range(NCH):
        r.v(lambda v, c=c, p=psA_: v.tensor_tensor(tmp32[:, :], xs[:, c * 512:(c + 1) * 512],
                                                   p[:, 0:512], ALU.mult), wait=[mw2])
        r.v(lambda v, c=c, p=psB_: v.tensor_tensor(tmp32[:, :], tmp32[:, :], p[:, 0:512],
                                                   ALU.add))
        if lwc is not None:
            r.v(lambda v, c=c: v.tensor_scalar(lnx[:, c * 512:(c + 1) * 512], tmp32[:, :],
                                               lwc[:, c:c + 1], lbc[:, c:c + 1],
                                               ALU.mult, ALU.add),
                inc=(ev.inc() if c == NCH - 1 else None))
        else:
            r.v(lambda v, c=c: v.tensor_copy(lnx[:, c * 512:(c + 1) * 512], tmp32[:, :]),
                inc=(ev.inc() if c == NCH - 1 else None))
    pool.done(iA, ev.now())
    pool.done(iB, ev.now())
    return ev.now()


def build_D_kv(x_percore, Wqkv, bqkv, ln1w_col, ln1b_col, name="d_kv"):
    b = Bld(name)
    r = b.r
    x_in = b.din("x", x_percore)
    wq_in = b.din("wqkv", np.ascontiguousarray(Wqkv))
    kcol = np.ascontiguousarray(bqkv[1024:2048].reshape(8, 128).T)
    kb_in = b.din("kbc", kcol)
    vbf = np.zeros(1040, np.float32)
    for h in range(16):
        vbf[h * 65:h * 65 + 64] = bqkv[2048 + h * 64: 2048 + h * 64 + 64]
    vb_in = b.din("vbf", np.ascontiguousarray(vbf.reshape(1, 1040)))
    l1w_in = b.din("l1w", np.ascontiguousarray(ln1w_col))
    l1b_in = b.din("l1b", np.ascontiguousarray(ln1b_col))
    hs_in = b.din("hsel", _hsel_const())
    K_out = b.dout("K", [128, 8 * T], BF16)
    V_out = b.dout("V", [128, 32 * 1040], BF16)
    ar_out = b.dout("arows", [8192], F32)
    mk_out = b.dout("mk2", [16, 1], F32)

    ld = b.sem()
    C = reg_consts(b, [("ones_row", F32), ("ones_col", BF16), ("cvals", F32)], ld)
    wq = [b.sbuf("wq%d" % i, [128, 2048], BF16) for i in range(NCH)]
    for i in range(NCH):
        r.g(lambda g, t=wq[i], i=i: g.dma_start(out=t[:, :],
                                                in_=wq_in[i * 128:(i + 1) * 128, 1024:3072]),
            inc=ld.inc(16))
    hsel = b.sbuf("hsel_s", [64, 256], BF16)
    cols = b.sbuf("cols_s", [128, 3 * 8], F32)
    kbc, l1wc, l1bc = cols[:, 0:8], cols[:, 8:16], cols[:, 16:24]
    vbb = b.sbuf("vbb", [128, 1040], F32)
    r.g(lambda g: g.dma_start(out=hsel[:, :], in_=hs_in[:, :]), inc=ld.inc(16))
    r.g(lambda g: g.dma_start(out=kbc, in_=kb_in[:, :]), inc=ld.inc(16))
    r.g(lambda g: g.dma_start(out=l1wc, in_=l1w_in[:, :]), inc=ld.inc(16))
    r.g(lambda g: g.dma_start(out=l1bc, in_=l1b_in[:, :]), inc=ld.inc(16))
    r.g(lambda g: g.dma_start(out=vbb[:, :], in_=vb_in.broadcast_to((128, 1040))),
        inc=ld.inc(16))

    ldx = b.sem()
    mm = b.sem()
    ev = b.sem()
    es = b.sem()
    st = b.sem()

    xs = b.sbuf("xs", [128, 8 * 512], F32)
    lnx = b.sbuf("lnx", [128, 8 * 512], BF16)
    xbs = b.sbuf("xbs", [128, 512], BF16)
    sqs = b.sbuf("sqs", [128, 512], BF16)
    tmp32 = b.sbuf("tmp32", [128, 512], F32)
    rows4 = b.sbuf("rows4", [1, 4 * 512], F32)
    arowst = b.sbuf("arowst", [1, 1024], F32)
    Kt = b.sbuf("Kt", [128, 8 * 512], BF16)
    Vt = b.sbuf("Vt", [128, 4 * 1040], BF16)
    sqh = b.sbuf("sqh", [64, 512], BF16)
    mk2 = b.sbuf("mk2_s", [16, 1], F32)
    mcol = b.sbuf("mcol", [16, 4], F32)

    pA = b.psum("pA", [128, 6 * 512], F32)
    pool = Pool([pA[:, i * 512:(i + 1) * 512] for i in range(4)])
    poolO = Pool([pA[:, i * 512:(i + 1) * 512] for i in range(4, 6)])

    xview = x_in.rearrange("(c p) t -> p c t", p=128)
    arows_d = ar_out

    r.v(lambda v: v.memset(mk2[:, :], 0.0), inc=ev.inc())

    sems = (ld, ldx, mm, ev, es, st)
    bufs = (xs, lnx, xbs, sqs, tmp32, rows4, arowst, pool)
    for ti in range(8):
        t0 = ti * 512
        lnw = _ln_gain_stream(b, C, r, sems, bufs, xview, ti, (l1wc, l1bc), True, arows_d)
        for oc in range(8):
            iP, psP_, wP = pool.take()
            for dc in range(NCH):
                r.t(lambda t, dc=dc, oc=oc, p=psP_:
                    MM(t, p[:, 0:512], wq[dc][:, oc * 128:oc * 128 + 128],
                       lnx[:, dc * 512:(dc + 1) * 512], dc == 0, dc == NCH - 1),
                    wait=([lnw] + wP) if dc == 0 else [],
                    inc=(mm.inc() if dc == NCH - 1 else None))
            r.v(lambda v, oc=oc, p=psP_: v.tensor_scalar(Kt[:, oc * 512:(oc + 1) * 512],
                                                         p[:, 0:512], kbc[:, oc:oc + 1],
                                                         None, ALU.add),
                wait=[mm.now()], inc=ev.inc())
            pool.done(iP, ev.now())
        iMK, psMK, wMK = poolO.take()
        for h in range(16):
            r0, hp = (h % 2) * 64, h // 2
            ksl = Kt[r0:r0 + 64, hp * 512: hp * 512 + 512]
            r.v(lambda v, ksl=ksl: v.tensor_tensor(sqh[:, :], ksl, ksl, ALU.mult),
                wait=[ev.now(), mm.now()], inc=ev.inc())
            r.t(lambda t, h=h, p=psMK: MM(t, p[0:16, 0:512], hsel[:, h * 16:h * 16 + 16],
                                          sqh[:, :], h == 0, h == 15),
                wait=[ev.now()] + (wMK if h == 0 else []), inc=mm.inc())
        r.v(lambda v, p=psMK: v.reduce_max(mcol[:, 0:1], p[0:16, 0:512], axis=AX),
            wait=[mm.now()], inc=ev.inc())
        r.v(lambda v: v.tensor_tensor(mk2[:, :], mk2[:, :], mcol[:, 0:1], ALU.max),
            wait=[ev.now()], inc=ev.inc())
        poolO.done(iMK, ev.now())
        for sub in range(4):
            for half in range(2):
                iP, psP_, wP = pool.take()
                for dc in range(NCH):
                    r.t(lambda t, dc=dc, sub=sub, half=half, p=psP_:
                        MM(t, p[:, 0:512],
                           lnx[:, dc * 512 + sub * 128: dc * 512 + sub * 128 + 128],
                           wq[dc][:, 1024 + half * 512: 1024 + half * 512 + 512],
                           dc == 0, dc == NCH - 1),
                        wait=([lnw] + wP) if dc == 0 else [],
                        inc=(mm.inc() if dc == NCH - 1 else None))
                voff = sub * 1040 + half * 520
                r.v(lambda v, voff=voff, half=half, p=psP_:
                    v.tensor_tensor(Vt[:, voff:voff + 520].rearrange("p (h e) -> p h e", e=65)[:, :, 0:64],
                                    p[:, 0:512].rearrange("p (h e) -> p h e", e=64),
                                    vbb[:, half * 520:half * 520 + 520]
                                    .rearrange("p (h e) -> p h e", e=65)[:, :, 0:64], ALU.add),
                    wait=[mm.now()], inc=ev.inc())
                r.v(lambda v, voff=voff:
                    v.memset(Vt[:, voff:voff + 520].rearrange("p (h e) -> p h e", e=65)[:, :, 64:65],
                             1.0), inc=ev.inc())
                pool.done(iP, ev.now())
        r.y(lambda y, t0=t0: y.dma_start(
            out=K_out.rearrange("p (g t) -> p g t", t=T)[:, :, t0:t0 + 512],
            in_=Kt[:, :].rearrange("p (g t) -> p g t", t=512)),
            wait=[ev.now()], inc=st.inc(16))
        r.y(lambda y, ti=ti: y.dma_start(out=V_out[:, ti * 4160: ti * 4160 + 4160],
                                         in_=Vt[:, :]),
            inc=st.inc(16))
    r.y(lambda y: y.dma_start(out=mk_out[:, :], in_=mk2[:, :]), wait=[ev.now()],
        inc=st.inc(16))
    r.y(lambda y: y.nop(), wait=[st.now()])
    return b.finish()


def build_D_attn(x_percore, K_percore, V_percore, arows_percore, mk2_percore,
                 Wqkv, bqkv, Wo, bo, ln1w_col, ln1b_col, name="d_attn"):
    b = Bld(name)
    r = b.r
    x_in = b.din("x", x_percore)
    K_in = b.din("K", K_percore, BF16)
    V_in = b.din("V", V_percore, BF16)
    ar_in = b.din("arows", arows_percore)
    mk_in = b.din("mk2", mk2_percore)
    wq_in = b.din("wqkv", np.ascontiguousarray(Wqkv))
    qcol = np.ascontiguousarray(bqkv[0:1024].reshape(8, 128).T)
    qb_in = b.din("qbc", qcol)
    wo_in = b.din("wo", np.ascontiguousarray(Wo))
    bo_in = b.din("boc", np.ascontiguousarray(bo.reshape(8, 128).T))
    hs_in = b.din("hsel", _hsel_const())
    l1w_in = b.din("l1w", np.ascontiguousarray(ln1w_col))
    l1b_in = b.din("l1b", np.ascontiguousarray(ln1b_col))
    x4_out = b.dout("x4", [D, T], F32)

    ld = b.sem()
    C = reg_consts(b, [("ones_row", F32), ("ones_row_b", BF16), ("ones_col", BF16),
                       ("ident", BF16), ("triu", BF16), ("cvals", F32)], ld)
    Ks = b.sbuf("Ks", [128, 8 * T], BF16)
    Vs = b.sbuf("Vs", [128, 32 * 1040], BF16)
    wq = [b.sbuf("wq%d" % i, [128, 1024], BF16) for i in range(NCH)]
    wo = [b.sbuf("wo%d" % i, [128, 1024], BF16) for i in range(NCH)]
    hsel = b.sbuf("hsel_s", [64, 256], BF16)
    cols = b.sbuf("cols_s", [128, 4 * 8], F32)
    qbc, boc, l1wc, l1bc = (cols[:, 0:8], cols[:, 8:16], cols[:, 16:24], cols[:, 24:32])
    mk2 = b.sbuf("mk2_s", [16, 1], F32)
    r.g(lambda g: g.dma_start(out=Ks[:, :], in_=K_in[:, :]), inc=ld.inc(16))
    r.g(lambda g: g.dma_start(out=Vs[:, :], in_=V_in[:, :]), inc=ld.inc(16))
    for i in range(NCH):
        r.g(lambda g, i=i: g.dma_start(out=wq[i][:, :], in_=wq_in[i * 128:(i + 1) * 128, 0:1024]),
            inc=ld.inc(16))
        r.g(lambda g, i=i: g.dma_start(out=wo[i][:, :], in_=wo_in[i * 128:(i + 1) * 128, :]),
            inc=ld.inc(16))
    r.g(lambda g: g.dma_start(out=hsel[:, :], in_=hs_in[:, :]), inc=ld.inc(16))
    r.g(lambda g: g.dma_start(out=qbc, in_=qb_in[:, :]), inc=ld.inc(16))
    r.g(lambda g: g.dma_start(out=boc, in_=bo_in[:, :]), inc=ld.inc(16))
    r.g(lambda g: g.dma_start(out=l1wc, in_=l1w_in[:, :]), inc=ld.inc(16))
    r.g(lambda g: g.dma_start(out=l1bc, in_=l1b_in[:, :]), inc=ld.inc(16))
    r.g(lambda g: g.dma_start(out=mk2[:, :], in_=mk_in[:, :]), inc=ld.inc(16))

    ldx = b.sem()
    mm = b.sem()
    ev = b.sem()
    es = b.sem()
    st = b.sem()

    xs = b.sbuf("xs", [128, 8 * 512], F32)
    lnx = b.sbuf("lnx", [128, 8 * 512], BF16)
    tmp32 = b.sbuf("tmp32", [128, 512], F32)
    arowst = b.sbuf("arowst", [1, 1024], F32)
    qsb = b.sbuf("qsb", [128, 8 * 512], BF16)
    sqh = b.sbuf("sqh", [64, 512], BF16)
    mcol = b.sbuf("mcol", [16, 4], F32)
    mbf = b.sbuf("mbf", [16, 1], BF16)
    mrow = b.sbuf("mrow", [1, 16], BF16)
    negM = b.sbuf("negM", [128, 16], F32)
    expT = b.sbuf("expT", [128, 512], BF16)
    rden = b.sbuf("rden", [1, 512], F32)
    rb = b.sbuf("rb", [64, 512], F32)
    ofm = lnx

    pA = b.psum("pA", [128, 6 * 512], F32)
    pT = b.psum("pT", [128, 1024], BF16)
    pool = Pool([pA[:, i * 512:(i + 1) * 512] for i in range(4)])
    poolO = Pool([pA[:, i * 512:(i + 1) * 512] for i in range(4, 6)])

    xview = x_in.rearrange("(c p) t -> p c t", p=128)
    oview = x4_out.rearrange("(c p) t -> p c t", p=128)
    arows_d = ar_in

    sems = (ld, ldx, mm, ev, es, st)
    bufs = (xs, lnx, None, None, tmp32, None, arowst, pool)
    for ti in range(8):
        t0 = ti * 512
        lnw = _ln_gain_stream(b, C, r, sems, bufs, xview, ti, (l1wc, l1bc), False, arows_d)
        for oc in range(8):
            iP, psP_, wP = pool.take()
            for dc in range(NCH):
                r.t(lambda t, dc=dc, oc=oc, p=psP_:
                    MM(t, p[:, 0:512], wq[dc][:, oc * 128:oc * 128 + 128],
                       lnx[:, dc * 512:(dc + 1) * 512], dc == 0, dc == NCH - 1),
                    wait=([lnw] + wP) if dc == 0 else [],
                    inc=(mm.inc() if dc == NCH - 1 else None))
            r.v(lambda v, oc=oc, p=psP_: v.tensor_scalar(qsb[:, oc * 512:(oc + 1) * 512],
                                                         p[:, 0:512], qbc[:, oc:oc + 1],
                                                         None, ALU.add),
                wait=[mm.now()], inc=ev.inc())
            pool.done(iP, ev.now())
        iNQ, psNQ, wNQ = poolO.take()
        for h in range(16):
            r0, hp = (h % 2) * 64, h // 2
            qop = qsb[r0:r0 + 64, hp * 512: hp * 512 + 512]
            r.v(lambda v, qop=qop: v.tensor_tensor(sqh[:, :], qop, qop, ALU.mult),
                wait=[ev.now(), mm.now()], inc=ev.inc())
            r.t(lambda t, h=h, p=psNQ: MM(t, p[0:16, 0:512], hsel[:, h * 16:h * 16 + 16],
                                          sqh[:, :], h == 0, h == 15),
                wait=[ev.now()] + (wNQ if h == 0 else []), inc=mm.inc())
        r.v(lambda v, p=psNQ: v.reduce_max(mcol[:, 1:2], p[0:16, 0:512], axis=AX),
            wait=[mm.now()], inc=ev.inc())
        poolO.done(iNQ, ev.now())
        r.v(lambda v: v.tensor_tensor(mcol[:, 2:3], mcol[:, 1:2], mk2[:, 0:1], ALU.mult),
            wait=[ev.now()], inc=ev.inc())
        r.v(lambda v: v.tensor_scalar(mcol[:, 3:4], mcol[:, 2:3], 1e-30, None, ALU.max),
            wait=[ev.now()], inc=ev.inc())
        r.s(lambda s: s.activation(mcol[:, 2:3], mcol[:, 3:4], AF.Ln), wait=[ev.now()],
            inc=es.inc())
        r.s(lambda s: s.activation(mcol[:, 3:4], mcol[:, 2:3], AF.Exp, scale=0.5),
            inc=es.inc())
        r.v(lambda v: v.tensor_scalar(mbf[:, :], mcol[:, 3:4], 0.133, None, ALU.mult),
            wait=[es.now()], inc=ev.inc())
        r.t(lambda t: t.transpose(pT[0:1, 0:16], mbf[:, :], C["ident"][0:16, 0:16]),
            wait=[ev.now()], inc=mm.inc())
        r.v(lambda v: v.tensor_copy(mrow[:, :], pT[0:1, 0:16]), wait=[mm.now()], inc=ev.inc())
        iM, psM_, wM = pool.take()
        r.t(lambda t, p=psM_: MM(t, p[:, 0:16], C["ones_row_b"][:, :], mrow[:, :], True, True),
            wait=[ev.now()] + wM, inc=mm.inc())
        r.v(lambda v, p=psM_: v.tensor_scalar(negM[:, :], p[:, 0:16], -1.0, None, ALU.mult),
            wait=[mm.now()], inc=ev.inc())
        pool.done(iM, ev.now())
        ndep = ev.now()
        nkc = ti * 4 + 4
        for h in range(16):
            r0, hp = (h % 2) * 64, h // 2
            qop = qsb[r0:r0 + 64, hp * 512: hp * 512 + 512]
            iO, psO_, wO = poolO.take()
            for tc in range(nkc):
                iS2, psS2, wS2 = pool.take()
                r.t(lambda t, r0=r0, hp=hp, tc=tc, qop=qop, p=psS2:
                    MM(t, p[:, 0:512], Ks[r0:r0 + 64, hp * T + tc * 128: hp * T + tc * 128 + 128],
                       qop, True, True),
                    wait=([ndep] + wS2), inc=mm.inc())
                r.s(lambda s, h=h, p=psS2: s.activation(expT[:, :], p[:, 0:512], AF.Exp,
                                                        scale=0.125, bias=negM[:, h:h + 1]),
                    wait=[mm.now()], inc=es.inc())
                pool.done(iS2, es.now())
                dj = tc - ti * 4
                if dj >= 0:
                    if dj > 0:
                        r.v(lambda v, dj=dj: v.memset(expT[:, 0:dj * 128], 0.0),
                            wait=[es.now()], inc=ev.inc())
                    r.v(lambda v, dj=dj: v.tensor_tensor(expT[:, dj * 128:dj * 128 + 128],
                                                         expT[:, dj * 128:dj * 128 + 128],
                                                         C["triu"][:, :], ALU.mult),
                        wait=[es.now()], inc=ev.inc())
                    pv_dep = [ev.now()]
                else:
                    pv_dep = [es.now()]
                r.t(lambda t, tc=tc, h=h, p=psO_, first=(tc == 0), last=(tc == nkc - 1):
                    MM(t, p[0:65, 0:512], Vs[:, tc * 1040 + h * 65: tc * 1040 + h * 65 + 65],
                       expT[:, :], first, last),
                    wait=pv_dep + (wO if tc == 0 else []), inc=mm.inc())
            r.v(lambda v, p=psO_: v.reciprocal(rden[:, :], p[64:65, 0:512]),
                wait=[mm.now()], inc=ev.inc())
            iR, psR_, wR = pool.take()
            r.t(lambda t, p=psR_: MM(t, p[0:64, 0:512], C["ones_row"][0:1, 0:64], rden[:, :],
                                     True, True),
                wait=[ev.now()] + wR, inc=mm.inc())
            r.v(lambda v, p=psR_: v.tensor_copy(rb[:, :], p[0:64, 0:512]),
                wait=[mm.now()], inc=ev.inc())
            pool.done(iR, ev.now())
            r.v(lambda v, r0=r0, hp=hp, p=psO_:
                v.tensor_tensor(ofm[r0:r0 + 64, hp * 512:hp * 512 + 512], p[0:64, 0:512],
                                rb[:, :], ALU.mult),
                wait=[ev.now(), mm.now()], inc=ev.inc())
            poolO.done(iO, ev.now())
        odep = ev.now()
        for dc in range(NCH):
            iP, psP_, wP = pool.take()
            for jc in range(NCH):
                r.t(lambda t, jc=jc, dc=dc, p=psP_:
                    MM(t, p[:, 0:512], wo[jc][:, dc * 128:dc * 128 + 128],
                       ofm[:, jc * 512:(jc + 1) * 512], jc == 0, jc == NCH - 1),
                    wait=([odep] + wP) if jc == 0 else [],
                    inc=(mm.inc() if jc == NCH - 1 else None))
            xsl = xs[:, dc * 512:(dc + 1) * 512]
            r.v(lambda v, xsl=xsl, p=psP_: v.tensor_tensor(xsl, xsl, p[:, 0:512], ALU.add),
                wait=[mm.now()])
            r.v(lambda v, xsl=xsl, dc=dc: v.tensor_scalar(xsl, xsl, boc[:, dc:dc + 1],
                                                          None, ALU.add),
                inc=ev.inc())
            pool.done(iP, ev.now())
        r.y(lambda y, t0=t0: y.dma_start(out=oview[:, :, t0:t0 + 512],
                                         in_=xs[:, :].rearrange("p (c t) -> p c t", t=512)),
            wait=[ev.now()], inc=st.inc(16))
    r.y(lambda y: y.nop(), wait=[st.now()])
    return b.finish()


def build_D_moe(x_percore, ln2w_col, ln2b_col, gate_W, fc1T, fc1_cols, fc2T, fc2_cols,
                name="d_moe", dbg=False):
    NE, NF = 8, 32
    b = Bld(name)
    r = b.r
    x_in = b.din("x", x_percore)
    l2w_in = b.din("l2w", np.ascontiguousarray(ln2w_col))
    l2b_in = b.din("l2b", np.ascontiguousarray(ln2b_col))
    gw_prep = np.ascontiguousarray(
        np.asarray(gate_W).reshape(8, 128, 8).transpose(1, 0, 2).reshape(128, 64))
    gw_in = b.din("gw", gw_prep)
    f1_in = b.din("fc1T", np.ascontiguousarray(fc1T))
    f1c_in = b.din("b1cs", np.ascontiguousarray(fc1_cols))
    f2_in = b.din("fc2T", np.ascontiguousarray(fc2T))
    f2c_in = b.din("b2cs", np.ascontiguousarray(fc2_cols))
    y_out = b.dout("y", [D, T], F32)
    if dbg:
        nx2_d = b.dout("nx2_d", [128, 8 * T], BF16)
        ge_d = b.dout("ge_d", [8, T], BF16)
    else:
        nx2_d = b.nc.dram_tensor("nx2_d", [128, 8 * T], BF16, kind="Internal").ap()
        ge_d = b.nc.dram_tensor("ge_d", [8, T], BF16, kind="Internal").ap()

    ld = b.sem()
    C = reg_consts(b, [("ones_row", F32), ("ones_row_b", BF16), ("ones_col", BF16),
                       ("ident", BF16), ("cvals", F32)], ld)
    cols = b.sbuf("colsb", [128, 8 * 32 + 8 * 8 + 2 * 8], F32)
    b1cs = cols[:, 0:256]
    b2cs = cols[:, 256:320]
    l2wc, l2bc = cols[:, 320:328], cols[:, 328:336]
    gws = b.sbuf("gws", [128, 8 * 8], BF16)
    r.g(lambda g: g.dma_start(out=b1cs, in_=f1c_in[:, :]), inc=ld.inc(16))
    r.g(lambda g: g.dma_start(out=b2cs, in_=f2c_in[:, :]), inc=ld.inc(16))
    r.g(lambda g: g.dma_start(out=l2wc, in_=l2w_in[:, :]), inc=ld.inc(16))
    r.g(lambda g: g.dma_start(out=l2bc, in_=l2b_in[:, :]), inc=ld.inc(16))
    r.g(lambda g: g.dma_start(out=gws[:, :], in_=gw_in[:, :]), inc=ld.inc(16))

    ldx = b.sem()
    ldw = b.sem()
    mm = b.sem()
    ev = b.sem()
    es = b.sem()
    st = b.sem()

    xs = b.sbuf("xs", [128, 8 * 512], F32)
    lnx = b.sbuf("lnx", [128, 8 * 512], BF16)
    xbs = b.sbuf("xbs", [128, 512], BF16)
    sqs = b.sbuf("sqs", [128, 512], BF16)
    tmp32 = b.sbuf("tmp32", [128, 512], F32)
    rows4 = b.sbuf("rows4", [1, 4 * 512], F32)
    gcols = b.sbuf("gcols", [128, 8], F32)
    gtmp = b.sbuf("gtmp", [128, 8], F32)
    gtmp2 = b.sbuf("gtmp2", [128, 8], F32)
    m1c = b.sbuf("m1c", [128, 4], F32)
    gerow = b.sbuf("gerow", [1, 8 * 512], BF16)
    gebf = b.sbuf("gebf", [128, 8], BF16)
    w1 = [b.sbuf("w1_%d" % i, [128, 4096], BF16) for i in range(NCH)]
    w2 = [b.sbuf("w2_%d" % i, [128, 1024], BF16) for i in range(NF)]
    h1 = b.sbuf("h1", [128, NF * 512], BF16)
    geb = b.sbuf("geb", [128, 512], BF16)

    pA = b.psum("pA", [128, 6 * 512], F32)
    pT = b.psum("pT", [128, 1024], BF16)
    pool = Pool([pA[:, i * 512:(i + 1) * 512] for i in range(6)])

    xview = x_in.rearrange("(c p) t -> p c t", p=128)
    yview = y_out.rearrange("(c p) t -> p c t", p=128)

    for ti in range(8):
        t0 = ti * 512
        barrier = [mm.now(), ev.now(), es.now(), ld.now(), st.now()]
        r.y(lambda y, t0=t0: y.dma_start(out=xs[:, :].rearrange("p (c t) -> p c t", t=512),
                                         in_=xview[:, :, t0:t0 + 512]),
            wait=barrier, inc=ldx.inc(16))
        xw = ldx.now()
        iS, psS_, wS = pool.take()
        iQ, psQ_, wQ = pool.take()
        for c in range(NCH):
            r.v(lambda v, c=c: v.tensor_copy(xbs[:, :], xs[:, c * 512:(c + 1) * 512]),
                wait=[xw, mm.now(), es.now()] + (wS + wQ if c == 0 else []), inc=ev.inc())
            r.s(lambda s: s.activation(sqs[:, :], xbs[:, :], AF.Square),
                wait=[ev.now(), mm.now()], inc=es.inc())
            r.t(lambda t, c=c, p=psS_: MM(t, p[0:1, 0:512], C["ones_col"][:, :], xbs[:, :],
                                          c == 0, c == NCH - 1), wait=[ev.now()])
            r.t(lambda t, c=c, p=psQ_: MM(t, p[0:1, 0:512], C["ones_col"][:, :], sqs[:, :],
                                          c == 0, c == NCH - 1), wait=[es.now()], inc=mm.inc())
        rstd, brow = layernorm_rows(b, rows4, psS_, psQ_, mm.now(), ev, es,
                                    eps_ap=C["cvals"][0:1, 1:2])
        pool.done(iS, ev.now())
        pool.done(iQ, ev.now())
        iA, psA_, wA = pool.take()
        iB, psB_, wB = pool.take()
        r.t(lambda t, p=psA_, q=rstd: MM(t, p[:, 0:512], C["ones_row"][:, :], q, True, True),
            wait=[ev.now()] + wA)
        r.t(lambda t, p=psB_, q=brow: MM(t, p[:, 0:512], C["ones_row"][:, :], q, True, True),
            wait=wB, inc=mm.inc())
        mw2 = mm.now()
        for c in range(NCH):
            r.v(lambda v, c=c, p=psA_: v.tensor_tensor(tmp32[:, :], xs[:, c * 512:(c + 1) * 512],
                                                       p[:, 0:512], ALU.mult), wait=[mw2])
            r.v(lambda v, c=c, p=psB_: v.tensor_tensor(tmp32[:, :], tmp32[:, :], p[:, 0:512],
                                                       ALU.add))
            r.v(lambda v, c=c: v.tensor_scalar(lnx[:, c * 512:(c + 1) * 512], tmp32[:, :],
                                               l2wc[:, c:c + 1], l2bc[:, c:c + 1],
                                               ALU.mult, ALU.add),
                inc=(ev.inc() if c == NCH - 1 else None))
        pool.done(iA, ev.now())
        pool.done(iB, ev.now())
        lnw = ev.now()
        r.g(lambda g, t0=t0: g.dma_start(out=nx2_d[:, t0 * 8:t0 * 8 + 4096], in_=lnx[:, :]),
            wait=[lnw], inc=st.inc(16))
        r.y(lambda y, t0=t0: y.dma_start(out=yview[:, :, t0:t0 + 512],
                                         in_=xs[:, :].rearrange("p (c t) -> p c t", t=512)),
            wait=[lnw], inc=st.inc(16))
        for sub in range(4):
            iG, psG, wG = pool.take()
            for dc in range(NCH):
                r.t(lambda t, dc=dc, sub=sub, p=psG:
                    MM(t, p[:, 0:8], lnx[:, dc * 512 + sub * 128: dc * 512 + sub * 128 + 128],
                       gws[:, dc * 8:(dc + 1) * 8], dc == 0, dc == NCH - 1),
                    wait=([lnw] + wG) if dc == 0 else [],
                    inc=(mm.inc() if dc == NCH - 1 else None))
            r.v(lambda v, p=psG: v.tensor_copy(gcols[:, :], p[:, 0:8]),
                wait=[mm.now()], inc=ev.inc())
            pool.done(iG, ev.now())
            r.v(lambda v: v.reduce_max(m1c[:, 0:1], gcols[:, :], axis=AX),
                wait=[ev.now()], inc=ev.inc())
            r.v(lambda v: v.tensor_scalar(gtmp[:, :], gcols[:, :], m1c[:, 0:1], None,
                                          ALU.is_lt), wait=[ev.now()], inc=ev.inc())
            r.v(lambda v: v.tensor_tensor(gtmp2[:, :], gcols[:, :], gtmp[:, :], ALU.mult),
                wait=[ev.now()], inc=ev.inc())
            r.v(lambda v: v.tensor_scalar(gtmp[:, :], gtmp[:, :], 1e30, -1e30,
                                          ALU.mult, ALU.add), wait=[ev.now()], inc=ev.inc())
            r.v(lambda v: v.tensor_tensor(gtmp2[:, :], gtmp2[:, :], gtmp[:, :], ALU.add),
                wait=[ev.now()], inc=ev.inc())
            r.v(lambda v: v.reduce_max(m1c[:, 1:2], gtmp2[:, :], axis=AX),
                wait=[ev.now()], inc=ev.inc())
            r.v(lambda v: v.tensor_scalar(gtmp[:, :], gcols[:, :], m1c[:, 1:2], None,
                                          ALU.is_ge), wait=[ev.now()], inc=ev.inc())
            r.v(lambda v: v.tensor_scalar(m1c[:, 2:3], m1c[:, 0:1], -1.0, None, ALU.mult),
                wait=[ev.now()], inc=ev.inc())
            r.s(lambda s: s.activation(gtmp2[:, :], gcols[:, :], AF.Exp, bias=m1c[:, 2:3]),
                wait=[ev.now()], inc=es.inc())
            r.v(lambda v: v.tensor_tensor(m1c[:, 3:4], m1c[:, 1:2], m1c[:, 2:3], ALU.add),
                wait=[es.now()], inc=ev.inc())
            r.s(lambda s: s.activation(m1c[:, 1:2], m1c[:, 3:4], AF.Exp),
                wait=[ev.now()], inc=es.inc())
            r.v(lambda v: v.tensor_scalar(m1c[:, 3:4], m1c[:, 1:2], 1.0, None, ALU.add),
                wait=[es.now()], inc=ev.inc())
            r.v(lambda v: v.reciprocal(m1c[:, 2:3], m1c[:, 3:4]),
                wait=[ev.now()], inc=ev.inc())
            r.v(lambda v: v.tensor_tensor(gcols[:, :], gtmp2[:, :], gtmp[:, :], ALU.mult),
                wait=[ev.now()], inc=ev.inc())
            r.v(lambda v: v.tensor_scalar(gebf[:, :], gcols[:, :], m1c[:, 2:3], None,
                                          ALU.mult), wait=[ev.now()], inc=ev.inc())
            for e in range(NE):
                r.t(lambda t, e=e, sub=sub: t.transpose(pT[0:1, e * 128:e * 128 + 128],
                                                        gebf[:, e:e + 1],
                                                        C["ident"][:, :]),
                    wait=[ev.now()], inc=mm.inc())
            for e in range(NE):
                r.v(lambda v, e=e, sub=sub: v.tensor_copy(gerow[0:1, e * 512 + sub * 128:
                                                                e * 512 + sub * 128 + 128],
                                                          pT[0:1, e * 128:e * 128 + 128]),
                    wait=[mm.now()], inc=ev.inc())
        for e in range(NE):
            r.y(lambda y, e=e, t0=t0: y.dma_start(out=ge_d[e:e + 1, t0:t0 + 512],
                                                  in_=gerow[0:1, e * 512:(e + 1) * 512]),
                wait=[ev.now()], inc=st.inc(16))

    for e in range(NE):
        wdep = [mm.now(), es.now()]
        for i in range(NCH):
            r.g(lambda g, i=i, e=e: g.dma_start(out=w1[i][:, :],
                                                in_=f1_in[e, i * 128:(i + 1) * 128, :]),
                wait=wdep if i == 0 else [], inc=ldw.inc(16))
        for i in range(NF):
            r.g(lambda g, i=i, e=e: g.dma_start(out=w2[i][:, :],
                                                in_=f2_in[e, i * 128:(i + 1) * 128, :]),
                inc=ldw.inc(16))
        wld = ldw.now()
        for ti in range(8):
            t0 = ti * 512
            r.y(lambda y, t0=t0: y.dma_start(out=lnx[:, :], in_=nx2_d[:, t0 * 8:t0 * 8 + 4096]),
                wait=[mm.now(), ev.now(), st.now()], inc=ldx.inc(16))
            r.y(lambda y, t0=t0: y.dma_start(out=xs[:, :].rearrange("p (c t) -> p c t", t=512),
                                             in_=yview[:, :, t0:t0 + 512]),
                inc=ldx.inc(16))
            r.g(lambda g, e=e, t0=t0: g.dma_start(out=gerow[0:1, 0:512],
                                                  in_=ge_d[e:e + 1, t0:t0 + 512]),
                wait=[ev.now()], inc=ldx.inc(16))
            xw = ldx.now()
            iG, psG, wG = pool.take()
            r.t(lambda t, p=psG: MM(t, p[:, 0:512], C["ones_row_b"][:, :], gerow[0:1, 0:512],
                                    True, True),
                wait=[xw] + wG, inc=mm.inc())
            r.v(lambda v, p=psG: v.tensor_copy(geb[:, :], p[:, 0:512]),
                wait=[mm.now()], inc=ev.inc())
            pool.done(iG, ev.now())
            for fc in range(NF):
                iP, psP_, wP = pool.take()
                for dc in range(NCH):
                    r.t(lambda t, dc=dc, fc=fc, p=psP_:
                        MM(t, p[:, 0:512], w1[dc][:, fc * 128:fc * 128 + 128],
                           lnx[:, dc * 512:(dc + 1) * 512], dc == 0, dc == NCH - 1),
                        wait=([xw, wld] + wP) if dc == 0 else [],
                        inc=(mm.inc() if dc == NCH - 1 else None))
                r.s(lambda s, fc=fc, e=e, p=psP_:
                    s.activation(h1[:, fc * 512:(fc + 1) * 512], p[:, 0:512], AF.Gelu,
                                 bias=b1cs[:, e * 32 + fc:e * 32 + fc + 1]),
                    wait=[mm.now()], inc=es.inc())
                pool.done(iP, es.now())
            gw_ = es.now()
            for dc in range(NCH):
                iP, psP_, wP = pool.take()
                for fc in range(NF):
                    r.t(lambda t, fc=fc, dc=dc, p=psP_:
                        MM(t, p[:, 0:512], w2[fc][:, dc * 128:dc * 128 + 128],
                           h1[:, fc * 512:(fc + 1) * 512], fc == 0, fc == NF - 1),
                        wait=([gw_] + wP) if fc == 0 else [],
                        inc=(mm.inc() if fc == NF - 1 else None))
                r.v(lambda v, dc=dc, e=e, p=psP_:
                    v.scalar_tensor_tensor(tmp32[:, :], p[:, 0:512],
                                           b2cs[:, e * 8 + dc:e * 8 + dc + 1], geb[:, :],
                                           ALU.add, ALU.mult),
                    wait=[mm.now(), ev.now()], inc=ev.inc())
                xsl = xs[:, dc * 512:(dc + 1) * 512]
                r.v(lambda v, xsl=xsl: v.tensor_tensor(xsl, xsl, tmp32[:, :], ALU.add),
                    wait=[ev.now()], inc=ev.inc())
                pool.done(iP, ev.now())
            r.y(lambda y, t0=t0: y.dma_start(out=yview[:, :, t0:t0 + 512],
                                             in_=xs[:, :].rearrange("p (c t) -> p c t", t=512)),
                wait=[ev.now()], inc=st.inc(16))
    r.y(lambda y: y.nop(), wait=[st.now()])
    return b.finish()


def _col(vec, nch):
    return np.ascontiguousarray(np.asarray(vec, np.float32).reshape(nch, 128).T)


def _pair_cols(bias16x64):
    out = np.zeros((128, 8), np.float32)
    for h in range(16):
        out[(h % 2) * 64:(h % 2) * 64 + 64, h // 2] = bias16x64[h]
    return out


def kernel(**ins):
    x = ins["L0_attn_x"]
    x_fm = [np.ascontiguousarray(x[c].T) for c in range(NCORES)]

    bA = build_blockA(x_fm, ins["L0_attn_qkv_w"], ins["L0_attn_proj_w"],
                      ins["L0_attn_alpha_log"], ins["L0_attn_beta_log"])
    res = bA.run()
    x1 = [res.results[c]["x1"] for c in range(NCORES)]

    res = build_ln_pass(x1, pad=1, name="b_ln").run()
    xb = [res.results[c]["y"] for c in range(NCORES)]
    w1p = np.zeros((2, 1024, 5632), np.float32)
    w1p[:, :, 0:2730] = ins["L0_ffn_w1"][:, :, 0:2730]
    w1p[:, :, 2816:2816 + 2730] = ins["L0_ffn_w1"][:, :, 2730:]
    b1p = np.zeros((2, 5632), np.float32)
    b1p[:, 0:2730] = ins["L0_ffn_b1"][:, 0:2730]
    b1p[:, 2816:2816 + 2730] = ins["L0_ffn_b1"][:, 2730:]
    w2p = np.zeros((2, 2816, 1024), np.float32)
    w2p[:, 0:2730, :] = ins["L0_ffn_w2"]
    for l in range(2):
        res = build_B_qkv(xb, ins["L0_ffn_wqkv"][l], ins["L0_ffn_q_scale"][l],
                          ins["L0_ffn_k_scale"][l], name="b_qkv%d" % l).run()
        qr = [res.results[c]["qr"] for c in range(NCORES)]
        kr = [res.results[c]["kr"] for c in range(NCORES)]
        vp = [res.results[c]["vp"] for c in range(NCORES)]
        res = build_B_attn(xb, qr, kr, vp, ins["L0_ffn_wo"][l], ins["L0_ffn_q_scale"][l],
                           ins["L0_ffn_k_scale"][l], name="b_attn%d" % l).run()
        xb = [res.results[c]["xb2"] for c in range(NCORES)]
        res = build_B_ffn(xb, w1p[l], _col(b1p[l], 44), w2p[l], _col(ins["L0_ffn_b2"][l], 8),
                          _col(ins["L0_ffn_ln_g"][l], 8), _col(ins["L0_ffn_ln_b"][l], 8),
                          name="b_ffn%d" % l).run()
        xb = [res.results[c]["xb2"] for c in range(NCORES)]
    res = build_add_pass(x1, xb, b_pad=1, name="b_add").run()
    x2 = [res.results[c]["y"] for c in range(NCORES)]

    res = build_ln_pass(x2, pad=0, name="c_ln").run()
    h = [res.results[c]["y"] for c in range(NCORES)]
    for l in range(8):
        qwT = np.ascontiguousarray(ins["L1_attn_qw"][l].transpose(2, 0, 1).reshape(64, 1024))
        kwT = np.ascontiguousarray(ins["L1_attn_kw"][l].transpose(2, 0, 1).reshape(64, 1024))
        vwT = np.ascontiguousarray(ins["L1_attn_vw"][l].transpose(2, 0, 1).reshape(64, 1024))
        vbf = np.zeros(1040, np.float32)
        for hh in range(16):
            vbf[hh * 65:hh * 65 + 64] = ins["L1_attn_vb"][l, hh]
        res = build_C_attn(h, _col(ins["L1_attn_ln1_w"][l], 8), _col(ins["L1_attn_ln1_b"][l], 8),
                           qwT, _pair_cols(ins["L1_attn_qb"][l]), kwT,
                           _pair_cols(ins["L1_attn_kb"][l]), vwT, vbf,
                           name="c_attn%d" % l).run()
        h = [res.results[c]["h2"] for c in range(NCORES)]
        res = build_C_ffn(h, _col(ins["L1_attn_ln2_w"][l], 8), _col(ins["L1_attn_ln2_b"][l], 8),
                          ins["L1_attn_w1"][l], _col(ins["L1_attn_b1"][l], 32),
                          ins["L1_attn_w2"][l], _col(ins["L1_attn_b2"][l], 8),
                          name="c_ffn%d" % l).run()
        h = [res.results[c]["h2"] for c in range(NCORES)]
    res = build_add_pass(x2, h, b_pad=0, name="c_add").run()
    x3 = [res.results[c]["y"] for c in range(NCORES)]

    res = build_ln_pass(x3, pad=0, name="d_ln").run()
    xd = [res.results[c]["y"] for c in range(NCORES)]
    res = build_D_kv(xd, ins["L1_ffn_Wqkv"], ins["L1_ffn_bqkv"],
                     _col(ins["L1_ffn_ln1_w"], 8), _col(ins["L1_ffn_ln1_b"], 8)).run()
    Kd = [res.results[c]["K"] for c in range(NCORES)]
    Vd = [res.results[c]["V"] for c in range(NCORES)]
    ar = [res.results[c]["arows"] for c in range(NCORES)]
    mk = [res.results[c]["mk2"] for c in range(NCORES)]
    res = build_D_attn(xd, Kd, Vd, ar, mk, ins["L1_ffn_Wqkv"], ins["L1_ffn_bqkv"],
                       ins["L1_ffn_Wo"], ins["L1_ffn_bo"],
                       _col(ins["L1_ffn_ln1_w"], 8), _col(ins["L1_ffn_ln1_b"], 8)).run()
    xd1 = [res.results[c]["x4"] for c in range(NCORES)]
    fc1T = np.ascontiguousarray(ins["L1_ffn_fc1_w"].transpose(0, 2, 1))
    fc2T = np.ascontiguousarray(ins["L1_ffn_fc2_w"].transpose(0, 2, 1))
    b1cs = np.concatenate([_col(ins["L1_ffn_fc1_b"][e], 32) for e in range(8)], axis=1)
    b2cs = np.concatenate([_col(ins["L1_ffn_fc2_b"][e], 8) for e in range(8)], axis=1)
    res = build_D_moe(xd1, _col(ins["L1_ffn_ln2_w"], 8), _col(ins["L1_ffn_ln2_b"], 8),
                      ins["L1_ffn_gate_W"], fc1T, b1cs, fc2T, b2cs).run()
    y_in = [res.results[c]["y"] for c in range(NCORES)]
    res = build_add_pass(x3, y_in, b_pad=0, name="d_add").run()
    y = [res.results[c]["y"] for c in range(NCORES)]

    return np.stack([np.ascontiguousarray(y[c].T) for c in range(NCORES)])


# version 04 (final)
# speedup vs baseline: 1.3987x; 1.3987x over previous
from contextlib import ExitStack

import numpy as np

import concourse.bass as bass
import concourse.mybir as mybir
from concourse.bass_utils import run_bass_kernel_spmd

F32 = mybir.dt.float32
BF16 = mybir.dt.bfloat16
AF = mybir.ActivationFunctionType
ALU = mybir.AluOpType
AX = mybir.AxisListType.X

NCORES = 8
T = 4096
D = 1024
NCH = D // 128


class Sem:
    def __init__(self, h):
        self.h = h
        self.n = 0

    def inc(self, k=1):
        self.n += k
        return (self.h, k)

    def now(self):
        return (self.h, self.n)


class Rec:
    ENGS = ("tensor", "vector", "scalar", "gpsimd", "sync")

    def __init__(self):
        self.ops = {e: [] for e in self.ENGS}

    def op(self, eng, fn, wait=(), inc=None):
        self.ops[eng].append((fn, tuple(wait), inc))

    def t(self, fn, wait=(), inc=None):
        self.op("tensor", fn, wait, inc)

    def v(self, fn, wait=(), inc=None):
        self.op("vector", fn, wait, inc)

    def s(self, fn, wait=(), inc=None):
        self.op("scalar", fn, wait, inc)

    def g(self, fn, wait=(), inc=None):
        self.op("gpsimd", fn, wait, inc)

    def y(self, fn, wait=(), inc=None):
        self.op("sync", fn, wait, inc)

    def emit(self, block):
        for ename in self.ENGS:
            ops = self.ops[ename]
            if not ops:
                continue

            def make(ops):
                def _(eng):
                    cur = {}
                    for fn, waits, inc in ops:
                        for (sem, v) in waits:
                            if v > 0 and cur.get(sem, -1) < v:
                                eng.wait_ge(sem, v)
                                cur[sem] = v
                        ins = fn(eng)
                        if inc is not None:
                            sem, k = inc
                            ins.then_inc(sem, k)
                return _

            getattr(block, ename)(make(ops))


class DramRef:

    def __init__(self, ap):
        self.ap = ap


class FakeRes:
    def __init__(self, outs):
        self.results = [outs] * NCORES


class FuseCtx:

    ARENA_F32 = 53160

    def __init__(self):
        self.nc = bass.Bass("TRN2", target_bir_lowering=False)
        self.stack = ExitStack()
        self.r = Rec()
        self.feeds = {}
        self.sems = [Sem(self.stack.enter_context(self.nc.semaphore("fsm%d" % i)))
                     for i in range(8)]
        self.bscr = self.stack.enter_context(self.nc.sbuf_tensor("fuse_bscr", [1, 8], F32))
        self.arena = self.stack.enter_context(
            self.nc.sbuf_tensor("fuse_arena", [128, self.ARENA_F32], F32))
        self.pA = self.stack.enter_context(self.nc.psum_tensor("fpA", [128, 3072], F32))
        self.pT = self.stack.enter_context(self.nc.psum_tensor("fpT", [128, 2048], BF16))
        self.final = None
        self.final_out = None

    def barrier(self):
        if _NO_BARRIER[0]:
            return
        r = self.r
        bs = self.sems[7]
        sc = self.bscr
        r.op("tensor", (lambda eng: eng.nop()), wait=(), inc=bs.inc())
        r.op("vector", (lambda eng: eng.engine_nop()), wait=(), inc=bs.inc())
        r.op("scalar", (lambda eng, sc=sc: eng.activation(sc[0:1, 0:1], sc[0:1, 1:2],
                                                          AF.Copy)), wait=(), inc=bs.inc())
        r.op("gpsimd", (lambda eng: eng.nop()), wait=(), inc=bs.inc())
        r.op("sync", (lambda eng: eng.nop()), wait=(), inc=bs.inc())
        mark = bs.now()
        for e in ("tensor", "vector", "scalar", "gpsimd", "sync"):
            r.op(e, (lambda eng: eng.nop()), wait=(mark,), inc=None)

    def finalize(self):
        with self.nc.Block() as blk:
            self.r.emit(blk)
        self.stack.close()

    def run(self, **kw):
        maps = [{k: (v[c] if isinstance(v, list) else v) for k, v in self.feeds.items()}
                for c in range(NCORES)]
        return run_bass_kernel_spmd(self.nc, maps, core_ids=list(range(NCORES)), **kw)


_FUSE = [None]
_NO_BARRIER = [False]


class Bld:

    def __init__(self, name):
        self.name = name
        self.ctx = _FUSE[0]
        self._semi = 0
        self._aoff = 0
        self.out_aps = {}
        if self.ctx is None:
            self.nc = bass.Bass("TRN2", target_bir_lowering=False)
            self.stack = ExitStack()
            self.r = Rec()
            self.feeds = {}
        else:
            self.nc = self.ctx.nc
            self.r = self.ctx.r
            self.feeds = self.ctx.feeds

    def sem(self):
        self._semi += 1
        if self.ctx is not None:
            assert self._semi <= 7, "segment wants too many semaphores"
            return self.ctx.sems[self._semi - 1]
        h = self.stack.enter_context(self.nc.semaphore("sm%d" % self._semi))
        return Sem(h)

    def sbuf(self, name, shape, dt):
        if self.ctx is None:
            return self.stack.enter_context(self.nc.sbuf_tensor(name, shape, dt))
        p, n = shape
        if dt is F32:
            nf = n
        else:
            nf = (n + 1) // 2
        off = self._aoff
        self._aoff = off + nf
        assert self._aoff <= FuseCtx.ARENA_F32, \
            "fuse arena overflow at %s/%s: %d" % (self.name, name, self._aoff)
        view = self.ctx.arena[0:p, off:off + nf]
        if dt is BF16:
            view = view.bitcast(BF16)[0:p, 0:n]
        return view

    def psum(self, name, shape, dt=F32):
        if self.ctx is None:
            return self.stack.enter_context(self.nc.psum_tensor(name, shape, dt))
        if dt is BF16:
            return self.ctx.pT[0:shape[0], 0:shape[1]]
        return self.ctx.pA[0:shape[0], 0:shape[1]]

    def din(self, name, arr, dtype=F32):
        if isinstance(arr, DramRef):
            return arr.ap
        if isinstance(arr, list) and isinstance(arr[0], DramRef):
            return arr[0].ap
        a0 = arr[0] if isinstance(arr, list) else arr
        nm = name if self.ctx is None else "%s_%s" % (self.name, name)
        self.feeds[nm] = arr
        return self.nc.dram_tensor(nm, list(a0.shape), dtype, kind="ExternalInput").ap()

    def dout(self, name, shape, dtype=F32):
        if self.ctx is not None and self.ctx.final != (self.name, name):
            ap = self.nc.dram_tensor("%s_%s" % (self.name, name), shape, dtype,
                                     kind="Internal").ap()
        else:
            ap = self.nc.dram_tensor(name, shape, dtype, kind="ExternalOutput").ap()
            if self.ctx is not None:
                self.ctx.final_out = name
        self.out_aps[name] = ap
        return ap

    def dram(self, name, shape, dtype=F32):
        nm = name if self.ctx is None else "%s_%s" % (self.name, name)
        return self.nc.dram_tensor(nm, shape, dtype, kind="Internal").ap()

    def finish(self):
        if self.ctx is None:
            with self.nc.Block() as blk:
                self.r.emit(blk)
            self.stack.close()
        else:
            self.ctx.barrier()
        return self

    def in_maps(self, n=NCORES):
        return [{k: (v[c] if isinstance(v, list) else v) for k, v in self.feeds.items()}
                for c in range(n)]

    def run(self, cores=None, **kw):
        if self.ctx is not None:
            return FakeRes({k: DramRef(v) for k, v in self.out_aps.items()})
        cores = cores if cores is not None else list(range(NCORES))
        maps = self.in_maps(len(cores))
        return run_bass_kernel_spmd(self.nc, maps, core_ids=cores, **kw)


class Pool:

    def __init__(self, tiles):
        self.tiles = tiles
        self.war = [None] * len(tiles)
        self.i = 0

    def take(self):
        i = self.i
        self.i = (self.i + 1) % len(self.tiles)
        w = self.war[i]
        return i, self.tiles[i], ([w] if w is not None else [])

    def done(self, i, dep):
        self.war[i] = dep


def MM(t, out, lhsT, rhs, start, stop):
    return t.matmul(out, lhsT, rhs, start=start, stop=stop, skip_group_check=True)


def common_consts():
    c = {}
    c["ones_row"] = np.ones((1, 128), np.float32)
    c["ones_row_b"] = np.ones((1, 128), np.float32)
    c["ones_col"] = np.ones((128, 1), np.float32)
    c["ident"] = np.eye(128, dtype=np.float32)
    c["triu"] = np.triu(np.ones((128, 128), np.float32))
    p = np.arange(128, dtype=np.float32)[:, None] + 128.0 * np.arange(32, dtype=np.float32)[None, :]
    c["pos_tm"] = p + 1.0
    c["cvals"] = np.array([[1.0, 1e-5]], np.float32)
    return c


def reg_consts(b, names, ld):
    cc = common_consts()
    out = {}
    for nm, dt in names:
        arr = cc[nm]
        ap = b.din("c_" + nm, np.ascontiguousarray(arr, np.float32))
        tile = b.sbuf("sb_" + nm, list(arr.shape), dt)
        b.r.g(lambda g, t=tile, a=ap: g.dma_start(out=t[:, :], in_=a[:, :]),
              inc=ld.inc(16))
        out[nm] = tile
    return out


def layernorm_rows(b, rows4, psS, psQ, dep_mm, ev, es, eps_ap=None):
    r = b.r
    mu = rows4[0:1, 0:512]
    var = rows4[0:1, 512:1024]
    rstd = rows4[0:1, 1024:1536]
    brow = rows4[0:1, 1536:2048]
    r.v(lambda v, mu=mu, psS=psS: v.tensor_scalar(mu, psS[0:1, 0:512], 1.0 / D, None, ALU.mult),
        wait=[dep_mm])
    r.v(lambda v, var=var, psQ=psQ: v.tensor_scalar(var, psQ[0:1, 0:512], 1.0 / D, None, ALU.mult),
        inc=ev.inc())
    r.s(lambda s, brow=brow, mu=mu: s.activation(brow, mu, AF.Square), wait=[ev.now()], inc=es.inc())
    r.v(lambda v, var=var, brow=brow: v.tensor_tensor(var, var, brow, ALU.subtract),
        wait=[es.now()], inc=ev.inc())
    r.s(lambda s, var=var, e=eps_ap: s.activation(var, var, AF.Ln, bias=e), wait=[ev.now()])
    r.s(lambda s, rstd=rstd, var=var: s.activation(rstd, var, AF.Exp, scale=-0.5), inc=es.inc())
    r.v(lambda v, brow=brow, mu=mu, rstd=rstd: v.tensor_tensor(brow, mu, rstd, ALU.mult),
        wait=[es.now()])
    r.v(lambda v, brow=brow: v.tensor_scalar(brow, brow, -1.0, None, ALU.mult), inc=ev.inc())
    return rstd, brow


def build_blockA(x_percore, qkv_w, proj_w, alpha_log, beta_log, dbg=False):
    b = Bld("blockA")
    r = b.r

    x_in = b.din("x", x_percore)
    wq_in = b.din("wqkv", np.ascontiguousarray(qkv_w.T))
    wp_in = b.din("wproj", np.ascontiguousarray(proj_w.T))
    al_in = b.din("alpha", np.ascontiguousarray(alpha_log.reshape(1, 16)))
    be_in = b.din("beta", np.ascontiguousarray(beta_log.reshape(1, 16)))
    x1_out = b.dout("x1", [D, T], F32)

    ld = b.sem()
    C = reg_consts(b, [("ones_row", F32), ("ones_col", BF16), ("ident", BF16),
                       ("triu", BF16), ("pos_tm", F32), ("cvals", F32)], ld)

    wq = [b.sbuf("wq%d" % i, [128, 3072], BF16) for i in range(NCH)]
    wp = [b.sbuf("wp%d" % i, [128, 1024], BF16) for i in range(NCH)]
    for i in range(NCH):
        r.g(lambda g, t=wq[i], i=i: g.dma_start(out=t[:, :], in_=wq_in[i * 128:(i + 1) * 128, :]),
            inc=ld.inc(16))
        r.g(lambda g, t=wp[i], i=i: g.dma_start(out=t[:, :], in_=wp_in[i * 128:(i + 1) * 128, :]),
            inc=ld.inc(16))
    alr = b.sbuf("alr", [1, 16], F32)
    ber = b.sbuf("ber", [1, 16], F32)
    r.g(lambda g: g.dma_start(out=alr[:, :], in_=al_in[:, :]), inc=ld.inc(16))
    r.g(lambda g: g.dma_start(out=ber[:, :], in_=be_in[:, :]), inc=ld.inc(16))

    ldx = b.sem()
    mm = b.sem()
    ev = b.sem()
    es = b.sem()
    st = b.sem()

    xs = b.sbuf("xs", [128, 8 * 1024], F32)
    lnx = b.sbuf("lnx", [128, 8 * 1024], BF16)
    xbs = b.sbuf("xbs", [128, 512], BF16)
    sqs = b.sbuf("sqs", [128, 512], BF16)
    tmp32 = b.sbuf("tmp32", [128, 512], F32)
    qphi = b.sbuf("qphi", [128, 8 * 1024], BF16)
    kphi = b.sbuf("kphi", [128, 8 * 1024], BF16)
    vtm = b.sbuf("vtm", [128, 8 * 1024], BF16)
    ofm = b.sbuf("ofm", [128, 8 * 1024], BF16)
    S32 = b.sbuf("S32", [64, 1024], F32)
    S_sb = b.sbuf("S_sb", [64, 1024], BF16)
    stage_q = b.sbuf("stage_q", [64, 128], BF16)
    stage_k = b.sbuf("stage_k", [64, 128], BF16)
    fkT = [b.sbuf("fkT%d" % i, [128, 64], BF16) for i in range(2)]
    At_sb = [b.sbuf("At%d" % i, [128, 128], BF16) for i in range(2)]
    otm = b.sbuf("otm", [128, 64], BF16)
    rows4 = b.sbuf("rows4", [1, 4 * 512], F32)
    sc16 = b.sbuf("sc16", [1, 16], F32)
    sp16 = b.sbuf("sp16", [1, 16], F32)
    scm16 = b.sbuf("scm16", [1, 16], F32)
    scf16 = b.sbuf("scf16", [1, 16], F32)
    rdn_t1 = b.sbuf("rdn_t1", [128, 32], F32)
    rdn_t2 = b.sbuf("rdn_t2", [128, 32], F32)
    scb = b.sbuf("scb", [128, 16], F32)
    rdn = b.sbuf("rdn", [128, 16 * 32], F32)

    pA = b.psum("pA", [128, 6 * 512], F32)
    pT = b.psum("pT", [128, 2048], BF16)
    pool = Pool([pA[:, i * 512:(i + 1) * 512] for i in range(4)])
    poolD = Pool([pA[:, i * 512:(i + 1) * 512] for i in range(4, 6)])
    poolT = Pool([pT[:, 0:128], pT[:, 1024:1152]])

    wld = ld.now()
    r.s(lambda s: s.activation(sc16[:, :], alr[:, :], AF.Exp), wait=[wld])
    r.s(lambda s: s.activation(sc16[:, :], sc16[:, :], AF.Ln, bias=C["cvals"][0:1, 0:1]))
    r.s(lambda s: s.activation(sp16[:, :], ber[:, :], AF.Exp))
    r.s(lambda s: s.activation(sp16[:, :], sp16[:, :], AF.Ln, bias=C["cvals"][0:1, 0:1]),
        inc=es.inc())
    r.v(lambda v: v.tensor_scalar(sc16[:, :], sc16[:, :], 1e-6, None, ALU.add), wait=[es.now()],
        inc=ev.inc())
    r.v(lambda v: v.tensor_scalar(sp16[:, :], sp16[:, :], 1e-6, None, ALU.add),
        wait=[ev.now()], inc=ev.inc())
    r.v(lambda v: v.tensor_tensor(scm16[:, :], sc16[:, :], sp16[:, :], ALU.mult),
        wait=[ev.now()], inc=ev.inc())
    r.v(lambda v: v.tensor_scalar(scf16[:, :], scm16[:, :], 8.0, None, ALU.mult),
        wait=[ev.now()], inc=ev.inc())
    i0, ps0, w0 = pool.take()
    r.t(lambda t, p=ps0: MM(t, p[:, 0:16], C["ones_row"][:, :], scf16[:, :], True, True),
        wait=[ev.now()] + w0, inc=mm.inc())
    r.v(lambda v, p=ps0: v.tensor_copy(scb[:, :], p[:, 0:16]), wait=[mm.now()], inc=ev.inc())
    pool.done(i0, ev.now())
    for h in range(16):
        sl = rdn[:, h * 32:(h + 1) * 32]
        r.v(lambda v, h=h: v.tensor_scalar(rdn_t1[:, :], C["pos_tm"][:, :], scb[:, h:h + 1],
                                           None, ALU.mult), wait=[ev.now()], inc=ev.inc())
        r.v(lambda v: v.tensor_scalar(rdn_t2[:, :], rdn_t1[:, :], 1e-6, None, ALU.max),
            wait=[ev.now()], inc=ev.inc())
        r.v(lambda v, sl=sl: v.reciprocal(sl, rdn_t2[:, :]), wait=[ev.now()], inc=ev.inc())
    r.v(lambda v: v.memset(S32[:, :], 0.0))
    r.v(lambda v: v.memset(S_sb[:, :], 0.0), inc=ev.inc())

    if dbg:
        d_scf = b.dout("d_scf", [1, 16], F32)
        d_sc16 = b.dout("d_sc16", [1, 16], F32)
        d_sp16 = b.dout("d_sp16", [1, 16], F32)
        r.y(lambda y: y.dma_start(out=d_scf[:, :], in_=scf16[:, :]), wait=[ev.now()],
            inc=st.inc(16))
        r.y(lambda y: y.dma_start(out=d_sc16[:, :], in_=sc16[:, :]), inc=st.inc(16))
        r.y(lambda y: y.dma_start(out=d_sp16[:, :], in_=sp16[:, :]), inc=st.inc(16))
        d_scb = b.dout("d_scb", [128, 16], F32)
        d_rdn = b.dout("d_rdn", [128, 512], F32)
        d_lnx = b.dout("d_lnx", [128, 8192], F32)
        d_qphi = b.dout("d_qphi", [128, 8192], F32)
        d_kphi = b.dout("d_kphi", [128, 8192], F32)
        d_vtm = b.dout("d_vtm", [128, 8192], F32)
        d_ofm = b.dout("d_ofm", [128, 8192], F32)
        d_S = b.dout("d_S", [64, 1024], F32)
        r.y(lambda y: y.dma_start(out=d_scb[:, :], in_=scb[:, :]), wait=[ev.now()],
            inc=st.inc(16))
        r.y(lambda y: y.dma_start(out=d_rdn[:, :], in_=rdn[:, :]), inc=st.inc(16))

    xview = x_in.rearrange("(c p) t -> p c t", p=128)
    oview = x1_out.rearrange("(c p) t -> p c t", p=128)

    for sti in range(4):
        t0 = sti * 1024
        barrier = [mm.now(), ev.now(), es.now(), st.now(), ld.now()]
        r.y(lambda y, t0=t0: y.dma_start(out=xs[:, :].rearrange("p (c t) -> p c t", t=1024),
                                         in_=xview[:, :, t0:t0 + 1024]),
            wait=barrier, inc=ldx.inc(16))
        xw = ldx.now()

        for tau in range(2):
            def xsl(c, tau=tau):
                return xs[:, c * 1024 + tau * 512: c * 1024 + tau * 512 + 512]

            def lsl(c, tau=tau):
                return lnx[:, c * 1024 + tau * 512: c * 1024 + tau * 512 + 512]

            iS, psS_, wS = pool.take()
            iQ, psQ_, wQ = pool.take()
            for c in range(NCH):
                r.v(lambda v, x=xsl(c): v.tensor_copy(xbs[:, :], x),
                    wait=[xw, mm.now(), es.now()] + (wS + wQ if c == 0 else []), inc=ev.inc())
                r.s(lambda s: s.activation(sqs[:, :], xbs[:, :], AF.Square),
                    wait=[ev.now(), mm.now()], inc=es.inc())
                r.t(lambda t, c=c, p=psS_: MM(t, p[0:1, 0:512], C["ones_col"][:, :], xbs[:, :],
                                              c == 0, c == NCH - 1),
                    wait=[ev.now()])
                r.t(lambda t, c=c, p=psQ_: MM(t, p[0:1, 0:512], C["ones_col"][:, :], sqs[:, :],
                                              c == 0, c == NCH - 1),
                    wait=[es.now()], inc=mm.inc())
            rstd, brow = layernorm_rows(b, rows4, psS_, psQ_, mm.now(), ev, es,
                                        eps_ap=C["cvals"][0:1, 1:2])
            pool.done(iS, ev.now())
            pool.done(iQ, ev.now())
            iA, psA_, wA = pool.take()
            iB, psB_, wB = pool.take()
            r.t(lambda t, p=psA_, q=rstd: MM(t, p[:, 0:512], C["ones_row"][:, :], q, True, True),
                wait=[ev.now()] + wA)
            r.t(lambda t, p=psB_, q=brow: MM(t, p[:, 0:512], C["ones_row"][:, :], q, True, True),
                wait=wB, inc=mm.inc())
            mw2 = mm.now()
            for c in range(NCH):
                r.v(lambda v, x=xsl(c), p=psA_: v.tensor_tensor(tmp32[:, :], x, p[:, 0:512], ALU.mult),
                    wait=[mw2])
                r.v(lambda v, o=lsl(c), p=psB_: v.tensor_tensor(o, tmp32[:, :], p[:, 0:512], ALU.add),
                    inc=(ev.inc() if c == NCH - 1 else None))
            pool.done(iA, ev.now())
            pool.done(iB, ev.now())
            lnw = ev.now()

            for qk in range(2):
                dst = qphi if qk == 0 else kphi
                for oc in range(8):
                    iP, psP_, wP = pool.take()
                    col = qk * 1024 + oc * 128
                    for dc in range(NCH):
                        r.t(lambda t, dc=dc, col=col, p=psP_, q=lsl(dc):
                            MM(t, p[:, 0:512], wq[dc][:, col:col + 128], q,
                               dc == 0, dc == NCH - 1),
                            wait=([lnw] + wP) if dc == 0 else [],
                            inc=(mm.inc() if dc == NCH - 1 else None))
                    r.v(lambda v, d=dst[:, oc * 1024 + tau * 512: oc * 1024 + tau * 512 + 512],
                        p=psP_:
                        v.tensor_scalar(d, p[:, 0:512], 0.0, 1e-6, ALU.max, ALU.add),
                        wait=[mm.now()], inc=ev.inc())
                    pool.done(iP, ev.now())
            for sub in range(4):
                cg = tau * 4 + sub
                for half in range(2):
                    iP, psP_, wP = pool.take()
                    for dc in range(NCH):
                        r.t(lambda t, dc=dc, cg=cg, half=half, p=psP_:
                            MM(t, p[:, 0:512],
                               lnx[:, dc * 1024 + cg * 128: dc * 1024 + cg * 128 + 128],
                               wq[dc][:, 2048 + half * 512: 2048 + half * 512 + 512],
                               dc == 0, dc == NCH - 1),
                            wait=([lnw] + wP) if dc == 0 else [],
                            inc=(mm.inc() if dc == NCH - 1 else None))
                    r.v(lambda v, d=vtm[:, cg * 1024 + half * 512: cg * 1024 + half * 512 + 512],
                        p=psP_: v.tensor_copy(d, p[:, 0:512]),
                        wait=[mm.now()], inc=ev.inc())
                    pool.done(iP, ev.now())

        for c in range(8):
            Cg = sti * 8 + c
            qkv_dep = ev.now()
            iD0, psD0, wD0 = poolD.take()
            iD1, psD1, wD1 = poolD.take()
            for h in range(16):
                hp, half = h // 2, h % 2
                r0 = half * 64
                qsl = qphi[r0:r0 + 64, hp * 1024 + c * 128: hp * 1024 + c * 128 + 128]
                ksl = kphi[r0:r0 + 64, hp * 1024 + c * 128: hp * 1024 + c * 128 + 128]
                vsl = vtm[:, c * 1024 + h * 64: c * 1024 + h * 64 + 64]
                if half == 1:
                    r.v(lambda v, q=qsl: v.tensor_copy(stage_q[:, :], q),
                        wait=[qkv_dep, mm.now()])
                    r.v(lambda v, k=ksl: v.tensor_copy(stage_k[:, :], k), inc=ev.inc())
                    qop, kop = stage_q[:, :], stage_k[:, :]
                    stg_dep = [ev.now()]
                else:
                    qop, kop = qsl, ksl
                    stg_dep = [qkv_dep]
                iS, psS_, wS = pool.take()
                r.t(lambda t, k=kop, q=qop, p=psS_: MM(t, p[:, 0:128], k, q, True, True),
                    wait=stg_dep + wS, inc=mm.inc())
                At = At_sb[h % 2][:, :]
                r.v(lambda v, p=psS_, At=At: v.tensor_tensor(At, p[:, 0:128], C["triu"][:, :],
                                                             ALU.mult),
                    wait=[mm.now()], inc=ev.inc())
                pool.done(iS, ev.now())
                at_dep = ev.now()
                iT, ptr_, wT = poolT.take()
                r.t(lambda t, k=kop, p=ptr_: t.transpose(p[:, 0:64], k, C["ident"][0:64, 0:64]),
                    wait=stg_dep + wT, inc=mm.inc())
                fk = fkT[h % 2][:, :]
                r.v(lambda v, p=ptr_, fk=fk: v.tensor_copy(fk, p[:, 0:64]),
                    wait=[mm.now()], inc=ev.inc())
                poolT.done(iT, ev.now())
                fk_dep = ev.now()
                iO, psO_, wO = pool.take()
                r.t(lambda t, At=At, vv=vsl, p=psO_: MM(t, p[:, 0:64], At, vv, True, False),
                    wait=[at_dep] + wO)
                ss = (h % 2) * 512 + (h // 2) * 64
                r.t(lambda t, q=qop, ss=ss, p=psO_:
                    MM(t, p[:, 0:64], q, S_sb[:, ss:ss + 64], False, True),
                    inc=mm.inc())
                psD = psD0 if half == 0 else psD1
                r.t(lambda t, fk=fk, vv=vsl, hp=hp, p=psD:
                    MM(t, p[0:64, hp * 64:hp * 64 + 64], fk, vv, True, True),
                    wait=[fk_dep] + (wD0 + wD1 if h < 2 else []), inc=mm.inc())
                mm_o = mm.now()
                r.v(lambda v, p=psO_, h=h, Cg=Cg:
                    v.tensor_scalar(otm[:, :], p[:, 0:64], rdn[:, h * 32 + Cg:h * 32 + Cg + 1],
                                    None, ALU.mult),
                    wait=[mm_o], inc=ev.inc())
                pool.done(iO, ev.now())
                iT2, ptr2_, wT2 = poolT.take()
                r.t(lambda t, p=ptr2_: t.transpose(p[0:64, 0:128], otm[:, :], C["ident"][:, :]),
                    wait=[ev.now()] + wT2, inc=mm.inc())
                r.v(lambda v, p=ptr2_,
                    d=ofm[r0:r0 + 64, hp * 1024 + c * 128: hp * 1024 + c * 128 + 128]:
                    v.tensor_copy(d, p[0:64, 0:128]),
                    wait=[mm.now()], inc=ev.inc())
                poolT.done(iT2, ev.now())
            dmm = mm.now()
            r.v(lambda v, p=psD0: v.tensor_tensor(S32[:, 0:512], S32[:, 0:512], p[0:64, 0:512],
                                                  ALU.add), wait=[dmm], inc=ev.inc())
            r.v(lambda v, p=psD1: v.tensor_tensor(S32[:, 512:1024], S32[:, 512:1024], p[0:64, 0:512],
                                                  ALU.add), wait=[ev.now()], inc=ev.inc())
            r.v(lambda v: v.tensor_copy(S_sb[:, :], S32[:, :]), wait=[ev.now()], inc=ev.inc())
            poolD.done(iD0, ev.now())
            poolD.done(iD1, ev.now())

        if dbg and sti == 0:
            r.g(lambda g: g.dma_start(out=d_lnx[:, :], in_=lnx[:, :]), wait=[ev.now()],
                inc=st.inc(16))
            r.g(lambda g: g.dma_start(out=d_qphi[:, :], in_=qphi[:, :]), inc=st.inc(16))
            r.g(lambda g: g.dma_start(out=d_kphi[:, :], in_=kphi[:, :]), inc=st.inc(16))
            r.g(lambda g: g.dma_start(out=d_vtm[:, :], in_=vtm[:, :]), inc=st.inc(16))
            r.g(lambda g: g.dma_start(out=d_ofm[:, :], in_=ofm[:, :]), inc=st.inc(16))
            r.g(lambda g: g.dma_start(out=d_S[:, :], in_=S32[:, :]), inc=st.inc(16))

        o_dep = ev.now()
        for ic in range(NCH):
            for tau2 in range(2):
                iP, psP_, wP = pool.take()
                for jc in range(NCH):
                    r.t(lambda t, jc=jc, ic=ic, tau2=tau2, p=psP_:
                        MM(t, p[:, 0:512], wp[jc][:, ic * 128:ic * 128 + 128],
                           ofm[:, jc * 1024 + tau2 * 512: jc * 1024 + tau2 * 512 + 512],
                           jc == 0, jc == NCH - 1),
                        wait=([o_dep] + wP) if jc == 0 else [],
                        inc=(mm.inc() if jc == NCH - 1 else None))
                r.v(lambda v, x=xs[:, ic * 1024 + tau2 * 512: ic * 1024 + tau2 * 512 + 512],
                    p=psP_: v.tensor_tensor(x, x, p[:, 0:512], ALU.add),
                    wait=[mm.now()], inc=ev.inc())
                pool.done(iP, ev.now())
        r.y(lambda y, t0=t0: y.dma_start(out=oview[:, :, t0:t0 + 1024],
                                         in_=xs[:, :].rearrange("p (c t) -> p c t", t=1024)),
            wait=[ev.now()], inc=st.inc(16))

    r.y(lambda y: y.nop(), wait=[st.now()])
    return b.finish()





def build_ln_pass(x_percore, pad=0, name="lnpass"):
    b = Bld(name)
    r = b.r
    x_in = b.din("x", x_percore)
    y_out = b.dout("y", [D, T + pad], F32)

    ld = b.sem()
    C = reg_consts(b, [("ones_row", F32), ("ones_col", BF16), ("cvals", F32)], ld)
    ldx = b.sem()
    mm = b.sem()
    ev = b.sem()
    es = b.sem()
    st = b.sem()

    xs = b.sbuf("xs", [128, 8 * 512], F32)
    ys = b.sbuf("ys", [128, 8 * 512], F32)
    xbs = b.sbuf("xbs", [128, 512], BF16)
    sqs = b.sbuf("sqs", [128, 512], BF16)
    tmp32 = b.sbuf("tmp32", [128, 512], F32)
    rows4 = b.sbuf("rows4", [1, 4 * 512], F32)
    zrow = b.sbuf("zrow", [128, 8], F32)

    pA = b.psum("pA", [128, 4 * 512], F32)
    pool = Pool([pA[:, i * 512:(i + 1) * 512] for i in range(4)])

    xview = x_in.rearrange("(c p) t -> p c t", p=128)
    yview = y_out.rearrange("(c p) t -> p c t", p=128)

    if pad:
        r.v(lambda v: v.memset(zrow[:, :], 0.0), inc=ev.inc())
        r.y(lambda y: y.dma_start(out=yview[:, :, 0:1],
                                  in_=zrow[:, 0:8].rearrange("p (c o) -> p c o", o=1),
                                  allow_slow_non_contiguous=True),
            wait=[ev.now()], inc=st.inc(16))

    for ti in range(8):
        t0 = ti * 512
        barrier = [mm.now(), ev.now(), es.now(), ld.now()]
        r.y(lambda y, t0=t0: y.dma_start(out=xs[:, :].rearrange("p (c t) -> p c t", t=512),
                                         in_=xview[:, :, t0:t0 + 512]),
            wait=barrier, inc=ldx.inc(16))
        xw = ldx.now()
        iS, psS_, wS = pool.take()
        iQ, psQ_, wQ = pool.take()
        for c in range(NCH):
            r.v(lambda v, c=c: v.tensor_copy(xbs[:, :], xs[:, c * 512:(c + 1) * 512]),
                wait=[xw, mm.now(), es.now()] + (wS + wQ if c == 0 else []), inc=ev.inc())
            r.s(lambda s: s.activation(sqs[:, :], xbs[:, :], AF.Square),
                wait=[ev.now(), mm.now()], inc=es.inc())
            r.t(lambda t, c=c, p=psS_: MM(t, p[0:1, 0:512], C["ones_col"][:, :], xbs[:, :],
                                          c == 0, c == NCH - 1), wait=[ev.now()])
            r.t(lambda t, c=c, p=psQ_: MM(t, p[0:1, 0:512], C["ones_col"][:, :], sqs[:, :],
                                          c == 0, c == NCH - 1), wait=[es.now()], inc=mm.inc())
        rstd, brow = layernorm_rows(b, rows4, psS_, psQ_, mm.now(), ev, es,
                                    eps_ap=C["cvals"][0:1, 1:2])
        pool.done(iS, ev.now())
        pool.done(iQ, ev.now())
        iA, psA_, wA = pool.take()
        iB, psB_, wB = pool.take()
        r.t(lambda t, p=psA_, q=rstd: MM(t, p[:, 0:512], C["ones_row"][:, :], q, True, True),
            wait=[ev.now()] + wA)
        r.t(lambda t, p=psB_, q=brow: MM(t, p[:, 0:512], C["ones_row"][:, :], q, True, True),
            wait=wB, inc=mm.inc())
        mw2 = mm.now()
        for c in range(NCH):
            r.v(lambda v, c=c, p=psA_: v.tensor_tensor(tmp32[:, :], xs[:, c * 512:(c + 1) * 512],
                                                       p[:, 0:512], ALU.mult), wait=[mw2])
            r.v(lambda v, c=c, p=psB_: v.tensor_tensor(ys[:, c * 512:(c + 1) * 512], tmp32[:, :],
                                                       p[:, 0:512], ALU.add),
                inc=(ev.inc() if c == NCH - 1 else None))
        pool.done(iA, ev.now())
        pool.done(iB, ev.now())
        r.y(lambda y, t0=t0: y.dma_start(out=yview[:, :, pad + t0: pad + t0 + 512],
                                         in_=ys[:, :].rearrange("p (c t) -> p c t", t=512)),
            wait=[ev.now()], inc=st.inc(16))
    r.y(lambda y: y.nop(), wait=[st.now()])
    return b.finish()


def rope_tables():
    DH, WINDOW = 64, 512
    half = np.arange(0, DH, 2, dtype=np.float64)
    inv_freq = 1.0 / 10000.0 ** (half / DH)
    t = np.arange(T, dtype=np.float64)
    freqs = t[:, None] * inv_freq[None, :]
    freqs = np.concatenate([freqs, freqs], axis=-1)
    base = (half + 0.4 * DH) / (1.4 * DH)
    power = (t - T // 2) / float(WINDOW // 2)
    scale = base[None, :] ** power[:, None]
    scale = np.concatenate([scale, scale], axis=-1)
    cos, sin = np.cos(freqs), np.sin(freqs)
    return (np.ascontiguousarray((cos * scale).T, np.float32),
            np.ascontiguousarray((sin * scale).T, np.float32),
            np.ascontiguousarray((cos / scale).T, np.float32),
            np.ascontiguousarray((sin / scale).T, np.float32))


def bB_masks():
    m = np.zeros((8, 128, 512), np.float32)
    tt = np.arange(128)[:, None]
    ss = np.arange(512)[None, :]
    for j in range(8):
        rel = (512 - 128 * j) + ss - tt
        m[j] = np.where((rel >= 0) & (rel <= 512), 0.0, -1e30)
    return m


def build_B_qkv(xb_percore, wqkv_l, q_scale_l, k_scale_l, name="b_qkv"):
    b = Bld(name)
    r = b.r
    xb_in = b.din("xb", xb_percore)
    wq_in = b.din("wqkv", np.ascontiguousarray(wqkv_l))
    qs_in = b.din("qs", np.ascontiguousarray(q_scale_l.reshape(1, 64)))
    ks_in = b.din("ks", np.ascontiguousarray(k_scale_l.reshape(1, 64)))
    qr_out = b.dout("qr", [128, 4 * T], BF16)
    kr_out = b.dout("kr", [128, 4 * T], BF16)
    vp_out = b.dout("vp", [128, 32 * 520], BF16)

    ld = b.sem()
    C = reg_consts(b, [("ones_row", F32), ("ones_col", BF16), ("cvals", F32)], ld)
    tabs = {}
    cms_a, sms_a, cds_a, sds_a = rope_tables()
    for nm, arr in (("cms", cms_a), ("sms", sms_a), ("cds", cds_a), ("sds", sds_a)):
        ap = b.din("t_" + nm, arr)
        tile = b.sbuf("sb_" + nm, [64, T], F32)
        r.g(lambda g, t=tile, a=ap: g.dma_start(out=t[:, :], in_=a[:, :]), inc=ld.inc(16))
        tabs[nm] = tile
    wq = [b.sbuf("wq%d" % i, [128, 1536], BF16) for i in range(NCH)]
    for i in range(NCH):
        r.g(lambda g, t=wq[i], i=i: g.dma_start(out=t[:, :], in_=wq_in[i * 128:(i + 1) * 128, :]),
            inc=ld.inc(16))
    qsr = b.sbuf("qsr", [1, 64], F32)
    ksr = b.sbuf("ksr", [1, 64], F32)
    r.g(lambda g: g.dma_start(out=qsr[:, :], in_=qs_in[:, :]), inc=ld.inc(16))
    r.g(lambda g: g.dma_start(out=ksr[:, :], in_=ks_in[:, :]), inc=ld.inc(16))

    ldx = b.sem()
    mm = b.sem()
    ev = b.sem()
    es = b.sem()
    st = b.sem()

    xs = b.sbuf("xs", [128, 8 * 513], F32)
    xsb = b.sbuf("xsb", [128, 8 * 512], BF16)
    sq = b.sbuf("sq", [64, 512], BF16)
    rows2 = b.sbuf("rows2", [1, 3 * 512], F32)
    qn = b.sbuf("qn", [64, 512], F32)
    qh = b.sbuf("qh", [64, 512], F32)
    qh2 = b.sbuf("qh2", [64, 512], F32)
    swp = b.sbuf("swp", [64, 512], F32)
    rt1 = b.sbuf("rt1", [64, 512], F32)
    rt2 = b.sbuf("rt2", [64, 512], F32)
    qrt = b.sbuf("qrt", [128, 4 * 512], BF16)
    krt = b.sbuf("krt", [128, 4 * 512], BF16)
    vpt = b.sbuf("vpt", [128, 4 * 520], BF16)
    scl = b.sbuf("scl", [64, 2], F32)

    pA = b.psum("pA", [128, 6 * 512], F32)
    pool = Pool([pA[:, i * 512:(i + 1) * 512] for i in range(6)])

    r.g(lambda g: g.dma_start(out=scl[:, 0:1], in_=qs_in.rearrange("o e -> e o")),
        inc=ld.inc(16))
    r.g(lambda g: g.dma_start(out=scl[:, 1:2], in_=ks_in.rearrange("o e -> e o")),
        inc=ld.inc(16))

    xview = xb_in.rearrange("(c p) t -> p c t", p=128)

    for ti in range(8):
        t0 = ti * 512
        barrier = [mm.now(), ev.now(), es.now(), st.now(), ld.now()]
        r.y(lambda y, t0=t0: y.dma_start(out=xs[:, :].rearrange("p (c t) -> p c t", t=513),
                                         in_=xview[:, :, t0:t0 + 513]),
            wait=barrier, inc=ldx.inc(16))
        xw = ldx.now()
        for c in range(NCH):
            off = 1 if c < 4 else 0
            r.v(lambda v, c=c, off=off:
                v.tensor_copy(xsb[:, c * 512:(c + 1) * 512],
                              xs[:, c * 513 + off: c * 513 + off + 512]),
                wait=[xw] + ([mm.now()] if ti > 0 else []),
                inc=(ev.inc() if c == NCH - 1 else None))
        xbw = ev.now()

        for qk in range(2):
            dst = qrt if qk == 0 else krt
            tabc = tabs["cms"] if qk == 0 else tabs["cds"]
            tabs_ = tabs["sms"] if qk == 0 else tabs["sds"]
            for h in range(8):
                iP, psP_, wP = pool.take()
                col = qk * 512 + h * 64
                for dc in range(NCH):
                    r.t(lambda t, dc=dc, col=col, p=psP_:
                        MM(t, p[0:64, 0:512], wq[dc][:, col:col + 64],
                           xsb[:, dc * 512:(dc + 1) * 512], dc == 0, dc == NCH - 1),
                        wait=([xbw] + wP) if dc == 0 else [],
                        inc=(mm.inc() if dc == NCH - 1 else None))
                mq = mm.now()
                r.s(lambda s, p=psP_: s.activation(sq[:, :], p[0:64, 0:512], AF.Square),
                    wait=[mq, ev.now()], inc=es.inc())
                iN, psN_, wN = pool.take()
                r.t(lambda t, p=psN_: MM(t, p[0:1, 0:512], C["ones_col"][0:64, :], sq[:, :],
                                         True, True), wait=[es.now()] + wN, inc=mm.inc())
                nr = rows2[0:1, 0:512]
                ns = rows2[0:1, 512:1024]
                r.v(lambda v, nr=nr, p=psN_: v.tensor_scalar(nr, p[0:1, 0:512], 1e-24, None,
                                                             ALU.max),
                    wait=[mm.now()], inc=ev.inc())
                pool.done(iN, ev.now())
                r.s(lambda s, nr=nr: s.activation(nr, nr, AF.Ln), wait=[ev.now()])
                r.s(lambda s, ns=ns, nr=nr: s.activation(ns, nr, AF.Exp, scale=-0.5),
                    inc=es.inc())
                iR, psR_, wR = pool.take()
                r.t(lambda t, ns=ns, p=psR_: MM(t, p[0:64, 0:512], C["ones_row"][0:1, 0:64],
                                                ns, True, True),
                    wait=[es.now()] + wR, inc=mm.inc())
                r.v(lambda v, p=psR_: v.tensor_copy(qn[:, :], p[0:64, 0:512]),
                    wait=[mm.now()], inc=ev.inc())
                pool.done(iR, ev.now())
                r.v(lambda v, p=psP_: v.tensor_tensor(qh[:, :], p[0:64, 0:512], qn[:, :],
                                                      ALU.mult),
                    wait=[ev.now()], inc=ev.inc())
                pool.done(iP, ev.now())
                if qk == 0:
                    r.v(lambda v: v.tensor_scalar(qh2[:, :], qh[:, :], scl[:, 0:1], 8.0,
                                                  ALU.mult, ALU.mult),
                        wait=[ev.now()], inc=ev.inc())
                else:
                    r.v(lambda v: v.tensor_scalar(qh2[:, :], qh[:, :], scl[:, 1:2], None,
                                                  ALU.mult),
                        wait=[ev.now()], inc=ev.inc())
                r.v(lambda v: v.tensor_scalar(swp[0:32, :], qh2[32:64, :], -1.0, None,
                                              ALU.mult), wait=[ev.now()], inc=ev.inc())
                r.v(lambda v: v.tensor_copy(swp[32:64, :], qh2[0:32, :]),
                    wait=[ev.now()], inc=ev.inc())
                r.v(lambda v, tc=tabc, t0=t0: v.tensor_tensor(rt1[:, :], qh2[:, :],
                                                              tc[:, t0:t0 + 512], ALU.mult),
                    wait=[ev.now()], inc=ev.inc())
                r.v(lambda v, ts_=tabs_, t0=t0: v.tensor_tensor(rt2[:, :], swp[:, :],
                                                                ts_[:, t0:t0 + 512], ALU.mult),
                    wait=[ev.now()], inc=ev.inc())
                r0 = (h % 2) * 64
                r.v(lambda v, d=dst[r0:r0 + 64, (h // 2) * 512:(h // 2) * 512 + 512]:
                    v.tensor_tensor(d, rt1[:, :], rt2[:, :], ALU.add),
                    wait=[ev.now()], inc=ev.inc())

        for sub in range(4):
            iP, psP_, wP = pool.take()
            for dc in range(NCH):
                r.t(lambda t, dc=dc, sub=sub, p=psP_:
                    MM(t, p[:, 0:512],
                       xsb[:, dc * 512 + sub * 128: dc * 512 + sub * 128 + 128],
                       wq[dc][:, 1024:1536], dc == 0, dc == NCH - 1),
                    wait=([xbw] + wP) if dc == 0 else [],
                    inc=(mm.inc() if dc == NCH - 1 else None))
            r.v(lambda v, sub=sub, p=psP_:
                v.tensor_copy(vpt[:, sub * 520:sub * 520 + 520]
                              .rearrange("p (h e) -> p h e", e=65)[:, :, 0:64],
                              p[:, 0:512].rearrange("p (h e) -> p h e", e=64)),
                wait=[mm.now()], inc=ev.inc())
            r.v(lambda v, sub=sub: v.memset(vpt[:, sub * 520:sub * 520 + 520]
                                            .rearrange("p (h e) -> p h e", e=65)[:, :, 64:65],
                                            1.0), inc=ev.inc())
            pool.done(iP, ev.now())

        r.y(lambda y, t0=t0: y.dma_start(
            out=qr_out.rearrange("p (g t) -> p g t", t=T)[:, :, t0:t0 + 512],
            in_=qrt[:, :].rearrange("p (g t) -> p g t", t=512)),
            wait=[ev.now()], inc=st.inc(16))
        r.y(lambda y, t0=t0: y.dma_start(
            out=kr_out.rearrange("p (g t) -> p g t", t=T)[:, :, t0:t0 + 512],
            in_=krt[:, :].rearrange("p (g t) -> p g t", t=512)),
            inc=st.inc(16))
        r.y(lambda y, ti=ti: y.dma_start(
            out=vp_out[:, ti * 2080: ti * 2080 + 2080], in_=vpt[:, :]),
            inc=st.inc(16))
    r.y(lambda y: y.nop(), wait=[st.now()])
    return b.finish()


def build_B_attn(xb_percore, qr_percore, kr_percore, vp_percore, wo_l,
                 q_scale_l, k_scale_l, name="b_attn"):
    b = Bld(name)
    r = b.r
    xb_in = b.din("xb", xb_percore)
    qr_in = b.din("qr", qr_percore, BF16)
    kr_in = b.din("kr", kr_percore, BF16)
    vp_in = b.din("vp", vp_percore, BF16)
    wo_in = b.din("wo", np.ascontiguousarray(wo_l))
    qs_in = b.din("qs", np.ascontiguousarray(q_scale_l.reshape(1, 64)))
    ks_in = b.din("ks", np.ascontiguousarray(k_scale_l.reshape(1, 64)))
    mk_in = b.din("masks", np.ascontiguousarray(
        bB_masks().transpose(1, 0, 2).reshape(128, 8 * 512)))
    xb_out = b.dout("xb2", [D, T + 1], F32)

    ld = b.sem()
    C = reg_consts(b, [("ones_row", F32), ("cvals", F32)], ld)
    qr = b.sbuf("qr_s", [128, 4 * T], BF16)
    kr = b.sbuf("kr_s", [128, 4 * T], BF16)
    vp = b.sbuf("vp_s", [128, 32 * 520], BF16)
    wo = [b.sbuf("wo%d" % i, [128, 1024], BF16) for i in range(4)]
    msk = b.sbuf("msk", [128, 8 * 512], BF16)
    qsr = b.sbuf("qsr", [1, 64], F32)
    ksr = b.sbuf("ksr", [1, 64], F32)
    r.g(lambda g: g.dma_start(out=qr[:, :], in_=qr_in[:, :]), inc=ld.inc(16))
    r.g(lambda g: g.dma_start(out=kr[:, :], in_=kr_in[:, :]), inc=ld.inc(16))
    r.g(lambda g: g.dma_start(out=vp[:, :], in_=vp_in[:, :]), inc=ld.inc(16))
    for i in range(4):
        r.g(lambda g, i=i: g.dma_start(out=wo[i][:, :], in_=wo_in[i * 128:(i + 1) * 128, :]),
            inc=ld.inc(16))
    r.g(lambda g: g.dma_start(out=msk[:, :], in_=mk_in[:, :]), inc=ld.inc(16))
    r.g(lambda g: g.dma_start(out=qsr[:, :], in_=qs_in[:, :]), inc=ld.inc(16))
    r.g(lambda g: g.dma_start(out=ksr[:, :], in_=ks_in[:, :]), inc=ld.inc(16))

    ldx = b.sem()
    mm = b.sem()
    ev = b.sem()
    es = b.sem()
    st = b.sem()

    rows = b.sbuf("rows", [1, 1024], F32)
    negM = b.sbuf("negM", [128, 1], F32)
    expT = b.sbuf("expT", [128, 512], BF16)
    rden = b.sbuf("rden", [1, 512], F32)
    rb = b.sbuf("rb", [64, 512], F32)
    ofm = b.sbuf("ofm", [128, 4 * 512], BF16)
    xt = b.sbuf("xt", [128, 8 * 512], F32)

    pA = b.psum("pA", [128, 6 * 512], F32)
    pool = Pool([pA[:, i * 512:(i + 1) * 512] for i in range(4)])
    poolO = Pool([pA[:, i * 512:(i + 1) * 512] for i in range(4, 6)])

    wld = ld.now()
    r.v(lambda v: v.reduce_max(rows[0:1, 0:1], qsr[:, :], axis=AX, apply_absolute_value=True),
        wait=[wld], inc=ev.inc())
    r.v(lambda v: v.reduce_max(rows[0:1, 1:2], ksr[:, :], axis=AX, apply_absolute_value=True),
        wait=[ev.now()], inc=ev.inc())
    r.v(lambda v: v.tensor_tensor(rows[0:1, 2:3], rows[0:1, 0:1], rows[0:1, 1:2], ALU.mult),
        wait=[ev.now()], inc=ev.inc())
    r.v(lambda v: v.tensor_scalar(rows[0:1, 3:4], rows[0:1, 2:3], -8.0, None, ALU.mult),
        wait=[ev.now()], inc=ev.inc())
    i0, ps0, w0 = pool.take()
    r.t(lambda t, p=ps0: MM(t, p[:, 0:1], C["ones_row"][:, :], rows[0:1, 3:4], True, True),
        wait=[ev.now()] + w0, inc=mm.inc())
    r.v(lambda v, p=ps0: v.tensor_copy(negM[:, :], p[:, 0:1]), wait=[mm.now()], inc=ev.inc())
    pool.done(i0, ev.now())

    xview = xb_in.rearrange("(c p) t -> p c t", p=128)
    oview = xb_out.rearrange("(c p) t -> p c t", p=128)
    zcol = b.sbuf("zcol", [128, 8], F32)
    r.v(lambda v: v.memset(zcol[:, :], 0.0), inc=ev.inc())

    for ti in range(8):
        t0 = ti * 512
        barrier = [mm.now(), ev.now(), es.now(), st.now()]
        r.y(lambda y, t0=t0: y.dma_start(out=xt[:, :].rearrange("p (c t) -> p c t", t=512),
                                         in_=xview[:, :, t0 + 1:t0 + 513]),
            wait=barrier, inc=ldx.inc(16))
        xw = ldx.now()
        for h in range(8):
            r0 = (h % 2) * 64
            qsl = qr[r0:r0 + 64, (h // 2) * T + t0: (h // 2) * T + t0 + 512]
            iO, psO_, wO = poolO.take()
            nkc = 0
            for j in range(8):
                kbase = t0 - 512 + 128 * j
                if kbase < 0:
                    continue
                kc = kbase // 128
                iS, psS_, wS = pool.take()
                r.t(lambda t, kc=kc, r0=r0, h=h, qsl=qsl, p=psS_:
                    MM(t, p[:, 0:512],
                       kr[r0:r0 + 64, (h // 2) * T + kc * 128: (h // 2) * T + kc * 128 + 128],
                       qsl, True, True),
                    wait=([wld] + wS), inc=mm.inc())
                r.v(lambda v, j=j, p=psS_:
                    v.tensor_tensor(p[:, 0:512], p[:, 0:512], msk[:, j * 512:(j + 1) * 512],
                                    ALU.add),
                    wait=[mm.now()], inc=ev.inc())
                r.s(lambda s, p=psS_: s.activation(expT[:, :], p[:, 0:512], AF.Exp,
                                                   bias=negM[:, 0:1]),
                    wait=[ev.now(), mm.now()], inc=es.inc())
                r.t(lambda t, kc=kc, h=h, p=psO_, first=(nkc == 0):
                    MM(t, p[0:65, 0:512], vp[:, kc * 520 + h * 65: kc * 520 + h * 65 + 65],
                       expT[:, :], first, False),
                    wait=[es.now()] + (wO if nkc == 0 else []), inc=mm.inc())
                pool.done(iS, es.now())
                nkc += 1
            r.v(lambda v, p=psO_: v.reciprocal(rden[:, :], p[64:65, 0:512]),
                wait=[mm.now()], inc=ev.inc())
            iR, psR_, wR = pool.take()
            r.t(lambda t, p=psR_: MM(t, p[0:64, 0:512], C["ones_row"][0:1, 0:64], rden[:, :],
                                     True, True),
                wait=[ev.now()] + wR, inc=mm.inc())
            r.v(lambda v, p=psR_: v.tensor_copy(rb[:, :], p[0:64, 0:512]),
                wait=[mm.now()], inc=ev.inc())
            pool.done(iR, ev.now())
            r.v(lambda v, p=psO_, d=ofm[r0:r0 + 64, (h // 2) * 512:(h // 2) * 512 + 512]:
                v.tensor_tensor(d, p[0:64, 0:512], rb[:, :], ALU.mult),
                wait=[ev.now(), mm.now()], inc=ev.inc())
            poolO.done(iO, ev.now())
        odep = ev.now()
        for dc in range(NCH):
            iP, psP_, wP = pool.take()
            for jc in range(4):
                r.t(lambda t, jc=jc, dc=dc, p=psP_:
                    MM(t, p[:, 0:512], wo[jc][:, dc * 128:dc * 128 + 128],
                       ofm[:, jc * 512:(jc + 1) * 512], jc == 0, jc == 3),
                    wait=([odep, xw] + wP) if jc == 0 else [],
                    inc=(mm.inc() if jc == 3 else None))
            r.v(lambda v, dc=dc, p=psP_:
                v.tensor_tensor(xt[:, dc * 512:(dc + 1) * 512], xt[:, dc * 512:(dc + 1) * 512],
                                p[:, 0:512], ALU.add),
                wait=[mm.now()], inc=ev.inc())
            pool.done(iP, ev.now())
        r.y(lambda y, t0=t0: y.dma_start(out=oview[:, :, t0 + 1:t0 + 513],
                                         in_=xt[:, :].rearrange("p (c t) -> p c t", t=512)),
            wait=[ev.now()], inc=st.inc(16))
    r.y(lambda y: y.dma_start(out=oview[:, :, 0:1],
                              in_=zcol[:, :].rearrange("p (c z) -> p c z", z=1),
                              allow_slow_non_contiguous=True),
        wait=[ev.now()], inc=st.inc(16))
    r.y(lambda y: y.nop(), wait=[st.now()])
    return b.finish()


def build_B_ffn(xb_percore, w1p_l, b1p_col, w2p_l, b2_col, lg_col, lb_col,
                name="b_ffn"):
    NF1 = 44
    NF2 = 22
    b = Bld(name)
    r = b.r
    xb_in = b.din("xb", xb_percore)
    w1_in = b.din("w1p", np.ascontiguousarray(w1p_l))
    w2_in = b.din("w2p", np.ascontiguousarray(w2p_l))
    b1_in = b.din("b1c", np.ascontiguousarray(b1p_col))
    b2_in = b.din("b2c", np.ascontiguousarray(b2_col))
    lg_in = b.din("lgc", np.ascontiguousarray(lg_col))
    lb_in = b.din("lbc", np.ascontiguousarray(lb_col))
    xb_out = b.dout("xb2", [D, T + 1], F32)

    ld = b.sem()
    C = reg_consts(b, [("ones_row", F32), ("ones_col", BF16), ("cvals", F32)], ld)
    w1 = [b.sbuf("w1_%d" % i, [128, 5632], BF16) for i in range(NCH)]
    for i in range(NCH):
        r.g(lambda g, t=w1[i], i=i: g.dma_start(out=t[:, :], in_=w1_in[i * 128:(i + 1) * 128, :]),
            inc=ld.inc(16))
    w2 = [b.sbuf("w2_%d" % i, [128, 1024], BF16) for i in range(NF2)]
    for i in range(NF2):
        r.g(lambda g, t=w2[i], i=i: g.dma_start(out=t[:, :], in_=w2_in[i * 128:(i + 1) * 128, :]),
            inc=ld.inc(16))
    cols = b.sbuf("cols", [128, NF1 + 3 * 8], F32)
    b1c = cols[:, 0:NF1]
    b2c = cols[:, NF1:NF1 + 8]
    lgc = cols[:, NF1 + 8:NF1 + 16]
    lbc = cols[:, NF1 + 16:NF1 + 24]
    r.g(lambda g: g.dma_start(out=b1c, in_=b1_in[:, :]), inc=ld.inc(16))
    r.g(lambda g: g.dma_start(out=b2c, in_=b2_in[:, :]), inc=ld.inc(16))
    r.g(lambda g: g.dma_start(out=lgc, in_=lg_in[:, :]), inc=ld.inc(16))
    r.g(lambda g: g.dma_start(out=lbc, in_=lb_in[:, :]), inc=ld.inc(16))

    ldx = b.sem()
    mm = b.sem()
    ev = b.sem()
    es = b.sem()
    st = b.sem()

    xs = b.sbuf("xs", [128, 8 * 513], F32)
    xsb = b.sbuf("xsb", [128, 8 * 512], BF16)
    sqs = b.sbuf("sqs", [128, 512], BF16)
    xbs2 = b.sbuf("xbs2", [128, 512], BF16)
    tmp32 = b.sbuf("tmp32", [128, 512], F32)
    lnh = b.sbuf("lnh", [128, 8 * 512], BF16)
    aslab = b.sbuf("aslab", [128, NF2 * 512], BF16)
    gtmp = b.sbuf("gtmp", [128, 512], BF16)
    rows4 = b.sbuf("rows4", [1, 4 * 512], F32)
    zcol2 = b.sbuf("zcol2", [128, 8], F32)

    pA = b.psum("pA", [128, 6 * 512], F32)
    pool = Pool([pA[:, i * 512:(i + 1) * 512] for i in range(6)])

    xview = xb_in.rearrange("(c p) t -> p c t", p=128)
    oview = xb_out.rearrange("(c p) t -> p c t", p=128)

    for ti in range(8):
        t0 = ti * 512
        barrier = [mm.now(), ev.now(), es.now(), st.now(), ld.now()]
        r.y(lambda y, t0=t0: y.dma_start(out=xs[:, :].rearrange("p (c t) -> p c t", t=513),
                                         in_=xview[:, :, t0:t0 + 513]),
            wait=barrier, inc=ldx.inc(16))
        xw = ldx.now()
        for c in range(NCH):
            off = 1 if c < 4 else 0
            r.v(lambda v, c=c, off=off:
                v.tensor_copy(xsb[:, c * 512:(c + 1) * 512],
                              xs[:, c * 513 + off: c * 513 + off + 512]),
                wait=[xw, mm.now(), es.now()],
                inc=(ev.inc() if c == NCH - 1 else None))
        xbw = ev.now()
        iS, psS_, wS = pool.take()
        iQ, psQ_, wQ = pool.take()
        for c in range(NCH):
            r.v(lambda v, c=c: v.tensor_copy(xbs2[:, :], xsb[:, c * 512:(c + 1) * 512]),
                wait=[xbw, mm.now(), es.now()] + (wS + wQ if c == 0 else []), inc=ev.inc())
            r.s(lambda s: s.activation(sqs[:, :], xbs2[:, :], AF.Square),
                wait=[ev.now(), mm.now()], inc=es.inc())
            r.t(lambda t, c=c, p=psS_: MM(t, p[0:1, 0:512], C["ones_col"][:, :], xbs2[:, :],
                                          c == 0, c == NCH - 1), wait=[ev.now()])
            r.t(lambda t, c=c, p=psQ_: MM(t, p[0:1, 0:512], C["ones_col"][:, :], sqs[:, :],
                                          c == 0, c == NCH - 1), wait=[es.now()], inc=mm.inc())
        rstd, brow = layernorm_rows(b, rows4, psS_, psQ_, mm.now(), ev, es,
                                    eps_ap=C["cvals"][0:1, 1:2])
        pool.done(iS, ev.now())
        pool.done(iQ, ev.now())
        iA, psA_, wA = pool.take()
        iB, psB_, wB = pool.take()
        r.t(lambda t, p=psA_, q=rstd: MM(t, p[:, 0:512], C["ones_row"][:, :], q, True, True),
            wait=[ev.now()] + wA)
        r.t(lambda t, p=psB_, q=brow: MM(t, p[:, 0:512], C["ones_row"][:, :], q, True, True),
            wait=wB, inc=mm.inc())
        mw2 = mm.now()
        for c in range(NCH):
            off = 1 if c < 4 else 0
            hsl = xs[:, c * 513 + off: c * 513 + off + 512]
            r.v(lambda v, hsl=hsl, p=psA_: v.tensor_tensor(tmp32[:, :], hsl, p[:, 0:512],
                                                           ALU.mult), wait=[mw2])
            r.v(lambda v, p=psB_: v.tensor_tensor(tmp32[:, :], tmp32[:, :], p[:, 0:512],
                                                  ALU.add))
            r.v(lambda v, c=c: v.tensor_scalar(lnh[:, c * 512:(c + 1) * 512], tmp32[:, :],
                                               lgc[:, c:c + 1], lbc[:, c:c + 1],
                                               ALU.mult, ALU.add),
                inc=(ev.inc() if c == NCH - 1 else None))
        pool.done(iA, ev.now())
        pool.done(iB, ev.now())
        lnw = ev.now()
        for fc in range(NF1):
            iP, psP_, wP = pool.take()
            for dc in range(NCH):
                r.t(lambda t, dc=dc, fc=fc, p=psP_:
                    MM(t, p[:, 0:512], w1[dc][:, fc * 128:fc * 128 + 128],
                       lnh[:, dc * 512:(dc + 1) * 512], dc == 0, dc == NCH - 1),
                    wait=([lnw] + wP) if dc == 0 else [],
                    inc=(mm.inc() if dc == NCH - 1 else None))
            if fc < NF2:
                r.v(lambda v, fc=fc, p=psP_:
                    v.tensor_scalar(aslab[:, fc * 512:(fc + 1) * 512], p[:, 0:512],
                                    b1c[:, fc:fc + 1], None, ALU.add),
                    wait=[mm.now()], inc=ev.inc())
                pool.done(iP, ev.now())
            else:
                r.s(lambda s, fc=fc, p=psP_:
                    s.activation(gtmp[:, :], p[:, 0:512], AF.Gelu, bias=b1c[:, fc:fc + 1]),
                    wait=[mm.now(), ev.now()], inc=es.inc())
                pool.done(iP, es.now())
                g2 = fc - NF2
                r.v(lambda v, g2=g2:
                    v.tensor_tensor(aslab[:, g2 * 512:(g2 + 1) * 512],
                                    aslab[:, g2 * 512:(g2 + 1) * 512], gtmp[:, :], ALU.mult),
                    wait=[es.now()], inc=ev.inc())
        gw = ev.now()
        for dc in range(NCH):
            iP, psP_, wP = pool.take()
            for fc in range(NF2):
                r.t(lambda t, fc=fc, dc=dc, p=psP_:
                    MM(t, p[:, 0:512], w2[fc][:, dc * 128:dc * 128 + 128],
                       aslab[:, fc * 512:(fc + 1) * 512], fc == 0, fc == NF2 - 1),
                    wait=([gw] + wP) if fc == 0 else [],
                    inc=(mm.inc() if fc == NF2 - 1 else None))
            xsl = xs[:, dc * 513 + 1: dc * 513 + 1 + 512]
            r.v(lambda v, xsl=xsl, p=psP_: v.tensor_tensor(tmp32[:, :], xsl, p[:, 0:512],
                                                           ALU.add),
                wait=[mm.now()])
            r.v(lambda v, xsl=xsl, dc=dc: v.tensor_scalar(xsl, tmp32[:, :],
                                                          b2c[:, dc:dc + 1], None, ALU.add),
                inc=ev.inc())
            pool.done(iP, ev.now())
        r.y(lambda y, t0=t0: y.dma_start(
            out=oview[:, :, t0 + 1:t0 + 513],
            in_=xs[:, :].rearrange("p (c t) -> p c t", t=513)[:, :, 1:513]),
            wait=[ev.now()], inc=st.inc(16))
    r.v(lambda v: v.memset(zcol2[:, :], 0.0), inc=ev.inc())
    r.y(lambda y: y.dma_start(out=oview[:, :, 0:1],
                              in_=zcol2[:, :].rearrange("p (c z) -> p c z", z=1),
                              allow_slow_non_contiguous=True),
        wait=[ev.now()], inc=st.inc(16))
    r.y(lambda y: y.nop(), wait=[st.now()])
    return b.finish()


def build_add_pass(a_percore, b_percore, b_pad=0, name="addpass"):
    b_ = Bld(name)
    r = b_.r
    a_in = b_.din("a", a_percore)
    bb_in = b_.din("b", b_percore)
    y_out = b_.dout("y", [D, T], F32)
    ldx = b_.sem()
    ev = b_.sem()
    st = b_.sem()
    at = b_.sbuf("at", [128, 8 * 512], F32)
    bt = b_.sbuf("bt", [128, 8 * 512], F32)
    aview = a_in.rearrange("(c p) t -> p c t", p=128)
    bview = bb_in.rearrange("(c p) t -> p c t", p=128)
    yview = y_out.rearrange("(c p) t -> p c t", p=128)
    for ti in range(8):
        t0 = ti * 512
        r.y(lambda y, t0=t0: y.dma_start(out=at[:, :].rearrange("p (c t) -> p c t", t=512),
                                         in_=aview[:, :, t0:t0 + 512]),
            wait=[ev.now(), st.now()], inc=ldx.inc(16))
        r.y(lambda y, t0=t0: y.dma_start(out=bt[:, :].rearrange("p (c t) -> p c t", t=512),
                                         in_=bview[:, :, b_pad + t0:b_pad + t0 + 512]),
            inc=ldx.inc(16))
        r.v(lambda v: v.tensor_tensor(at[:, :], at[:, :], bt[:, :], ALU.add),
            wait=[ldx.now()], inc=ev.inc())
        r.y(lambda y, t0=t0: y.dma_start(out=yview[:, :, t0:t0 + 512],
                                         in_=at[:, :].rearrange("p (c t) -> p c t", t=512)),
            wait=[ev.now()], inc=st.inc(16))
    r.y(lambda y: y.nop(), wait=[st.now()])
    return b_.finish()


def _hsel_const():
    a = np.zeros((64, 16 * 16), np.float32)
    for h in range(16):
        a[:, h * 16 + h] = 1.0
    return a


def _hsel_const():
    a = np.zeros((64, 16 * 16), np.float32)
    for h in range(16):
        a[:, h * 16 + h] = 1.0
    return a


def build_C_attn(h_percore, ln1w_col, ln1b_col, qwT_sb, qb_col, kwT_sb, kb_col,
                 vwT_sb, vb_flat, name="c_attn"):
    b = Bld(name)
    r = b.r
    h_in = b.din("h", h_percore)
    qw_in = b.din("qw", np.ascontiguousarray(qwT_sb))
    kw_in = b.din("kw", np.ascontiguousarray(kwT_sb))
    vw_in = b.din("vw", np.ascontiguousarray(vwT_sb))
    l1w_in = b.din("l1w", np.ascontiguousarray(ln1w_col))
    l1b_in = b.din("l1b", np.ascontiguousarray(ln1b_col))
    qb_in = b.din("qbc", np.ascontiguousarray(qb_col))
    kb_in = b.din("kbc", np.ascontiguousarray(kb_col))
    vb_in = b.din("vbf", np.ascontiguousarray(vb_flat.reshape(1, 1040)))
    hs_in = b.din("hsel", _hsel_const())
    h_out = b.dout("h2", [D, T], F32)

    ld = b.sem()
    C = reg_consts(b, [("ones_row", F32), ("ones_row_b", BF16), ("ones_col", BF16),
                       ("ident", BF16), ("cvals", F32)], ld)
    qw = b.sbuf("qw_s", [64, 1024], BF16)
    kw = b.sbuf("kw_s", [64, 1024], BF16)
    vw = b.sbuf("vw_s", [64, 1024], BF16)
    hsel = b.sbuf("hsel_s", [64, 256], BF16)
    cols = b.sbuf("cols_s", [128, 4 * 8], F32)
    l1wc, l1bc, qbc, kbc = (cols[:, 0:8], cols[:, 8:16], cols[:, 16:24], cols[:, 24:32])
    vbb = b.sbuf("vbb", [128, 1040], F32)
    r.g(lambda g: g.dma_start(out=qw[:, :], in_=qw_in[:, :]), inc=ld.inc(16))
    r.g(lambda g: g.dma_start(out=kw[:, :], in_=kw_in[:, :]), inc=ld.inc(16))
    r.g(lambda g: g.dma_start(out=vw[:, :], in_=vw_in[:, :]), inc=ld.inc(16))
    r.g(lambda g: g.dma_start(out=hsel[:, :], in_=hs_in[:, :]), inc=ld.inc(16))
    r.g(lambda g: g.dma_start(out=l1wc, in_=l1w_in[:, :]), inc=ld.inc(16))
    r.g(lambda g: g.dma_start(out=l1bc, in_=l1b_in[:, :]), inc=ld.inc(16))
    r.g(lambda g: g.dma_start(out=qbc, in_=qb_in[:, :]), inc=ld.inc(16))
    r.g(lambda g: g.dma_start(out=kbc, in_=kb_in[:, :]), inc=ld.inc(16))
    r.g(lambda g: g.dma_start(out=vbb[:, :], in_=vb_in.broadcast_to((128, 1040))),
        inc=ld.inc(16))

    ldx = b.sem()
    mm = b.sem()
    ev = b.sem()
    es = b.sem()
    st = b.sem()

    Ks = b.sbuf("Ks", [128, 8 * T], BF16)
    Vs = b.sbuf("Vs", [128, 32 * 1040], BF16)
    xs = b.sbuf("xs", [128, 8 * 512], F32)
    lnx = b.sbuf("lnx", [128, 8 * 512], BF16)
    lstg = b.sbuf("lstg", [64, 512], BF16)
    xbs = b.sbuf("xbs", [128, 512], BF16)
    sqs = b.sbuf("sqs", [128, 512], BF16)
    tmp32 = b.sbuf("tmp32", [128, 512], F32)
    tmp32c = b.sbuf("tmp32c", [128, 512], F32)
    rows4 = b.sbuf("rows4", [1, 4 * 512], F32)
    arowst = b.sbuf("arowst", [1, 1024], F32)
    arows_d = b.dram("arows_d", [8192], F32)
    sqh = b.sbuf("sqh", [64, 512], BF16)
    qsb = b.sbuf("qsb", [128, 8 * 512], BF16)
    mk2 = b.sbuf("mk2", [16, 1], F32)
    mcol = b.sbuf("mcol", [16, 4], F32)
    mbf = b.sbuf("mbf", [16, 1], BF16)
    mrow = b.sbuf("mrow", [1, 16], BF16)
    negM = b.sbuf("negM", [128, 16], F32)
    exp2 = [b.sbuf("exp2a", [128, 1024], BF16), b.sbuf("exp2b", [128, 1024], BF16)]
    rden = b.sbuf("rden", [1, 512], F32)
    rb = b.sbuf("rb", [64, 512], F32)

    pA = b.psum("pA", [128, 6 * 512], F32)
    pT = b.psum("pT", [128, 1024], BF16)
    pool = Pool([pA[:, i * 512:(i + 1) * 512] for i in range(4)])
    poolO = Pool([pA[:, i * 512:(i + 1) * 512] for i in range(4, 6)])

    hview = h_in.rearrange("(c p) t -> p c t", p=128)
    oview = h_out.rearrange("(c p) t -> p c t", p=128)

    r.v(lambda v: v.memset(mk2[:, :], 0.0), inc=ev.inc())

    def stream_ln(ti, save_rows):
        t0 = ti * 512
        barrier = [mm.now(), ev.now(), es.now(), ld.now(), st.now()]
        r.y(lambda y, t0=t0: y.dma_start(out=xs[:, :].rearrange("p (c t) -> p c t", t=512),
                                         in_=hview[:, :, t0:t0 + 512]),
            wait=barrier, inc=ldx.inc(16))
        xw = ldx.now()
        if save_rows:
            iS, psS_, wS = pool.take()
            iQ, psQ_, wQ = pool.take()
            for c in range(NCH):
                r.v(lambda v, c=c: v.tensor_copy(xbs[:, :], xs[:, c * 512:(c + 1) * 512]),
                    wait=[xw, mm.now(), es.now()] + (wS + wQ if c == 0 else []), inc=ev.inc())
                r.s(lambda s: s.activation(sqs[:, :], xbs[:, :], AF.Square),
                    wait=[ev.now(), mm.now()], inc=es.inc())
                r.t(lambda t, c=c, p=psS_: MM(t, p[0:1, 0:512], C["ones_col"][:, :], xbs[:, :],
                                              c == 0, c == NCH - 1), wait=[ev.now()])
                r.t(lambda t, c=c, p=psQ_: MM(t, p[0:1, 0:512], C["ones_col"][:, :], sqs[:, :],
                                              c == 0, c == NCH - 1), wait=[es.now()],
                    inc=mm.inc())
            rstd, brow = layernorm_rows(b, rows4, psS_, psQ_, mm.now(), ev, es,
                                        eps_ap=C["cvals"][0:1, 1:2])
            r.y(lambda y, ti=ti, q=rstd: y.dma_start(out=arows_d[ti * 1024:ti * 1024 + 512][None, :],
                                                     in_=q),
                wait=[ev.now()], inc=st.inc(16))
            r.y(lambda y, ti=ti, q=brow: y.dma_start(
                out=arows_d[ti * 1024 + 512:ti * 1024 + 1024][None, :], in_=q),
                inc=st.inc(16))
            pool.done(iS, ev.now())
            pool.done(iQ, ev.now())
        else:
            r.y(lambda y, ti=ti: y.dma_start(out=arowst[:, :],
                                             in_=arows_d[ti * 1024:(ti + 1) * 1024][None, :]),
                wait=[ev.now(), mm.now()], inc=ldx.inc(16))
            rstd = arowst[0:1, 0:512]
            brow = arowst[0:1, 512:1024]
        iA, psA_, wA = pool.take()
        iB, psB_, wB = pool.take()
        r.t(lambda t, p=psA_, q=rstd: MM(t, p[:, 0:512], C["ones_row"][:, :], q, True, True),
            wait=[ev.now(), ldx.now()] + wA)
        r.t(lambda t, p=psB_, q=brow: MM(t, p[:, 0:512], C["ones_row"][:, :], q, True, True),
            wait=wB, inc=mm.inc())
        mw2 = mm.now()
        for c in range(NCH):
            r.v(lambda v, c=c, p=psA_: v.tensor_tensor(tmp32[:, :], xs[:, c * 512:(c + 1) * 512],
                                                       p[:, 0:512], ALU.mult), wait=[mw2])
            r.v(lambda v, c=c, p=psB_: v.tensor_tensor(tmp32[:, :], tmp32[:, :], p[:, 0:512],
                                                       ALU.add))
            r.v(lambda v, c=c: v.tensor_scalar(lnx[:, c * 512:(c + 1) * 512], tmp32[:, :],
                                               l1wc[:, c:c + 1], l1bc[:, c:c + 1],
                                               ALU.mult, ALU.add),
                inc=(ev.inc() if c == NCH - 1 else None))
        pool.done(iA, ev.now())
        pool.done(iB, ev.now())
        return ev.now()

    for ti in range(8):
        t0 = ti * 512
        lnw = stream_ln(ti, save_rows=True)
        iMK, psMK, wMK = poolO.take()
        for h in range(16):
            r0, hp = (h % 2) * 64, h // 2
            lsl = lnx[r0:r0 + 64, hp * 512: hp * 512 + 512]
            if h % 2 == 1:
                r.v(lambda v, lsl=lsl: v.tensor_copy(lstg[:, :], lsl),
                    wait=[lnw, mm.now()], inc=ev.inc())
                lop = lstg[:, :]
                ldep = [ev.now()]
            else:
                lop = lsl
                ldep = [lnw]
            iP, psP_, wP = pool.take()
            r.t(lambda t, h=h, q=lop, p=psP_: MM(t, p[0:64, 0:512], kw[:, h * 64:h * 64 + 64],
                                                 q, True, True),
                wait=ldep + wP, inc=mm.inc())
            ksl = Ks[r0:r0 + 64, hp * T + t0: hp * T + t0 + 512]
            r.v(lambda v, h=h, r0=r0, ksl=ksl, p=psP_:
                v.tensor_scalar(ksl, p[0:64, 0:512], kbc[r0:r0 + 64, hp_col(h)], None, ALU.add),
                wait=[mm.now()], inc=ev.inc())
            pool.done(iP, ev.now())
            r.v(lambda v, ksl=ksl: v.tensor_tensor(sqh[:, :], ksl, ksl, ALU.mult),
                wait=[ev.now()], inc=ev.inc())
            r.t(lambda t, h=h, p=psMK: MM(t, p[0:16, 0:512], hsel[:, h * 16:h * 16 + 16],
                                          sqh[:, :], h == 0, h == 15),
                wait=[ev.now()] + (wMK if h == 0 else []), inc=mm.inc())
            for sub in range(4):
                iV, psV_, wV = pool.take()
                r.t(lambda t, h=h, sub=sub, q=lop, p=psV_:
                    MM(t, p[:, 0:64], q[:, sub * 128:sub * 128 + 128],
                       vw[:, h * 64:h * 64 + 64], True, True),
                    wait=ldep + wV, inc=mm.inc())
                voff = (ti * 4 + sub) * 1040 + h * 65
                r.v(lambda v, voff=voff, h=h, p=psV_:
                    v.tensor_tensor(Vs[:, voff:voff + 64], p[:, 0:64],
                                    vbb[:, h * 65:h * 65 + 64], ALU.add),
                    wait=[mm.now()], inc=ev.inc())
                pool.done(iV, ev.now())
        r.v(lambda v, p=psMK: v.reduce_max(mcol[:, 0:1], p[0:16, 0:512], axis=AX),
            wait=[mm.now()], inc=ev.inc())
        r.v(lambda v: v.tensor_tensor(mk2[:, :], mk2[:, :], mcol[:, 0:1], ALU.max),
            wait=[ev.now()], inc=ev.inc())
        poolO.done(iMK, ev.now())
        for sub in range(4):
            voff = (ti * 4 + sub) * 1040
            r.v(lambda v, voff=voff:
                v.memset(Vs[:, voff:voff + 1040].rearrange("p (h e) -> p h e", e=65)[:, :, 64:65],
                         1.0), inc=ev.inc())

    for ti in range(8):
        t0 = ti * 512
        lnw = stream_ln(ti, save_rows=False)
        iNQ, psNQ, wNQ = poolO.take()
        for h in range(16):
            r0, hp = (h % 2) * 64, h // 2
            lsl = lnx[r0:r0 + 64, hp * 512: hp * 512 + 512]
            if h % 2 == 1:
                r.v(lambda v, lsl=lsl: v.tensor_copy(lstg[:, :], lsl),
                    wait=[lnw, mm.now()], inc=ev.inc())
                lop = lstg[:, :]
                ldep = [ev.now()]
            else:
                lop = lsl
                ldep = [lnw]
            iP, psP_, wP = pool.take()
            r.t(lambda t, h=h, q=lop, p=psP_: MM(t, p[0:64, 0:512], qw[:, h * 64:h * 64 + 64],
                                                 q, True, True),
                wait=ldep + wP, inc=mm.inc())
            qdst = qsb[r0:r0 + 64, hp * 512: hp * 512 + 512]
            r.v(lambda v, h=h, r0=r0, qdst=qdst, p=psP_:
                v.tensor_scalar(qdst, p[0:64, 0:512], qbc[r0:r0 + 64, hp_col(h)], None, ALU.add),
                wait=[mm.now()], inc=ev.inc())
            pool.done(iP, ev.now())
            r.v(lambda v, qdst=qdst: v.tensor_tensor(sqh[:, :], qdst, qdst, ALU.mult),
                wait=[ev.now()], inc=ev.inc())
            r.t(lambda t, h=h, p=psNQ: MM(t, p[0:16, 0:512], hsel[:, h * 16:h * 16 + 16],
                                          sqh[:, :], h == 0, h == 15),
                wait=[ev.now()] + (wNQ if h == 0 else []), inc=mm.inc())
        r.v(lambda v, p=psNQ: v.reduce_max(mcol[:, 1:2], p[0:16, 0:512], axis=AX),
            wait=[mm.now()], inc=ev.inc())
        poolO.done(iNQ, ev.now())
        r.v(lambda v: v.tensor_tensor(mcol[:, 2:3], mcol[:, 1:2], mk2[:, 0:1], ALU.mult),
            wait=[ev.now()], inc=ev.inc())
        r.v(lambda v: v.tensor_scalar(mcol[:, 3:4], mcol[:, 2:3], 1e-30, None, ALU.max),
            wait=[ev.now()], inc=ev.inc())
        r.s(lambda s: s.activation(mcol[:, 2:3], mcol[:, 3:4], AF.Ln), wait=[ev.now()],
            inc=es.inc())
        r.s(lambda s: s.activation(mcol[:, 3:4], mcol[:, 2:3], AF.Exp, scale=0.5),
            inc=es.inc())
        r.v(lambda v: v.tensor_scalar(mbf[:, :], mcol[:, 3:4], 0.133, None, ALU.mult),
            wait=[es.now()], inc=ev.inc())
        iT, ptr_, wT = Pool([pT[:, 0:16]]).take()
        r.t(lambda t: t.transpose(pT[0:1, 0:16], mbf[:, :], C["ident"][0:16, 0:16]),
            wait=[ev.now()], inc=mm.inc())
        r.v(lambda v: v.tensor_copy(mrow[:, :], pT[0:1, 0:16]), wait=[mm.now()], inc=ev.inc())
        iM, psM_, wM = pool.take()
        r.t(lambda t, p=psM_: MM(t, p[:, 0:16], C["ones_row_b"][:, :], mrow[:, :], True, True),
            wait=[ev.now()] + wM, inc=mm.inc())
        r.v(lambda v, p=psM_: v.tensor_scalar(negM[:, :], p[:, 0:16], -1.0, None, ALU.mult),
            wait=[mm.now()], inc=ev.inc())
        pool.done(iM, ev.now())
        ndep = ev.now()
        pv_war = ev.now()
        for h in range(16):
            r0, hp = (h % 2) * 64, h // 2
            qop = qsb[r0:r0 + 64, hp * 512: hp * 512 + 512]
            psO_ = poolO.tiles[0]
            es_exp = {}
            mm_pv = {}

            def rec_pv(g, h=h, psO_=psO_):
                for j in range(2):
                    tc = 2 * g + j
                    r.t(lambda t, tc=tc, h=h, g=g, j=j, p=psO_:
                        MM(t, p[0:65, 0:512],
                           Vs[:, tc * 1040 + h * 65: tc * 1040 + h * 65 + 65],
                           exp2[g % 2][:, j * 512:(j + 1) * 512],
                           (g == 0 and j == 0), (g == 15 and j == 1)),
                        wait=([es_exp[g]] + ([pv_war] if (g == 0 and j == 0) else [])),
                        inc=(mm.inc() if j == 1 else None))
                mm_pv[g] = mm.now()

            for g in range(16):
                half = g % 2
                for j in range(2):
                    tc = 2 * g + j
                    bank = pool.tiles[half * 2 + j]
                    w = [ndep] if g < 2 else [es_exp[g - 2]]
                    r.t(lambda t, h=h, r0=r0, hp=hp, tc=tc, qop=qop, p=bank:
                        MM(t, p[:, 0:512],
                           Ks[r0:r0 + 64, hp * T + tc * 128: hp * T + tc * 128 + 128],
                           qop, True, True),
                        wait=w, inc=(mm.inc() if j == 1 else None))
                w = [mm.now()]
                if g >= 2:
                    w.append(mm_pv[g - 2])
                r.s(lambda s, h=h, half=half: s.activation(
                    exp2[half][:, :], pA[:, half * 1024:half * 1024 + 1024], AF.Exp,
                    scale=0.125, bias=negM[:, h:h + 1]),
                    wait=w, inc=es.inc())
                es_exp[g] = es.now()
                if g >= 1:
                    rec_pv(g - 1)
            rec_pv(15)
            r.v(lambda v, p=psO_: v.reciprocal(rden[:, :], p[64:65, 0:512]),
                wait=[mm.now()], inc=ev.inc())
            psR_ = poolO.tiles[1]
            r.t(lambda t, p=psR_: MM(t, p[0:64, 0:512], C["ones_row"][0:1, 0:64], rden[:, :],
                                     True, True),
                wait=[ev.now()], inc=mm.inc())
            r.v(lambda v, p=psR_: v.tensor_copy(rb[:, :], p[0:64, 0:512]),
                wait=[mm.now()], inc=ev.inc())
            r.v(lambda v, r0=r0, p=psO_: v.tensor_tensor(tmp32c[r0:r0 + 64, :], p[0:64, 0:512],
                                                         rb[:, :], ALU.mult),
                wait=[ev.now()], inc=ev.inc())
            pv_war = ev.now()
            xsl = xs[r0:r0 + 64, hp * 512: hp * 512 + 512]
            r.v(lambda v, xsl=xsl, r0=r0: v.tensor_tensor(xsl, xsl, tmp32c[r0:r0 + 64, :],
                                                          ALU.add),
                wait=[ev.now()], inc=ev.inc())
        r.y(lambda y, t0=t0: y.dma_start(out=oview[:, :, t0:t0 + 512],
                                         in_=xs[:, :].rearrange("p (c t) -> p c t", t=512)),
            wait=[ev.now()], inc=st.inc(16))
    r.y(lambda y: y.nop(), wait=[st.now()])
    return b.finish()


def hp_col(h):
    return slice(h // 2, h // 2 + 1)


def build_C_ffn(h_percore, ln2w_col, ln2b_col, w1_l, b1_col, w2_l, b2_col,
                name="c_ffn"):
    NF = 32
    b = Bld(name)
    r = b.r
    h_in = b.din("h", h_percore)
    w1_in = b.din("w1", np.ascontiguousarray(w1_l))
    w2_in = b.din("w2", np.ascontiguousarray(w2_l))
    l2w_in = b.din("l2w", np.ascontiguousarray(ln2w_col))
    l2b_in = b.din("l2b", np.ascontiguousarray(ln2b_col))
    b1_in = b.din("b1c", np.ascontiguousarray(b1_col))
    b2_in = b.din("b2c", np.ascontiguousarray(b2_col))
    h_out = b.dout("h2", [D, T], F32)

    ld = b.sem()
    C = reg_consts(b, [("ones_row", F32), ("ones_col", BF16), ("cvals", F32)], ld)
    w1 = [b.sbuf("w1_%d" % i, [128, 4096], BF16) for i in range(NCH)]
    for i in range(NCH):
        r.g(lambda g, t=w1[i], i=i: g.dma_start(out=t[:, :], in_=w1_in[i * 128:(i + 1) * 128, :]),
            inc=ld.inc(16))
    w2 = [b.sbuf("w2_%d" % i, [128, 1024], BF16) for i in range(NF)]
    for i in range(NF):
        r.g(lambda g, t=w2[i], i=i: g.dma_start(out=t[:, :], in_=w2_in[i * 128:(i + 1) * 128, :]),
            inc=ld.inc(16))
    cols = b.sbuf("cols", [128, 32 + 3 * 8], F32)
    b1c, b2c, l2wc, l2bc = (cols[:, 0:32], cols[:, 32:40], cols[:, 40:48], cols[:, 48:56])
    r.g(lambda g: g.dma_start(out=b1c, in_=b1_in[:, :]), inc=ld.inc(16))
    r.g(lambda g: g.dma_start(out=b2c, in_=b2_in[:, :]), inc=ld.inc(16))
    r.g(lambda g: g.dma_start(out=l2wc, in_=l2w_in[:, :]), inc=ld.inc(16))
    r.g(lambda g: g.dma_start(out=l2bc, in_=l2b_in[:, :]), inc=ld.inc(16))

    ldx = b.sem()
    mm = b.sem()
    ev = b.sem()
    es = b.sem()
    st = b.sem()

    xs = b.sbuf("xs", [128, 8 * 512], F32)
    lnx = b.sbuf("lnx", [128, 8 * 512], BF16)
    h1 = b.sbuf("h1", [128, NF * 512], BF16)
    xbs = b.sbuf("xbs", [128, 512], BF16)
    sqs = b.sbuf("sqs", [128, 512], BF16)
    tmp32 = b.sbuf("tmp32", [128, 512], F32)
    rows4 = b.sbuf("rows4", [1, 4 * 512], F32)

    pA = b.psum("pA", [128, 6 * 512], F32)
    pool = Pool([pA[:, i * 512:(i + 1) * 512] for i in range(6)])

    hview = h_in.rearrange("(c p) t -> p c t", p=128)
    oview = h_out.rearrange("(c p) t -> p c t", p=128)

    for ti in range(8):
        t0 = ti * 512
        barrier = [mm.now(), ev.now(), es.now(), ld.now(), st.now()]
        r.y(lambda y, t0=t0: y.dma_start(out=xs[:, :].rearrange("p (c t) -> p c t", t=512),
                                         in_=hview[:, :, t0:t0 + 512]),
            wait=barrier, inc=ldx.inc(16))
        xw = ldx.now()
        iS, psS_, wS = pool.take()
        iQ, psQ_, wQ = pool.take()
        for c in range(NCH):
            r.v(lambda v, c=c: v.tensor_copy(xbs[:, :], xs[:, c * 512:(c + 1) * 512]),
                wait=[xw, mm.now(), es.now()] + (wS + wQ if c == 0 else []), inc=ev.inc())
            r.s(lambda s: s.activation(sqs[:, :], xbs[:, :], AF.Square),
                wait=[ev.now(), mm.now()], inc=es.inc())
            r.t(lambda t, c=c, p=psS_: MM(t, p[0:1, 0:512], C["ones_col"][:, :], xbs[:, :],
                                          c == 0, c == NCH - 1), wait=[ev.now()])
            r.t(lambda t, c=c, p=psQ_: MM(t, p[0:1, 0:512], C["ones_col"][:, :], sqs[:, :],
                                          c == 0, c == NCH - 1), wait=[es.now()], inc=mm.inc())
        rstd, brow = layernorm_rows(b, rows4, psS_, psQ_, mm.now(), ev, es,
                                    eps_ap=C["cvals"][0:1, 1:2])
        pool.done(iS, ev.now())
        pool.done(iQ, ev.now())
        iA, psA_, wA = pool.take()
        iB, psB_, wB = pool.take()
        r.t(lambda t, p=psA_, q=rstd: MM(t, p[:, 0:512], C["ones_row"][:, :], q, True, True),
            wait=[ev.now()] + wA)
        r.t(lambda t, p=psB_, q=brow: MM(t, p[:, 0:512], C["ones_row"][:, :], q, True, True),
            wait=wB, inc=mm.inc())
        mw2 = mm.now()
        for c in range(NCH):
            r.v(lambda v, c=c, p=psA_: v.tensor_tensor(tmp32[:, :], xs[:, c * 512:(c + 1) * 512],
                                                       p[:, 0:512], ALU.mult), wait=[mw2])
            r.v(lambda v, c=c, p=psB_: v.tensor_tensor(tmp32[:, :], tmp32[:, :], p[:, 0:512],
                                                       ALU.add))
            r.v(lambda v, c=c: v.tensor_scalar(lnx[:, c * 512:(c + 1) * 512], tmp32[:, :],
                                               l2wc[:, c:c + 1], l2bc[:, c:c + 1],
                                               ALU.mult, ALU.add),
                inc=(ev.inc() if c == NCH - 1 else None))
        pool.done(iA, ev.now())
        pool.done(iB, ev.now())
        lnw = ev.now()
        for fc in range(NF):
            iP, psP_, wP = pool.take()
            for dc in range(NCH):
                r.t(lambda t, dc=dc, fc=fc, p=psP_:
                    MM(t, p[:, 0:512], w1[dc][:, fc * 128:fc * 128 + 128],
                       lnx[:, dc * 512:(dc + 1) * 512], dc == 0, dc == NCH - 1),
                    wait=([lnw] + wP) if dc == 0 else [],
                    inc=(mm.inc() if dc == NCH - 1 else None))
            r.s(lambda s, fc=fc, p=psP_:
                s.activation(h1[:, fc * 512:(fc + 1) * 512], p[:, 0:512], AF.Gelu,
                             bias=b1c[:, fc:fc + 1]),
                wait=[mm.now()], inc=es.inc())
            pool.done(iP, es.now())
        gw = es.now()
        for dc in range(NCH):
            iP, psP_, wP = pool.take()
            for fc in range(NF):
                r.t(lambda t, fc=fc, dc=dc, p=psP_:
                    MM(t, p[:, 0:512], w2[fc][:, dc * 128:dc * 128 + 128],
                       h1[:, fc * 512:(fc + 1) * 512], fc == 0, fc == NF - 1),
                    wait=([gw] + wP) if fc == 0 else [],
                    inc=(mm.inc() if fc == NF - 1 else None))
            xsl = xs[:, dc * 512:(dc + 1) * 512]
            r.v(lambda v, xsl=xsl, p=psP_: v.tensor_tensor(xsl, xsl, p[:, 0:512], ALU.add),
                wait=[mm.now()])
            r.v(lambda v, xsl=xsl, dc=dc: v.tensor_scalar(xsl, xsl, b2c[:, dc:dc + 1],
                                                          None, ALU.add),
                inc=ev.inc())
            pool.done(iP, ev.now())
        r.y(lambda y, t0=t0: y.dma_start(out=oview[:, :, t0:t0 + 512],
                                         in_=xs[:, :].rearrange("p (c t) -> p c t", t=512)),
            wait=[ev.now()], inc=st.inc(16))
    r.y(lambda y: y.nop(), wait=[st.now()])
    return b.finish()


def bD_masks01():
    m = np.zeros((4, 128, 512), np.float32)
    tt = np.arange(128)[:, None]
    ss = np.arange(512)[None, :]
    for j in range(4):
        m[j] = (tt <= ss - 128 * j).astype(np.float32)
    return np.ascontiguousarray(m.transpose(1, 0, 2).reshape(128, 4 * 512))


def _ln_gain_stream(b, C, r, sems, bufs, hview, ti, wb_cols, save, arows_d):
    ld, ldx, mm, ev, es, st = sems
    xs, lnx, xbs, sqs, tmp32, rows4, arowst, pool = bufs
    lwc, lbc = wb_cols
    t0 = ti * 512
    barrier = [mm.now(), ev.now(), es.now(), ld.now(), st.now()]
    r.y(lambda y, t0=t0: y.dma_start(out=xs[:, :].rearrange("p (c t) -> p c t", t=512),
                                     in_=hview[:, :, t0:t0 + 512]),
        wait=barrier, inc=ldx.inc(16))
    xw = ldx.now()
    if save:
        iS, psS_, wS = pool.take()
        iQ, psQ_, wQ = pool.take()
        for c in range(NCH):
            r.v(lambda v, c=c: v.tensor_copy(xbs[:, :], xs[:, c * 512:(c + 1) * 512]),
                wait=[xw, mm.now(), es.now()] + (wS + wQ if c == 0 else []), inc=ev.inc())
            r.s(lambda s: s.activation(sqs[:, :], xbs[:, :], AF.Square),
                wait=[ev.now(), mm.now()], inc=es.inc())
            r.t(lambda t, c=c, p=psS_: MM(t, p[0:1, 0:512], C["ones_col"][:, :], xbs[:, :],
                                          c == 0, c == NCH - 1), wait=[ev.now()])
            r.t(lambda t, c=c, p=psQ_: MM(t, p[0:1, 0:512], C["ones_col"][:, :], sqs[:, :],
                                          c == 0, c == NCH - 1), wait=[es.now()], inc=mm.inc())
        rstd, brow = layernorm_rows(b, rows4, psS_, psQ_, mm.now(), ev, es,
                                    eps_ap=C["cvals"][0:1, 1:2])
        if arows_d is not None:
            r.y(lambda y, ti=ti, q=rstd: y.dma_start(
                out=arows_d[ti * 1024:ti * 1024 + 512][None, :], in_=q),
                wait=[ev.now()], inc=st.inc(16))
            r.y(lambda y, ti=ti, q=brow: y.dma_start(
                out=arows_d[ti * 1024 + 512:ti * 1024 + 1024][None, :], in_=q),
                inc=st.inc(16))
        pool.done(iS, ev.now())
        pool.done(iQ, ev.now())
    else:
        r.y(lambda y, ti=ti: y.dma_start(out=arowst[:, :],
                                         in_=arows_d[ti * 1024:(ti + 1) * 1024][None, :]),
            wait=[ev.now(), mm.now()], inc=ldx.inc(16))
        rstd = arowst[0:1, 0:512]
        brow = arowst[0:1, 512:1024]
    iA, psA_, wA = pool.take()
    iB, psB_, wB = pool.take()
    r.t(lambda t, p=psA_, q=rstd: MM(t, p[:, 0:512], C["ones_row"][:, :], q, True, True),
        wait=[ev.now(), ldx.now()] + wA)
    r.t(lambda t, p=psB_, q=brow: MM(t, p[:, 0:512], C["ones_row"][:, :], q, True, True),
        wait=wB, inc=mm.inc())
    mw2 = mm.now()
    for c in range(NCH):
        r.v(lambda v, c=c, p=psA_: v.tensor_tensor(tmp32[:, :], xs[:, c * 512:(c + 1) * 512],
                                                   p[:, 0:512], ALU.mult), wait=[mw2])
        r.v(lambda v, c=c, p=psB_: v.tensor_tensor(tmp32[:, :], tmp32[:, :], p[:, 0:512],
                                                   ALU.add))
        if lwc is not None:
            r.v(lambda v, c=c: v.tensor_scalar(lnx[:, c * 512:(c + 1) * 512], tmp32[:, :],
                                               lwc[:, c:c + 1], lbc[:, c:c + 1],
                                               ALU.mult, ALU.add),
                inc=(ev.inc() if c == NCH - 1 else None))
        else:
            r.v(lambda v, c=c: v.tensor_copy(lnx[:, c * 512:(c + 1) * 512], tmp32[:, :]),
                inc=(ev.inc() if c == NCH - 1 else None))
    pool.done(iA, ev.now())
    pool.done(iB, ev.now())
    return ev.now()


def build_D_kv(x_percore, Wqkv, bqkv, ln1w_col, ln1b_col, name="d_kv"):
    b = Bld(name)
    r = b.r
    x_in = b.din("x", x_percore)
    wq_in = b.din("wqkv", np.ascontiguousarray(Wqkv))
    kcol = np.ascontiguousarray(bqkv[1024:2048].reshape(8, 128).T)
    kb_in = b.din("kbc", kcol)
    vbf = np.zeros(1040, np.float32)
    for h in range(16):
        vbf[h * 65:h * 65 + 64] = bqkv[2048 + h * 64: 2048 + h * 64 + 64]
    vb_in = b.din("vbf", np.ascontiguousarray(vbf.reshape(1, 1040)))
    l1w_in = b.din("l1w", np.ascontiguousarray(ln1w_col))
    l1b_in = b.din("l1b", np.ascontiguousarray(ln1b_col))
    hs_in = b.din("hsel", _hsel_const())
    K_out = b.dout("K", [128, 8 * T], BF16)
    V_out = b.dout("V", [128, 32 * 1040], BF16)
    ar_out = b.dout("arows", [8192], F32)
    mk_out = b.dout("mk2", [16, 1], F32)

    ld = b.sem()
    C = reg_consts(b, [("ones_row", F32), ("ones_col", BF16), ("cvals", F32)], ld)
    wq = [b.sbuf("wq%d" % i, [128, 2048], BF16) for i in range(NCH)]
    for i in range(NCH):
        r.g(lambda g, t=wq[i], i=i: g.dma_start(out=t[:, :],
                                                in_=wq_in[i * 128:(i + 1) * 128, 1024:3072]),
            inc=ld.inc(16))
    hsel = b.sbuf("hsel_s", [64, 256], BF16)
    cols = b.sbuf("cols_s", [128, 3 * 8], F32)
    kbc, l1wc, l1bc = cols[:, 0:8], cols[:, 8:16], cols[:, 16:24]
    vbb = b.sbuf("vbb", [128, 1040], F32)
    r.g(lambda g: g.dma_start(out=hsel[:, :], in_=hs_in[:, :]), inc=ld.inc(16))
    r.g(lambda g: g.dma_start(out=kbc, in_=kb_in[:, :]), inc=ld.inc(16))
    r.g(lambda g: g.dma_start(out=l1wc, in_=l1w_in[:, :]), inc=ld.inc(16))
    r.g(lambda g: g.dma_start(out=l1bc, in_=l1b_in[:, :]), inc=ld.inc(16))
    r.g(lambda g: g.dma_start(out=vbb[:, :], in_=vb_in.broadcast_to((128, 1040))),
        inc=ld.inc(16))

    ldx = b.sem()
    mm = b.sem()
    ev = b.sem()
    es = b.sem()
    st = b.sem()

    xs = b.sbuf("xs", [128, 8 * 512], F32)
    lnx = b.sbuf("lnx", [128, 8 * 512], BF16)
    xbs = b.sbuf("xbs", [128, 512], BF16)
    sqs = b.sbuf("sqs", [128, 512], BF16)
    tmp32 = b.sbuf("tmp32", [128, 512], F32)
    rows4 = b.sbuf("rows4", [1, 4 * 512], F32)
    arowst = b.sbuf("arowst", [1, 1024], F32)
    Kt = b.sbuf("Kt", [128, 8 * 512], BF16)
    Vt = b.sbuf("Vt", [128, 4 * 1040], BF16)
    sqh = b.sbuf("sqh", [64, 512], BF16)
    mk2 = b.sbuf("mk2_s", [16, 1], F32)
    mcol = b.sbuf("mcol", [16, 4], F32)

    pA = b.psum("pA", [128, 6 * 512], F32)
    pool = Pool([pA[:, i * 512:(i + 1) * 512] for i in range(4)])
    poolO = Pool([pA[:, i * 512:(i + 1) * 512] for i in range(4, 6)])

    xview = x_in.rearrange("(c p) t -> p c t", p=128)
    arows_d = ar_out

    r.v(lambda v: v.memset(mk2[:, :], 0.0), inc=ev.inc())

    sems = (ld, ldx, mm, ev, es, st)
    bufs = (xs, lnx, xbs, sqs, tmp32, rows4, arowst, pool)
    for ti in range(8):
        t0 = ti * 512
        lnw = _ln_gain_stream(b, C, r, sems, bufs, xview, ti, (l1wc, l1bc), True, arows_d)
        for oc in range(8):
            iP, psP_, wP = pool.take()
            for dc in range(NCH):
                r.t(lambda t, dc=dc, oc=oc, p=psP_:
                    MM(t, p[:, 0:512], wq[dc][:, oc * 128:oc * 128 + 128],
                       lnx[:, dc * 512:(dc + 1) * 512], dc == 0, dc == NCH - 1),
                    wait=([lnw] + wP) if dc == 0 else [],
                    inc=(mm.inc() if dc == NCH - 1 else None))
            r.v(lambda v, oc=oc, p=psP_: v.tensor_scalar(Kt[:, oc * 512:(oc + 1) * 512],
                                                         p[:, 0:512], kbc[:, oc:oc + 1],
                                                         None, ALU.add),
                wait=[mm.now()], inc=ev.inc())
            pool.done(iP, ev.now())
        iMK, psMK, wMK = poolO.take()
        for h in range(16):
            r0, hp = (h % 2) * 64, h // 2
            ksl = Kt[r0:r0 + 64, hp * 512: hp * 512 + 512]
            r.v(lambda v, ksl=ksl: v.tensor_tensor(sqh[:, :], ksl, ksl, ALU.mult),
                wait=[ev.now(), mm.now()], inc=ev.inc())
            r.t(lambda t, h=h, p=psMK: MM(t, p[0:16, 0:512], hsel[:, h * 16:h * 16 + 16],
                                          sqh[:, :], h == 0, h == 15),
                wait=[ev.now()] + (wMK if h == 0 else []), inc=mm.inc())
        r.v(lambda v, p=psMK: v.reduce_max(mcol[:, 0:1], p[0:16, 0:512], axis=AX),
            wait=[mm.now()], inc=ev.inc())
        r.v(lambda v: v.tensor_tensor(mk2[:, :], mk2[:, :], mcol[:, 0:1], ALU.max),
            wait=[ev.now()], inc=ev.inc())
        poolO.done(iMK, ev.now())
        for sub in range(4):
            for half in range(2):
                iP, psP_, wP = pool.take()
                for dc in range(NCH):
                    r.t(lambda t, dc=dc, sub=sub, half=half, p=psP_:
                        MM(t, p[:, 0:512],
                           lnx[:, dc * 512 + sub * 128: dc * 512 + sub * 128 + 128],
                           wq[dc][:, 1024 + half * 512: 1024 + half * 512 + 512],
                           dc == 0, dc == NCH - 1),
                        wait=([lnw] + wP) if dc == 0 else [],
                        inc=(mm.inc() if dc == NCH - 1 else None))
                voff = sub * 1040 + half * 520
                r.v(lambda v, voff=voff, half=half, p=psP_:
                    v.tensor_tensor(Vt[:, voff:voff + 520].rearrange("p (h e) -> p h e", e=65)[:, :, 0:64],
                                    p[:, 0:512].rearrange("p (h e) -> p h e", e=64),
                                    vbb[:, half * 520:half * 520 + 520]
                                    .rearrange("p (h e) -> p h e", e=65)[:, :, 0:64], ALU.add),
                    wait=[mm.now()], inc=ev.inc())
                r.v(lambda v, voff=voff:
                    v.memset(Vt[:, voff:voff + 520].rearrange("p (h e) -> p h e", e=65)[:, :, 64:65],
                             1.0), inc=ev.inc())
                pool.done(iP, ev.now())
        r.y(lambda y, t0=t0: y.dma_start(
            out=K_out.rearrange("p (g t) -> p g t", t=T)[:, :, t0:t0 + 512],
            in_=Kt[:, :].rearrange("p (g t) -> p g t", t=512)),
            wait=[ev.now()], inc=st.inc(16))
        r.y(lambda y, ti=ti: y.dma_start(out=V_out[:, ti * 4160: ti * 4160 + 4160],
                                         in_=Vt[:, :]),
            inc=st.inc(16))
    r.y(lambda y: y.dma_start(out=mk_out[:, :], in_=mk2[:, :]), wait=[ev.now()],
        inc=st.inc(16))
    r.y(lambda y: y.nop(), wait=[st.now()])
    return b.finish()


def build_D_attn(x_percore, K_percore, V_percore, arows_percore, mk2_percore,
                 Wqkv, bqkv, Wo, bo, ln1w_col, ln1b_col, name="d_attn"):
    b = Bld(name)
    r = b.r
    x_in = b.din("x", x_percore)
    K_in = b.din("K", K_percore, BF16)
    V_in = b.din("V", V_percore, BF16)
    ar_in = b.din("arows", arows_percore)
    mk_in = b.din("mk2", mk2_percore)
    wq_in = b.din("wqkv", np.ascontiguousarray(Wqkv))
    qcol = np.ascontiguousarray(bqkv[0:1024].reshape(8, 128).T)
    qb_in = b.din("qbc", qcol)
    wo_in = b.din("wo", np.ascontiguousarray(Wo))
    bo_in = b.din("boc", np.ascontiguousarray(bo.reshape(8, 128).T))
    hs_in = b.din("hsel", _hsel_const())
    l1w_in = b.din("l1w", np.ascontiguousarray(ln1w_col))
    l1b_in = b.din("l1b", np.ascontiguousarray(ln1b_col))
    x4_out = b.dout("x4", [D, T], F32)

    ld = b.sem()
    C = reg_consts(b, [("ones_row", F32), ("ones_row_b", BF16), ("ones_col", BF16),
                       ("ident", BF16), ("triu", BF16), ("cvals", F32)], ld)
    Ks = b.sbuf("Ks", [128, 8 * T], BF16)
    Vs = b.sbuf("Vs", [128, 32 * 1040], BF16)
    wq = [b.sbuf("wq%d" % i, [128, 1024], BF16) for i in range(NCH)]
    wo = [b.sbuf("wo%d" % i, [128, 1024], BF16) for i in range(NCH)]
    hsel = b.sbuf("hsel_s", [64, 256], BF16)
    cols = b.sbuf("cols_s", [128, 4 * 8], F32)
    qbc, boc, l1wc, l1bc = (cols[:, 0:8], cols[:, 8:16], cols[:, 16:24], cols[:, 24:32])
    mk2 = b.sbuf("mk2_s", [16, 1], F32)
    r.g(lambda g: g.dma_start(out=Ks[:, :], in_=K_in[:, :]), inc=ld.inc(16))
    r.g(lambda g: g.dma_start(out=Vs[:, :], in_=V_in[:, :]), inc=ld.inc(16))
    for i in range(NCH):
        r.g(lambda g, i=i: g.dma_start(out=wq[i][:, :], in_=wq_in[i * 128:(i + 1) * 128, 0:1024]),
            inc=ld.inc(16))
        r.g(lambda g, i=i: g.dma_start(out=wo[i][:, :], in_=wo_in[i * 128:(i + 1) * 128, :]),
            inc=ld.inc(16))
    r.g(lambda g: g.dma_start(out=hsel[:, :], in_=hs_in[:, :]), inc=ld.inc(16))
    r.g(lambda g: g.dma_start(out=qbc, in_=qb_in[:, :]), inc=ld.inc(16))
    r.g(lambda g: g.dma_start(out=boc, in_=bo_in[:, :]), inc=ld.inc(16))
    r.g(lambda g: g.dma_start(out=l1wc, in_=l1w_in[:, :]), inc=ld.inc(16))
    r.g(lambda g: g.dma_start(out=l1bc, in_=l1b_in[:, :]), inc=ld.inc(16))
    r.g(lambda g: g.dma_start(out=mk2[:, :], in_=mk_in[:, :]), inc=ld.inc(16))

    ldx = b.sem()
    mm = b.sem()
    ev = b.sem()
    es = b.sem()
    st = b.sem()

    xs = b.sbuf("xs", [128, 8 * 512], F32)
    lnx = b.sbuf("lnx", [128, 8 * 512], BF16)
    tmp32 = b.sbuf("tmp32", [128, 512], F32)
    arowst = b.sbuf("arowst", [1, 1024], F32)
    qsb = b.sbuf("qsb", [128, 8 * 512], BF16)
    sqh = b.sbuf("sqh", [64, 512], BF16)
    mcol = b.sbuf("mcol", [16, 4], F32)
    mbf = b.sbuf("mbf", [16, 1], BF16)
    mrow = b.sbuf("mrow", [1, 16], BF16)
    negM = b.sbuf("negM", [128, 16], F32)
    exp2 = [b.sbuf("exp2a", [128, 1024], BF16), b.sbuf("exp2b", [128, 1024], BF16)]
    rden = arowst[0:1, 0:512]
    rb = tmp32[0:64, :]
    ofm = lnx

    pA = b.psum("pA", [128, 6 * 512], F32)
    pT = b.psum("pT", [128, 1024], BF16)
    pool = Pool([pA[:, i * 512:(i + 1) * 512] for i in range(4)])
    poolO = Pool([pA[:, i * 512:(i + 1) * 512] for i in range(4, 6)])

    xview = x_in.rearrange("(c p) t -> p c t", p=128)
    oview = x4_out.rearrange("(c p) t -> p c t", p=128)
    arows_d = ar_in

    sems = (ld, ldx, mm, ev, es, st)
    bufs = (xs, lnx, None, None, tmp32, None, arowst, pool)
    for ti in range(8):
        t0 = ti * 512
        lnw = _ln_gain_stream(b, C, r, sems, bufs, xview, ti, (l1wc, l1bc), False, arows_d)
        for oc in range(8):
            iP, psP_, wP = pool.take()
            for dc in range(NCH):
                r.t(lambda t, dc=dc, oc=oc, p=psP_:
                    MM(t, p[:, 0:512], wq[dc][:, oc * 128:oc * 128 + 128],
                       lnx[:, dc * 512:(dc + 1) * 512], dc == 0, dc == NCH - 1),
                    wait=([lnw] + wP) if dc == 0 else [],
                    inc=(mm.inc() if dc == NCH - 1 else None))
            r.v(lambda v, oc=oc, p=psP_: v.tensor_scalar(qsb[:, oc * 512:(oc + 1) * 512],
                                                         p[:, 0:512], qbc[:, oc:oc + 1],
                                                         None, ALU.add),
                wait=[mm.now()], inc=ev.inc())
            pool.done(iP, ev.now())
        iNQ, psNQ, wNQ = poolO.take()
        for h in range(16):
            r0, hp = (h % 2) * 64, h // 2
            qop = qsb[r0:r0 + 64, hp * 512: hp * 512 + 512]
            r.v(lambda v, qop=qop: v.tensor_tensor(sqh[:, :], qop, qop, ALU.mult),
                wait=[ev.now(), mm.now()], inc=ev.inc())
            r.t(lambda t, h=h, p=psNQ: MM(t, p[0:16, 0:512], hsel[:, h * 16:h * 16 + 16],
                                          sqh[:, :], h == 0, h == 15),
                wait=[ev.now()] + (wNQ if h == 0 else []), inc=mm.inc())
        r.v(lambda v, p=psNQ: v.reduce_max(mcol[:, 1:2], p[0:16, 0:512], axis=AX),
            wait=[mm.now()], inc=ev.inc())
        poolO.done(iNQ, ev.now())
        r.v(lambda v: v.tensor_tensor(mcol[:, 2:3], mcol[:, 1:2], mk2[:, 0:1], ALU.mult),
            wait=[ev.now()], inc=ev.inc())
        r.v(lambda v: v.tensor_scalar(mcol[:, 3:4], mcol[:, 2:3], 1e-30, None, ALU.max),
            wait=[ev.now()], inc=ev.inc())
        r.s(lambda s: s.activation(mcol[:, 2:3], mcol[:, 3:4], AF.Ln), wait=[ev.now()],
            inc=es.inc())
        r.s(lambda s: s.activation(mcol[:, 3:4], mcol[:, 2:3], AF.Exp, scale=0.5),
            inc=es.inc())
        r.v(lambda v: v.tensor_scalar(mbf[:, :], mcol[:, 3:4], 0.133, None, ALU.mult),
            wait=[es.now()], inc=ev.inc())
        r.t(lambda t: t.transpose(pT[0:1, 0:16], mbf[:, :], C["ident"][0:16, 0:16]),
            wait=[ev.now()], inc=mm.inc())
        r.v(lambda v: v.tensor_copy(mrow[:, :], pT[0:1, 0:16]), wait=[mm.now()], inc=ev.inc())
        iM, psM_, wM = pool.take()
        r.t(lambda t, p=psM_: MM(t, p[:, 0:16], C["ones_row_b"][:, :], mrow[:, :], True, True),
            wait=[ev.now()] + wM, inc=mm.inc())
        r.v(lambda v, p=psM_: v.tensor_scalar(negM[:, :], p[:, 0:16], -1.0, None, ALU.mult),
            wait=[mm.now()], inc=ev.inc())
        pool.done(iM, ev.now())
        ndep = ev.now()
        nkc = ti * 4 + 4
        ngr = (nkc + 1) // 2
        pv_war = ev.now()
        for h in range(16):
            r0, hp = (h % 2) * 64, h // 2
            qop = qsb[r0:r0 + 64, hp * 512: hp * 512 + 512]
            psO_ = poolO.tiles[0]
            es_exp = {}
            pv_dep = {}
            mm_pv = {}

            def rec_pv(g, h=h, psO_=psO_, nkc=nkc, ngr=ngr):
                njs = min(2, nkc - 2 * g)
                for j in range(njs):
                    tc = 2 * g + j
                    r.t(lambda t, tc=tc, h=h, g=g, j=j, p=psO_:
                        MM(t, p[0:65, 0:512],
                           Vs[:, tc * 1040 + h * 65: tc * 1040 + h * 65 + 65],
                           exp2[g % 2][:, j * 512:(j + 1) * 512],
                           (g == 0 and j == 0), (g == ngr - 1 and j == njs - 1)),
                        wait=(pv_dep[g] + ([pv_war] if (g == 0 and j == 0) else [])),
                        inc=(mm.inc() if j == njs - 1 else None))
                mm_pv[g] = mm.now()

            for g in range(ngr):
                half = g % 2
                njs = min(2, nkc - 2 * g)
                for j in range(njs):
                    tc = 2 * g + j
                    bank = pool.tiles[half * 2 + j]
                    w = [ndep] if g < 2 else [es_exp[g - 2]]
                    r.t(lambda t, r0=r0, hp=hp, tc=tc, qop=qop, p=bank:
                        MM(t, p[:, 0:512],
                           Ks[r0:r0 + 64, hp * T + tc * 128: hp * T + tc * 128 + 128],
                           qop, True, True),
                        wait=w, inc=(mm.inc() if j == njs - 1 else None))
                w = [mm.now()]
                if g >= 2:
                    w.append(mm_pv[g - 2])
                r.s(lambda s, h=h, half=half, njs=njs: s.activation(
                    exp2[half][:, 0:njs * 512], pA[:, half * 1024:half * 1024 + njs * 512],
                    AF.Exp, scale=0.125, bias=negM[:, h:h + 1]),
                    wait=w, inc=es.inc())
                es_exp[g] = es.now()
                dep = [es.now()]
                for j in range(njs):
                    tc = 2 * g + j
                    dj = tc - ti * 4
                    if dj >= 0:
                        if dj > 0:
                            r.v(lambda v, half=half, j=j, dj=dj:
                                v.memset(exp2[half][:, j * 512:j * 512 + dj * 128], 0.0),
                                wait=[es_exp[g]], inc=ev.inc())
                        r.v(lambda v, half=half, j=j, dj=dj: v.tensor_tensor(
                            exp2[half][:, j * 512 + dj * 128:j * 512 + dj * 128 + 128],
                            exp2[half][:, j * 512 + dj * 128:j * 512 + dj * 128 + 128],
                            C["triu"][:, :], ALU.mult),
                            wait=[es_exp[g]], inc=ev.inc())
                        dep = [es_exp[g], ev.now()]
                pv_dep[g] = dep
                if g >= 1:
                    rec_pv(g - 1)
            rec_pv(ngr - 1)
            r.v(lambda v, p=psO_: v.reciprocal(rden[:, :], p[64:65, 0:512]),
                wait=[mm.now()], inc=ev.inc())
            psR_ = poolO.tiles[1]
            r.t(lambda t, p=psR_: MM(t, p[0:64, 0:512], C["ones_row"][0:1, 0:64], rden[:, :],
                                     True, True),
                wait=[ev.now()], inc=mm.inc())
            r.v(lambda v, p=psR_: v.tensor_copy(rb[:, :], p[0:64, 0:512]),
                wait=[mm.now()], inc=ev.inc())
            r.v(lambda v, r0=r0, hp=hp, p=psO_:
                v.tensor_tensor(ofm[r0:r0 + 64, hp * 512:hp * 512 + 512], p[0:64, 0:512],
                                rb[:, :], ALU.mult),
                wait=[ev.now(), mm.now()], inc=ev.inc())
            pv_war = ev.now()
        odep = ev.now()
        for dc in range(NCH):
            iP, psP_, wP = pool.take()
            for jc in range(NCH):
                r.t(lambda t, jc=jc, dc=dc, p=psP_:
                    MM(t, p[:, 0:512], wo[jc][:, dc * 128:dc * 128 + 128],
                       ofm[:, jc * 512:(jc + 1) * 512], jc == 0, jc == NCH - 1),
                    wait=([odep] + wP) if jc == 0 else [],
                    inc=(mm.inc() if jc == NCH - 1 else None))
            xsl = xs[:, dc * 512:(dc + 1) * 512]
            r.v(lambda v, xsl=xsl, p=psP_: v.tensor_tensor(xsl, xsl, p[:, 0:512], ALU.add),
                wait=[mm.now()])
            r.v(lambda v, xsl=xsl, dc=dc: v.tensor_scalar(xsl, xsl, boc[:, dc:dc + 1],
                                                          None, ALU.add),
                inc=ev.inc())
            pool.done(iP, ev.now())
        r.y(lambda y, t0=t0: y.dma_start(out=oview[:, :, t0:t0 + 512],
                                         in_=xs[:, :].rearrange("p (c t) -> p c t", t=512)),
            wait=[ev.now()], inc=st.inc(16))
    r.y(lambda y: y.nop(), wait=[st.now()])
    return b.finish()


def build_D_moe(x_percore, ln2w_col, ln2b_col, gate_W, fc1T, fc1_cols, fc2T, fc2_cols,
                name="d_moe", dbg=False):
    NE, NF = 8, 32
    b = Bld(name)
    r = b.r
    x_in = b.din("x", x_percore)
    l2w_in = b.din("l2w", np.ascontiguousarray(ln2w_col))
    l2b_in = b.din("l2b", np.ascontiguousarray(ln2b_col))
    gw_prep = np.ascontiguousarray(
        np.asarray(gate_W).reshape(8, 128, 8).transpose(1, 0, 2).reshape(128, 64))
    gw_in = b.din("gw", gw_prep)
    f1_in = b.din("fc1T", np.ascontiguousarray(fc1T))
    f1c_in = b.din("b1cs", np.ascontiguousarray(fc1_cols))
    f2_in = b.din("fc2T", np.ascontiguousarray(fc2T))
    f2c_in = b.din("b2cs", np.ascontiguousarray(fc2_cols))
    y_out = b.dout("y", [D, T], F32)
    if dbg:
        nx2_d = b.dout("nx2_d", [128, 8 * T], BF16)
        ge_d = b.dout("ge_d", [8, T], BF16)
    else:
        nx2_d = b.dram("nx2_d", [128, 8 * T], BF16)
        ge_d = b.dram("ge_d", [8, T], BF16)

    ld = b.sem()
    C = reg_consts(b, [("ones_row", F32), ("ones_row_b", BF16), ("ones_col", BF16),
                       ("ident", BF16), ("cvals", F32)], ld)
    cols = b.sbuf("colsb", [128, 8 * 32 + 8 * 8 + 2 * 8], F32)
    b1cs = cols[:, 0:256]
    b2cs = cols[:, 256:320]
    l2wc, l2bc = cols[:, 320:328], cols[:, 328:336]
    gws = b.sbuf("gws", [128, 8 * 8], BF16)
    r.g(lambda g: g.dma_start(out=b1cs, in_=f1c_in[:, :]), inc=ld.inc(16))
    r.g(lambda g: g.dma_start(out=b2cs, in_=f2c_in[:, :]), inc=ld.inc(16))
    r.g(lambda g: g.dma_start(out=l2wc, in_=l2w_in[:, :]), inc=ld.inc(16))
    r.g(lambda g: g.dma_start(out=l2bc, in_=l2b_in[:, :]), inc=ld.inc(16))
    r.g(lambda g: g.dma_start(out=gws[:, :], in_=gw_in[:, :]), inc=ld.inc(16))

    ldx = b.sem()
    ldw = b.sem()
    mm = b.sem()
    ev = b.sem()
    es = b.sem()
    st = b.sem()

    xs = b.sbuf("xs", [128, 8 * 512], F32)
    lnx = b.sbuf("lnx", [128, 8 * 512], BF16)
    xbs = b.sbuf("xbs", [128, 512], BF16)
    sqs = b.sbuf("sqs", [128, 512], BF16)
    tmp32 = b.sbuf("tmp32", [128, 512], F32)
    rows4 = b.sbuf("rows4", [1, 4 * 512], F32)
    gcols = b.sbuf("gcols", [128, 8], F32)
    gtmp = b.sbuf("gtmp", [128, 8], F32)
    gtmp2 = b.sbuf("gtmp2", [128, 8], F32)
    m1c = b.sbuf("m1c", [128, 4], F32)
    gerow = b.sbuf("gerow", [1, 8 * 512], BF16)
    gebf = b.sbuf("gebf", [128, 8], BF16)
    w1 = [b.sbuf("w1_%d" % i, [128, 4096], BF16) for i in range(NCH)]
    w2 = [b.sbuf("w2_%d" % i, [128, 1024], BF16) for i in range(NF)]
    h1 = b.sbuf("h1", [128, NF * 512], BF16)
    geb = b.sbuf("geb", [128, 512], BF16)

    pA = b.psum("pA", [128, 6 * 512], F32)
    pT = b.psum("pT", [128, 1024], BF16)
    pool = Pool([pA[:, i * 512:(i + 1) * 512] for i in range(6)])

    xview = x_in.rearrange("(c p) t -> p c t", p=128)
    yview = y_out.rearrange("(c p) t -> p c t", p=128)

    for ti in range(8):
        t0 = ti * 512
        barrier = [mm.now(), ev.now(), es.now(), ld.now(), st.now()]
        r.y(lambda y, t0=t0: y.dma_start(out=xs[:, :].rearrange("p (c t) -> p c t", t=512),
                                         in_=xview[:, :, t0:t0 + 512]),
            wait=barrier, inc=ldx.inc(16))
        xw = ldx.now()
        iS, psS_, wS = pool.take()
        iQ, psQ_, wQ = pool.take()
        for c in range(NCH):
            r.v(lambda v, c=c: v.tensor_copy(xbs[:, :], xs[:, c * 512:(c + 1) * 512]),
                wait=[xw, mm.now(), es.now()] + (wS + wQ if c == 0 else []), inc=ev.inc())
            r.s(lambda s: s.activation(sqs[:, :], xbs[:, :], AF.Square),
                wait=[ev.now(), mm.now()], inc=es.inc())
            r.t(lambda t, c=c, p=psS_: MM(t, p[0:1, 0:512], C["ones_col"][:, :], xbs[:, :],
                                          c == 0, c == NCH - 1), wait=[ev.now()])
            r.t(lambda t, c=c, p=psQ_: MM(t, p[0:1, 0:512], C["ones_col"][:, :], sqs[:, :],
                                          c == 0, c == NCH - 1), wait=[es.now()], inc=mm.inc())
        rstd, brow = layernorm_rows(b, rows4, psS_, psQ_, mm.now(), ev, es,
                                    eps_ap=C["cvals"][0:1, 1:2])
        pool.done(iS, ev.now())
        pool.done(iQ, ev.now())
        iA, psA_, wA = pool.take()
        iB, psB_, wB = pool.take()
        r.t(lambda t, p=psA_, q=rstd: MM(t, p[:, 0:512], C["ones_row"][:, :], q, True, True),
            wait=[ev.now()] + wA)
        r.t(lambda t, p=psB_, q=brow: MM(t, p[:, 0:512], C["ones_row"][:, :], q, True, True),
            wait=wB, inc=mm.inc())
        mw2 = mm.now()
        for c in range(NCH):
            r.v(lambda v, c=c, p=psA_: v.tensor_tensor(tmp32[:, :], xs[:, c * 512:(c + 1) * 512],
                                                       p[:, 0:512], ALU.mult), wait=[mw2])
            r.v(lambda v, c=c, p=psB_: v.tensor_tensor(tmp32[:, :], tmp32[:, :], p[:, 0:512],
                                                       ALU.add))
            r.v(lambda v, c=c: v.tensor_scalar(lnx[:, c * 512:(c + 1) * 512], tmp32[:, :],
                                               l2wc[:, c:c + 1], l2bc[:, c:c + 1],
                                               ALU.mult, ALU.add),
                inc=(ev.inc() if c == NCH - 1 else None))
        pool.done(iA, ev.now())
        pool.done(iB, ev.now())
        lnw = ev.now()
        r.g(lambda g, t0=t0: g.dma_start(out=nx2_d[:, t0 * 8:t0 * 8 + 4096], in_=lnx[:, :]),
            wait=[lnw], inc=st.inc(16))
        r.y(lambda y, t0=t0: y.dma_start(out=yview[:, :, t0:t0 + 512],
                                         in_=xs[:, :].rearrange("p (c t) -> p c t", t=512)),
            wait=[lnw], inc=st.inc(16))
        for sub in range(4):
            iG, psG, wG = pool.take()
            for dc in range(NCH):
                r.t(lambda t, dc=dc, sub=sub, p=psG:
                    MM(t, p[:, 0:8], lnx[:, dc * 512 + sub * 128: dc * 512 + sub * 128 + 128],
                       gws[:, dc * 8:(dc + 1) * 8], dc == 0, dc == NCH - 1),
                    wait=([lnw] + wG) if dc == 0 else [],
                    inc=(mm.inc() if dc == NCH - 1 else None))
            r.v(lambda v, p=psG: v.tensor_copy(gcols[:, :], p[:, 0:8]),
                wait=[mm.now()], inc=ev.inc())
            pool.done(iG, ev.now())
            r.v(lambda v: v.reduce_max(m1c[:, 0:1], gcols[:, :], axis=AX),
                wait=[ev.now()], inc=ev.inc())
            r.v(lambda v: v.tensor_scalar(gtmp[:, :], gcols[:, :], m1c[:, 0:1], None,
                                          ALU.is_lt), wait=[ev.now()], inc=ev.inc())
            r.v(lambda v: v.tensor_tensor(gtmp2[:, :], gcols[:, :], gtmp[:, :], ALU.mult),
                wait=[ev.now()], inc=ev.inc())
            r.v(lambda v: v.tensor_scalar(gtmp[:, :], gtmp[:, :], 1e30, -1e30,
                                          ALU.mult, ALU.add), wait=[ev.now()], inc=ev.inc())
            r.v(lambda v: v.tensor_tensor(gtmp2[:, :], gtmp2[:, :], gtmp[:, :], ALU.add),
                wait=[ev.now()], inc=ev.inc())
            r.v(lambda v: v.reduce_max(m1c[:, 1:2], gtmp2[:, :], axis=AX),
                wait=[ev.now()], inc=ev.inc())
            r.v(lambda v: v.tensor_scalar(gtmp[:, :], gcols[:, :], m1c[:, 1:2], None,
                                          ALU.is_ge), wait=[ev.now()], inc=ev.inc())
            r.v(lambda v: v.tensor_scalar(m1c[:, 2:3], m1c[:, 0:1], -1.0, None, ALU.mult),
                wait=[ev.now()], inc=ev.inc())
            r.s(lambda s: s.activation(gtmp2[:, :], gcols[:, :], AF.Exp, bias=m1c[:, 2:3]),
                wait=[ev.now()], inc=es.inc())
            r.v(lambda v: v.tensor_tensor(m1c[:, 3:4], m1c[:, 1:2], m1c[:, 2:3], ALU.add),
                wait=[es.now()], inc=ev.inc())
            r.s(lambda s: s.activation(m1c[:, 1:2], m1c[:, 3:4], AF.Exp),
                wait=[ev.now()], inc=es.inc())
            r.v(lambda v: v.tensor_scalar(m1c[:, 3:4], m1c[:, 1:2], 1.0, None, ALU.add),
                wait=[es.now()], inc=ev.inc())
            r.v(lambda v: v.reciprocal(m1c[:, 2:3], m1c[:, 3:4]),
                wait=[ev.now()], inc=ev.inc())
            r.v(lambda v: v.tensor_tensor(gcols[:, :], gtmp2[:, :], gtmp[:, :], ALU.mult),
                wait=[ev.now()], inc=ev.inc())
            r.v(lambda v: v.tensor_scalar(gebf[:, :], gcols[:, :], m1c[:, 2:3], None,
                                          ALU.mult), wait=[ev.now()], inc=ev.inc())
            for e in range(NE):
                r.t(lambda t, e=e, sub=sub: t.transpose(pT[0:1, e * 128:e * 128 + 128],
                                                        gebf[:, e:e + 1],
                                                        C["ident"][:, :]),
                    wait=[ev.now()], inc=mm.inc())
            for e in range(NE):
                r.v(lambda v, e=e, sub=sub: v.tensor_copy(gerow[0:1, e * 512 + sub * 128:
                                                                e * 512 + sub * 128 + 128],
                                                          pT[0:1, e * 128:e * 128 + 128]),
                    wait=[mm.now()], inc=ev.inc())
        for e in range(NE):
            r.y(lambda y, e=e, t0=t0: y.dma_start(out=ge_d[e:e + 1, t0:t0 + 512],
                                                  in_=gerow[0:1, e * 512:(e + 1) * 512]),
                wait=[ev.now()], inc=st.inc(16))

    for e in range(NE):
        wdep = [mm.now(), es.now()]
        for i in range(NCH):
            r.g(lambda g, i=i, e=e: g.dma_start(out=w1[i][:, :],
                                                in_=f1_in[e, i * 128:(i + 1) * 128, :]),
                wait=wdep if i == 0 else [], inc=ldw.inc(16))
        for i in range(NF):
            r.g(lambda g, i=i, e=e: g.dma_start(out=w2[i][:, :],
                                                in_=f2_in[e, i * 128:(i + 1) * 128, :]),
                inc=ldw.inc(16))
        wld = ldw.now()
        for ti in range(8):
            t0 = ti * 512
            r.y(lambda y, t0=t0: y.dma_start(out=lnx[:, :], in_=nx2_d[:, t0 * 8:t0 * 8 + 4096]),
                wait=[mm.now(), ev.now(), st.now()], inc=ldx.inc(16))
            r.y(lambda y, t0=t0: y.dma_start(out=xs[:, :].rearrange("p (c t) -> p c t", t=512),
                                             in_=yview[:, :, t0:t0 + 512]),
                inc=ldx.inc(16))
            r.g(lambda g, e=e, t0=t0: g.dma_start(out=gerow[0:1, 0:512],
                                                  in_=ge_d[e:e + 1, t0:t0 + 512]),
                wait=[ev.now()], inc=ldx.inc(16))
            xw = ldx.now()
            iG, psG, wG = pool.take()
            r.t(lambda t, p=psG: MM(t, p[:, 0:512], C["ones_row_b"][:, :], gerow[0:1, 0:512],
                                    True, True),
                wait=[xw] + wG, inc=mm.inc())
            r.v(lambda v, p=psG: v.tensor_copy(geb[:, :], p[:, 0:512]),
                wait=[mm.now()], inc=ev.inc())
            pool.done(iG, ev.now())
            for fc in range(NF):
                iP, psP_, wP = pool.take()
                for dc in range(NCH):
                    r.t(lambda t, dc=dc, fc=fc, p=psP_:
                        MM(t, p[:, 0:512], w1[dc][:, fc * 128:fc * 128 + 128],
                           lnx[:, dc * 512:(dc + 1) * 512], dc == 0, dc == NCH - 1),
                        wait=([xw, wld] + wP) if dc == 0 else [],
                        inc=(mm.inc() if dc == NCH - 1 else None))
                r.s(lambda s, fc=fc, e=e, p=psP_:
                    s.activation(h1[:, fc * 512:(fc + 1) * 512], p[:, 0:512], AF.Gelu,
                                 bias=b1cs[:, e * 32 + fc:e * 32 + fc + 1]),
                    wait=[mm.now()], inc=es.inc())
                pool.done(iP, es.now())
            gw_ = es.now()
            for dc in range(NCH):
                iP, psP_, wP = pool.take()
                for fc in range(NF):
                    r.t(lambda t, fc=fc, dc=dc, p=psP_:
                        MM(t, p[:, 0:512], w2[fc][:, dc * 128:dc * 128 + 128],
                           h1[:, fc * 512:(fc + 1) * 512], fc == 0, fc == NF - 1),
                        wait=([gw_] + wP) if fc == 0 else [],
                        inc=(mm.inc() if fc == NF - 1 else None))
                r.v(lambda v, dc=dc, e=e, p=psP_:
                    v.scalar_tensor_tensor(tmp32[:, :], p[:, 0:512],
                                           b2cs[:, e * 8 + dc:e * 8 + dc + 1], geb[:, :],
                                           ALU.add, ALU.mult),
                    wait=[mm.now(), ev.now()], inc=ev.inc())
                xsl = xs[:, dc * 512:(dc + 1) * 512]
                r.v(lambda v, xsl=xsl: v.tensor_tensor(xsl, xsl, tmp32[:, :], ALU.add),
                    wait=[ev.now()], inc=ev.inc())
                pool.done(iP, ev.now())
            r.y(lambda y, t0=t0: y.dma_start(out=yview[:, :, t0:t0 + 512],
                                             in_=xs[:, :].rearrange("p (c t) -> p c t", t=512)),
                wait=[ev.now()], inc=st.inc(16))
    r.y(lambda y: y.nop(), wait=[st.now()])
    return b.finish()


def _col(vec, nch):
    return np.ascontiguousarray(np.asarray(vec, np.float32).reshape(nch, 128).T)


def _pair_cols(bias16x64):
    out = np.zeros((128, 8), np.float32)
    for h in range(16):
        out[(h % 2) * 64:(h % 2) * 64 + 64, h // 2] = bias16x64[h]
    return out


def kernel(**ins):
    return _kernel_impl(fused=True, **ins)


def _kernel_impl(fused=True, **ins):
    if fused:
        ctx = FuseCtx()
        ctx.final = ("d_add", "y")
        _FUSE[0] = ctx
    try:
        return _kernel_chain(ctx if fused else None, **ins)
    finally:
        _FUSE[0] = None


def _kernel_chain(ctx, **ins):
    x = ins["L0_attn_x"]
    x_fm = [np.ascontiguousarray(x[c].T) for c in range(NCORES)]

    bA = build_blockA(x_fm, ins["L0_attn_qkv_w"], ins["L0_attn_proj_w"],
                      ins["L0_attn_alpha_log"], ins["L0_attn_beta_log"])
    res = bA.run()
    x1 = [res.results[c]["x1"] for c in range(NCORES)]

    res = build_ln_pass(x1, pad=1, name="b_ln").run()
    xb = [res.results[c]["y"] for c in range(NCORES)]
    w1p = np.zeros((2, 1024, 5632), np.float32)
    w1p[:, :, 0:2730] = ins["L0_ffn_w1"][:, :, 0:2730]
    w1p[:, :, 2816:2816 + 2730] = ins["L0_ffn_w1"][:, :, 2730:]
    b1p = np.zeros((2, 5632), np.float32)
    b1p[:, 0:2730] = ins["L0_ffn_b1"][:, 0:2730]
    b1p[:, 2816:2816 + 2730] = ins["L0_ffn_b1"][:, 2730:]
    w2p = np.zeros((2, 2816, 1024), np.float32)
    w2p[:, 0:2730, :] = ins["L0_ffn_w2"]
    for l in range(2):
        res = build_B_qkv(xb, ins["L0_ffn_wqkv"][l], ins["L0_ffn_q_scale"][l],
                          ins["L0_ffn_k_scale"][l], name="b_qkv%d" % l).run()
        qr = [res.results[c]["qr"] for c in range(NCORES)]
        kr = [res.results[c]["kr"] for c in range(NCORES)]
        vp = [res.results[c]["vp"] for c in range(NCORES)]
        res = build_B_attn(xb, qr, kr, vp, ins["L0_ffn_wo"][l], ins["L0_ffn_q_scale"][l],
                           ins["L0_ffn_k_scale"][l], name="b_attn%d" % l).run()
        xb = [res.results[c]["xb2"] for c in range(NCORES)]
        res = build_B_ffn(xb, w1p[l], _col(b1p[l], 44), w2p[l], _col(ins["L0_ffn_b2"][l], 8),
                          _col(ins["L0_ffn_ln_g"][l], 8), _col(ins["L0_ffn_ln_b"][l], 8),
                          name="b_ffn%d" % l).run()
        xb = [res.results[c]["xb2"] for c in range(NCORES)]
    res = build_add_pass(x1, xb, b_pad=1, name="b_add").run()
    x2 = [res.results[c]["y"] for c in range(NCORES)]

    res = build_ln_pass(x2, pad=0, name="c_ln").run()
    h = [res.results[c]["y"] for c in range(NCORES)]
    for l in range(8):
        qwT = np.ascontiguousarray(ins["L1_attn_qw"][l].transpose(2, 0, 1).reshape(64, 1024))
        kwT = np.ascontiguousarray(ins["L1_attn_kw"][l].transpose(2, 0, 1).reshape(64, 1024))
        vwT = np.ascontiguousarray(ins["L1_attn_vw"][l].transpose(2, 0, 1).reshape(64, 1024))
        vbf = np.zeros(1040, np.float32)
        for hh in range(16):
            vbf[hh * 65:hh * 65 + 64] = ins["L1_attn_vb"][l, hh]
        res = build_C_attn(h, _col(ins["L1_attn_ln1_w"][l], 8), _col(ins["L1_attn_ln1_b"][l], 8),
                           qwT, _pair_cols(ins["L1_attn_qb"][l]), kwT,
                           _pair_cols(ins["L1_attn_kb"][l]), vwT, vbf,
                           name="c_attn%d" % l).run()
        h = [res.results[c]["h2"] for c in range(NCORES)]
        res = build_C_ffn(h, _col(ins["L1_attn_ln2_w"][l], 8), _col(ins["L1_attn_ln2_b"][l], 8),
                          ins["L1_attn_w1"][l], _col(ins["L1_attn_b1"][l], 32),
                          ins["L1_attn_w2"][l], _col(ins["L1_attn_b2"][l], 8),
                          name="c_ffn%d" % l).run()
        h = [res.results[c]["h2"] for c in range(NCORES)]
    res = build_add_pass(x2, h, b_pad=0, name="c_add").run()
    x3 = [res.results[c]["y"] for c in range(NCORES)]

    res = build_ln_pass(x3, pad=0, name="d_ln").run()
    xd = [res.results[c]["y"] for c in range(NCORES)]
    res = build_D_kv(xd, ins["L1_ffn_Wqkv"], ins["L1_ffn_bqkv"],
                     _col(ins["L1_ffn_ln1_w"], 8), _col(ins["L1_ffn_ln1_b"], 8)).run()
    Kd = [res.results[c]["K"] for c in range(NCORES)]
    Vd = [res.results[c]["V"] for c in range(NCORES)]
    ar = [res.results[c]["arows"] for c in range(NCORES)]
    mk = [res.results[c]["mk2"] for c in range(NCORES)]
    res = build_D_attn(xd, Kd, Vd, ar, mk, ins["L1_ffn_Wqkv"], ins["L1_ffn_bqkv"],
                       ins["L1_ffn_Wo"], ins["L1_ffn_bo"],
                       _col(ins["L1_ffn_ln1_w"], 8), _col(ins["L1_ffn_ln1_b"], 8)).run()
    xd1 = [res.results[c]["x4"] for c in range(NCORES)]
    fc1T = np.ascontiguousarray(ins["L1_ffn_fc1_w"].transpose(0, 2, 1))
    fc2T = np.ascontiguousarray(ins["L1_ffn_fc2_w"].transpose(0, 2, 1))
    b1cs = np.concatenate([_col(ins["L1_ffn_fc1_b"][e], 32) for e in range(8)], axis=1)
    b2cs = np.concatenate([_col(ins["L1_ffn_fc2_b"][e], 8) for e in range(8)], axis=1)
    res = build_D_moe(xd1, _col(ins["L1_ffn_ln2_w"], 8), _col(ins["L1_ffn_ln2_b"], 8),
                      ins["L1_ffn_gate_W"], fc1T, b1cs, fc2T, b2cs).run()
    y_in = [res.results[c]["y"] for c in range(NCORES)]
    res = build_add_pass(x3, y_in, b_pad=0, name="d_add").run()
    if ctx is not None:
        _FUSE[0] = None
        ctx.finalize()
        res = ctx.run()
    y = [res.results[c]["y"] for c in range(NCORES)]

    return np.stack([np.ascontiguousarray(y[c].T) for c in range(NCORES)])


# version 05
# speedup vs baseline: 1.4219x; 1.0166x over previous
from contextlib import ExitStack

import numpy as np

import concourse.bass as bass
import concourse.mybir as mybir
from concourse.bass_utils import run_bass_kernel_spmd

F32 = mybir.dt.float32
BF16 = mybir.dt.bfloat16
AF = mybir.ActivationFunctionType
ALU = mybir.AluOpType
AX = mybir.AxisListType.X

NCORES = 8
T = 4096
D = 1024
NCH = D // 128


class Sem:
    def __init__(self, h):
        self.h = h
        self.n = 0

    def inc(self, k=1):
        self.n += k
        return (self.h, k)

    def now(self):
        return (self.h, self.n)


class Rec:
    ENGS = ("tensor", "vector", "scalar", "gpsimd", "sync")

    def __init__(self):
        self.ops = {e: [] for e in self.ENGS}

    def op(self, eng, fn, wait=(), inc=None):
        self.ops[eng].append((fn, tuple(wait), inc))

    def t(self, fn, wait=(), inc=None):
        self.op("tensor", fn, wait, inc)

    def v(self, fn, wait=(), inc=None):
        self.op("vector", fn, wait, inc)

    def s(self, fn, wait=(), inc=None):
        self.op("scalar", fn, wait, inc)

    def g(self, fn, wait=(), inc=None):
        self.op("gpsimd", fn, wait, inc)

    def y(self, fn, wait=(), inc=None):
        self.op("sync", fn, wait, inc)

    def emit(self, block):
        for ename in self.ENGS:
            ops = self.ops[ename]
            if not ops:
                continue

            def make(ops):
                def _(eng):
                    cur = {}
                    for fn, waits, inc in ops:
                        for (sem, v) in waits:
                            if v > 0 and cur.get(sem, -1) < v:
                                eng.wait_ge(sem, v)
                                cur[sem] = v
                        ins = fn(eng)
                        if inc is not None:
                            sem, k = inc
                            ins.then_inc(sem, k)
                return _

            getattr(block, ename)(make(ops))


class DramRef:

    def __init__(self, ap):
        self.ap = ap


class FakeRes:
    def __init__(self, outs):
        self.results = [outs] * NCORES


class FuseCtx:

    ARENA_F32 = 53160

    def __init__(self):
        self.nc = bass.Bass("TRN2", target_bir_lowering=False)
        self.stack = ExitStack()
        self.r = Rec()
        self.feeds = {}
        self.sems = [Sem(self.stack.enter_context(self.nc.semaphore("fsm%d" % i)))
                     for i in range(8)]
        self.bscr = self.stack.enter_context(self.nc.sbuf_tensor("fuse_bscr", [1, 8], F32))
        self.arena = self.stack.enter_context(
            self.nc.sbuf_tensor("fuse_arena", [128, self.ARENA_F32], F32))
        self.pA = self.stack.enter_context(self.nc.psum_tensor("fpA", [128, 3072], F32))
        self.pT = self.stack.enter_context(self.nc.psum_tensor("fpT", [128, 2048], BF16))
        self.final = None
        self.final_out = None

    def barrier(self):
        if _NO_BARRIER[0]:
            return
        r = self.r
        bs = self.sems[7]
        sc = self.bscr
        r.op("tensor", (lambda eng: eng.nop()), wait=(), inc=bs.inc())
        r.op("vector", (lambda eng: eng.engine_nop()), wait=(), inc=bs.inc())
        r.op("scalar", (lambda eng, sc=sc: eng.activation(sc[0:1, 0:1], sc[0:1, 1:2],
                                                          AF.Copy)), wait=(), inc=bs.inc())
        r.op("gpsimd", (lambda eng: eng.nop()), wait=(), inc=bs.inc())
        r.op("sync", (lambda eng: eng.nop()), wait=(), inc=bs.inc())
        mark = bs.now()
        for e in ("tensor", "vector", "scalar", "gpsimd", "sync"):
            r.op(e, (lambda eng: eng.nop()), wait=(mark,), inc=None)

    def finalize(self):
        with self.nc.Block() as blk:
            self.r.emit(blk)
        self.stack.close()

    def run(self, **kw):
        maps = [{k: (v[c] if isinstance(v, list) else v) for k, v in self.feeds.items()}
                for c in range(NCORES)]
        return run_bass_kernel_spmd(self.nc, maps, core_ids=list(range(NCORES)), **kw)


_FUSE = [None]
_NO_BARRIER = [False]


class Bld:

    def __init__(self, name):
        self.name = name
        self.ctx = _FUSE[0]
        self._semi = 0
        self._aoff = 0
        self.out_aps = {}
        if self.ctx is None:
            self.nc = bass.Bass("TRN2", target_bir_lowering=False)
            self.stack = ExitStack()
            self.r = Rec()
            self.feeds = {}
        else:
            self.nc = self.ctx.nc
            self.r = self.ctx.r
            self.feeds = self.ctx.feeds

    def sem(self):
        self._semi += 1
        if self.ctx is not None:
            assert self._semi <= 7, "segment wants too many semaphores"
            return self.ctx.sems[self._semi - 1]
        h = self.stack.enter_context(self.nc.semaphore("sm%d" % self._semi))
        return Sem(h)

    def sbuf(self, name, shape, dt):
        if self.ctx is None:
            return self.stack.enter_context(self.nc.sbuf_tensor(name, shape, dt))
        p, n = shape
        if dt is F32:
            nf = n
        else:
            nf = (n + 1) // 2
        off = self._aoff
        self._aoff = off + nf
        assert self._aoff <= FuseCtx.ARENA_F32, \
            "fuse arena overflow at %s/%s: %d" % (self.name, name, self._aoff)
        view = self.ctx.arena[0:p, off:off + nf]
        if dt is BF16:
            view = view.bitcast(BF16)[0:p, 0:n]
        return view

    def psum(self, name, shape, dt=F32):
        if self.ctx is None:
            return self.stack.enter_context(self.nc.psum_tensor(name, shape, dt))
        if dt is BF16:
            return self.ctx.pT[0:shape[0], 0:shape[1]]
        return self.ctx.pA[0:shape[0], 0:shape[1]]

    def din(self, name, arr, dtype=F32):
        if isinstance(arr, DramRef):
            return arr.ap
        if isinstance(arr, list) and isinstance(arr[0], DramRef):
            return arr[0].ap
        a0 = arr[0] if isinstance(arr, list) else arr
        nm = name if self.ctx is None else "%s_%s" % (self.name, name)
        self.feeds[nm] = arr
        return self.nc.dram_tensor(nm, list(a0.shape), dtype, kind="ExternalInput").ap()

    def dout(self, name, shape, dtype=F32):
        if self.ctx is not None and self.ctx.final != (self.name, name):
            ap = self.nc.dram_tensor("%s_%s" % (self.name, name), shape, dtype,
                                     kind="Internal").ap()
        else:
            ap = self.nc.dram_tensor(name, shape, dtype, kind="ExternalOutput").ap()
            if self.ctx is not None:
                self.ctx.final_out = name
        self.out_aps[name] = ap
        return ap

    def dram(self, name, shape, dtype=F32):
        nm = name if self.ctx is None else "%s_%s" % (self.name, name)
        return self.nc.dram_tensor(nm, shape, dtype, kind="Internal").ap()

    def finish(self):
        if self.ctx is None:
            with self.nc.Block() as blk:
                self.r.emit(blk)
            self.stack.close()
        else:
            self.ctx.barrier()
        return self

    def in_maps(self, n=NCORES):
        return [{k: (v[c] if isinstance(v, list) else v) for k, v in self.feeds.items()}
                for c in range(n)]

    def run(self, cores=None, **kw):
        if self.ctx is not None:
            return FakeRes({k: DramRef(v) for k, v in self.out_aps.items()})
        cores = cores if cores is not None else list(range(NCORES))
        maps = self.in_maps(len(cores))
        return run_bass_kernel_spmd(self.nc, maps, core_ids=cores, **kw)


class Pool:

    def __init__(self, tiles):
        self.tiles = tiles
        self.war = [None] * len(tiles)
        self.i = 0

    def take(self):
        i = self.i
        self.i = (self.i + 1) % len(self.tiles)
        w = self.war[i]
        return i, self.tiles[i], ([w] if w is not None else [])

    def done(self, i, dep):
        self.war[i] = dep


def MM(t, out, lhsT, rhs, start, stop):
    return t.matmul(out, lhsT, rhs, start=start, stop=stop, skip_group_check=True)


def common_consts():
    c = {}
    c["ones_row"] = np.ones((1, 128), np.float32)
    c["ones_row_b"] = np.ones((1, 128), np.float32)
    c["ones_col"] = np.ones((128, 1), np.float32)
    c["ident"] = np.eye(128, dtype=np.float32)
    c["triu"] = np.triu(np.ones((128, 128), np.float32))
    p = np.arange(128, dtype=np.float32)[:, None] + 128.0 * np.arange(32, dtype=np.float32)[None, :]
    c["pos_tm"] = p + 1.0
    c["cvals"] = np.array([[1.0, 1e-5]], np.float32)
    return c


def reg_consts(b, names, ld):
    cc = common_consts()
    out = {}
    for nm, dt in names:
        arr = cc[nm]
        ap = b.din("c_" + nm, np.ascontiguousarray(arr, np.float32))
        tile = b.sbuf("sb_" + nm, list(arr.shape), dt)
        b.r.g(lambda g, t=tile, a=ap: g.dma_start(out=t[:, :], in_=a[:, :]),
              inc=ld.inc(16))
        out[nm] = tile
    return out


def layernorm_rows(b, rows4, psS, psQ, dep_mm, ev, es, eps_ap=None):
    r = b.r
    mu = rows4[0:1, 0:512]
    var = rows4[0:1, 512:1024]
    rstd = rows4[0:1, 1024:1536]
    brow = rows4[0:1, 1536:2048]
    r.v(lambda v, mu=mu, psS=psS: v.tensor_scalar(mu, psS[0:1, 0:512], 1.0 / D, None, ALU.mult),
        wait=[dep_mm])
    r.v(lambda v, var=var, psQ=psQ: v.tensor_scalar(var, psQ[0:1, 0:512], 1.0 / D, None, ALU.mult),
        inc=ev.inc())
    r.s(lambda s, brow=brow, mu=mu: s.activation(brow, mu, AF.Square), wait=[ev.now()], inc=es.inc())
    r.v(lambda v, var=var, brow=brow: v.tensor_tensor(var, var, brow, ALU.subtract),
        wait=[es.now()], inc=ev.inc())
    r.s(lambda s, var=var, e=eps_ap: s.activation(var, var, AF.Ln, bias=e), wait=[ev.now()])
    r.s(lambda s, rstd=rstd, var=var: s.activation(rstd, var, AF.Exp, scale=-0.5), inc=es.inc())
    r.v(lambda v, brow=brow, mu=mu, rstd=rstd: v.tensor_tensor(brow, mu, rstd, ALU.mult),
        wait=[es.now()])
    r.v(lambda v, brow=brow: v.tensor_scalar(brow, brow, -1.0, None, ALU.mult), inc=ev.inc())
    return rstd, brow


def build_blockA(x_percore, qkv_w, proj_w, alpha_log, beta_log, dbg=False):
    b = Bld("blockA")
    r = b.r

    x_in = b.din("x", x_percore)
    wq_in = b.din("wqkv", np.ascontiguousarray(qkv_w.T))
    wp_in = b.din("wproj", np.ascontiguousarray(proj_w.T))
    al_in = b.din("alpha", np.ascontiguousarray(alpha_log.reshape(1, 16)))
    be_in = b.din("beta", np.ascontiguousarray(beta_log.reshape(1, 16)))
    x1_out = b.dout("x1", [D, T], F32)

    ld = b.sem()
    C = reg_consts(b, [("ones_row", F32), ("ones_col", BF16), ("ident", BF16),
                       ("triu", BF16), ("pos_tm", F32), ("cvals", F32)], ld)

    wq = [b.sbuf("wq%d" % i, [128, 3072], BF16) for i in range(NCH)]
    wp = [b.sbuf("wp%d" % i, [128, 1024], BF16) for i in range(NCH)]
    for i in range(NCH):
        r.g(lambda g, t=wq[i], i=i: g.dma_start(out=t[:, :], in_=wq_in[i * 128:(i + 1) * 128, :]),
            inc=ld.inc(16))
        r.g(lambda g, t=wp[i], i=i: g.dma_start(out=t[:, :], in_=wp_in[i * 128:(i + 1) * 128, :]),
            inc=ld.inc(16))
    alr = b.sbuf("alr", [1, 16], F32)
    ber = b.sbuf("ber", [1, 16], F32)
    r.g(lambda g: g.dma_start(out=alr[:, :], in_=al_in[:, :]), inc=ld.inc(16))
    r.g(lambda g: g.dma_start(out=ber[:, :], in_=be_in[:, :]), inc=ld.inc(16))

    ldx = b.sem()
    mm = b.sem()
    ev = b.sem()
    es = b.sem()
    st = b.sem()

    xs = b.sbuf("xs", [128, 8 * 1024], F32)
    lnx = b.sbuf("lnx", [128, 8 * 1024], BF16)
    xbs = b.sbuf("xbs", [128, 512], BF16)
    sqs = b.sbuf("sqs", [128, 512], BF16)
    tmp32 = b.sbuf("tmp32", [128, 512], F32)
    qphi = b.sbuf("qphi", [128, 8 * 1024], BF16)
    kphi = b.sbuf("kphi", [128, 8 * 1024], BF16)
    vtm = b.sbuf("vtm", [128, 8 * 1024], BF16)
    ofm = b.sbuf("ofm", [128, 8 * 1024], BF16)
    S32 = b.sbuf("S32", [64, 1024], F32)
    S_sb = b.sbuf("S_sb", [64, 1024], BF16)
    stage_q = b.sbuf("stage_q", [64, 128], BF16)
    stage_k = b.sbuf("stage_k", [64, 128], BF16)
    fkT = [b.sbuf("fkT%d" % i, [128, 64], BF16) for i in range(2)]
    At_sb = [b.sbuf("At%d" % i, [128, 128], BF16) for i in range(2)]
    otm = b.sbuf("otm", [128, 64], BF16)
    rows4 = b.sbuf("rows4", [1, 4 * 512], F32)
    sc16 = b.sbuf("sc16", [1, 16], F32)
    sp16 = b.sbuf("sp16", [1, 16], F32)
    scm16 = b.sbuf("scm16", [1, 16], F32)
    scf16 = b.sbuf("scf16", [1, 16], F32)
    rdn_t1 = b.sbuf("rdn_t1", [128, 32], F32)
    rdn_t2 = b.sbuf("rdn_t2", [128, 32], F32)
    scb = b.sbuf("scb", [128, 16], F32)
    rdn = b.sbuf("rdn", [128, 16 * 32], F32)

    pA = b.psum("pA", [128, 6 * 512], F32)
    pT = b.psum("pT", [128, 2048], BF16)
    pool = Pool([pA[:, i * 512:(i + 1) * 512] for i in range(4)])
    poolD = Pool([pA[:, i * 512:(i + 1) * 512] for i in range(4, 6)])
    poolT = Pool([pT[:, 0:128], pT[:, 1024:1152]])

    wld = ld.now()
    r.s(lambda s: s.activation(sc16[:, :], alr[:, :], AF.Exp), wait=[wld])
    r.s(lambda s: s.activation(sc16[:, :], sc16[:, :], AF.Ln, bias=C["cvals"][0:1, 0:1]))
    r.s(lambda s: s.activation(sp16[:, :], ber[:, :], AF.Exp))
    r.s(lambda s: s.activation(sp16[:, :], sp16[:, :], AF.Ln, bias=C["cvals"][0:1, 0:1]),
        inc=es.inc())
    r.v(lambda v: v.tensor_scalar(sc16[:, :], sc16[:, :], 1e-6, None, ALU.add), wait=[es.now()],
        inc=ev.inc())
    r.v(lambda v: v.tensor_scalar(sp16[:, :], sp16[:, :], 1e-6, None, ALU.add),
        wait=[ev.now()], inc=ev.inc())
    r.v(lambda v: v.tensor_tensor(scm16[:, :], sc16[:, :], sp16[:, :], ALU.mult),
        wait=[ev.now()], inc=ev.inc())
    r.v(lambda v: v.tensor_scalar(scf16[:, :], scm16[:, :], 8.0, None, ALU.mult),
        wait=[ev.now()], inc=ev.inc())
    i0, ps0, w0 = pool.take()
    r.t(lambda t, p=ps0: MM(t, p[:, 0:16], C["ones_row"][:, :], scf16[:, :], True, True),
        wait=[ev.now()] + w0, inc=mm.inc())
    r.v(lambda v, p=ps0: v.tensor_copy(scb[:, :], p[:, 0:16]), wait=[mm.now()], inc=ev.inc())
    pool.done(i0, ev.now())
    for h in range(16):
        sl = rdn[:, h * 32:(h + 1) * 32]
        r.v(lambda v, h=h: v.tensor_scalar(rdn_t1[:, :], C["pos_tm"][:, :], scb[:, h:h + 1],
                                           None, ALU.mult), wait=[ev.now()], inc=ev.inc())
        r.v(lambda v: v.tensor_scalar(rdn_t2[:, :], rdn_t1[:, :], 1e-6, None, ALU.max),
            wait=[ev.now()], inc=ev.inc())
        r.v(lambda v, sl=sl: v.reciprocal(sl, rdn_t2[:, :]), wait=[ev.now()], inc=ev.inc())
    r.v(lambda v: v.memset(S32[:, :], 0.0))
    r.v(lambda v: v.memset(S_sb[:, :], 0.0), inc=ev.inc())

    if dbg:
        d_scf = b.dout("d_scf", [1, 16], F32)
        d_sc16 = b.dout("d_sc16", [1, 16], F32)
        d_sp16 = b.dout("d_sp16", [1, 16], F32)
        r.y(lambda y: y.dma_start(out=d_scf[:, :], in_=scf16[:, :]), wait=[ev.now()],
            inc=st.inc(16))
        r.y(lambda y: y.dma_start(out=d_sc16[:, :], in_=sc16[:, :]), inc=st.inc(16))
        r.y(lambda y: y.dma_start(out=d_sp16[:, :], in_=sp16[:, :]), inc=st.inc(16))
        d_scb = b.dout("d_scb", [128, 16], F32)
        d_rdn = b.dout("d_rdn", [128, 512], F32)
        d_lnx = b.dout("d_lnx", [128, 8192], F32)
        d_qphi = b.dout("d_qphi", [128, 8192], F32)
        d_kphi = b.dout("d_kphi", [128, 8192], F32)
        d_vtm = b.dout("d_vtm", [128, 8192], F32)
        d_ofm = b.dout("d_ofm", [128, 8192], F32)
        d_S = b.dout("d_S", [64, 1024], F32)
        r.y(lambda y: y.dma_start(out=d_scb[:, :], in_=scb[:, :]), wait=[ev.now()],
            inc=st.inc(16))
        r.y(lambda y: y.dma_start(out=d_rdn[:, :], in_=rdn[:, :]), inc=st.inc(16))

    xview = x_in.rearrange("(c p) t -> p c t", p=128)
    oview = x1_out.rearrange("(c p) t -> p c t", p=128)

    for sti in range(4):
        t0 = sti * 1024
        barrier = [mm.now(), ev.now(), es.now(), st.now(), ld.now()]
        r.y(lambda y, t0=t0: y.dma_start(out=xs[:, :].rearrange("p (c t) -> p c t", t=1024),
                                         in_=xview[:, :, t0:t0 + 1024]),
            wait=barrier, inc=ldx.inc(16))
        xw = ldx.now()

        for tau in range(2):
            def xsl(c, tau=tau):
                return xs[:, c * 1024 + tau * 512: c * 1024 + tau * 512 + 512]

            def lsl(c, tau=tau):
                return lnx[:, c * 1024 + tau * 512: c * 1024 + tau * 512 + 512]

            iS, psS_, wS = pool.take()
            iQ, psQ_, wQ = pool.take()
            for c in range(NCH):
                r.v(lambda v, x=xsl(c): v.tensor_copy(xbs[:, :], x),
                    wait=[xw, mm.now(), es.now()] + (wS + wQ if c == 0 else []), inc=ev.inc())
                r.s(lambda s: s.activation(sqs[:, :], xbs[:, :], AF.Square),
                    wait=[ev.now(), mm.now()], inc=es.inc())
                r.t(lambda t, c=c, p=psS_: MM(t, p[0:1, 0:512], C["ones_col"][:, :], xbs[:, :],
                                              c == 0, c == NCH - 1),
                    wait=[ev.now()])
                r.t(lambda t, c=c, p=psQ_: MM(t, p[0:1, 0:512], C["ones_col"][:, :], sqs[:, :],
                                              c == 0, c == NCH - 1),
                    wait=[es.now()], inc=mm.inc())
            rstd, brow = layernorm_rows(b, rows4, psS_, psQ_, mm.now(), ev, es,
                                        eps_ap=C["cvals"][0:1, 1:2])
            pool.done(iS, ev.now())
            pool.done(iQ, ev.now())
            iA, psA_, wA = pool.take()
            iB, psB_, wB = pool.take()
            r.t(lambda t, p=psA_, q=rstd: MM(t, p[:, 0:512], C["ones_row"][:, :], q, True, True),
                wait=[ev.now()] + wA)
            r.t(lambda t, p=psB_, q=brow: MM(t, p[:, 0:512], C["ones_row"][:, :], q, True, True),
                wait=wB, inc=mm.inc())
            mw2 = mm.now()
            for c in range(NCH):
                r.v(lambda v, x=xsl(c), p=psA_: v.tensor_tensor(tmp32[:, :], x, p[:, 0:512], ALU.mult),
                    wait=[mw2])
                r.v(lambda v, o=lsl(c), p=psB_: v.tensor_tensor(o, tmp32[:, :], p[:, 0:512], ALU.add),
                    inc=(ev.inc() if c == NCH - 1 else None))
            pool.done(iA, ev.now())
            pool.done(iB, ev.now())
            lnw = ev.now()

            for qk in range(2):
                dst = qphi if qk == 0 else kphi
                for oc in range(8):
                    iP, psP_, wP = pool.take()
                    col = qk * 1024 + oc * 128
                    for dc in range(NCH):
                        r.t(lambda t, dc=dc, col=col, p=psP_, q=lsl(dc):
                            MM(t, p[:, 0:512], wq[dc][:, col:col + 128], q,
                               dc == 0, dc == NCH - 1),
                            wait=([lnw] + wP) if dc == 0 else [],
                            inc=(mm.inc() if dc == NCH - 1 else None))
                    r.v(lambda v, d=dst[:, oc * 1024 + tau * 512: oc * 1024 + tau * 512 + 512],
                        p=psP_:
                        v.tensor_scalar(d, p[:, 0:512], 0.0, 1e-6, ALU.max, ALU.add),
                        wait=[mm.now()], inc=ev.inc())
                    pool.done(iP, ev.now())
            for sub in range(4):
                cg = tau * 4 + sub
                for half in range(2):
                    iP, psP_, wP = pool.take()
                    for dc in range(NCH):
                        r.t(lambda t, dc=dc, cg=cg, half=half, p=psP_:
                            MM(t, p[:, 0:512],
                               lnx[:, dc * 1024 + cg * 128: dc * 1024 + cg * 128 + 128],
                               wq[dc][:, 2048 + half * 512: 2048 + half * 512 + 512],
                               dc == 0, dc == NCH - 1),
                            wait=([lnw] + wP) if dc == 0 else [],
                            inc=(mm.inc() if dc == NCH - 1 else None))
                    r.v(lambda v, d=vtm[:, cg * 1024 + half * 512: cg * 1024 + half * 512 + 512],
                        p=psP_: v.tensor_copy(d, p[:, 0:512]),
                        wait=[mm.now()], inc=ev.inc())
                    pool.done(iP, ev.now())

        for c in range(8):
            Cg = sti * 8 + c
            qkv_dep = ev.now()
            iD0, psD0, wD0 = poolD.take()
            iD1, psD1, wD1 = poolD.take()
            for h in range(16):
                hp, half = h // 2, h % 2
                r0 = half * 64
                qsl = qphi[r0:r0 + 64, hp * 1024 + c * 128: hp * 1024 + c * 128 + 128]
                ksl = kphi[r0:r0 + 64, hp * 1024 + c * 128: hp * 1024 + c * 128 + 128]
                vsl = vtm[:, c * 1024 + h * 64: c * 1024 + h * 64 + 64]
                if half == 1:
                    r.v(lambda v, q=qsl: v.tensor_copy(stage_q[:, :], q),
                        wait=[qkv_dep, mm.now()])
                    r.v(lambda v, k=ksl: v.tensor_copy(stage_k[:, :], k), inc=ev.inc())
                    qop, kop = stage_q[:, :], stage_k[:, :]
                    stg_dep = [ev.now()]
                else:
                    qop, kop = qsl, ksl
                    stg_dep = [qkv_dep]
                iS, psS_, wS = pool.take()
                r.t(lambda t, k=kop, q=qop, p=psS_: MM(t, p[:, 0:128], k, q, True, True),
                    wait=stg_dep + wS, inc=mm.inc())
                At = At_sb[h % 2][:, :]
                r.v(lambda v, p=psS_, At=At: v.tensor_tensor(At, p[:, 0:128], C["triu"][:, :],
                                                             ALU.mult),
                    wait=[mm.now()], inc=ev.inc())
                pool.done(iS, ev.now())
                at_dep = ev.now()
                iT, ptr_, wT = poolT.take()
                r.t(lambda t, k=kop, p=ptr_: t.transpose(p[:, 0:64], k, C["ident"][0:64, 0:64]),
                    wait=stg_dep + wT, inc=mm.inc())
                fk = fkT[h % 2][:, :]
                r.v(lambda v, p=ptr_, fk=fk: v.tensor_copy(fk, p[:, 0:64]),
                    wait=[mm.now()], inc=ev.inc())
                poolT.done(iT, ev.now())
                fk_dep = ev.now()
                iO, psO_, wO = pool.take()
                r.t(lambda t, At=At, vv=vsl, p=psO_: MM(t, p[:, 0:64], At, vv, True, False),
                    wait=[at_dep] + wO)
                ss = (h % 2) * 512 + (h // 2) * 64
                r.t(lambda t, q=qop, ss=ss, p=psO_:
                    MM(t, p[:, 0:64], q, S_sb[:, ss:ss + 64], False, True),
                    inc=mm.inc())
                psD = psD0 if half == 0 else psD1
                r.t(lambda t, fk=fk, vv=vsl, hp=hp, p=psD:
                    MM(t, p[0:64, hp * 64:hp * 64 + 64], fk, vv, True, True),
                    wait=[fk_dep] + (wD0 + wD1 if h < 2 else []), inc=mm.inc())
                mm_o = mm.now()
                r.v(lambda v, p=psO_, h=h, Cg=Cg:
                    v.tensor_scalar(otm[:, :], p[:, 0:64], rdn[:, h * 32 + Cg:h * 32 + Cg + 1],
                                    None, ALU.mult),
                    wait=[mm_o], inc=ev.inc())
                pool.done(iO, ev.now())
                iT2, ptr2_, wT2 = poolT.take()
                r.t(lambda t, p=ptr2_: t.transpose(p[0:64, 0:128], otm[:, :], C["ident"][:, :]),
                    wait=[ev.now()] + wT2, inc=mm.inc())
                r.v(lambda v, p=ptr2_,
                    d=ofm[r0:r0 + 64, hp * 1024 + c * 128: hp * 1024 + c * 128 + 128]:
                    v.tensor_copy(d, p[0:64, 0:128]),
                    wait=[mm.now()], inc=ev.inc())
                poolT.done(iT2, ev.now())
            dmm = mm.now()
            r.v(lambda v, p=psD0: v.tensor_tensor(S32[:, 0:512], S32[:, 0:512], p[0:64, 0:512],
                                                  ALU.add), wait=[dmm], inc=ev.inc())
            r.v(lambda v, p=psD1: v.tensor_tensor(S32[:, 512:1024], S32[:, 512:1024], p[0:64, 0:512],
                                                  ALU.add), wait=[ev.now()], inc=ev.inc())
            r.v(lambda v: v.tensor_copy(S_sb[:, :], S32[:, :]), wait=[ev.now()], inc=ev.inc())
            poolD.done(iD0, ev.now())
            poolD.done(iD1, ev.now())

        if dbg and sti == 0:
            r.g(lambda g: g.dma_start(out=d_lnx[:, :], in_=lnx[:, :]), wait=[ev.now()],
                inc=st.inc(16))
            r.g(lambda g: g.dma_start(out=d_qphi[:, :], in_=qphi[:, :]), inc=st.inc(16))
            r.g(lambda g: g.dma_start(out=d_kphi[:, :], in_=kphi[:, :]), inc=st.inc(16))
            r.g(lambda g: g.dma_start(out=d_vtm[:, :], in_=vtm[:, :]), inc=st.inc(16))
            r.g(lambda g: g.dma_start(out=d_ofm[:, :], in_=ofm[:, :]), inc=st.inc(16))
            r.g(lambda g: g.dma_start(out=d_S[:, :], in_=S32[:, :]), inc=st.inc(16))

        o_dep = ev.now()
        for ic in range(NCH):
            for tau2 in range(2):
                iP, psP_, wP = pool.take()
                for jc in range(NCH):
                    r.t(lambda t, jc=jc, ic=ic, tau2=tau2, p=psP_:
                        MM(t, p[:, 0:512], wp[jc][:, ic * 128:ic * 128 + 128],
                           ofm[:, jc * 1024 + tau2 * 512: jc * 1024 + tau2 * 512 + 512],
                           jc == 0, jc == NCH - 1),
                        wait=([o_dep] + wP) if jc == 0 else [],
                        inc=(mm.inc() if jc == NCH - 1 else None))
                r.v(lambda v, x=xs[:, ic * 1024 + tau2 * 512: ic * 1024 + tau2 * 512 + 512],
                    p=psP_: v.tensor_tensor(x, x, p[:, 0:512], ALU.add),
                    wait=[mm.now()], inc=ev.inc())
                pool.done(iP, ev.now())
        r.y(lambda y, t0=t0: y.dma_start(out=oview[:, :, t0:t0 + 1024],
                                         in_=xs[:, :].rearrange("p (c t) -> p c t", t=1024)),
            wait=[ev.now()], inc=st.inc(16))

    r.y(lambda y: y.nop(), wait=[st.now()])
    return b.finish()





def build_ln_pass(x_percore, pad=0, name="lnpass"):
    b = Bld(name)
    r = b.r
    x_in = b.din("x", x_percore)
    y_out = b.dout("y", [D, T + pad], F32)

    ld = b.sem()
    C = reg_consts(b, [("ones_row", F32), ("ones_col", BF16), ("cvals", F32)], ld)
    ldx = b.sem()
    mm = b.sem()
    ev = b.sem()
    es = b.sem()
    st = b.sem()

    xs = b.sbuf("xs", [128, 8 * 512], F32)
    ys = b.sbuf("ys", [128, 8 * 512], F32)
    xbs = b.sbuf("xbs", [128, 512], BF16)
    sqs = b.sbuf("sqs", [128, 512], BF16)
    tmp32 = b.sbuf("tmp32", [128, 512], F32)
    rows4 = b.sbuf("rows4", [1, 4 * 512], F32)
    zrow = b.sbuf("zrow", [128, 8], F32)

    pA = b.psum("pA", [128, 4 * 512], F32)
    pool = Pool([pA[:, i * 512:(i + 1) * 512] for i in range(4)])

    xview = x_in.rearrange("(c p) t -> p c t", p=128)
    yview = y_out.rearrange("(c p) t -> p c t", p=128)

    if pad:
        r.v(lambda v: v.memset(zrow[:, :], 0.0), inc=ev.inc())
        r.y(lambda y: y.dma_start(out=yview[:, :, 0:1],
                                  in_=zrow[:, 0:8].rearrange("p (c o) -> p c o", o=1),
                                  allow_slow_non_contiguous=True),
            wait=[ev.now()], inc=st.inc(16))

    for ti in range(8):
        t0 = ti * 512
        barrier = [mm.now(), ev.now(), es.now(), ld.now()]
        r.y(lambda y, t0=t0: y.dma_start(out=xs[:, :].rearrange("p (c t) -> p c t", t=512),
                                         in_=xview[:, :, t0:t0 + 512]),
            wait=barrier, inc=ldx.inc(16))
        xw = ldx.now()
        iS, psS_, wS = pool.take()
        iQ, psQ_, wQ = pool.take()
        for c in range(NCH):
            r.v(lambda v, c=c: v.tensor_copy(xbs[:, :], xs[:, c * 512:(c + 1) * 512]),
                wait=[xw, mm.now(), es.now()] + (wS + wQ if c == 0 else []), inc=ev.inc())
            r.s(lambda s: s.activation(sqs[:, :], xbs[:, :], AF.Square),
                wait=[ev.now(), mm.now()], inc=es.inc())
            r.t(lambda t, c=c, p=psS_: MM(t, p[0:1, 0:512], C["ones_col"][:, :], xbs[:, :],
                                          c == 0, c == NCH - 1), wait=[ev.now()])
            r.t(lambda t, c=c, p=psQ_: MM(t, p[0:1, 0:512], C["ones_col"][:, :], sqs[:, :],
                                          c == 0, c == NCH - 1), wait=[es.now()], inc=mm.inc())
        rstd, brow = layernorm_rows(b, rows4, psS_, psQ_, mm.now(), ev, es,
                                    eps_ap=C["cvals"][0:1, 1:2])
        pool.done(iS, ev.now())
        pool.done(iQ, ev.now())
        iA, psA_, wA = pool.take()
        iB, psB_, wB = pool.take()
        r.t(lambda t, p=psA_, q=rstd: MM(t, p[:, 0:512], C["ones_row"][:, :], q, True, True),
            wait=[ev.now()] + wA)
        r.t(lambda t, p=psB_, q=brow: MM(t, p[:, 0:512], C["ones_row"][:, :], q, True, True),
            wait=wB, inc=mm.inc())
        mw2 = mm.now()
        for c in range(NCH):
            r.v(lambda v, c=c, p=psA_: v.tensor_tensor(tmp32[:, :], xs[:, c * 512:(c + 1) * 512],
                                                       p[:, 0:512], ALU.mult), wait=[mw2])
            r.v(lambda v, c=c, p=psB_: v.tensor_tensor(ys[:, c * 512:(c + 1) * 512], tmp32[:, :],
                                                       p[:, 0:512], ALU.add),
                inc=(ev.inc() if c == NCH - 1 else None))
        pool.done(iA, ev.now())
        pool.done(iB, ev.now())
        r.y(lambda y, t0=t0: y.dma_start(out=yview[:, :, pad + t0: pad + t0 + 512],
                                         in_=ys[:, :].rearrange("p (c t) -> p c t", t=512)),
            wait=[ev.now()], inc=st.inc(16))
    r.y(lambda y: y.nop(), wait=[st.now()])
    return b.finish()


def rope_tables():
    DH, WINDOW = 64, 512
    half = np.arange(0, DH, 2, dtype=np.float64)
    inv_freq = 1.0 / 10000.0 ** (half / DH)
    t = np.arange(T, dtype=np.float64)
    freqs = t[:, None] * inv_freq[None, :]
    freqs = np.concatenate([freqs, freqs], axis=-1)
    base = (half + 0.4 * DH) / (1.4 * DH)
    power = (t - T // 2) / float(WINDOW // 2)
    scale = base[None, :] ** power[:, None]
    scale = np.concatenate([scale, scale], axis=-1)
    cos, sin = np.cos(freqs), np.sin(freqs)
    return (np.ascontiguousarray((cos * scale).T, np.float32),
            np.ascontiguousarray((sin * scale).T, np.float32),
            np.ascontiguousarray((cos / scale).T, np.float32),
            np.ascontiguousarray((sin / scale).T, np.float32))


def bB_masks():
    m = np.zeros((8, 128, 512), np.float32)
    tt = np.arange(128)[:, None]
    ss = np.arange(512)[None, :]
    for j in range(8):
        rel = (512 - 128 * j) + ss - tt
        m[j] = np.where((rel >= 0) & (rel <= 512), 0.0, -1e30)
    return m


def build_B_qkv(xb_percore, wqkv_l, q_scale_l, k_scale_l, name="b_qkv"):
    b = Bld(name)
    r = b.r
    xb_in = b.din("xb", xb_percore)
    wq_in = b.din("wqkv", np.ascontiguousarray(wqkv_l))
    qs_in = b.din("qs", np.ascontiguousarray(q_scale_l.reshape(1, 64)))
    ks_in = b.din("ks", np.ascontiguousarray(k_scale_l.reshape(1, 64)))
    qr_out = b.dout("qr", [128, 4 * T], BF16)
    kr_out = b.dout("kr", [128, 4 * T], BF16)
    vp_out = b.dout("vp", [128, 32 * 520], BF16)

    ld = b.sem()
    C = reg_consts(b, [("ones_row", F32), ("ones_col", BF16), ("cvals", F32)], ld)
    tabs = {}
    cms_a, sms_a, cds_a, sds_a = rope_tables()
    for nm, arr in (("cms", cms_a), ("sms", sms_a), ("cds", cds_a), ("sds", sds_a)):
        ap = b.din("t_" + nm, arr)
        tile = b.sbuf("sb_" + nm, [64, T], F32)
        r.g(lambda g, t=tile, a=ap: g.dma_start(out=t[:, :], in_=a[:, :]), inc=ld.inc(16))
        tabs[nm] = tile
    wq = [b.sbuf("wq%d" % i, [128, 1536], BF16) for i in range(NCH)]
    for i in range(NCH):
        r.g(lambda g, t=wq[i], i=i: g.dma_start(out=t[:, :], in_=wq_in[i * 128:(i + 1) * 128, :]),
            inc=ld.inc(16))
    qsr = b.sbuf("qsr", [1, 64], F32)
    ksr = b.sbuf("ksr", [1, 64], F32)
    r.g(lambda g: g.dma_start(out=qsr[:, :], in_=qs_in[:, :]), inc=ld.inc(16))
    r.g(lambda g: g.dma_start(out=ksr[:, :], in_=ks_in[:, :]), inc=ld.inc(16))

    ldx = b.sem()
    mm = b.sem()
    ev = b.sem()
    es = b.sem()
    st = b.sem()

    xs = b.sbuf("xs", [128, 8 * 513], F32)
    xsb = b.sbuf("xsb", [128, 8 * 512], BF16)
    sq = b.sbuf("sq", [64, 512], BF16)
    rows2 = b.sbuf("rows2", [1, 3 * 512], F32)
    qn = b.sbuf("qn", [64, 512], F32)
    qh = b.sbuf("qh", [64, 512], F32)
    qh2 = b.sbuf("qh2", [64, 512], F32)
    swp = b.sbuf("swp", [64, 512], F32)
    rt1 = b.sbuf("rt1", [64, 512], F32)
    rt2 = b.sbuf("rt2", [64, 512], F32)
    qrt = b.sbuf("qrt", [128, 4 * 512], BF16)
    krt = b.sbuf("krt", [128, 4 * 512], BF16)
    vpt = b.sbuf("vpt", [128, 4 * 520], BF16)
    scl = b.sbuf("scl", [64, 2], F32)

    pA = b.psum("pA", [128, 6 * 512], F32)
    pool = Pool([pA[:, i * 512:(i + 1) * 512] for i in range(6)])

    r.g(lambda g: g.dma_start(out=scl[:, 0:1], in_=qs_in.rearrange("o e -> e o")),
        inc=ld.inc(16))
    r.g(lambda g: g.dma_start(out=scl[:, 1:2], in_=ks_in.rearrange("o e -> e o")),
        inc=ld.inc(16))

    xview = xb_in.rearrange("(c p) t -> p c t", p=128)

    for ti in range(8):
        t0 = ti * 512
        barrier = [mm.now(), ev.now(), es.now(), st.now(), ld.now()]
        r.y(lambda y, t0=t0: y.dma_start(out=xs[:, :].rearrange("p (c t) -> p c t", t=513),
                                         in_=xview[:, :, t0:t0 + 513]),
            wait=barrier, inc=ldx.inc(16))
        xw = ldx.now()
        for c in range(NCH):
            off = 1 if c < 4 else 0
            r.v(lambda v, c=c, off=off:
                v.tensor_copy(xsb[:, c * 512:(c + 1) * 512],
                              xs[:, c * 513 + off: c * 513 + off + 512]),
                wait=[xw] + ([mm.now()] if ti > 0 else []),
                inc=(ev.inc() if c == NCH - 1 else None))
        xbw = ev.now()

        for qk in range(2):
            dst = qrt if qk == 0 else krt
            tabc = tabs["cms"] if qk == 0 else tabs["cds"]
            tabs_ = tabs["sms"] if qk == 0 else tabs["sds"]
            for h in range(8):
                iP, psP_, wP = pool.take()
                col = qk * 512 + h * 64
                for dc in range(NCH):
                    r.t(lambda t, dc=dc, col=col, p=psP_:
                        MM(t, p[0:64, 0:512], wq[dc][:, col:col + 64],
                           xsb[:, dc * 512:(dc + 1) * 512], dc == 0, dc == NCH - 1),
                        wait=([xbw] + wP) if dc == 0 else [],
                        inc=(mm.inc() if dc == NCH - 1 else None))
                mq = mm.now()
                r.s(lambda s, p=psP_: s.activation(sq[:, :], p[0:64, 0:512], AF.Square),
                    wait=[mq, ev.now()], inc=es.inc())
                iN, psN_, wN = pool.take()
                r.t(lambda t, p=psN_: MM(t, p[0:1, 0:512], C["ones_col"][0:64, :], sq[:, :],
                                         True, True), wait=[es.now()] + wN, inc=mm.inc())
                nr = rows2[0:1, 0:512]
                ns = rows2[0:1, 512:1024]
                r.v(lambda v, nr=nr, p=psN_: v.tensor_scalar(nr, p[0:1, 0:512], 1e-24, None,
                                                             ALU.max),
                    wait=[mm.now()], inc=ev.inc())
                pool.done(iN, ev.now())
                r.s(lambda s, nr=nr: s.activation(nr, nr, AF.Ln), wait=[ev.now()])
                r.s(lambda s, ns=ns, nr=nr: s.activation(ns, nr, AF.Exp, scale=-0.5),
                    inc=es.inc())
                iR, psR_, wR = pool.take()
                r.t(lambda t, ns=ns, p=psR_: MM(t, p[0:64, 0:512], C["ones_row"][0:1, 0:64],
                                                ns, True, True),
                    wait=[es.now()] + wR, inc=mm.inc())
                r.v(lambda v, p=psR_: v.tensor_copy(qn[:, :], p[0:64, 0:512]),
                    wait=[mm.now()], inc=ev.inc())
                pool.done(iR, ev.now())
                r.v(lambda v, p=psP_: v.tensor_tensor(qh[:, :], p[0:64, 0:512], qn[:, :],
                                                      ALU.mult),
                    wait=[ev.now()], inc=ev.inc())
                pool.done(iP, ev.now())
                if qk == 0:
                    r.v(lambda v: v.tensor_scalar(qh2[:, :], qh[:, :], scl[:, 0:1], 8.0,
                                                  ALU.mult, ALU.mult),
                        wait=[ev.now()], inc=ev.inc())
                else:
                    r.v(lambda v: v.tensor_scalar(qh2[:, :], qh[:, :], scl[:, 1:2], None,
                                                  ALU.mult),
                        wait=[ev.now()], inc=ev.inc())
                r.v(lambda v: v.tensor_scalar(swp[0:32, :], qh2[32:64, :], -1.0, None,
                                              ALU.mult), wait=[ev.now()], inc=ev.inc())
                r.v(lambda v: v.tensor_copy(swp[32:64, :], qh2[0:32, :]),
                    wait=[ev.now()], inc=ev.inc())
                r.v(lambda v, tc=tabc, t0=t0: v.tensor_tensor(rt1[:, :], qh2[:, :],
                                                              tc[:, t0:t0 + 512], ALU.mult),
                    wait=[ev.now()], inc=ev.inc())
                r.v(lambda v, ts_=tabs_, t0=t0: v.tensor_tensor(rt2[:, :], swp[:, :],
                                                                ts_[:, t0:t0 + 512], ALU.mult),
                    wait=[ev.now()], inc=ev.inc())
                r0 = (h % 2) * 64
                r.v(lambda v, d=dst[r0:r0 + 64, (h // 2) * 512:(h // 2) * 512 + 512]:
                    v.tensor_tensor(d, rt1[:, :], rt2[:, :], ALU.add),
                    wait=[ev.now()], inc=ev.inc())

        for sub in range(4):
            iP, psP_, wP = pool.take()
            for dc in range(NCH):
                r.t(lambda t, dc=dc, sub=sub, p=psP_:
                    MM(t, p[:, 0:512],
                       xsb[:, dc * 512 + sub * 128: dc * 512 + sub * 128 + 128],
                       wq[dc][:, 1024:1536], dc == 0, dc == NCH - 1),
                    wait=([xbw] + wP) if dc == 0 else [],
                    inc=(mm.inc() if dc == NCH - 1 else None))
            r.v(lambda v, sub=sub, p=psP_:
                v.tensor_copy(vpt[:, sub * 520:sub * 520 + 520]
                              .rearrange("p (h e) -> p h e", e=65)[:, :, 0:64],
                              p[:, 0:512].rearrange("p (h e) -> p h e", e=64)),
                wait=[mm.now()], inc=ev.inc())
            r.v(lambda v, sub=sub: v.memset(vpt[:, sub * 520:sub * 520 + 520]
                                            .rearrange("p (h e) -> p h e", e=65)[:, :, 64:65],
                                            1.0), inc=ev.inc())
            pool.done(iP, ev.now())

        r.y(lambda y, t0=t0: y.dma_start(
            out=qr_out.rearrange("p (g t) -> p g t", t=T)[:, :, t0:t0 + 512],
            in_=qrt[:, :].rearrange("p (g t) -> p g t", t=512)),
            wait=[ev.now()], inc=st.inc(16))
        r.y(lambda y, t0=t0: y.dma_start(
            out=kr_out.rearrange("p (g t) -> p g t", t=T)[:, :, t0:t0 + 512],
            in_=krt[:, :].rearrange("p (g t) -> p g t", t=512)),
            inc=st.inc(16))
        r.y(lambda y, ti=ti: y.dma_start(
            out=vp_out[:, ti * 2080: ti * 2080 + 2080], in_=vpt[:, :]),
            inc=st.inc(16))
    r.y(lambda y: y.nop(), wait=[st.now()])
    return b.finish()


def build_B_attn(xb_percore, qr_percore, kr_percore, vp_percore, wo_l,
                 q_scale_l, k_scale_l, name="b_attn"):
    b = Bld(name)
    r = b.r
    xb_in = b.din("xb", xb_percore)
    qr_in = b.din("qr", qr_percore, BF16)
    kr_in = b.din("kr", kr_percore, BF16)
    vp_in = b.din("vp", vp_percore, BF16)
    wo_in = b.din("wo", np.ascontiguousarray(wo_l))
    qs_in = b.din("qs", np.ascontiguousarray(q_scale_l.reshape(1, 64)))
    ks_in = b.din("ks", np.ascontiguousarray(k_scale_l.reshape(1, 64)))
    mk_in = b.din("masks", np.ascontiguousarray(
        bB_masks().transpose(1, 0, 2).reshape(128, 8 * 512)))
    xb_out = b.dout("xb2", [D, T + 1], F32)

    ld = b.sem()
    C = reg_consts(b, [("ones_row", F32), ("cvals", F32)], ld)
    qr = b.sbuf("qr_s", [128, 4 * T], BF16)
    kr = b.sbuf("kr_s", [128, 4 * T], BF16)
    vp = b.sbuf("vp_s", [128, 32 * 520], BF16)
    wo = [b.sbuf("wo%d" % i, [128, 1024], BF16) for i in range(4)]
    msk = b.sbuf("msk", [128, 8 * 512], BF16)
    qsr = b.sbuf("qsr", [1, 64], F32)
    ksr = b.sbuf("ksr", [1, 64], F32)
    r.g(lambda g: g.dma_start(out=qr[:, :], in_=qr_in[:, :]), inc=ld.inc(16))
    r.g(lambda g: g.dma_start(out=kr[:, :], in_=kr_in[:, :]), inc=ld.inc(16))
    r.g(lambda g: g.dma_start(out=vp[:, :], in_=vp_in[:, :]), inc=ld.inc(16))
    for i in range(4):
        r.g(lambda g, i=i: g.dma_start(out=wo[i][:, :], in_=wo_in[i * 128:(i + 1) * 128, :]),
            inc=ld.inc(16))
    r.g(lambda g: g.dma_start(out=msk[:, :], in_=mk_in[:, :]), inc=ld.inc(16))
    r.g(lambda g: g.dma_start(out=qsr[:, :], in_=qs_in[:, :]), inc=ld.inc(16))
    r.g(lambda g: g.dma_start(out=ksr[:, :], in_=ks_in[:, :]), inc=ld.inc(16))

    ldx = b.sem()
    mm = b.sem()
    ev = b.sem()
    es = b.sem()
    st = b.sem()

    rows = b.sbuf("rows", [1, 1024], F32)
    negM = b.sbuf("negM", [128, 1], F32)
    exp2 = [b.sbuf("exp2a", [128, 1024], BF16), b.sbuf("exp2b", [128, 1024], BF16)]
    rden = b.sbuf("rden", [1, 512], F32)
    rb = b.sbuf("rb", [64, 512], F32)
    ofm = b.sbuf("ofm", [128, 4 * 512], BF16)
    xt = b.sbuf("xt", [128, 8 * 512], F32)

    pA = b.psum("pA", [128, 6 * 512], F32)
    pool = Pool([pA[:, i * 512:(i + 1) * 512] for i in range(4)])
    poolO = Pool([pA[:, i * 512:(i + 1) * 512] for i in range(4, 6)])

    wld = ld.now()
    r.v(lambda v: v.reduce_max(rows[0:1, 0:1], qsr[:, :], axis=AX, apply_absolute_value=True),
        wait=[wld], inc=ev.inc())
    r.v(lambda v: v.reduce_max(rows[0:1, 1:2], ksr[:, :], axis=AX, apply_absolute_value=True),
        wait=[ev.now()], inc=ev.inc())
    r.v(lambda v: v.tensor_tensor(rows[0:1, 2:3], rows[0:1, 0:1], rows[0:1, 1:2], ALU.mult),
        wait=[ev.now()], inc=ev.inc())
    r.v(lambda v: v.tensor_scalar(rows[0:1, 3:4], rows[0:1, 2:3], -8.0, None, ALU.mult),
        wait=[ev.now()], inc=ev.inc())
    i0, ps0, w0 = pool.take()
    r.t(lambda t, p=ps0: MM(t, p[:, 0:1], C["ones_row"][:, :], rows[0:1, 3:4], True, True),
        wait=[ev.now()] + w0, inc=mm.inc())
    r.v(lambda v, p=ps0: v.tensor_copy(negM[:, :], p[:, 0:1]), wait=[mm.now()], inc=ev.inc())
    pool.done(i0, ev.now())

    xview = xb_in.rearrange("(c p) t -> p c t", p=128)
    oview = xb_out.rearrange("(c p) t -> p c t", p=128)
    zcol = b.sbuf("zcol", [128, 8], F32)
    r.v(lambda v: v.memset(zcol[:, :], 0.0), inc=ev.inc())

    for ti in range(8):
        t0 = ti * 512
        barrier = [mm.now(), ev.now(), es.now(), st.now()]
        r.y(lambda y, t0=t0: y.dma_start(out=xt[:, :].rearrange("p (c t) -> p c t", t=512),
                                         in_=xview[:, :, t0 + 1:t0 + 513]),
            wait=barrier, inc=ldx.inc(16))
        xw = ldx.now()
        for h in range(8):
            r0 = (h % 2) * 64
            qsl = qr[r0:r0 + 64, (h // 2) * T + t0: (h // 2) * T + t0 + 512]
            iO, psO_, wO = poolO.take()
            hd_ev = ev.now()
            js = [j for j in range(8) if t0 - 512 + 128 * j >= 0]
            grs = [js[i:i + 2] for i in range(0, len(js), 2)]
            es_exp = {}
            mm_pv = {}

            def rec_pv(g, h=h, psO_=psO_, grs=grs, t0=t0, wO=wO):
                for u, j in enumerate(grs[g]):
                    kc = (t0 - 512 + 128 * j) // 128
                    r.t(lambda t, kc=kc, h=h, g=g, u=u, p=psO_:
                        MM(t, p[0:65, 0:512],
                           vp[:, kc * 520 + h * 65: kc * 520 + h * 65 + 65],
                           exp2[g % 2][:, u * 512:(u + 1) * 512],
                           (g == 0 and u == 0), False),
                        wait=[es_exp[g]] + (wO if (g == 0 and u == 0) else []),
                        inc=(mm.inc() if u == len(grs[g]) - 1 else None))
                mm_pv[g] = mm.now()

            for g, grp in enumerate(grs):
                half = g % 2
                for u, j in enumerate(grp):
                    kc = (t0 - 512 + 128 * j) // 128
                    bank = pool.tiles[half * 2 + u]
                    w = [wld, hd_ev] if g < 2 else [es_exp[g - 2]]
                    r.t(lambda t, kc=kc, r0=r0, h=h, qsl=qsl, p=bank:
                        MM(t, p[:, 0:512],
                           kr[r0:r0 + 64, (h // 2) * T + kc * 128:
                              (h // 2) * T + kc * 128 + 128],
                           qsl, True, True),
                        wait=w, inc=(mm.inc() if u == len(grp) - 1 else None))
                mw = mm.now()
                for u, j in enumerate(grp):
                    bank = pool.tiles[half * 2 + u]
                    r.v(lambda v, j=j, p=bank:
                        v.tensor_tensor(p[:, 0:512], p[:, 0:512],
                                        msk[:, j * 512:(j + 1) * 512], ALU.add),
                        wait=[mw], inc=ev.inc())
                w2_ = [ev.now()]
                if g >= 2:
                    w2_.append(mm_pv[g - 2])
                r.s(lambda s, half=half, n=len(grp): s.activation(
                    exp2[half][:, 0:n * 512], pA[:, half * 1024:half * 1024 + n * 512],
                    AF.Exp, bias=negM[:, 0:1]),
                    wait=w2_, inc=es.inc())
                es_exp[g] = es.now()
                if g >= 1:
                    rec_pv(g - 1)
            rec_pv(len(grs) - 1)
            r.v(lambda v, p=psO_: v.reciprocal(rden[:, :], p[64:65, 0:512]),
                wait=[mm.now()], inc=ev.inc())
            iR, psR_, wR = pool.take()
            r.t(lambda t, p=psR_: MM(t, p[0:64, 0:512], C["ones_row"][0:1, 0:64], rden[:, :],
                                     True, True),
                wait=[ev.now()] + wR, inc=mm.inc())
            r.v(lambda v, p=psR_: v.tensor_copy(rb[:, :], p[0:64, 0:512]),
                wait=[mm.now()], inc=ev.inc())
            pool.done(iR, ev.now())
            r.v(lambda v, p=psO_, d=ofm[r0:r0 + 64, (h // 2) * 512:(h // 2) * 512 + 512]:
                v.tensor_tensor(d, p[0:64, 0:512], rb[:, :], ALU.mult),
                wait=[ev.now(), mm.now()], inc=ev.inc())
            poolO.done(iO, ev.now())
        odep = ev.now()
        for dc in range(NCH):
            iP, psP_, wP = pool.take()
            for jc in range(4):
                r.t(lambda t, jc=jc, dc=dc, p=psP_:
                    MM(t, p[:, 0:512], wo[jc][:, dc * 128:dc * 128 + 128],
                       ofm[:, jc * 512:(jc + 1) * 512], jc == 0, jc == 3),
                    wait=([odep, xw] + wP) if jc == 0 else [],
                    inc=(mm.inc() if jc == 3 else None))
            r.v(lambda v, dc=dc, p=psP_:
                v.tensor_tensor(xt[:, dc * 512:(dc + 1) * 512], xt[:, dc * 512:(dc + 1) * 512],
                                p[:, 0:512], ALU.add),
                wait=[mm.now()], inc=ev.inc())
            pool.done(iP, ev.now())
        r.y(lambda y, t0=t0: y.dma_start(out=oview[:, :, t0 + 1:t0 + 513],
                                         in_=xt[:, :].rearrange("p (c t) -> p c t", t=512)),
            wait=[ev.now()], inc=st.inc(16))
    r.y(lambda y: y.dma_start(out=oview[:, :, 0:1],
                              in_=zcol[:, :].rearrange("p (c z) -> p c z", z=1),
                              allow_slow_non_contiguous=True),
        wait=[ev.now()], inc=st.inc(16))
    r.y(lambda y: y.nop(), wait=[st.now()])
    return b.finish()


def build_B_ffn(xb_percore, w1p_l, b1p_col, w2p_l, b2_col, lg_col, lb_col,
                name="b_ffn"):
    NF1 = 44
    NF2 = 22
    b = Bld(name)
    r = b.r
    xb_in = b.din("xb", xb_percore)
    w1_in = b.din("w1p", np.ascontiguousarray(w1p_l))
    w2_in = b.din("w2p", np.ascontiguousarray(w2p_l))
    b1_in = b.din("b1c", np.ascontiguousarray(b1p_col))
    b2_in = b.din("b2c", np.ascontiguousarray(b2_col))
    lg_in = b.din("lgc", np.ascontiguousarray(lg_col))
    lb_in = b.din("lbc", np.ascontiguousarray(lb_col))
    xb_out = b.dout("xb2", [D, T + 1], F32)

    ld = b.sem()
    C = reg_consts(b, [("ones_row", F32), ("ones_col", BF16), ("cvals", F32)], ld)
    w1 = [b.sbuf("w1_%d" % i, [128, 5632], BF16) for i in range(NCH)]
    for i in range(NCH):
        r.g(lambda g, t=w1[i], i=i: g.dma_start(out=t[:, :], in_=w1_in[i * 128:(i + 1) * 128, :]),
            inc=ld.inc(16))
    w2 = [b.sbuf("w2_%d" % i, [128, 1024], BF16) for i in range(NF2)]
    for i in range(NF2):
        r.g(lambda g, t=w2[i], i=i: g.dma_start(out=t[:, :], in_=w2_in[i * 128:(i + 1) * 128, :]),
            inc=ld.inc(16))
    cols = b.sbuf("cols", [128, NF1 + 3 * 8], F32)
    b1c = cols[:, 0:NF1]
    b2c = cols[:, NF1:NF1 + 8]
    lgc = cols[:, NF1 + 8:NF1 + 16]
    lbc = cols[:, NF1 + 16:NF1 + 24]
    r.g(lambda g: g.dma_start(out=b1c, in_=b1_in[:, :]), inc=ld.inc(16))
    r.g(lambda g: g.dma_start(out=b2c, in_=b2_in[:, :]), inc=ld.inc(16))
    r.g(lambda g: g.dma_start(out=lgc, in_=lg_in[:, :]), inc=ld.inc(16))
    r.g(lambda g: g.dma_start(out=lbc, in_=lb_in[:, :]), inc=ld.inc(16))

    ldx = b.sem()
    mm = b.sem()
    ev = b.sem()
    es = b.sem()
    st = b.sem()

    xs = b.sbuf("xs", [128, 8 * 513], F32)
    xsb = b.sbuf("xsb", [128, 8 * 512], BF16)
    sqs = b.sbuf("sqs", [128, 512], BF16)
    xbs2 = b.sbuf("xbs2", [128, 512], BF16)
    tmp32 = b.sbuf("tmp32", [128, 512], F32)
    lnh = b.sbuf("lnh", [128, 8 * 512], BF16)
    aslab = b.sbuf("aslab", [128, NF2 * 512], BF16)
    gtmp = b.sbuf("gtmp", [128, 512], BF16)
    rows4 = b.sbuf("rows4", [1, 4 * 512], F32)
    zcol2 = b.sbuf("zcol2", [128, 8], F32)

    pA = b.psum("pA", [128, 6 * 512], F32)
    pool = Pool([pA[:, i * 512:(i + 1) * 512] for i in range(6)])

    xview = xb_in.rearrange("(c p) t -> p c t", p=128)
    oview = xb_out.rearrange("(c p) t -> p c t", p=128)

    for ti in range(8):
        t0 = ti * 512
        barrier = [mm.now(), ev.now(), es.now(), st.now(), ld.now()]
        r.y(lambda y, t0=t0: y.dma_start(out=xs[:, :].rearrange("p (c t) -> p c t", t=513),
                                         in_=xview[:, :, t0:t0 + 513]),
            wait=barrier, inc=ldx.inc(16))
        xw = ldx.now()
        for c in range(NCH):
            off = 1 if c < 4 else 0
            r.v(lambda v, c=c, off=off:
                v.tensor_copy(xsb[:, c * 512:(c + 1) * 512],
                              xs[:, c * 513 + off: c * 513 + off + 512]),
                wait=[xw, mm.now(), es.now()],
                inc=(ev.inc() if c == NCH - 1 else None))
        xbw = ev.now()
        iS, psS_, wS = pool.take()
        iQ, psQ_, wQ = pool.take()
        for c in range(NCH):
            r.v(lambda v, c=c: v.tensor_copy(xbs2[:, :], xsb[:, c * 512:(c + 1) * 512]),
                wait=[xbw, mm.now(), es.now()] + (wS + wQ if c == 0 else []), inc=ev.inc())
            r.s(lambda s: s.activation(sqs[:, :], xbs2[:, :], AF.Square),
                wait=[ev.now(), mm.now()], inc=es.inc())
            r.t(lambda t, c=c, p=psS_: MM(t, p[0:1, 0:512], C["ones_col"][:, :], xbs2[:, :],
                                          c == 0, c == NCH - 1), wait=[ev.now()])
            r.t(lambda t, c=c, p=psQ_: MM(t, p[0:1, 0:512], C["ones_col"][:, :], sqs[:, :],
                                          c == 0, c == NCH - 1), wait=[es.now()], inc=mm.inc())
        rstd, brow = layernorm_rows(b, rows4, psS_, psQ_, mm.now(), ev, es,
                                    eps_ap=C["cvals"][0:1, 1:2])
        pool.done(iS, ev.now())
        pool.done(iQ, ev.now())
        iA, psA_, wA = pool.take()
        iB, psB_, wB = pool.take()
        r.t(lambda t, p=psA_, q=rstd: MM(t, p[:, 0:512], C["ones_row"][:, :], q, True, True),
            wait=[ev.now()] + wA)
        r.t(lambda t, p=psB_, q=brow: MM(t, p[:, 0:512], C["ones_row"][:, :], q, True, True),
            wait=wB, inc=mm.inc())
        mw2 = mm.now()
        for c in range(NCH):
            off = 1 if c < 4 else 0
            hsl = xs[:, c * 513 + off: c * 513 + off + 512]
            r.v(lambda v, hsl=hsl, p=psA_: v.tensor_tensor(tmp32[:, :], hsl, p[:, 0:512],
                                                           ALU.mult), wait=[mw2])
            r.v(lambda v, p=psB_: v.tensor_tensor(tmp32[:, :], tmp32[:, :], p[:, 0:512],
                                                  ALU.add))
            r.v(lambda v, c=c: v.tensor_scalar(lnh[:, c * 512:(c + 1) * 512], tmp32[:, :],
                                               lgc[:, c:c + 1], lbc[:, c:c + 1],
                                               ALU.mult, ALU.add),
                inc=(ev.inc() if c == NCH - 1 else None))
        pool.done(iA, ev.now())
        pool.done(iB, ev.now())
        lnw = ev.now()
        for fc in range(NF1):
            iP, psP_, wP = pool.take()
            for dc in range(NCH):
                r.t(lambda t, dc=dc, fc=fc, p=psP_:
                    MM(t, p[:, 0:512], w1[dc][:, fc * 128:fc * 128 + 128],
                       lnh[:, dc * 512:(dc + 1) * 512], dc == 0, dc == NCH - 1),
                    wait=([lnw] + wP) if dc == 0 else [],
                    inc=(mm.inc() if dc == NCH - 1 else None))
            if fc < NF2:
                r.v(lambda v, fc=fc, p=psP_:
                    v.tensor_scalar(aslab[:, fc * 512:(fc + 1) * 512], p[:, 0:512],
                                    b1c[:, fc:fc + 1], None, ALU.add),
                    wait=[mm.now()], inc=ev.inc())
                pool.done(iP, ev.now())
            else:
                r.s(lambda s, fc=fc, p=psP_:
                    s.activation(gtmp[:, :], p[:, 0:512], AF.Gelu, bias=b1c[:, fc:fc + 1]),
                    wait=[mm.now(), ev.now()], inc=es.inc())
                pool.done(iP, es.now())
                g2 = fc - NF2
                r.v(lambda v, g2=g2:
                    v.tensor_tensor(aslab[:, g2 * 512:(g2 + 1) * 512],
                                    aslab[:, g2 * 512:(g2 + 1) * 512], gtmp[:, :], ALU.mult),
                    wait=[es.now()], inc=ev.inc())
        gw = ev.now()
        for dc in range(NCH):
            iP, psP_, wP = pool.take()
            for fc in range(NF2):
                r.t(lambda t, fc=fc, dc=dc, p=psP_:
                    MM(t, p[:, 0:512], w2[fc][:, dc * 128:dc * 128 + 128],
                       aslab[:, fc * 512:(fc + 1) * 512], fc == 0, fc == NF2 - 1),
                    wait=([gw] + wP) if fc == 0 else [],
                    inc=(mm.inc() if fc == NF2 - 1 else None))
            xsl = xs[:, dc * 513 + 1: dc * 513 + 1 + 512]
            r.v(lambda v, xsl=xsl, p=psP_: v.tensor_tensor(tmp32[:, :], xsl, p[:, 0:512],
                                                           ALU.add),
                wait=[mm.now()])
            r.v(lambda v, xsl=xsl, dc=dc: v.tensor_scalar(xsl, tmp32[:, :],
                                                          b2c[:, dc:dc + 1], None, ALU.add),
                inc=ev.inc())
            pool.done(iP, ev.now())
        r.y(lambda y, t0=t0: y.dma_start(
            out=oview[:, :, t0 + 1:t0 + 513],
            in_=xs[:, :].rearrange("p (c t) -> p c t", t=513)[:, :, 1:513]),
            wait=[ev.now()], inc=st.inc(16))
    r.v(lambda v: v.memset(zcol2[:, :], 0.0), inc=ev.inc())
    r.y(lambda y: y.dma_start(out=oview[:, :, 0:1],
                              in_=zcol2[:, :].rearrange("p (c z) -> p c z", z=1),
                              allow_slow_non_contiguous=True),
        wait=[ev.now()], inc=st.inc(16))
    r.y(lambda y: y.nop(), wait=[st.now()])
    return b.finish()


def build_add_pass(a_percore, b_percore, b_pad=0, name="addpass"):
    b_ = Bld(name)
    r = b_.r
    a_in = b_.din("a", a_percore)
    bb_in = b_.din("b", b_percore)
    y_out = b_.dout("y", [D, T], F32)
    ldx = b_.sem()
    ev = b_.sem()
    st = b_.sem()
    at = b_.sbuf("at", [128, 8 * 512], F32)
    bt = b_.sbuf("bt", [128, 8 * 512], F32)
    aview = a_in.rearrange("(c p) t -> p c t", p=128)
    bview = bb_in.rearrange("(c p) t -> p c t", p=128)
    yview = y_out.rearrange("(c p) t -> p c t", p=128)
    for ti in range(8):
        t0 = ti * 512
        r.y(lambda y, t0=t0: y.dma_start(out=at[:, :].rearrange("p (c t) -> p c t", t=512),
                                         in_=aview[:, :, t0:t0 + 512]),
            wait=[ev.now(), st.now()], inc=ldx.inc(16))
        r.y(lambda y, t0=t0: y.dma_start(out=bt[:, :].rearrange("p (c t) -> p c t", t=512),
                                         in_=bview[:, :, b_pad + t0:b_pad + t0 + 512]),
            inc=ldx.inc(16))
        r.v(lambda v: v.tensor_tensor(at[:, :], at[:, :], bt[:, :], ALU.add),
            wait=[ldx.now()], inc=ev.inc())
        r.y(lambda y, t0=t0: y.dma_start(out=yview[:, :, t0:t0 + 512],
                                         in_=at[:, :].rearrange("p (c t) -> p c t", t=512)),
            wait=[ev.now()], inc=st.inc(16))
    r.y(lambda y: y.nop(), wait=[st.now()])
    return b_.finish()


def _hsel_const():
    a = np.zeros((64, 16 * 16), np.float32)
    for h in range(16):
        a[:, h * 16 + h] = 1.0
    return a


def _hsel_const():
    a = np.zeros((64, 16 * 16), np.float32)
    for h in range(16):
        a[:, h * 16 + h] = 1.0
    return a


def build_C_attn(h_percore, ln1w_col, ln1b_col, qwT_sb, qb_col, kwT_sb, kb_col,
                 vwT_sb, vb_flat, name="c_attn"):
    b = Bld(name)
    r = b.r
    h_in = b.din("h", h_percore)
    qw_in = b.din("qw", np.ascontiguousarray(qwT_sb))
    kw_in = b.din("kw", np.ascontiguousarray(kwT_sb))
    vw_in = b.din("vw", np.ascontiguousarray(vwT_sb))
    l1w_in = b.din("l1w", np.ascontiguousarray(ln1w_col))
    l1b_in = b.din("l1b", np.ascontiguousarray(ln1b_col))
    qb_in = b.din("qbc", np.ascontiguousarray(qb_col))
    kb_in = b.din("kbc", np.ascontiguousarray(kb_col))
    vb_in = b.din("vbf", np.ascontiguousarray(vb_flat.reshape(1, 1040)))
    hs_in = b.din("hsel", _hsel_const())
    h_out = b.dout("h2", [D, T], F32)

    ld = b.sem()
    C = reg_consts(b, [("ones_row", F32), ("ones_row_b", BF16), ("ones_col", BF16),
                       ("ident", BF16), ("cvals", F32)], ld)
    qw = b.sbuf("qw_s", [64, 1024], BF16)
    kw = b.sbuf("kw_s", [64, 1024], BF16)
    vw = b.sbuf("vw_s", [64, 1024], BF16)
    hsel = b.sbuf("hsel_s", [64, 256], BF16)
    cols = b.sbuf("cols_s", [128, 4 * 8], F32)
    l1wc, l1bc, qbc, kbc = (cols[:, 0:8], cols[:, 8:16], cols[:, 16:24], cols[:, 24:32])
    vbb = b.sbuf("vbb", [128, 1040], F32)
    r.g(lambda g: g.dma_start(out=qw[:, :], in_=qw_in[:, :]), inc=ld.inc(16))
    r.g(lambda g: g.dma_start(out=kw[:, :], in_=kw_in[:, :]), inc=ld.inc(16))
    r.g(lambda g: g.dma_start(out=vw[:, :], in_=vw_in[:, :]), inc=ld.inc(16))
    r.g(lambda g: g.dma_start(out=hsel[:, :], in_=hs_in[:, :]), inc=ld.inc(16))
    r.g(lambda g: g.dma_start(out=l1wc, in_=l1w_in[:, :]), inc=ld.inc(16))
    r.g(lambda g: g.dma_start(out=l1bc, in_=l1b_in[:, :]), inc=ld.inc(16))
    r.g(lambda g: g.dma_start(out=qbc, in_=qb_in[:, :]), inc=ld.inc(16))
    r.g(lambda g: g.dma_start(out=kbc, in_=kb_in[:, :]), inc=ld.inc(16))
    r.g(lambda g: g.dma_start(out=vbb[:, :], in_=vb_in.broadcast_to((128, 1040))),
        inc=ld.inc(16))

    ldx = b.sem()
    mm = b.sem()
    ev = b.sem()
    es = b.sem()
    st = b.sem()

    Ks = b.sbuf("Ks", [128, 8 * T], BF16)
    Vs = b.sbuf("Vs", [128, 32 * 1040], BF16)
    xs = b.sbuf("xs", [128, 8 * 512], F32)
    lnx = b.sbuf("lnx", [128, 8 * 512], BF16)
    lstg = b.sbuf("lstg", [64, 512], BF16)
    xbs = b.sbuf("xbs", [128, 512], BF16)
    sqs = b.sbuf("sqs", [128, 512], BF16)
    tmp32 = b.sbuf("tmp32", [128, 512], F32)
    tmp32c = b.sbuf("tmp32c", [128, 512], F32)
    rows4 = b.sbuf("rows4", [1, 4 * 512], F32)
    arowst = b.sbuf("arowst", [1, 1024], F32)
    arows_d = b.dram("arows_d", [8192], F32)
    sqh = b.sbuf("sqh", [64, 512], BF16)
    qsb = b.sbuf("qsb", [128, 8 * 512], BF16)
    mk2 = b.sbuf("mk2", [16, 1], F32)
    mcol = b.sbuf("mcol", [16, 4], F32)
    mbf = b.sbuf("mbf", [16, 1], BF16)
    mrow = b.sbuf("mrow", [1, 16], BF16)
    negM = b.sbuf("negM", [128, 16], F32)
    exp2 = [b.sbuf("exp2a", [128, 1024], BF16), b.sbuf("exp2b", [128, 1024], BF16)]
    rden = b.sbuf("rden", [1, 512], F32)
    rb = b.sbuf("rb", [64, 512], F32)

    pA = b.psum("pA", [128, 6 * 512], F32)
    pT = b.psum("pT", [128, 1024], BF16)
    pool = Pool([pA[:, i * 512:(i + 1) * 512] for i in range(4)])
    poolO = Pool([pA[:, i * 512:(i + 1) * 512] for i in range(4, 6)])

    hview = h_in.rearrange("(c p) t -> p c t", p=128)
    oview = h_out.rearrange("(c p) t -> p c t", p=128)

    r.v(lambda v: v.memset(mk2[:, :], 0.0), inc=ev.inc())

    def stream_ln(ti, save_rows):
        t0 = ti * 512
        barrier = [mm.now(), ev.now(), es.now(), ld.now(), st.now()]
        r.y(lambda y, t0=t0: y.dma_start(out=xs[:, :].rearrange("p (c t) -> p c t", t=512),
                                         in_=hview[:, :, t0:t0 + 512]),
            wait=barrier, inc=ldx.inc(16))
        xw = ldx.now()
        if save_rows:
            iS, psS_, wS = pool.take()
            iQ, psQ_, wQ = pool.take()
            for c in range(NCH):
                r.v(lambda v, c=c: v.tensor_copy(xbs[:, :], xs[:, c * 512:(c + 1) * 512]),
                    wait=[xw, mm.now(), es.now()] + (wS + wQ if c == 0 else []), inc=ev.inc())
                r.s(lambda s: s.activation(sqs[:, :], xbs[:, :], AF.Square),
                    wait=[ev.now(), mm.now()], inc=es.inc())
                r.t(lambda t, c=c, p=psS_: MM(t, p[0:1, 0:512], C["ones_col"][:, :], xbs[:, :],
                                              c == 0, c == NCH - 1), wait=[ev.now()])
                r.t(lambda t, c=c, p=psQ_: MM(t, p[0:1, 0:512], C["ones_col"][:, :], sqs[:, :],
                                              c == 0, c == NCH - 1), wait=[es.now()],
                    inc=mm.inc())
            rstd, brow = layernorm_rows(b, rows4, psS_, psQ_, mm.now(), ev, es,
                                        eps_ap=C["cvals"][0:1, 1:2])
            r.y(lambda y, ti=ti, q=rstd: y.dma_start(out=arows_d[ti * 1024:ti * 1024 + 512][None, :],
                                                     in_=q),
                wait=[ev.now()], inc=st.inc(16))
            r.y(lambda y, ti=ti, q=brow: y.dma_start(
                out=arows_d[ti * 1024 + 512:ti * 1024 + 1024][None, :], in_=q),
                inc=st.inc(16))
            pool.done(iS, ev.now())
            pool.done(iQ, ev.now())
        else:
            r.y(lambda y, ti=ti: y.dma_start(out=arowst[:, :],
                                             in_=arows_d[ti * 1024:(ti + 1) * 1024][None, :]),
                wait=[ev.now(), mm.now()], inc=ldx.inc(16))
            rstd = arowst[0:1, 0:512]
            brow = arowst[0:1, 512:1024]
        iA, psA_, wA = pool.take()
        iB, psB_, wB = pool.take()
        r.t(lambda t, p=psA_, q=rstd: MM(t, p[:, 0:512], C["ones_row"][:, :], q, True, True),
            wait=[ev.now(), ldx.now()] + wA)
        r.t(lambda t, p=psB_, q=brow: MM(t, p[:, 0:512], C["ones_row"][:, :], q, True, True),
            wait=wB, inc=mm.inc())
        mw2 = mm.now()
        for c in range(NCH):
            r.v(lambda v, c=c, p=psA_: v.tensor_tensor(tmp32[:, :], xs[:, c * 512:(c + 1) * 512],
                                                       p[:, 0:512], ALU.mult), wait=[mw2])
            r.v(lambda v, c=c, p=psB_: v.tensor_tensor(tmp32[:, :], tmp32[:, :], p[:, 0:512],
                                                       ALU.add))
            r.v(lambda v, c=c: v.tensor_scalar(lnx[:, c * 512:(c + 1) * 512], tmp32[:, :],
                                               l1wc[:, c:c + 1], l1bc[:, c:c + 1],
                                               ALU.mult, ALU.add),
                inc=(ev.inc() if c == NCH - 1 else None))
        pool.done(iA, ev.now())
        pool.done(iB, ev.now())
        return ev.now()

    for ti in range(8):
        t0 = ti * 512
        lnw = stream_ln(ti, save_rows=True)
        iMK, psMK, wMK = poolO.take()
        for h in range(16):
            r0, hp = (h % 2) * 64, h // 2
            lsl = lnx[r0:r0 + 64, hp * 512: hp * 512 + 512]
            if h % 2 == 1:
                r.v(lambda v, lsl=lsl: v.tensor_copy(lstg[:, :], lsl),
                    wait=[lnw, mm.now()], inc=ev.inc())
                lop = lstg[:, :]
                ldep = [ev.now()]
            else:
                lop = lsl
                ldep = [lnw]
            iP, psP_, wP = pool.take()
            r.t(lambda t, h=h, q=lop, p=psP_: MM(t, p[0:64, 0:512], kw[:, h * 64:h * 64 + 64],
                                                 q, True, True),
                wait=ldep + wP, inc=mm.inc())
            ksl = Ks[r0:r0 + 64, hp * T + t0: hp * T + t0 + 512]
            r.v(lambda v, h=h, r0=r0, ksl=ksl, p=psP_:
                v.tensor_scalar(ksl, p[0:64, 0:512], kbc[r0:r0 + 64, hp_col(h)], None, ALU.add),
                wait=[mm.now()], inc=ev.inc())
            pool.done(iP, ev.now())
            r.v(lambda v, ksl=ksl: v.tensor_tensor(sqh[:, :], ksl, ksl, ALU.mult),
                wait=[ev.now()], inc=ev.inc())
            r.t(lambda t, h=h, p=psMK: MM(t, p[0:16, 0:512], hsel[:, h * 16:h * 16 + 16],
                                          sqh[:, :], h == 0, h == 15),
                wait=[ev.now()] + (wMK if h == 0 else []), inc=mm.inc())
            for sub in range(4):
                iV, psV_, wV = pool.take()
                r.t(lambda t, h=h, sub=sub, q=lop, p=psV_:
                    MM(t, p[:, 0:64], q[:, sub * 128:sub * 128 + 128],
                       vw[:, h * 64:h * 64 + 64], True, True),
                    wait=ldep + wV, inc=mm.inc())
                voff = (ti * 4 + sub) * 1040 + h * 65
                r.v(lambda v, voff=voff, h=h, p=psV_:
                    v.tensor_tensor(Vs[:, voff:voff + 64], p[:, 0:64],
                                    vbb[:, h * 65:h * 65 + 64], ALU.add),
                    wait=[mm.now()], inc=ev.inc())
                pool.done(iV, ev.now())
        r.v(lambda v, p=psMK: v.reduce_max(mcol[:, 0:1], p[0:16, 0:512], axis=AX),
            wait=[mm.now()], inc=ev.inc())
        r.v(lambda v: v.tensor_tensor(mk2[:, :], mk2[:, :], mcol[:, 0:1], ALU.max),
            wait=[ev.now()], inc=ev.inc())
        poolO.done(iMK, ev.now())
        for sub in range(4):
            voff = (ti * 4 + sub) * 1040
            r.v(lambda v, voff=voff:
                v.memset(Vs[:, voff:voff + 1040].rearrange("p (h e) -> p h e", e=65)[:, :, 64:65],
                         1.0), inc=ev.inc())

    for ti in range(8):
        t0 = ti * 512
        lnw = stream_ln(ti, save_rows=False)
        iNQ, psNQ, wNQ = poolO.take()
        for h in range(16):
            r0, hp = (h % 2) * 64, h // 2
            lsl = lnx[r0:r0 + 64, hp * 512: hp * 512 + 512]
            if h % 2 == 1:
                r.v(lambda v, lsl=lsl: v.tensor_copy(lstg[:, :], lsl),
                    wait=[lnw, mm.now()], inc=ev.inc())
                lop = lstg[:, :]
                ldep = [ev.now()]
            else:
                lop = lsl
                ldep = [lnw]
            iP, psP_, wP = pool.take()
            r.t(lambda t, h=h, q=lop, p=psP_: MM(t, p[0:64, 0:512], qw[:, h * 64:h * 64 + 64],
                                                 q, True, True),
                wait=ldep + wP, inc=mm.inc())
            qdst = qsb[r0:r0 + 64, hp * 512: hp * 512 + 512]
            r.v(lambda v, h=h, r0=r0, qdst=qdst, p=psP_:
                v.tensor_scalar(qdst, p[0:64, 0:512], qbc[r0:r0 + 64, hp_col(h)], None, ALU.add),
                wait=[mm.now()], inc=ev.inc())
            pool.done(iP, ev.now())
            r.v(lambda v, qdst=qdst: v.tensor_tensor(sqh[:, :], qdst, qdst, ALU.mult),
                wait=[ev.now()], inc=ev.inc())
            r.t(lambda t, h=h, p=psNQ: MM(t, p[0:16, 0:512], hsel[:, h * 16:h * 16 + 16],
                                          sqh[:, :], h == 0, h == 15),
                wait=[ev.now()] + (wNQ if h == 0 else []), inc=mm.inc())
        r.v(lambda v, p=psNQ: v.reduce_max(mcol[:, 1:2], p[0:16, 0:512], axis=AX),
            wait=[mm.now()], inc=ev.inc())
        poolO.done(iNQ, ev.now())
        r.v(lambda v: v.tensor_tensor(mcol[:, 2:3], mcol[:, 1:2], mk2[:, 0:1], ALU.mult),
            wait=[ev.now()], inc=ev.inc())
        r.v(lambda v: v.tensor_scalar(mcol[:, 3:4], mcol[:, 2:3], 1e-30, None, ALU.max),
            wait=[ev.now()], inc=ev.inc())
        r.s(lambda s: s.activation(mcol[:, 2:3], mcol[:, 3:4], AF.Ln), wait=[ev.now()],
            inc=es.inc())
        r.s(lambda s: s.activation(mcol[:, 3:4], mcol[:, 2:3], AF.Exp, scale=0.5),
            inc=es.inc())
        r.v(lambda v: v.tensor_scalar(mbf[:, :], mcol[:, 3:4], 0.133, None, ALU.mult),
            wait=[es.now()], inc=ev.inc())
        iT, ptr_, wT = Pool([pT[:, 0:16]]).take()
        r.t(lambda t: t.transpose(pT[0:1, 0:16], mbf[:, :], C["ident"][0:16, 0:16]),
            wait=[ev.now()], inc=mm.inc())
        r.v(lambda v: v.tensor_copy(mrow[:, :], pT[0:1, 0:16]), wait=[mm.now()], inc=ev.inc())
        iM, psM_, wM = pool.take()
        r.t(lambda t, p=psM_: MM(t, p[:, 0:16], C["ones_row_b"][:, :], mrow[:, :], True, True),
            wait=[ev.now()] + wM, inc=mm.inc())
        r.v(lambda v, p=psM_: v.tensor_scalar(negM[:, :], p[:, 0:16], -1.0, None, ALU.mult),
            wait=[mm.now()], inc=ev.inc())
        pool.done(iM, ev.now())
        ndep = ev.now()
        pv_war = ev.now()
        for h in range(16):
            r0, hp = (h % 2) * 64, h // 2
            qop = qsb[r0:r0 + 64, hp * 512: hp * 512 + 512]
            psO_ = poolO.tiles[0]
            es_exp = {}
            mm_pv = {}

            def rec_pv(g, h=h, psO_=psO_):
                for j in range(2):
                    tc = 2 * g + j
                    r.t(lambda t, tc=tc, h=h, g=g, j=j, p=psO_:
                        MM(t, p[0:65, 0:512],
                           Vs[:, tc * 1040 + h * 65: tc * 1040 + h * 65 + 65],
                           exp2[g % 2][:, j * 512:(j + 1) * 512],
                           (g == 0 and j == 0), (g == 15 and j == 1)),
                        wait=([es_exp[g]] + ([pv_war] if (g == 0 and j == 0) else [])),
                        inc=(mm.inc() if j == 1 else None))
                mm_pv[g] = mm.now()

            for g in range(16):
                half = g % 2
                for j in range(2):
                    tc = 2 * g + j
                    bank = pool.tiles[half * 2 + j]
                    w = [ndep] if g < 2 else [es_exp[g - 2]]
                    r.t(lambda t, h=h, r0=r0, hp=hp, tc=tc, qop=qop, p=bank:
                        MM(t, p[:, 0:512],
                           Ks[r0:r0 + 64, hp * T + tc * 128: hp * T + tc * 128 + 128],
                           qop, True, True),
                        wait=w, inc=(mm.inc() if j == 1 else None))
                w = [mm.now()]
                if g >= 2:
                    w.append(mm_pv[g - 2])
                r.s(lambda s, h=h, half=half: s.activation(
                    exp2[half][:, :], pA[:, half * 1024:half * 1024 + 1024], AF.Exp,
                    scale=0.125, bias=negM[:, h:h + 1]),
                    wait=w, inc=es.inc())
                es_exp[g] = es.now()
                if g >= 1:
                    rec_pv(g - 1)
            rec_pv(15)
            r.v(lambda v, p=psO_: v.reciprocal(rden[:, :], p[64:65, 0:512]),
                wait=[mm.now()], inc=ev.inc())
            psR_ = poolO.tiles[1]
            r.t(lambda t, p=psR_: MM(t, p[0:64, 0:512], C["ones_row"][0:1, 0:64], rden[:, :],
                                     True, True),
                wait=[ev.now()], inc=mm.inc())
            r.v(lambda v, p=psR_: v.tensor_copy(rb[:, :], p[0:64, 0:512]),
                wait=[mm.now()], inc=ev.inc())
            r.v(lambda v, r0=r0, p=psO_: v.tensor_tensor(tmp32c[r0:r0 + 64, :], p[0:64, 0:512],
                                                         rb[:, :], ALU.mult),
                wait=[ev.now()], inc=ev.inc())
            pv_war = ev.now()
            xsl = xs[r0:r0 + 64, hp * 512: hp * 512 + 512]
            r.v(lambda v, xsl=xsl, r0=r0: v.tensor_tensor(xsl, xsl, tmp32c[r0:r0 + 64, :],
                                                          ALU.add),
                wait=[ev.now()], inc=ev.inc())
        r.y(lambda y, t0=t0: y.dma_start(out=oview[:, :, t0:t0 + 512],
                                         in_=xs[:, :].rearrange("p (c t) -> p c t", t=512)),
            wait=[ev.now()], inc=st.inc(16))
    r.y(lambda y: y.nop(), wait=[st.now()])
    return b.finish()


def hp_col(h):
    return slice(h // 2, h // 2 + 1)


def build_C_ffn(h_percore, ln2w_col, ln2b_col, w1_l, b1_col, w2_l, b2_col,
                name="c_ffn"):
    NF = 32
    b = Bld(name)
    r = b.r
    h_in = b.din("h", h_percore)
    w1_in = b.din("w1", np.ascontiguousarray(w1_l))
    w2_in = b.din("w2", np.ascontiguousarray(w2_l))
    l2w_in = b.din("l2w", np.ascontiguousarray(ln2w_col))
    l2b_in = b.din("l2b", np.ascontiguousarray(ln2b_col))
    b1_in = b.din("b1c", np.ascontiguousarray(b1_col))
    b2_in = b.din("b2c", np.ascontiguousarray(b2_col))
    h_out = b.dout("h2", [D, T], F32)

    ld = b.sem()
    C = reg_consts(b, [("ones_row", F32), ("ones_col", BF16), ("cvals", F32)], ld)
    w1 = [b.sbuf("w1_%d" % i, [128, 4096], BF16) for i in range(NCH)]
    for i in range(NCH):
        r.g(lambda g, t=w1[i], i=i: g.dma_start(out=t[:, :], in_=w1_in[i * 128:(i + 1) * 128, :]),
            inc=ld.inc(16))
    w2 = [b.sbuf("w2_%d" % i, [128, 1024], BF16) for i in range(NF)]
    for i in range(NF):
        r.g(lambda g, t=w2[i], i=i: g.dma_start(out=t[:, :], in_=w2_in[i * 128:(i + 1) * 128, :]),
            inc=ld.inc(16))
    cols = b.sbuf("cols", [128, 32 + 3 * 8], F32)
    b1c, b2c, l2wc, l2bc = (cols[:, 0:32], cols[:, 32:40], cols[:, 40:48], cols[:, 48:56])
    r.g(lambda g: g.dma_start(out=b1c, in_=b1_in[:, :]), inc=ld.inc(16))
    r.g(lambda g: g.dma_start(out=b2c, in_=b2_in[:, :]), inc=ld.inc(16))
    r.g(lambda g: g.dma_start(out=l2wc, in_=l2w_in[:, :]), inc=ld.inc(16))
    r.g(lambda g: g.dma_start(out=l2bc, in_=l2b_in[:, :]), inc=ld.inc(16))

    ldx = b.sem()
    mm = b.sem()
    ev = b.sem()
    es = b.sem()
    st = b.sem()

    xs = b.sbuf("xs", [128, 8 * 512], F32)
    lnx = b.sbuf("lnx", [128, 8 * 512], BF16)
    h1 = b.sbuf("h1", [128, NF * 512], BF16)
    xbs = b.sbuf("xbs", [128, 512], BF16)
    sqs = b.sbuf("sqs", [128, 512], BF16)
    tmp32 = b.sbuf("tmp32", [128, 512], F32)
    rows4 = b.sbuf("rows4", [1, 4 * 512], F32)

    pA = b.psum("pA", [128, 6 * 512], F32)
    pool = Pool([pA[:, i * 512:(i + 1) * 512] for i in range(6)])

    hview = h_in.rearrange("(c p) t -> p c t", p=128)
    oview = h_out.rearrange("(c p) t -> p c t", p=128)

    for ti in range(8):
        t0 = ti * 512
        barrier = [mm.now(), ev.now(), es.now(), ld.now(), st.now()]
        r.y(lambda y, t0=t0: y.dma_start(out=xs[:, :].rearrange("p (c t) -> p c t", t=512),
                                         in_=hview[:, :, t0:t0 + 512]),
            wait=barrier, inc=ldx.inc(16))
        xw = ldx.now()
        iS, psS_, wS = pool.take()
        iQ, psQ_, wQ = pool.take()
        for c in range(NCH):
            r.v(lambda v, c=c: v.tensor_copy(xbs[:, :], xs[:, c * 512:(c + 1) * 512]),
                wait=[xw, mm.now(), es.now()] + (wS + wQ if c == 0 else []), inc=ev.inc())
            r.s(lambda s: s.activation(sqs[:, :], xbs[:, :], AF.Square),
                wait=[ev.now(), mm.now()], inc=es.inc())
            r.t(lambda t, c=c, p=psS_: MM(t, p[0:1, 0:512], C["ones_col"][:, :], xbs[:, :],
                                          c == 0, c == NCH - 1), wait=[ev.now()])
            r.t(lambda t, c=c, p=psQ_: MM(t, p[0:1, 0:512], C["ones_col"][:, :], sqs[:, :],
                                          c == 0, c == NCH - 1), wait=[es.now()], inc=mm.inc())
        rstd, brow = layernorm_rows(b, rows4, psS_, psQ_, mm.now(), ev, es,
                                    eps_ap=C["cvals"][0:1, 1:2])
        pool.done(iS, ev.now())
        pool.done(iQ, ev.now())
        iA, psA_, wA = pool.take()
        iB, psB_, wB = pool.take()
        r.t(lambda t, p=psA_, q=rstd: MM(t, p[:, 0:512], C["ones_row"][:, :], q, True, True),
            wait=[ev.now()] + wA)
        r.t(lambda t, p=psB_, q=brow: MM(t, p[:, 0:512], C["ones_row"][:, :], q, True, True),
            wait=wB, inc=mm.inc())
        mw2 = mm.now()
        for c in range(NCH):
            r.v(lambda v, c=c, p=psA_: v.tensor_tensor(tmp32[:, :], xs[:, c * 512:(c + 1) * 512],
                                                       p[:, 0:512], ALU.mult), wait=[mw2])
            r.v(lambda v, c=c, p=psB_: v.tensor_tensor(tmp32[:, :], tmp32[:, :], p[:, 0:512],
                                                       ALU.add))
            r.v(lambda v, c=c: v.tensor_scalar(lnx[:, c * 512:(c + 1) * 512], tmp32[:, :],
                                               l2wc[:, c:c + 1], l2bc[:, c:c + 1],
                                               ALU.mult, ALU.add),
                inc=(ev.inc() if c == NCH - 1 else None))
        pool.done(iA, ev.now())
        pool.done(iB, ev.now())
        lnw = ev.now()
        for fc in range(NF):
            iP, psP_, wP = pool.take()
            for dc in range(NCH):
                r.t(lambda t, dc=dc, fc=fc, p=psP_:
                    MM(t, p[:, 0:512], w1[dc][:, fc * 128:fc * 128 + 128],
                       lnx[:, dc * 512:(dc + 1) * 512], dc == 0, dc == NCH - 1),
                    wait=([lnw] + wP) if dc == 0 else [],
                    inc=(mm.inc() if dc == NCH - 1 else None))
            r.s(lambda s, fc=fc, p=psP_:
                s.activation(h1[:, fc * 512:(fc + 1) * 512], p[:, 0:512], AF.Gelu,
                             bias=b1c[:, fc:fc + 1]),
                wait=[mm.now()], inc=es.inc())
            pool.done(iP, es.now())
        gw = es.now()
        for dc in range(NCH):
            iP, psP_, wP = pool.take()
            for fc in range(NF):
                r.t(lambda t, fc=fc, dc=dc, p=psP_:
                    MM(t, p[:, 0:512], w2[fc][:, dc * 128:dc * 128 + 128],
                       h1[:, fc * 512:(fc + 1) * 512], fc == 0, fc == NF - 1),
                    wait=([gw] + wP) if fc == 0 else [],
                    inc=(mm.inc() if fc == NF - 1 else None))
            xsl = xs[:, dc * 512:(dc + 1) * 512]
            r.v(lambda v, xsl=xsl, p=psP_: v.tensor_tensor(xsl, xsl, p[:, 0:512], ALU.add),
                wait=[mm.now()])
            r.v(lambda v, xsl=xsl, dc=dc: v.tensor_scalar(xsl, xsl, b2c[:, dc:dc + 1],
                                                          None, ALU.add),
                inc=ev.inc())
            pool.done(iP, ev.now())
        r.y(lambda y, t0=t0: y.dma_start(out=oview[:, :, t0:t0 + 512],
                                         in_=xs[:, :].rearrange("p (c t) -> p c t", t=512)),
            wait=[ev.now()], inc=st.inc(16))
    r.y(lambda y: y.nop(), wait=[st.now()])
    return b.finish()


def bD_masks01():
    m = np.zeros((4, 128, 512), np.float32)
    tt = np.arange(128)[:, None]
    ss = np.arange(512)[None, :]
    for j in range(4):
        m[j] = (tt <= ss - 128 * j).astype(np.float32)
    return np.ascontiguousarray(m.transpose(1, 0, 2).reshape(128, 4 * 512))


def _ln_gain_stream(b, C, r, sems, bufs, hview, ti, wb_cols, save, arows_d):
    ld, ldx, mm, ev, es, st = sems
    xs, lnx, xbs, sqs, tmp32, rows4, arowst, pool = bufs
    lwc, lbc = wb_cols
    t0 = ti * 512
    barrier = [mm.now(), ev.now(), es.now(), ld.now(), st.now()]
    r.y(lambda y, t0=t0: y.dma_start(out=xs[:, :].rearrange("p (c t) -> p c t", t=512),
                                     in_=hview[:, :, t0:t0 + 512]),
        wait=barrier, inc=ldx.inc(16))
    xw = ldx.now()
    if save:
        iS, psS_, wS = pool.take()
        iQ, psQ_, wQ = pool.take()
        for c in range(NCH):
            r.v(lambda v, c=c: v.tensor_copy(xbs[:, :], xs[:, c * 512:(c + 1) * 512]),
                wait=[xw, mm.now(), es.now()] + (wS + wQ if c == 0 else []), inc=ev.inc())
            r.s(lambda s: s.activation(sqs[:, :], xbs[:, :], AF.Square),
                wait=[ev.now(), mm.now()], inc=es.inc())
            r.t(lambda t, c=c, p=psS_: MM(t, p[0:1, 0:512], C["ones_col"][:, :], xbs[:, :],
                                          c == 0, c == NCH - 1), wait=[ev.now()])
            r.t(lambda t, c=c, p=psQ_: MM(t, p[0:1, 0:512], C["ones_col"][:, :], sqs[:, :],
                                          c == 0, c == NCH - 1), wait=[es.now()], inc=mm.inc())
        rstd, brow = layernorm_rows(b, rows4, psS_, psQ_, mm.now(), ev, es,
                                    eps_ap=C["cvals"][0:1, 1:2])
        if arows_d is not None:
            r.y(lambda y, ti=ti, q=rstd: y.dma_start(
                out=arows_d[ti * 1024:ti * 1024 + 512][None, :], in_=q),
                wait=[ev.now()], inc=st.inc(16))
            r.y(lambda y, ti=ti, q=brow: y.dma_start(
                out=arows_d[ti * 1024 + 512:ti * 1024 + 1024][None, :], in_=q),
                inc=st.inc(16))
        pool.done(iS, ev.now())
        pool.done(iQ, ev.now())
    else:
        r.y(lambda y, ti=ti: y.dma_start(out=arowst[:, :],
                                         in_=arows_d[ti * 1024:(ti + 1) * 1024][None, :]),
            wait=[ev.now(), mm.now()], inc=ldx.inc(16))
        rstd = arowst[0:1, 0:512]
        brow = arowst[0:1, 512:1024]
    iA, psA_, wA = pool.take()
    iB, psB_, wB = pool.take()
    r.t(lambda t, p=psA_, q=rstd: MM(t, p[:, 0:512], C["ones_row"][:, :], q, True, True),
        wait=[ev.now(), ldx.now()] + wA)
    r.t(lambda t, p=psB_, q=brow: MM(t, p[:, 0:512], C["ones_row"][:, :], q, True, True),
        wait=wB, inc=mm.inc())
    mw2 = mm.now()
    for c in range(NCH):
        r.v(lambda v, c=c, p=psA_: v.tensor_tensor(tmp32[:, :], xs[:, c * 512:(c + 1) * 512],
                                                   p[:, 0:512], ALU.mult), wait=[mw2])
        r.v(lambda v, c=c, p=psB_: v.tensor_tensor(tmp32[:, :], tmp32[:, :], p[:, 0:512],
                                                   ALU.add))
        if lwc is not None:
            r.v(lambda v, c=c: v.tensor_scalar(lnx[:, c * 512:(c + 1) * 512], tmp32[:, :],
                                               lwc[:, c:c + 1], lbc[:, c:c + 1],
                                               ALU.mult, ALU.add),
                inc=(ev.inc() if c == NCH - 1 else None))
        else:
            r.v(lambda v, c=c: v.tensor_copy(lnx[:, c * 512:(c + 1) * 512], tmp32[:, :]),
                inc=(ev.inc() if c == NCH - 1 else None))
    pool.done(iA, ev.now())
    pool.done(iB, ev.now())
    return ev.now()


def build_D_kv(x_percore, Wqkv, bqkv, ln1w_col, ln1b_col, name="d_kv"):
    b = Bld(name)
    r = b.r
    x_in = b.din("x", x_percore)
    wq_in = b.din("wqkv", np.ascontiguousarray(Wqkv))
    kcol = np.ascontiguousarray(bqkv[1024:2048].reshape(8, 128).T)
    kb_in = b.din("kbc", kcol)
    vbf = np.zeros(1040, np.float32)
    for h in range(16):
        vbf[h * 65:h * 65 + 64] = bqkv[2048 + h * 64: 2048 + h * 64 + 64]
    vb_in = b.din("vbf", np.ascontiguousarray(vbf.reshape(1, 1040)))
    l1w_in = b.din("l1w", np.ascontiguousarray(ln1w_col))
    l1b_in = b.din("l1b", np.ascontiguousarray(ln1b_col))
    hs_in = b.din("hsel", _hsel_const())
    K_out = b.dout("K", [128, 8 * T], BF16)
    V_out = b.dout("V", [128, 32 * 1040], BF16)
    ar_out = b.dout("arows", [8192], F32)
    mk_out = b.dout("mk2", [16, 1], F32)

    ld = b.sem()
    C = reg_consts(b, [("ones_row", F32), ("ones_col", BF16), ("cvals", F32)], ld)
    wq = [b.sbuf("wq%d" % i, [128, 2048], BF16) for i in range(NCH)]
    for i in range(NCH):
        r.g(lambda g, t=wq[i], i=i: g.dma_start(out=t[:, :],
                                                in_=wq_in[i * 128:(i + 1) * 128, 1024:3072]),
            inc=ld.inc(16))
    hsel = b.sbuf("hsel_s", [64, 256], BF16)
    cols = b.sbuf("cols_s", [128, 3 * 8], F32)
    kbc, l1wc, l1bc = cols[:, 0:8], cols[:, 8:16], cols[:, 16:24]
    vbb = b.sbuf("vbb", [128, 1040], F32)
    r.g(lambda g: g.dma_start(out=hsel[:, :], in_=hs_in[:, :]), inc=ld.inc(16))
    r.g(lambda g: g.dma_start(out=kbc, in_=kb_in[:, :]), inc=ld.inc(16))
    r.g(lambda g: g.dma_start(out=l1wc, in_=l1w_in[:, :]), inc=ld.inc(16))
    r.g(lambda g: g.dma_start(out=l1bc, in_=l1b_in[:, :]), inc=ld.inc(16))
    r.g(lambda g: g.dma_start(out=vbb[:, :], in_=vb_in.broadcast_to((128, 1040))),
        inc=ld.inc(16))

    ldx = b.sem()
    mm = b.sem()
    ev = b.sem()
    es = b.sem()
    st = b.sem()

    xs = b.sbuf("xs", [128, 8 * 512], F32)
    lnx = b.sbuf("lnx", [128, 8 * 512], BF16)
    xbs = b.sbuf("xbs", [128, 512], BF16)
    sqs = b.sbuf("sqs", [128, 512], BF16)
    tmp32 = b.sbuf("tmp32", [128, 512], F32)
    rows4 = b.sbuf("rows4", [1, 4 * 512], F32)
    arowst = b.sbuf("arowst", [1, 1024], F32)
    Kt = b.sbuf("Kt", [128, 8 * 512], BF16)
    Vt = b.sbuf("Vt", [128, 4 * 1040], BF16)
    sqh = b.sbuf("sqh", [64, 512], BF16)
    mk2 = b.sbuf("mk2_s", [16, 1], F32)
    mcol = b.sbuf("mcol", [16, 4], F32)

    pA = b.psum("pA", [128, 6 * 512], F32)
    pool = Pool([pA[:, i * 512:(i + 1) * 512] for i in range(4)])
    poolO = Pool([pA[:, i * 512:(i + 1) * 512] for i in range(4, 6)])

    xview = x_in.rearrange("(c p) t -> p c t", p=128)
    arows_d = ar_out

    r.v(lambda v: v.memset(mk2[:, :], 0.0), inc=ev.inc())

    sems = (ld, ldx, mm, ev, es, st)
    bufs = (xs, lnx, xbs, sqs, tmp32, rows4, arowst, pool)
    for ti in range(8):
        t0 = ti * 512
        lnw = _ln_gain_stream(b, C, r, sems, bufs, xview, ti, (l1wc, l1bc), True, arows_d)
        for oc in range(8):
            iP, psP_, wP = pool.take()
            for dc in range(NCH):
                r.t(lambda t, dc=dc, oc=oc, p=psP_:
                    MM(t, p[:, 0:512], wq[dc][:, oc * 128:oc * 128 + 128],
                       lnx[:, dc * 512:(dc + 1) * 512], dc == 0, dc == NCH - 1),
                    wait=([lnw] + wP) if dc == 0 else [],
                    inc=(mm.inc() if dc == NCH - 1 else None))
            r.v(lambda v, oc=oc, p=psP_: v.tensor_scalar(Kt[:, oc * 512:(oc + 1) * 512],
                                                         p[:, 0:512], kbc[:, oc:oc + 1],
                                                         None, ALU.add),
                wait=[mm.now()], inc=ev.inc())
            pool.done(iP, ev.now())
        iMK, psMK, wMK = poolO.take()
        for h in range(16):
            r0, hp = (h % 2) * 64, h // 2
            ksl = Kt[r0:r0 + 64, hp * 512: hp * 512 + 512]
            r.v(lambda v, ksl=ksl: v.tensor_tensor(sqh[:, :], ksl, ksl, ALU.mult),
                wait=[ev.now(), mm.now()], inc=ev.inc())
            r.t(lambda t, h=h, p=psMK: MM(t, p[0:16, 0:512], hsel[:, h * 16:h * 16 + 16],
                                          sqh[:, :], h == 0, h == 15),
                wait=[ev.now()] + (wMK if h == 0 else []), inc=mm.inc())
        r.v(lambda v, p=psMK: v.reduce_max(mcol[:, 0:1], p[0:16, 0:512], axis=AX),
            wait=[mm.now()], inc=ev.inc())
        r.v(lambda v: v.tensor_tensor(mk2[:, :], mk2[:, :], mcol[:, 0:1], ALU.max),
            wait=[ev.now()], inc=ev.inc())
        poolO.done(iMK, ev.now())
        for sub in range(4):
            for half in range(2):
                iP, psP_, wP = pool.take()
                for dc in range(NCH):
                    r.t(lambda t, dc=dc, sub=sub, half=half, p=psP_:
                        MM(t, p[:, 0:512],
                           lnx[:, dc * 512 + sub * 128: dc * 512 + sub * 128 + 128],
                           wq[dc][:, 1024 + half * 512: 1024 + half * 512 + 512],
                           dc == 0, dc == NCH - 1),
                        wait=([lnw] + wP) if dc == 0 else [],
                        inc=(mm.inc() if dc == NCH - 1 else None))
                voff = sub * 1040 + half * 520
                r.v(lambda v, voff=voff, half=half, p=psP_:
                    v.tensor_tensor(Vt[:, voff:voff + 520].rearrange("p (h e) -> p h e", e=65)[:, :, 0:64],
                                    p[:, 0:512].rearrange("p (h e) -> p h e", e=64),
                                    vbb[:, half * 520:half * 520 + 520]
                                    .rearrange("p (h e) -> p h e", e=65)[:, :, 0:64], ALU.add),
                    wait=[mm.now()], inc=ev.inc())
                r.v(lambda v, voff=voff:
                    v.memset(Vt[:, voff:voff + 520].rearrange("p (h e) -> p h e", e=65)[:, :, 64:65],
                             1.0), inc=ev.inc())
                pool.done(iP, ev.now())
        r.y(lambda y, t0=t0: y.dma_start(
            out=K_out.rearrange("p (g t) -> p g t", t=T)[:, :, t0:t0 + 512],
            in_=Kt[:, :].rearrange("p (g t) -> p g t", t=512)),
            wait=[ev.now()], inc=st.inc(16))
        r.y(lambda y, ti=ti: y.dma_start(out=V_out[:, ti * 4160: ti * 4160 + 4160],
                                         in_=Vt[:, :]),
            inc=st.inc(16))
    r.y(lambda y: y.dma_start(out=mk_out[:, :], in_=mk2[:, :]), wait=[ev.now()],
        inc=st.inc(16))
    r.y(lambda y: y.nop(), wait=[st.now()])
    return b.finish()


def build_D_attn(x_percore, K_percore, V_percore, arows_percore, mk2_percore,
                 Wqkv, bqkv, Wo, bo, ln1w_col, ln1b_col, name="d_attn"):
    b = Bld(name)
    r = b.r
    x_in = b.din("x", x_percore)
    K_in = b.din("K", K_percore, BF16)
    V_in = b.din("V", V_percore, BF16)
    ar_in = b.din("arows", arows_percore)
    mk_in = b.din("mk2", mk2_percore)
    wq_in = b.din("wqkv", np.ascontiguousarray(Wqkv))
    qcol = np.ascontiguousarray(bqkv[0:1024].reshape(8, 128).T)
    qb_in = b.din("qbc", qcol)
    wo_in = b.din("wo", np.ascontiguousarray(Wo))
    bo_in = b.din("boc", np.ascontiguousarray(bo.reshape(8, 128).T))
    hs_in = b.din("hsel", _hsel_const())
    l1w_in = b.din("l1w", np.ascontiguousarray(ln1w_col))
    l1b_in = b.din("l1b", np.ascontiguousarray(ln1b_col))
    x4_out = b.dout("x4", [D, T], F32)

    ld = b.sem()
    C = reg_consts(b, [("ones_row", F32), ("ones_row_b", BF16), ("ones_col", BF16),
                       ("ident", BF16), ("triu", BF16), ("cvals", F32)], ld)
    Ks = b.sbuf("Ks", [128, 8 * T], BF16)
    Vs = b.sbuf("Vs", [128, 32 * 1040], BF16)
    wq = [b.sbuf("wq%d" % i, [128, 1024], BF16) for i in range(NCH)]
    wo = [b.sbuf("wo%d" % i, [128, 1024], BF16) for i in range(NCH)]
    hsel = b.sbuf("hsel_s", [64, 256], BF16)
    cols = b.sbuf("cols_s", [128, 4 * 8], F32)
    qbc, boc, l1wc, l1bc = (cols[:, 0:8], cols[:, 8:16], cols[:, 16:24], cols[:, 24:32])
    mk2 = b.sbuf("mk2_s", [16, 1], F32)
    r.g(lambda g: g.dma_start(out=Ks[:, :], in_=K_in[:, :]), inc=ld.inc(16))
    r.g(lambda g: g.dma_start(out=Vs[:, :], in_=V_in[:, :]), inc=ld.inc(16))
    for i in range(NCH):
        r.g(lambda g, i=i: g.dma_start(out=wq[i][:, :], in_=wq_in[i * 128:(i + 1) * 128, 0:1024]),
            inc=ld.inc(16))
        r.g(lambda g, i=i: g.dma_start(out=wo[i][:, :], in_=wo_in[i * 128:(i + 1) * 128, :]),
            inc=ld.inc(16))
    r.g(lambda g: g.dma_start(out=hsel[:, :], in_=hs_in[:, :]), inc=ld.inc(16))
    r.g(lambda g: g.dma_start(out=qbc, in_=qb_in[:, :]), inc=ld.inc(16))
    r.g(lambda g: g.dma_start(out=boc, in_=bo_in[:, :]), inc=ld.inc(16))
    r.g(lambda g: g.dma_start(out=l1wc, in_=l1w_in[:, :]), inc=ld.inc(16))
    r.g(lambda g: g.dma_start(out=l1bc, in_=l1b_in[:, :]), inc=ld.inc(16))
    r.g(lambda g: g.dma_start(out=mk2[:, :], in_=mk_in[:, :]), inc=ld.inc(16))

    ldx = b.sem()
    mm = b.sem()
    ev = b.sem()
    es = b.sem()
    st = b.sem()

    xs = b.sbuf("xs", [128, 8 * 512], F32)
    lnx = b.sbuf("lnx", [128, 8 * 512], BF16)
    tmp32 = b.sbuf("tmp32", [128, 512], F32)
    arowst = b.sbuf("arowst", [1, 1024], F32)
    qsb = b.sbuf("qsb", [128, 8 * 512], BF16)
    sqh = b.sbuf("sqh", [64, 512], BF16)
    mcol = b.sbuf("mcol", [16, 4], F32)
    mbf = b.sbuf("mbf", [16, 1], BF16)
    mrow = b.sbuf("mrow", [1, 16], BF16)
    negM = b.sbuf("negM", [128, 16], F32)
    exp2 = [b.sbuf("exp2a", [128, 1024], BF16), b.sbuf("exp2b", [128, 1024], BF16)]
    rden = arowst[0:1, 0:512]
    rb = tmp32[0:64, :]
    ofm = lnx

    pA = b.psum("pA", [128, 6 * 512], F32)
    pT = b.psum("pT", [128, 1024], BF16)
    pool = Pool([pA[:, i * 512:(i + 1) * 512] for i in range(4)])
    poolO = Pool([pA[:, i * 512:(i + 1) * 512] for i in range(4, 6)])

    xview = x_in.rearrange("(c p) t -> p c t", p=128)
    oview = x4_out.rearrange("(c p) t -> p c t", p=128)
    arows_d = ar_in

    sems = (ld, ldx, mm, ev, es, st)
    bufs = (xs, lnx, None, None, tmp32, None, arowst, pool)
    for ti in range(8):
        t0 = ti * 512
        lnw = _ln_gain_stream(b, C, r, sems, bufs, xview, ti, (l1wc, l1bc), False, arows_d)
        for oc in range(8):
            iP, psP_, wP = pool.take()
            for dc in range(NCH):
                r.t(lambda t, dc=dc, oc=oc, p=psP_:
                    MM(t, p[:, 0:512], wq[dc][:, oc * 128:oc * 128 + 128],
                       lnx[:, dc * 512:(dc + 1) * 512], dc == 0, dc == NCH - 1),
                    wait=([lnw] + wP) if dc == 0 else [],
                    inc=(mm.inc() if dc == NCH - 1 else None))
            r.v(lambda v, oc=oc, p=psP_: v.tensor_scalar(qsb[:, oc * 512:(oc + 1) * 512],
                                                         p[:, 0:512], qbc[:, oc:oc + 1],
                                                         None, ALU.add),
                wait=[mm.now()], inc=ev.inc())
            pool.done(iP, ev.now())
        iNQ, psNQ, wNQ = poolO.take()
        for h in range(16):
            r0, hp = (h % 2) * 64, h // 2
            qop = qsb[r0:r0 + 64, hp * 512: hp * 512 + 512]
            r.v(lambda v, qop=qop: v.tensor_tensor(sqh[:, :], qop, qop, ALU.mult),
                wait=[ev.now(), mm.now()], inc=ev.inc())
            r.t(lambda t, h=h, p=psNQ: MM(t, p[0:16, 0:512], hsel[:, h * 16:h * 16 + 16],
                                          sqh[:, :], h == 0, h == 15),
                wait=[ev.now()] + (wNQ if h == 0 else []), inc=mm.inc())
        r.v(lambda v, p=psNQ: v.reduce_max(mcol[:, 1:2], p[0:16, 0:512], axis=AX),
            wait=[mm.now()], inc=ev.inc())
        poolO.done(iNQ, ev.now())
        r.v(lambda v: v.tensor_tensor(mcol[:, 2:3], mcol[:, 1:2], mk2[:, 0:1], ALU.mult),
            wait=[ev.now()], inc=ev.inc())
        r.v(lambda v: v.tensor_scalar(mcol[:, 3:4], mcol[:, 2:3], 1e-30, None, ALU.max),
            wait=[ev.now()], inc=ev.inc())
        r.s(lambda s: s.activation(mcol[:, 2:3], mcol[:, 3:4], AF.Ln), wait=[ev.now()],
            inc=es.inc())
        r.s(lambda s: s.activation(mcol[:, 3:4], mcol[:, 2:3], AF.Exp, scale=0.5),
            inc=es.inc())
        r.v(lambda v: v.tensor_scalar(mbf[:, :], mcol[:, 3:4], 0.133, None, ALU.mult),
            wait=[es.now()], inc=ev.inc())
        r.t(lambda t: t.transpose(pT[0:1, 0:16], mbf[:, :], C["ident"][0:16, 0:16]),
            wait=[ev.now()], inc=mm.inc())
        r.v(lambda v: v.tensor_copy(mrow[:, :], pT[0:1, 0:16]), wait=[mm.now()], inc=ev.inc())
        iM, psM_, wM = pool.take()
        r.t(lambda t, p=psM_: MM(t, p[:, 0:16], C["ones_row_b"][:, :], mrow[:, :], True, True),
            wait=[ev.now()] + wM, inc=mm.inc())
        r.v(lambda v, p=psM_: v.tensor_scalar(negM[:, :], p[:, 0:16], -1.0, None, ALU.mult),
            wait=[mm.now()], inc=ev.inc())
        pool.done(iM, ev.now())
        ndep = ev.now()
        nkc = ti * 4 + 4
        ngr = (nkc + 1) // 2
        pv_war = ev.now()
        for h in range(16):
            r0, hp = (h % 2) * 64, h // 2
            qop = qsb[r0:r0 + 64, hp * 512: hp * 512 + 512]
            psO_ = poolO.tiles[0]
            es_exp = {}
            pv_dep = {}
            mm_pv = {}

            def rec_pv(g, h=h, psO_=psO_, nkc=nkc, ngr=ngr):
                njs = min(2, nkc - 2 * g)
                for j in range(njs):
                    tc = 2 * g + j
                    r.t(lambda t, tc=tc, h=h, g=g, j=j, p=psO_:
                        MM(t, p[0:65, 0:512],
                           Vs[:, tc * 1040 + h * 65: tc * 1040 + h * 65 + 65],
                           exp2[g % 2][:, j * 512:(j + 1) * 512],
                           (g == 0 and j == 0), (g == ngr - 1 and j == njs - 1)),
                        wait=(pv_dep[g] + ([pv_war] if (g == 0 and j == 0) else [])),
                        inc=(mm.inc() if j == njs - 1 else None))
                mm_pv[g] = mm.now()

            for g in range(ngr):
                half = g % 2
                njs = min(2, nkc - 2 * g)
                for j in range(njs):
                    tc = 2 * g + j
                    bank = pool.tiles[half * 2 + j]
                    w = [ndep] if g < 2 else [es_exp[g - 2]]
                    r.t(lambda t, r0=r0, hp=hp, tc=tc, qop=qop, p=bank:
                        MM(t, p[:, 0:512],
                           Ks[r0:r0 + 64, hp * T + tc * 128: hp * T + tc * 128 + 128],
                           qop, True, True),
                        wait=w, inc=(mm.inc() if j == njs - 1 else None))
                w = [mm.now()]
                if g >= 2:
                    w.append(mm_pv[g - 2])
                r.s(lambda s, h=h, half=half, njs=njs: s.activation(
                    exp2[half][:, 0:njs * 512], pA[:, half * 1024:half * 1024 + njs * 512],
                    AF.Exp, scale=0.125, bias=negM[:, h:h + 1]),
                    wait=w, inc=es.inc())
                es_exp[g] = es.now()
                dep = [es.now()]
                for j in range(njs):
                    tc = 2 * g + j
                    dj = tc - ti * 4
                    if dj >= 0:
                        if dj > 0:
                            r.v(lambda v, half=half, j=j, dj=dj:
                                v.memset(exp2[half][:, j * 512:j * 512 + dj * 128], 0.0),
                                wait=[es_exp[g]], inc=ev.inc())
                        r.v(lambda v, half=half, j=j, dj=dj: v.tensor_tensor(
                            exp2[half][:, j * 512 + dj * 128:j * 512 + dj * 128 + 128],
                            exp2[half][:, j * 512 + dj * 128:j * 512 + dj * 128 + 128],
                            C["triu"][:, :], ALU.mult),
                            wait=[es_exp[g]], inc=ev.inc())
                        dep = [es_exp[g], ev.now()]
                pv_dep[g] = dep
                if g >= 1:
                    rec_pv(g - 1)
            rec_pv(ngr - 1)
            r.v(lambda v, p=psO_: v.reciprocal(rden[:, :], p[64:65, 0:512]),
                wait=[mm.now()], inc=ev.inc())
            psR_ = poolO.tiles[1]
            r.t(lambda t, p=psR_: MM(t, p[0:64, 0:512], C["ones_row"][0:1, 0:64], rden[:, :],
                                     True, True),
                wait=[ev.now()], inc=mm.inc())
            r.v(lambda v, p=psR_: v.tensor_copy(rb[:, :], p[0:64, 0:512]),
                wait=[mm.now()], inc=ev.inc())
            r.v(lambda v, r0=r0, hp=hp, p=psO_:
                v.tensor_tensor(ofm[r0:r0 + 64, hp * 512:hp * 512 + 512], p[0:64, 0:512],
                                rb[:, :], ALU.mult),
                wait=[ev.now(), mm.now()], inc=ev.inc())
            pv_war = ev.now()
        odep = ev.now()
        for dc in range(NCH):
            iP, psP_, wP = pool.take()
            for jc in range(NCH):
                r.t(lambda t, jc=jc, dc=dc, p=psP_:
                    MM(t, p[:, 0:512], wo[jc][:, dc * 128:dc * 128 + 128],
                       ofm[:, jc * 512:(jc + 1) * 512], jc == 0, jc == NCH - 1),
                    wait=([odep] + wP) if jc == 0 else [],
                    inc=(mm.inc() if jc == NCH - 1 else None))
            xsl = xs[:, dc * 512:(dc + 1) * 512]
            r.v(lambda v, xsl=xsl, p=psP_: v.tensor_tensor(xsl, xsl, p[:, 0:512], ALU.add),
                wait=[mm.now()])
            r.v(lambda v, xsl=xsl, dc=dc: v.tensor_scalar(xsl, xsl, boc[:, dc:dc + 1],
                                                          None, ALU.add),
                inc=ev.inc())
            pool.done(iP, ev.now())
        r.y(lambda y, t0=t0: y.dma_start(out=oview[:, :, t0:t0 + 512],
                                         in_=xs[:, :].rearrange("p (c t) -> p c t", t=512)),
            wait=[ev.now()], inc=st.inc(16))
    r.y(lambda y: y.nop(), wait=[st.now()])
    return b.finish()


def build_D_moe(x_percore, ln2w_col, ln2b_col, gate_W, fc1T, fc1_cols, fc2T, fc2_cols,
                name="d_moe", dbg=False):
    NE, NF = 8, 32
    b = Bld(name)
    r = b.r
    x_in = b.din("x", x_percore)
    l2w_in = b.din("l2w", np.ascontiguousarray(ln2w_col))
    l2b_in = b.din("l2b", np.ascontiguousarray(ln2b_col))
    gw_prep = np.ascontiguousarray(
        np.asarray(gate_W).reshape(8, 128, 8).transpose(1, 0, 2).reshape(128, 64))
    gw_in = b.din("gw", gw_prep)
    f1_in = b.din("fc1T", np.ascontiguousarray(fc1T))
    f1c_in = b.din("b1cs", np.ascontiguousarray(fc1_cols))
    f2_in = b.din("fc2T", np.ascontiguousarray(fc2T))
    f2c_in = b.din("b2cs", np.ascontiguousarray(fc2_cols))
    y_out = b.dout("y", [D, T], F32)
    if dbg:
        nx2_d = b.dout("nx2_d", [128, 8 * T], BF16)
        ge_d = b.dout("ge_d", [8, T], BF16)
    else:
        nx2_d = b.dram("nx2_d", [128, 8 * T], BF16)
        ge_d = b.dram("ge_d", [8, T], BF16)

    ld = b.sem()
    C = reg_consts(b, [("ones_row", F32), ("ones_row_b", BF16), ("ones_col", BF16),
                       ("ident", BF16), ("cvals", F32)], ld)
    cols = b.sbuf("colsb", [128, 8 * 32 + 8 * 8 + 2 * 8], F32)
    b1cs = cols[:, 0:256]
    b2cs = cols[:, 256:320]
    l2wc, l2bc = cols[:, 320:328], cols[:, 328:336]
    gws = b.sbuf("gws", [128, 8 * 8], BF16)
    r.g(lambda g: g.dma_start(out=b1cs, in_=f1c_in[:, :]), inc=ld.inc(16))
    r.g(lambda g: g.dma_start(out=b2cs, in_=f2c_in[:, :]), inc=ld.inc(16))
    r.g(lambda g: g.dma_start(out=l2wc, in_=l2w_in[:, :]), inc=ld.inc(16))
    r.g(lambda g: g.dma_start(out=l2bc, in_=l2b_in[:, :]), inc=ld.inc(16))
    r.g(lambda g: g.dma_start(out=gws[:, :], in_=gw_in[:, :]), inc=ld.inc(16))

    ldx = b.sem()
    ldw = b.sem()
    mm = b.sem()
    ev = b.sem()
    es = b.sem()
    st = b.sem()

    xs = b.sbuf("xs", [128, 8 * 512], F32)
    lnx = b.sbuf("lnx", [128, 8 * 512], BF16)
    xbs = b.sbuf("xbs", [128, 512], BF16)
    sqs = b.sbuf("sqs", [128, 512], BF16)
    tmp32 = b.sbuf("tmp32", [128, 512], F32)
    rows4 = b.sbuf("rows4", [1, 4 * 512], F32)
    gcols = b.sbuf("gcols", [128, 8], F32)
    gtmp = b.sbuf("gtmp", [128, 8], F32)
    gtmp2 = b.sbuf("gtmp2", [128, 8], F32)
    m1c = b.sbuf("m1c", [128, 4], F32)
    gerow = b.sbuf("gerow", [1, 8 * 512], BF16)
    gebf = b.sbuf("gebf", [128, 8], BF16)
    w1 = [b.sbuf("w1_%d" % i, [128, 4096], BF16) for i in range(NCH)]
    w2 = [b.sbuf("w2_%d" % i, [128, 1024], BF16) for i in range(NF)]
    h1 = b.sbuf("h1", [128, NF * 512], BF16)
    geb = b.sbuf("geb", [128, 512], BF16)

    pA = b.psum("pA", [128, 6 * 512], F32)
    pT = b.psum("pT", [128, 1024], BF16)
    pool = Pool([pA[:, i * 512:(i + 1) * 512] for i in range(6)])

    xview = x_in.rearrange("(c p) t -> p c t", p=128)
    yview = y_out.rearrange("(c p) t -> p c t", p=128)

    for ti in range(8):
        t0 = ti * 512
        barrier = [mm.now(), ev.now(), es.now(), ld.now(), st.now()]
        r.y(lambda y, t0=t0: y.dma_start(out=xs[:, :].rearrange("p (c t) -> p c t", t=512),
                                         in_=xview[:, :, t0:t0 + 512]),
            wait=barrier, inc=ldx.inc(16))
        xw = ldx.now()
        iS, psS_, wS = pool.take()
        iQ, psQ_, wQ = pool.take()
        for c in range(NCH):
            r.v(lambda v, c=c: v.tensor_copy(xbs[:, :], xs[:, c * 512:(c + 1) * 512]),
                wait=[xw, mm.now(), es.now()] + (wS + wQ if c == 0 else []), inc=ev.inc())
            r.s(lambda s: s.activation(sqs[:, :], xbs[:, :], AF.Square),
                wait=[ev.now(), mm.now()], inc=es.inc())
            r.t(lambda t, c=c, p=psS_: MM(t, p[0:1, 0:512], C["ones_col"][:, :], xbs[:, :],
                                          c == 0, c == NCH - 1), wait=[ev.now()])
            r.t(lambda t, c=c, p=psQ_: MM(t, p[0:1, 0:512], C["ones_col"][:, :], sqs[:, :],
                                          c == 0, c == NCH - 1), wait=[es.now()], inc=mm.inc())
        rstd, brow = layernorm_rows(b, rows4, psS_, psQ_, mm.now(), ev, es,
                                    eps_ap=C["cvals"][0:1, 1:2])
        pool.done(iS, ev.now())
        pool.done(iQ, ev.now())
        iA, psA_, wA = pool.take()
        iB, psB_, wB = pool.take()
        r.t(lambda t, p=psA_, q=rstd: MM(t, p[:, 0:512], C["ones_row"][:, :], q, True, True),
            wait=[ev.now()] + wA)
        r.t(lambda t, p=psB_, q=brow: MM(t, p[:, 0:512], C["ones_row"][:, :], q, True, True),
            wait=wB, inc=mm.inc())
        mw2 = mm.now()
        for c in range(NCH):
            r.v(lambda v, c=c, p=psA_: v.tensor_tensor(tmp32[:, :], xs[:, c * 512:(c + 1) * 512],
                                                       p[:, 0:512], ALU.mult), wait=[mw2])
            r.v(lambda v, c=c, p=psB_: v.tensor_tensor(tmp32[:, :], tmp32[:, :], p[:, 0:512],
                                                       ALU.add))
            r.v(lambda v, c=c: v.tensor_scalar(lnx[:, c * 512:(c + 1) * 512], tmp32[:, :],
                                               l2wc[:, c:c + 1], l2bc[:, c:c + 1],
                                               ALU.mult, ALU.add),
                inc=(ev.inc() if c == NCH - 1 else None))
        pool.done(iA, ev.now())
        pool.done(iB, ev.now())
        lnw = ev.now()
        r.g(lambda g, t0=t0: g.dma_start(out=nx2_d[:, t0 * 8:t0 * 8 + 4096], in_=lnx[:, :]),
            wait=[lnw], inc=st.inc(16))
        r.y(lambda y, t0=t0: y.dma_start(out=yview[:, :, t0:t0 + 512],
                                         in_=xs[:, :].rearrange("p (c t) -> p c t", t=512)),
            wait=[lnw], inc=st.inc(16))
        for sub in range(4):
            iG, psG, wG = pool.take()
            for dc in range(NCH):
                r.t(lambda t, dc=dc, sub=sub, p=psG:
                    MM(t, p[:, 0:8], lnx[:, dc * 512 + sub * 128: dc * 512 + sub * 128 + 128],
                       gws[:, dc * 8:(dc + 1) * 8], dc == 0, dc == NCH - 1),
                    wait=([lnw] + wG) if dc == 0 else [],
                    inc=(mm.inc() if dc == NCH - 1 else None))
            r.v(lambda v, p=psG: v.tensor_copy(gcols[:, :], p[:, 0:8]),
                wait=[mm.now()], inc=ev.inc())
            pool.done(iG, ev.now())
            r.v(lambda v: v.reduce_max(m1c[:, 0:1], gcols[:, :], axis=AX),
                wait=[ev.now()], inc=ev.inc())
            r.v(lambda v: v.tensor_scalar(gtmp[:, :], gcols[:, :], m1c[:, 0:1], None,
                                          ALU.is_lt), wait=[ev.now()], inc=ev.inc())
            r.v(lambda v: v.tensor_tensor(gtmp2[:, :], gcols[:, :], gtmp[:, :], ALU.mult),
                wait=[ev.now()], inc=ev.inc())
            r.v(lambda v: v.tensor_scalar(gtmp[:, :], gtmp[:, :], 1e30, -1e30,
                                          ALU.mult, ALU.add), wait=[ev.now()], inc=ev.inc())
            r.v(lambda v: v.tensor_tensor(gtmp2[:, :], gtmp2[:, :], gtmp[:, :], ALU.add),
                wait=[ev.now()], inc=ev.inc())
            r.v(lambda v: v.reduce_max(m1c[:, 1:2], gtmp2[:, :], axis=AX),
                wait=[ev.now()], inc=ev.inc())
            r.v(lambda v: v.tensor_scalar(gtmp[:, :], gcols[:, :], m1c[:, 1:2], None,
                                          ALU.is_ge), wait=[ev.now()], inc=ev.inc())
            r.v(lambda v: v.tensor_scalar(m1c[:, 2:3], m1c[:, 0:1], -1.0, None, ALU.mult),
                wait=[ev.now()], inc=ev.inc())
            r.s(lambda s: s.activation(gtmp2[:, :], gcols[:, :], AF.Exp, bias=m1c[:, 2:3]),
                wait=[ev.now()], inc=es.inc())
            r.v(lambda v: v.tensor_tensor(m1c[:, 3:4], m1c[:, 1:2], m1c[:, 2:3], ALU.add),
                wait=[es.now()], inc=ev.inc())
            r.s(lambda s: s.activation(m1c[:, 1:2], m1c[:, 3:4], AF.Exp),
                wait=[ev.now()], inc=es.inc())
            r.v(lambda v: v.tensor_scalar(m1c[:, 3:4], m1c[:, 1:2], 1.0, None, ALU.add),
                wait=[es.now()], inc=ev.inc())
            r.v(lambda v: v.reciprocal(m1c[:, 2:3], m1c[:, 3:4]),
                wait=[ev.now()], inc=ev.inc())
            r.v(lambda v: v.tensor_tensor(gcols[:, :], gtmp2[:, :], gtmp[:, :], ALU.mult),
                wait=[ev.now()], inc=ev.inc())
            r.v(lambda v: v.tensor_scalar(gebf[:, :], gcols[:, :], m1c[:, 2:3], None,
                                          ALU.mult), wait=[ev.now()], inc=ev.inc())
            for e in range(NE):
                r.t(lambda t, e=e, sub=sub: t.transpose(pT[0:1, e * 128:e * 128 + 128],
                                                        gebf[:, e:e + 1],
                                                        C["ident"][:, :]),
                    wait=[ev.now()], inc=mm.inc())
            for e in range(NE):
                r.v(lambda v, e=e, sub=sub: v.tensor_copy(gerow[0:1, e * 512 + sub * 128:
                                                                e * 512 + sub * 128 + 128],
                                                          pT[0:1, e * 128:e * 128 + 128]),
                    wait=[mm.now()], inc=ev.inc())
        for e in range(NE):
            r.y(lambda y, e=e, t0=t0: y.dma_start(out=ge_d[e:e + 1, t0:t0 + 512],
                                                  in_=gerow[0:1, e * 512:(e + 1) * 512]),
                wait=[ev.now()], inc=st.inc(16))

    for e in range(NE):
        wdep = [mm.now(), es.now()]
        for i in range(NCH):
            r.g(lambda g, i=i, e=e: g.dma_start(out=w1[i][:, :],
                                                in_=f1_in[e, i * 128:(i + 1) * 128, :]),
                wait=wdep if i == 0 else [], inc=ldw.inc(16))
        for i in range(NF):
            r.g(lambda g, i=i, e=e: g.dma_start(out=w2[i][:, :],
                                                in_=f2_in[e, i * 128:(i + 1) * 128, :]),
                inc=ldw.inc(16))
        wld = ldw.now()
        for ti in range(8):
            t0 = ti * 512
            r.y(lambda y, t0=t0: y.dma_start(out=lnx[:, :], in_=nx2_d[:, t0 * 8:t0 * 8 + 4096]),
                wait=[mm.now(), ev.now(), st.now()], inc=ldx.inc(16))
            r.y(lambda y, t0=t0: y.dma_start(out=xs[:, :].rearrange("p (c t) -> p c t", t=512),
                                             in_=yview[:, :, t0:t0 + 512]),
                inc=ldx.inc(16))
            r.g(lambda g, e=e, t0=t0: g.dma_start(out=gerow[0:1, 0:512],
                                                  in_=ge_d[e:e + 1, t0:t0 + 512]),
                wait=[ev.now()], inc=ldx.inc(16))
            xw = ldx.now()
            iG, psG, wG = pool.take()
            r.t(lambda t, p=psG: MM(t, p[:, 0:512], C["ones_row_b"][:, :], gerow[0:1, 0:512],
                                    True, True),
                wait=[xw] + wG, inc=mm.inc())
            r.v(lambda v, p=psG: v.tensor_copy(geb[:, :], p[:, 0:512]),
                wait=[mm.now()], inc=ev.inc())
            pool.done(iG, ev.now())
            for fc in range(NF):
                iP, psP_, wP = pool.take()
                for dc in range(NCH):
                    r.t(lambda t, dc=dc, fc=fc, p=psP_:
                        MM(t, p[:, 0:512], w1[dc][:, fc * 128:fc * 128 + 128],
                           lnx[:, dc * 512:(dc + 1) * 512], dc == 0, dc == NCH - 1),
                        wait=([xw, wld] + wP) if dc == 0 else [],
                        inc=(mm.inc() if dc == NCH - 1 else None))
                r.s(lambda s, fc=fc, e=e, p=psP_:
                    s.activation(h1[:, fc * 512:(fc + 1) * 512], p[:, 0:512], AF.Gelu,
                                 bias=b1cs[:, e * 32 + fc:e * 32 + fc + 1]),
                    wait=[mm.now()], inc=es.inc())
                pool.done(iP, es.now())
            gw_ = es.now()
            for dc in range(NCH):
                iP, psP_, wP = pool.take()
                for fc in range(NF):
                    r.t(lambda t, fc=fc, dc=dc, p=psP_:
                        MM(t, p[:, 0:512], w2[fc][:, dc * 128:dc * 128 + 128],
                           h1[:, fc * 512:(fc + 1) * 512], fc == 0, fc == NF - 1),
                        wait=([gw_] + wP) if fc == 0 else [],
                        inc=(mm.inc() if fc == NF - 1 else None))
                r.v(lambda v, dc=dc, e=e, p=psP_:
                    v.scalar_tensor_tensor(tmp32[:, :], p[:, 0:512],
                                           b2cs[:, e * 8 + dc:e * 8 + dc + 1], geb[:, :],
                                           ALU.add, ALU.mult),
                    wait=[mm.now(), ev.now()], inc=ev.inc())
                xsl = xs[:, dc * 512:(dc + 1) * 512]
                r.v(lambda v, xsl=xsl: v.tensor_tensor(xsl, xsl, tmp32[:, :], ALU.add),
                    wait=[ev.now()], inc=ev.inc())
                pool.done(iP, ev.now())
            r.y(lambda y, t0=t0: y.dma_start(out=yview[:, :, t0:t0 + 512],
                                             in_=xs[:, :].rearrange("p (c t) -> p c t", t=512)),
                wait=[ev.now()], inc=st.inc(16))
    r.y(lambda y: y.nop(), wait=[st.now()])
    return b.finish()


def _col(vec, nch):
    return np.ascontiguousarray(np.asarray(vec, np.float32).reshape(nch, 128).T)


def _pair_cols(bias16x64):
    out = np.zeros((128, 8), np.float32)
    for h in range(16):
        out[(h % 2) * 64:(h % 2) * 64 + 64, h // 2] = bias16x64[h]
    return out


def kernel(**ins):
    return _kernel_impl(fused=True, **ins)


def _kernel_impl(fused=True, **ins):
    if fused:
        ctx = FuseCtx()
        ctx.final = ("d_add", "y")
        _FUSE[0] = ctx
    try:
        return _kernel_chain(ctx if fused else None, **ins)
    finally:
        _FUSE[0] = None


def _kernel_chain(ctx, **ins):
    x = ins["L0_attn_x"]
    x_fm = [np.ascontiguousarray(x[c].T) for c in range(NCORES)]

    bA = build_blockA(x_fm, ins["L0_attn_qkv_w"], ins["L0_attn_proj_w"],
                      ins["L0_attn_alpha_log"], ins["L0_attn_beta_log"])
    res = bA.run()
    x1 = [res.results[c]["x1"] for c in range(NCORES)]

    res = build_ln_pass(x1, pad=1, name="b_ln").run()
    xb = [res.results[c]["y"] for c in range(NCORES)]
    w1p = np.zeros((2, 1024, 5632), np.float32)
    w1p[:, :, 0:2730] = ins["L0_ffn_w1"][:, :, 0:2730]
    w1p[:, :, 2816:2816 + 2730] = ins["L0_ffn_w1"][:, :, 2730:]
    b1p = np.zeros((2, 5632), np.float32)
    b1p[:, 0:2730] = ins["L0_ffn_b1"][:, 0:2730]
    b1p[:, 2816:2816 + 2730] = ins["L0_ffn_b1"][:, 2730:]
    w2p = np.zeros((2, 2816, 1024), np.float32)
    w2p[:, 0:2730, :] = ins["L0_ffn_w2"]
    for l in range(2):
        res = build_B_qkv(xb, ins["L0_ffn_wqkv"][l], ins["L0_ffn_q_scale"][l],
                          ins["L0_ffn_k_scale"][l], name="b_qkv%d" % l).run()
        qr = [res.results[c]["qr"] for c in range(NCORES)]
        kr = [res.results[c]["kr"] for c in range(NCORES)]
        vp = [res.results[c]["vp"] for c in range(NCORES)]
        res = build_B_attn(xb, qr, kr, vp, ins["L0_ffn_wo"][l], ins["L0_ffn_q_scale"][l],
                           ins["L0_ffn_k_scale"][l], name="b_attn%d" % l).run()
        xb = [res.results[c]["xb2"] for c in range(NCORES)]
        res = build_B_ffn(xb, w1p[l], _col(b1p[l], 44), w2p[l], _col(ins["L0_ffn_b2"][l], 8),
                          _col(ins["L0_ffn_ln_g"][l], 8), _col(ins["L0_ffn_ln_b"][l], 8),
                          name="b_ffn%d" % l).run()
        xb = [res.results[c]["xb2"] for c in range(NCORES)]
    res = build_add_pass(x1, xb, b_pad=1, name="b_add").run()
    x2 = [res.results[c]["y"] for c in range(NCORES)]

    res = build_ln_pass(x2, pad=0, name="c_ln").run()
    h = [res.results[c]["y"] for c in range(NCORES)]
    for l in range(8):
        qwT = np.ascontiguousarray(ins["L1_attn_qw"][l].transpose(2, 0, 1).reshape(64, 1024))
        kwT = np.ascontiguousarray(ins["L1_attn_kw"][l].transpose(2, 0, 1).reshape(64, 1024))
        vwT = np.ascontiguousarray(ins["L1_attn_vw"][l].transpose(2, 0, 1).reshape(64, 1024))
        vbf = np.zeros(1040, np.float32)
        for hh in range(16):
            vbf[hh * 65:hh * 65 + 64] = ins["L1_attn_vb"][l, hh]
        res = build_C_attn(h, _col(ins["L1_attn_ln1_w"][l], 8), _col(ins["L1_attn_ln1_b"][l], 8),
                           qwT, _pair_cols(ins["L1_attn_qb"][l]), kwT,
                           _pair_cols(ins["L1_attn_kb"][l]), vwT, vbf,
                           name="c_attn%d" % l).run()
        h = [res.results[c]["h2"] for c in range(NCORES)]
        res = build_C_ffn(h, _col(ins["L1_attn_ln2_w"][l], 8), _col(ins["L1_attn_ln2_b"][l], 8),
                          ins["L1_attn_w1"][l], _col(ins["L1_attn_b1"][l], 32),
                          ins["L1_attn_w2"][l], _col(ins["L1_attn_b2"][l], 8),
                          name="c_ffn%d" % l).run()
        h = [res.results[c]["h2"] for c in range(NCORES)]
    res = build_add_pass(x2, h, b_pad=0, name="c_add").run()
    x3 = [res.results[c]["y"] for c in range(NCORES)]

    res = build_ln_pass(x3, pad=0, name="d_ln").run()
    xd = [res.results[c]["y"] for c in range(NCORES)]
    res = build_D_kv(xd, ins["L1_ffn_Wqkv"], ins["L1_ffn_bqkv"],
                     _col(ins["L1_ffn_ln1_w"], 8), _col(ins["L1_ffn_ln1_b"], 8)).run()
    Kd = [res.results[c]["K"] for c in range(NCORES)]
    Vd = [res.results[c]["V"] for c in range(NCORES)]
    ar = [res.results[c]["arows"] for c in range(NCORES)]
    mk = [res.results[c]["mk2"] for c in range(NCORES)]
    res = build_D_attn(xd, Kd, Vd, ar, mk, ins["L1_ffn_Wqkv"], ins["L1_ffn_bqkv"],
                       ins["L1_ffn_Wo"], ins["L1_ffn_bo"],
                       _col(ins["L1_ffn_ln1_w"], 8), _col(ins["L1_ffn_ln1_b"], 8)).run()
    xd1 = [res.results[c]["x4"] for c in range(NCORES)]
    fc1T = np.ascontiguousarray(ins["L1_ffn_fc1_w"].transpose(0, 2, 1))
    fc2T = np.ascontiguousarray(ins["L1_ffn_fc2_w"].transpose(0, 2, 1))
    b1cs = np.concatenate([_col(ins["L1_ffn_fc1_b"][e], 32) for e in range(8)], axis=1)
    b2cs = np.concatenate([_col(ins["L1_ffn_fc2_b"][e], 8) for e in range(8)], axis=1)
    res = build_D_moe(xd1, _col(ins["L1_ffn_ln2_w"], 8), _col(ins["L1_ffn_ln2_b"], 8),
                      ins["L1_ffn_gate_W"], fc1T, b1cs, fc2T, b2cs).run()
    y_in = [res.results[c]["y"] for c in range(NCORES)]
    res = build_add_pass(x3, y_in, b_pad=0, name="d_add").run()
    if ctx is not None:
        _FUSE[0] = None
        ctx.finalize()
        res = ctx.run()
    y = [res.results[c]["y"] for c in range(NCORES)]

    return np.stack([np.ascontiguousarray(y[c].T) for c in range(NCORES)])
